# version 14
# baseline (speedup 1.0000x reference)
import numpy as np
import ml_dtypes
from contextlib import ExitStack, contextmanager
import concourse.bass as bass
import concourse.mybir as mybir
from concourse.bass_utils import run_bass_kernel_spmd

F32 = mybir.dt.float32
BF16 = mybir.dt.bfloat16
I32 = mybir.dt.int32
AF = mybir.ActivationFunctionType
ALU = mybir.AluOpType
AX = mybir.AxisListType

N_CORES = 8
L, D, H, DK = 4096, 1024, 4, 256
NT, NST, G, HID = 32, 8, 8, 3072

DEBUG = []

ENG2BLK = {"pe": "tensor", "dve": "vector", "act": "scalar",
           "pool": "gpsimd", "sync": "sync"}


class Res:
    __slots__ = ("name", "w_tok", "readers")

    def __init__(self, name):
        self.name = name
        self.w_tok = None
        self.readers = {}


class Prog:
    def __init__(self, nc):
        self.nc = nc
        self.steps = {e: [] for e in ENG2BLK}
        self.cnt = {e: 0 for e in ENG2BLK if e != "sync"}
        self.dma_cnt = 0
        self.hwm = {}
        self.in_stage = False
        self.touch = None
        self.emitted = None
        self.sems = {}
        self._res = {}
        self.dma_safe = {e: 0 for e in ("pe", "dve", "act", "pool")}

    def res(self, name):
        if name not in self._res:
            self._res[name] = Res(name)
        return self._res[name]

    def make_sems(self, es):
        for e in ("pe", "dve", "act", "pool", "dma", "dmadone"):
            self.sems[e] = es.enter_context(self.nc.semaphore("sem_" + e))
        self.dmadone_cnt = 0
        self.dmadone_pub = {}

    def raw(self, eng, fn):
        self.steps[eng].append(fn)

    def _wait(self, waiter, tok, gate=False):
        key, val = tok
        if gate and waiter == "sync" and key in ("pe", "dve", "act", "pool"):
            if self.dma_safe[key] < val:
                self.cnt[key] += 1
                sem0 = self.sems[key]
                self.raw(key, lambda e, s=sem0: e.drain().then_inc(s, 1))
                self.dma_safe[key] = self.cnt[key]
            val = self.dma_safe[key]
            tok = (key, val)
        if key == "dma" and waiter != "sync":
            total = self.dma_cnt
            if total not in self.dmadone_pub:
                self.dmadone_cnt += 1
                self.dmadone_pub[total] = self.dmadone_cnt
                ds, dd = self.sems["dma"], self.sems["dmadone"]
                if self.hwm.get(("sync", "dma"), 0) < total:
                    self.hwm[("sync", "dma")] = total
                    self.raw("sync", lambda e, s=ds, v=total: e.wait_ge(s, v))
                self.raw("sync", lambda e, s=dd: e.sem_inc(s, 1))
            key, val = "dmadone", self.dmadone_pub[total]
        if self.hwm.get((waiter, key), 0) >= val:
            return
        self.hwm[(waiter, key)] = val
        sem = self.sems[key]
        self.raw(waiter, lambda e, s=sem, v=val: e.wait_ge(s, v))

    def _hazards(self, eng, r, w):
        for b in r:
            if b.w_tok is not None:
                self._wait(eng, b.w_tok)
        for b in w:
            if b.w_tok is not None:
                self._wait(eng, b.w_tok)
            for rk, rv in b.readers.items():
                self._wait(eng, (rk, rv))

    def do(self, eng, fn, r=(), w=()):
        assert self.in_stage
        self._hazards(eng, r, w)
        self.raw(eng, fn)
        self.last_op_idx[eng] = len(self.steps[eng]) - 1
        self.emitted.add(eng)
        for b in r:
            self.touch.append((eng, b, "r"))
        for b in w:
            self.touch.append((eng, b, "w"))

    def dma(self, out_ap, in_ap, r=(), w=()):
        assert self.in_stage
        for b in r:
            if b.w_tok is not None:
                self._wait("sync", b.w_tok, gate=True)
        self._hazards("sync", (), w)
        self.dma_cnt += 16
        val = self.dma_cnt
        sem = self.sems["dma"]
        self.raw("sync",
                 lambda e, o=out_ap, i=in_ap, s=sem: e.dma_start(out=o, in_=i)
                 .then_inc(s, 16))
        for b in r:
            b.readers["dma"] = max(b.readers.get("dma", 0), val)
        for b in w:
            b.w_tok = ("dma", val)
            b.readers = {}

    @contextmanager
    def stage(self, use_drain=False, dma_out=False):
        use_drain = use_drain or dma_out
        assert not self.in_stage
        self.in_stage = True
        self.touch, self.emitted = [], set()
        self.last_op_idx = {}
        yield
        touched, emitted = self.touch, self.emitted
        self.in_stage = False
        self.touch = self.emitted = None
        toks = {}
        for e in emitted:
            self.cnt[e] += 1
            toks[e] = (e, self.cnt[e])
            sem = self.sems[e]
            if not use_drain:
                idx = self.last_op_idx[e]
                orig = self.steps[e][idx]
                self.steps[e][idx] = (
                    lambda en, o=orig, s=sem: o(en).then_inc(s, 1))
            else:
                self.raw(e, lambda en, s=sem: en.drain().then_inc(s, 1))
                self.dma_safe[e] = self.cnt[e]
        for eng, b, kind in touched:
            tok = toks[eng]
            if kind == "w":
                b.w_tok = tok
                b.readers = {}
            else:
                k, v = tok
                b.readers[k] = max(b.readers.get(k, 0), v)

    def barrier(self):
        with self.stage(use_drain=True):
            for e in ("pe", "dve", "act", "pool"):
                self.do(e, lambda en: en.nop())
        toks = [(e, self.cnt[e]) for e in ("pe", "dve", "act", "pool")]
        toks.append(("dma", self.dma_cnt))
        for e in ("pe", "dve", "act", "pool", "sync"):
            for t in toks:
                self._wait(e, t)

    def finalize(self, blk):
        for e, bname in ENG2BLK.items():
            fns = self.steps[e]
            getattr(blk, bname)(lambda eng, fns=fns: [f(eng) for f in fns])


def host_prep(inputs):
    f = lambda k: np.asarray(inputs[k])
    qkv = np.concatenate([f("L0_attn_q_w"), f("L0_attn_k_w"),
                          f("L0_attn_v_w")], axis=1)
    bw = f("L0_attn_b_w")

    def taps(name):
        return f(name)[:, 0, ::-1]
    wtab = np.concatenate([taps("L0_attn_qc_w"), taps("L0_attn_kc_w"),
                           taps("L0_attn_vc_w"), taps("L0_attn_fir_w1"),
                           taps("L0_attn_fir_w3"), taps("L0_attn_fir_w7"),
                           taps("L0_attn_fir_w31")], axis=1)
    sel = np.eye(24, dtype=np.float32)[:, :, None]
    c = lambda a: np.ascontiguousarray(a)
    return dict(
        qkv=c(qkv.astype(np.float32)),
        bw=c(bw.astype(np.float32)),
        w1=c(f("L0_attn_mlp_w1").astype(np.float32)),
        w2=c(f("L0_attn_mlp_w2").astype(np.float32)),
        ow=c(f("L0_attn_o_w").astype(np.float32)),
        wtab=c(wtab.astype(np.float32)),
        wgu=c(np.concatenate([f("L0_ffn_w_gate").T, f("L0_ffn_w_up").T], axis=1)),
        wdT=c(f("L0_ffn_w_down").T),
        b1fm=c(f("L0_attn_mlp_b1").reshape(8, 128).T.astype(np.float32)),
        onorm2=c(f("L0_attn_onorm_w").reshape(2, 128).T.astype(np.float32)),
        b2row=c(f("L0_attn_mlp_b2").reshape(1, 24).astype(np.float32)),
        gltrow=c(f("L0_attn_gate_log_temp").reshape(1, 4).astype(np.float32)),
        id128=np.eye(128, dtype=np.float32),
        maskUs=c(np.triu(np.ones((128, 128), np.float32), 1)),
        maskUi=c(np.triu(np.ones((128, 128), np.float32), 0)),
        sel24=c(sel.reshape(24, 24)),
        ones1r=np.ones((1, 128), np.float32),
    )


class Ctx:
    pass


def build_program():
    nc = bass.Bass("TRN2", target_bir_lowering=False)
    C = Ctx()
    C.nc = nc
    d = lambda n, sh, dt, kind: nc.dram_tensor(n, sh, dt, kind=kind).ap()
    C.x_h = d("x", [L, D], F32, "ExternalInput")
    C.qkv_h = d("qkv", [D, 3072], F32, "ExternalInput")
    C.bw_h = d("bw", [D, 4], F32, "ExternalInput")
    C.w1_h = d("w1", [1120, D], F32, "ExternalInput")
    C.w2_h = d("w2", [D, 24], F32, "ExternalInput")
    C.ow_h = d("ow", [D, D], F32, "ExternalInput")
    C.wtab_h = d("wtab", [D, 54], F32, "ExternalInput")
    C.wgu_h = d("wgu", [D, 2 * HID], I32, "ExternalInput")
    C.wdT_h = d("wdT", [HID, D], I32, "ExternalInput")
    C.b1fm_h = d("b1fm", [128, 8], F32, "ExternalInput")
    C.onorm2_h = d("onorm2", [128, 2], F32, "ExternalInput")
    C.b2row_h = d("b2row", [1, 24], F32, "ExternalInput")
    C.gltrow_h = d("gltrow", [1, 4], F32, "ExternalInput")
    C.id128_h = d("id128", [128, 128], F32, "ExternalInput")
    C.maskUs_h = d("maskUs", [128, 128], F32, "ExternalInput")
    C.maskUi_h = d("maskUi", [128, 128], F32, "ExternalInput")
    C.sel24_h = d("sel24", [24, 24], F32, "ExternalInput")
    C.ones1r_h = d("ones1r", [1, 128], F32, "ExternalInput")
    C.out_h = d("out", [L, D], F32, "ExternalOutput")
    C.dbg = {}
    if "xn" in DEBUG:
        C.dbg["xn"] = nc.dram_tensor("dbg_xn", [NT, 128, D], BF16,
                                     kind="ExternalOutput").ap()
    for name, sh, dt_ in [("dd", [128, 4096], BF16), ("dd2", [128, 2048], BF16),
                          ("rst", [128, 16], F32),
                          ("q", [128, 4096], BF16), ("k", [128, 4096], BF16),
                          ("v", [128, 4096], BF16), ("beta", [128, 16], F32),
                          ("delta", [128, 1024], BF16), ("stats", [128, 96], F32),
                          ("wgt", [128, 24], F32), ("s", [128, 1024], F32),
                          ("hg", [128, 1024], F32), ("g", [128, 1024], F32)]:
        if name in DEBUG:
            C.dbg[name] = d("dbg_" + name, sh, dt_, "ExternalOutput")
    C.xn_s = d("xn_s", [NT, 128, D], BF16, "Internal")
    C.s_s = d("s_s", [NT, 128, D], F32, "Internal")
    C.diag_s = d("diag_s", [G, 128, 54 * 128], BF16, "Internal")
    C.ow_s = d("ow_s", [8, 128, D], BF16, "Internal")
    C.qkv_s = d("qkv_s", [12, 128, 2048], BF16, "Internal")
    C.wgu_s = d("wgu_s", [8, 128, 2 * HID], BF16, "Internal")
    C.wdT_s = d("wdT_s", [24, 128, D], BF16, "Internal")

    es = ExitStack()
    P = Prog(nc)
    P.make_sems(es)
    sbg = lambda n, sh, dt=F32: es.enter_context(nc.sbuf_tensor(n, sh, dt))

    C.idb = sbg("idb", [128, 128], BF16)
    C.mUsb = sbg("mUsb", [128, 128], BF16)
    C.mUib = sbg("mUib", [128, 128], BF16)
    C.i4b = sbg("i4b", [128, 512], BF16)
    C.onesb = sbg("onesb", [128, 1], BF16)
    C.onesrow = sbg("onesrow", [1, 128], BF16)
    C.rows = sbg("rows", [1, 96])
    C.tiny = sbg("tiny", [128, 32])
    C.invtempT = sbg("invtempT", [128, 24])
    C.b2i = sbg("b2i", [128, 24])
    C.b1fm = sbg("b1fm_s", [128, 8])
    C.wtab = sbg("wtab_s", [128, G * 54])
    C.w2b = sbg("w2b", [128, 8 * 24], BF16)
    C.wbb = sbg("wbb", [128, 8 * 4], BF16)
    C.sel24b = sbg("sel24b", [24, 24], BF16)
    C.constt = sbg("constt", [128, 8])
    CONSTS = [1e-5, 1e-6, 1e-30, 1.0]
    for i, v in enumerate(CONSTS):
        nc.const_aps.register(F32, v, C.constt[:, i:i + 1])

    C.PA = es.enter_context(nc.psum_tensor("PA", [128, 1024], F32))
    C.PB = es.enter_context(nc.psum_tensor("PB", [128, 1024], F32))
    C.PC = es.enter_context(nc.psum_tensor("PC", [128, 1024], F32))
    C.PS = es.enter_context(nc.psum_tensor("PS", [128, 512], F32))
    C.PT = es.enter_context(nc.psum_tensor("PT", [128, 1024], BF16))
    C.P = P

    with es, nc.Block() as blk:
        with P.stage():
            for i, v in enumerate(CONSTS):
                P.do("dve", lambda e, i=i, v=v: e.memset(C.constt[:, i:i + 1], v),
                     w=[P.res("constt")])
        P.barrier()
        with ExitStack() as es_pa:
            sba = lambda n, sh, dt=F32: es_pa.enter_context(nc.sbuf_tensor(n, sh, dt))
            C.w1b = sba("w1b", [128, 9 * 1024], BF16)
            with ExitStack() as es_p0:
                phase0(C, es_p0)
            P.barrier()
            with ExitStack() as es_a:
                phaseA(C, es_a)
        P.barrier()
        with ExitStack() as es_b:
            phaseB(C, es_b)
        P.raw("sync", lambda e, s=P.sems["dma"], v=P.dma_cnt: e.wait_ge(s, v))
        P.finalize(blk)
    return nc, list(C.dbg.keys())


def phase0(C, es):
    nc, P = C.nc, C.P
    R = P.res
    sb = lambda n, sh, dt=F32: es.enter_context(nc.sbuf_tensor(n, sh, dt))
    stg_f = [sb("p0f0", [128, 3072]), sb("p0f1", [128, 3072])]
    stg_i = [sb("p0i0", [128, 3072], I32), sb("p0i1", [128, 3072], I32)]
    stg_b = [sb("p0b0", [128, 3072], BF16), sb("p0b1", [128, 3072], BF16)]
    dgb = [sb("p0d0", [128, 54 * 128], BF16), sb("p0d1", [128, 54 * 128], BF16)]
    cons = sb("p0cons", [128, 512])
    on2 = sb("p0on2", [128, 2])
    NW = 4
    xt = [sb("p0x%d" % i, [128, D]) for i in range(NW)]
    xsq = [sb("p0sq%d" % i, [128, D]) for i in range(NW)]
    xnt = [sb("p0xnt%d" % i, [128, D], BF16) for i in range(NW)]
    xnf = [sb("p0xnf%d" % i, [128, D], BF16) for i in range(NW)]

    rf = lambda i: R("p0f%d" % i)
    ri = lambda i: R("p0i%d" % i)
    rb = lambda i: R("p0b%d" % i)

    with P.stage():
        P.dma(cons[:, 0:128], C.id128_h[:], w=[R("cons")])
        P.dma(cons[:, 128:256], C.maskUs_h[:], w=[R("cons")])
        P.dma(cons[:, 256:384], C.maskUi_h[:], w=[R("cons")])
        P.dma(C.rows[0:1, 0:24], C.b2row_h[:], w=[R("rows")])
        P.dma(C.rows[0:1, 24:28], C.gltrow_h[:], w=[R("rows")])
        P.dma(C.b1fm[:], C.b1fm_h[:], w=[R("b1fm")])
        P.dma(on2[:], C.onorm2_h[:], w=[R("on2")])
        P.dma(C.wtab[:].rearrange("p (g w) -> p g w", g=8),
              C.wtab_h.rearrange("(g p) w -> p g w", p=128), w=[R("wtab")])
        P.dma(stg_f[0][0:24, 0:24], C.sel24_h[:], w=[rf(0)])
        P.dma(stg_f[1][0:1, 0:128], C.ones1r_h[:], w=[rf(1)])
    with P.stage():
        P.do("dve", lambda e: e.tensor_copy(out=C.idb[:], in_=cons[:, 0:128]),
             r=[R("cons")], w=[R("idb")])
        P.do("dve", lambda e: e.tensor_copy(out=C.mUsb[:], in_=cons[:, 128:256]),
             r=[R("cons")], w=[R("mUsb")])
        P.do("dve", lambda e: e.tensor_copy(out=C.mUib[:], in_=cons[:, 256:384]),
             r=[R("cons")], w=[R("mUib")])
        P.do("dve", lambda e: e.memset(C.onesb[:], 1.0), w=[R("onesb")])
        P.do("dve", lambda e: e.tensor_copy(out=C.onesrow[:],
             in_=stg_f[1][0:1, 0:128]), r=[rf(1)], w=[R("onesrow")])
        P.do("dve", lambda e: e.tensor_copy(out=C.sel24b[:],
             in_=stg_f[0][0:24, 0:24]), r=[rf(0)], w=[R("sel24b")])
        P.do("pool", lambda e: e.tensor_copy(out=C.i4b[:, 0:128], in_=cons[:, 0:128]),
             r=[R("cons")], w=[R("i4b")])
        for j in range(1, 4):
            P.do("pool", lambda e, j=j: e.tensor_copy(
                out=C.i4b[:, j * 128:(j + 1) * 128], in_=cons[:, 0:128]),
                r=[R("cons")], w=[R("i4b")])

    ro = C.rows
    with P.stage():
        P.do("dve", lambda e: e.tensor_scalar(out=ro[0:1, 28:32], in0=ro[0:1, 24:28],
             scalar1=0.0, scalar2=None, op0=ALU.max), r=[R("rows")], w=[R("rows")])
        P.do("act", lambda e: e.activation(out=ro[0:1, 32:36], in_=ro[0:1, 24:28],
             func=AF.Abs), r=[R("rows")], w=[R("rows2")])
    with P.stage():
        P.do("act", lambda e: e.activation(out=ro[0:1, 32:36], in_=ro[0:1, 32:36],
             func=AF.Exp, scale=-1.0), r=[R("rows2")], w=[R("rows2")])
    with P.stage():
        P.do("act", lambda e: e.activation(out=ro[0:1, 32:36], in_=ro[0:1, 32:36],
             func=AF.Ln, bias=1.0), r=[R("rows2")], w=[R("rows2")])
    with P.stage():
        P.do("dve", lambda e: e.tensor_tensor(out=ro[0:1, 36:40], in0=ro[0:1, 28:32],
             in1=ro[0:1, 32:36], op=ALU.add), r=[R("rows"), R("rows2")], w=[R("rows")])
    with P.stage():
        P.do("dve", lambda e: e.tensor_scalar(out=ro[0:1, 36:40], in0=ro[0:1, 36:40],
             scalar1=0.5, scalar2=None, op0=ALU.add), r=[R("rows")], w=[R("rows")])
    with P.stage():
        P.do("dve", lambda e: e.reciprocal(out=ro[0:1, 40:44], in_=ro[0:1, 36:40]),
             r=[R("rows")], w=[R("rows")])
    with P.stage():
        P.do("dve", lambda e: e.tensor_copy(
            out=ro[0:1, 44:68].rearrange("p (h s) -> p h s", h=4),
            in_=ro[0:1, 40:44].unsqueeze(2).broadcast_to([1, 4, 6])),
            r=[R("rows")], w=[R("rows")])
    row_b = es.enter_context(nc.sbuf_tensor("p0rowb", [1, 64], BF16))
    with P.stage():
        P.do("dve", lambda e: e.tensor_copy(out=row_b[0:1, 0:24],
             in_=ro[0:1, 44:68]), r=[R("rows")], w=[R("rowb")])
        P.do("dve", lambda e: e.tensor_copy(out=row_b[0:1, 32:56],
             in_=ro[0:1, 0:24]), r=[R("rows")], w=[R("rowb")])
    with P.stage():
        P.do("pe", lambda e: e.matmul(C.PS[:, 0:24], lhsT=C.onesrow[:],
             rhs=row_b[0:1, 0:24], start=True, stop=True),
             r=[R("rowb"), R("onesrow")], w=[R("PS")])
        P.do("pe", lambda e: e.matmul(C.PS[:, 32:56], lhsT=C.onesrow[:],
             rhs=row_b[0:1, 32:56], start=True, stop=True),
             r=[R("rowb"), R("onesrow")], w=[R("PS")])
    with P.stage():
        P.do("dve", lambda e: e.tensor_copy(out=C.invtempT[:], in_=C.PS[:, 0:24]),
             r=[R("PS")], w=[R("invtempT")])
    with P.stage():
        P.do("dve", lambda e: e.tensor_tensor(out=C.b2i[:], in0=C.PS[:, 32:56],
             in1=C.invtempT[:], op=ALU.mult),
             r=[R("PS"), R("invtempT")], w=[R("b2i")])

    for k in range(8):
        p = k % 2
        with P.stage():
            P.dma(stg_f[p][:], C.qkv_h[k * 128:(k + 1) * 128, :], w=[rf(p)])
        with P.stage(dma_out=True):
            P.do("dve", lambda e, p=p: e.tensor_copy(out=stg_b[p][:],
                 in_=stg_f[p][:]), r=[rf(p)], w=[rb(p)])
        with P.stage():
            P.dma(C.qkv_s[:, :, k * 256:(k + 1) * 256].transpose([1, 0, 2]),
                  stg_b[p][:].rearrange("p (m n) -> p m n", m=12), r=[rb(p)])
    with P.stage():
        P.dma(stg_f[0][:, 0:32].rearrange("p (k n) -> p k n", k=8),
              C.bw_h.rearrange("(k p) n -> p k n", p=128), w=[rf(0)])
    with P.stage():
        P.do("dve", lambda e: e.tensor_copy(out=C.wbb[:], in_=stg_f[0][:, 0:32]),
             r=[rf(0)], w=[R("wbb")])
    for k in range(9):
        rows_n = 128 if k < 8 else 96
        p = k % 2
        with P.stage():
            P.dma(stg_f[p][0:rows_n, 0:1024], C.w1_h[k * 128:k * 128 + rows_n, :],
                  w=[rf(p)])
        with P.stage():
            P.do("dve", lambda e, k=k, rows_n=rows_n, p=p: e.tensor_copy(
                out=C.w1b[0:rows_n, k * 1024:(k + 1) * 1024],
                in_=stg_f[p][0:rows_n, 0:1024]), r=[rf(p)], w=[R("w1b")])
    with P.stage():
        P.dma(stg_f[0][:, 0:192].rearrange("p (c n) -> p c n", c=8),
              C.w2_h.rearrange("(c p) n -> p c n", p=128), w=[rf(0)])
    with P.stage():
        P.do("dve", lambda e: e.tensor_copy(out=C.w2b[:], in_=stg_f[0][:, 0:192]),
             r=[rf(0)], w=[R("w2b")])
    for k in range(8):
        p = k % 2
        with P.stage():
            P.dma(stg_f[p][:, 0:1024], C.ow_h[k * 128:(k + 1) * 128, :], w=[rf(p)])
        with P.stage(dma_out=True):
            P.do("dve", lambda e, p=p, k=k: e.tensor_scalar(
                out=stg_b[p][:, 0:1024], in0=stg_f[p][:, 0:1024],
                scalar1=on2[:, k % 2:k % 2 + 1], scalar2=None, op0=ALU.mult),
                r=[rf(p), R("on2")], w=[rb(p)])
        with P.stage():
            P.dma(C.ow_s[k], stg_b[p][:, 0:1024], r=[rb(p)])
    for k in range(8):
        for half in range(2):
            p = (2 * k + half) % 2
            with P.stage():
                P.dma(stg_i[p][:], C.wgu_h[k * 128:(k + 1) * 128,
                                           half * HID:(half + 1) * HID], w=[ri(p)])
            with P.stage(dma_out=True):
                P.do("dve", lambda e, p=p: e.tensor_copy(out=stg_b[p][:],
                     in_=stg_i[p][:]), r=[ri(p)], w=[rb(p)])
            with P.stage():
                P.dma(C.wgu_s[k][:, half * HID:(half + 1) * HID], stg_b[p][:],
                      r=[rb(p)])
    for c in range(24):
        p = c % 2
        with P.stage():
            P.dma(stg_i[p][:, 0:1024], C.wdT_h[c * 128:(c + 1) * 128, :], w=[ri(p)])
        with P.stage(dma_out=True):
            P.do("dve", lambda e, p=p: e.tensor_copy(out=stg_b[p][:, 0:1024],
                 in_=stg_i[p][:, 0:1024]), r=[ri(p)], w=[rb(p)])
        with P.stage():
            P.dma(C.wdT_s[c], stg_b[p][:, 0:1024], r=[rb(p)])
    for gg in range(G):
        p = gg % 2
        with P.stage(dma_out=True):
            for j in range(54):
                P.do("dve", lambda e, p=p, gg=gg, j=j: e.tensor_scalar(
                    out=dgb[p][:, j * 128:(j + 1) * 128], in0=C.idb[:],
                    scalar1=C.wtab[:, gg * 54 + j:gg * 54 + j + 1], scalar2=None,
                    op0=ALU.mult), r=[R("idb"), R("wtab")], w=[R("dg%d" % p)])
        with P.stage():
            P.dma(C.diag_s[gg], dgb[p][:], r=[R("dg%d" % p)])

    T = C.tiny
    for t in range(NT):
        p = t % 4
        xr, sqr = R("p0x%d" % p), R("p0sq%d" % p)
        ntr, nfr = R("p0xnt%d" % p), R("p0xnf%d" % p)
        ln = R("p0ln%d" % p)
        co = p * 8
        with P.stage():
            P.dma(xt[p][:], C.x_h[t * 128:(t + 1) * 128, :], w=[xr])
        with P.stage():
            P.do("act", lambda e, p=p: e.activation(out=xsq[p][:], in_=xt[p][:],
                 func=AF.Square), r=[xr], w=[sqr])
            P.do("dve", lambda e, p=p, co=co: e.reduce_sum(out=T[:, co:co + 1],
                 in_=xt[p][:], axis=AX.XYZW), r=[xr], w=[ln])
        with P.stage():
            P.do("dve", lambda e, p=p, co=co: e.reduce_sum(out=T[:, co + 1:co + 2],
                 in_=xsq[p][:], axis=AX.XYZW), r=[sqr, ln], w=[ln])
            P.do("dve", lambda e, co=co: e.tensor_scalar(out=T[:, co + 2:co + 3],
                 in0=T[:, co:co + 1], scalar1=1.0 / D, scalar2=None, op0=ALU.mult),
                 r=[ln], w=[ln])
        with P.stage():
            P.do("dve", lambda e, co=co: e.tensor_tensor(out=T[:, co + 3:co + 4],
                 in0=T[:, co + 2:co + 3], in1=T[:, co + 2:co + 3], op=ALU.mult),
                 r=[ln], w=[ln])
        with P.stage():
            P.do("dve", lambda e, co=co: e.scalar_tensor_tensor(
                 out=T[:, co + 4:co + 5], in0=T[:, co + 1:co + 2], scalar=1.0 / D,
                 in1=T[:, co + 3:co + 4], op0=ALU.mult, op1=ALU.subtract),
                 r=[ln], w=[ln])
        with P.stage():
            P.do("act", lambda e, co=co: e.activation(out=T[:, co + 5:co + 6],
                 in_=T[:, co + 4:co + 5], func=AF.Ln, bias=1e-5), r=[ln], w=[ln])
        with P.stage():
            P.do("act", lambda e, co=co: e.activation(out=T[:, co + 6:co + 7],
                 in_=T[:, co + 5:co + 6], func=AF.Exp, scale=-0.5), r=[ln], w=[ln])
        with P.stage():
            P.do("dve", lambda e, p=p, co=co: e.tensor_scalar(out=xsq[p][:],
                 in0=xt[p][:], scalar1=T[:, co + 2:co + 3], scalar2=None,
                 op0=ALU.subtract), r=[xr, ln], w=[sqr])
        with P.stage():
            P.do("dve", lambda e, p=p, co=co: e.tensor_scalar(out=xnt[p][:],
                 in0=xsq[p][:], scalar1=T[:, co + 6:co + 7], scalar2=None,
                 op0=ALU.mult), r=[sqr, ln], w=[ntr])
        with P.stage():
            for cch in range(8):
                P.do("pe", lambda e, p=p, cch=cch: e.transpose(
                    C.PT[:, cch * 128:(cch + 1) * 128],
                    in_=xnt[p][:, cch * 128:(cch + 1) * 128], identity=C.idb[:]),
                    r=[ntr, R("idb")], w=[R("PT")])
        with P.stage(dma_out=True):
            P.do("dve", lambda e, p=p: e.tensor_copy(out=xnf[p][:],
                 in_=C.PT[:, 0:1024]), r=[R("PT")], w=[nfr])
        with P.stage():
            P.dma(C.xn_s[t], xnf[p][:], r=[nfr])
            if "xn" in C.dbg:
                P.dma(C.dbg["xn"][t], xnf[p][:], r=[nfr])


def phaseA(C, es):
    nc, P = C.nc, C.P
    R = P.res
    sb = lambda n, sh, dt=F32: es.enter_context(nc.sbuf_tensor(n, sh, dt))
    B = Ctx()
    B.xnst = sb("a_xnst", [128, 4096], BF16)
    B.qpre = sb("a_qpre", [128, 8 * 516], BF16)
    B.kpre = sb("a_kpre", [128, 8 * 516], BF16)
    B.vpre = sb("a_vpre", [128, 8 * 516], BF16)
    B.vf = sb("a_vf", [128, 8 * 544], BF16)
    B.f3 = sb("a_f3", [128, 4096], BF16)
    B.f7 = sb("a_f7", [128, 4096], BF16)
    B.f31 = sb("a_f31", [128, 4096], BF16)
    B.wpair = [sb("a_wp0", [128, 2048], BF16), sb("a_wp1", [128, 2048], BF16)]
    B.dg = [sb("a_dg0", [128, 54 * 128], BF16), sb("a_dg1", [128, 54 * 128], BF16)]
    B.betaf = sb("a_betaf", [4, 512], BF16)
    B.betat = sb("a_betat", [128, 16])
    B.rst = sb("a_rst", [128, 32])
    B.rr2 = sb("a_rr2", [128, 8])
    B.sq2 = sb("a_sq2", [128, 1024], BF16)
    B.kntm = sb("a_kntm", [128, 2048], BF16)
    B.kbtm = sb("a_kbtm", [128, 2048], BF16)
    B.vbtm = sb("a_vbtm", [128, 2048], BF16)
    B.wnf = sb("a_wnf", [128, 2048], BF16)
    B.Apre = sb("a_apre", [128, 1024], BF16)
    B.ATs = sb("a_ats0", [128, 1024], BF16)
    B.As = B.Apre
    B.XXT = [sb("a_xp0", [128, 2048], BF16), sb("a_xp1", [128, 2048], BF16)]
    B.PTs = sb("a_pts", [128, 1024], BF16)
    B.attb = sb("a_attb", [128, 1024], BF16)
    B.uhat = sb("a_uhat", [128, 1024], BF16)
    B.S = sb("a_S", [128, 2048])
    B.Sbb = sb("a_Sbb", [128, 2048], BF16)
    B.tmp_tm = sb("a_tmptm", [128, 1024], BF16)
    B.dfm2 = sb("a_dfm2", [128, 2048], BF16)
    B.b1f2 = sb("a_b1f2", [128, 2048], BF16)
    B.babs = sb("a_babs", [128, 1024], BF16)
    B.bsq = sb("a_bsq", [128, 1024], BF16)
    B.st96 = sb("a_st96", [128, 96], BF16)
    B.stT = sb("a_stT", [96, 128], BF16)
    B.l24 = sb("a_l24", [128, 24])
    B.w24 = sb("a_w24", [128, 24], BF16)
    B.wgtT = sb("a_wgtT", [24, 128], BF16)
    B.hgst = sb("a_hgst", [128, 2048], BF16)
    B.facc = sb("a_facc", [128, 1024], BF16)
    B.tmpc = sb("a_tmpc", [128, 1024], BF16)
    B.ffm = sb("a_ffm", [128, 1024], BF16)
    B.sot = sb("a_sot", [128, 1024])
    B.rrow = sb("a_rrow", [1, 512], BF16)
    B.owc = [sb("a_owc%d" % j, [128, 1024], BF16) for j in range(3)]
    PA, PB, PC, PS, PT = C.PA, C.PB, C.PC, C.PS, C.PT
    B.half = lambda pt, i: pt[:, i * 512:(i + 1) * 512]

    with P.stage():
        P.do("dve", lambda e: e.memset(B.S[:], 0.0), w=[R("a_S")])
        P.do("pool", lambda e: e.memset(B.Sbb[:], 0.0), w=[R("a_Sbb")])
        P.do("pool", lambda e: e.memset(B.qpre[:], 0.0), w=[R("a_qpre")])
        P.do("pool", lambda e: e.memset(B.kpre[:], 0.0), w=[R("a_kpre")])
        P.do("pool", lambda e: e.memset(B.vpre[:], 0.0), w=[R("a_vpre")])
        P.do("dve", lambda e: e.memset(B.vf[:], 0.0), w=[R("a_vf")])

    for st in range(NST):
        t0 = st * 4
        with P.stage():
            for i in range(4):
                P.dma(B.xnst[:].rearrange("p (g t) -> p g t", g=8)
                      [:, :, i * 128:(i + 1) * 128],
                      C.xn_s[t0 + i].rearrange("p (g t) -> p g t", g=8),
                      w=[R("a_xnst")])
            P.dma(B.dg[0][:], C.diag_s[0], w=[R("a_dg0")])
            P.dma(B.dg[1][:], C.diag_s[1], w=[R("a_dg1")])
            P.dma(B.wpair[0][:], C.qkv_s[0], w=[R("a_wp0")])
            P.dma(B.wpair[1][:], C.qkv_s[1], w=[R("a_wp1")])
        dsts = [(B.qpre, "a_qpre"), (B.kpre, "a_kpre"), (B.vpre, "a_vpre")]
        for pair in range(12):
            pw = pair % 2
            dstbuf, dstres = dsts[pair // 4]
            PP, ppres = (PA, ("PA0", "PA1")) if pair % 2 == 0 else (PB, ("PB0", "PB1"))
            with P.stage():
                for m2 in range(2):
                    for k in range(8):
                        P.do("pe", lambda e, pw=pw, m2=m2, k=k, PP=PP: e.matmul(
                            B.half(PP, m2),
                            lhsT=B.wpair[pw][:, k * 256 + m2 * 128:k * 256 + m2 * 128 + 128],
                            rhs=B.xnst[:, k * 512:(k + 1) * 512],
                            start=(k == 0), stop=(k == 7)),
                            r=[R("a_wp%d" % pw), R("a_xnst")],
                            w=[R(ppres[m2])])
            with P.stage():
                if pair + 2 < 12:
                    P.dma(B.wpair[pw][:], C.qkv_s[pair + 2], w=[R("a_wp%d" % pw)])
                for m2 in range(2):
                    gout = (pair % 4) * 2 + m2
                    if m2 == 0:
                        P.do("dve", lambda e, PP=PP, dstbuf=dstbuf, gout=gout:
                             e.tensor_copy(out=dstbuf[:, gout * 516 + 4:gout * 516 + 516],
                                           in_=B.half(PP, 0)),
                             r=[R(ppres[0])], w=[R(dstres)])
                    else:
                        P.do("act", lambda e, PP=PP, dstbuf=dstbuf, gout=gout:
                             e.activation(out=dstbuf[:, gout * 516 + 4:gout * 516 + 516],
                                          in_=B.half(PP, 1), func=AF.Identity),
                             r=[R(ppres[1])], w=[R(dstres)])
        with P.stage():
            for k in range(8):
                P.do("pe", lambda e, k=k: e.matmul(
                    PS[0:4, 0:512], lhsT=C.wbb[:, k * 4:(k + 1) * 4],
                    rhs=B.xnst[:, k * 512:(k + 1) * 512],
                    start=(k == 0), stop=(k == 7)),
                    r=[R("wbb"), R("a_xnst")], w=[R("PS")])
        with P.stage():
            P.do("act", lambda e: e.activation(out=B.betaf[:], in_=PS[0:4, 0:512],
                 func=AF.Sigmoid), r=[R("PS")], w=[R("a_betaf")])
        with P.stage():
            for i in range(4):
                P.do("pe", lambda e, i=i: e.transpose(
                    PT[:, i * 4:(i + 1) * 4],
                    in_=B.betaf[0:4, i * 128:(i + 1) * 128], identity=C.idb[0:4, 0:4]),
                    r=[R("a_betaf"), R("idb")], w=[R("PT")])
        with P.stage():
            P.do("dve", lambda e: e.tensor_copy(out=B.betat[:], in_=PT[:, 0:16]),
                 r=[R("PT")], w=[R("a_betat")])
        CONV = [("q", B.qpre, "a_qpre", 0, 4), ("k", B.kpre, "a_kpre", 4, 4),
                ("v", B.vpre, "a_vpre", 8, 4)]
        pcrot = [("PA0", lambda: B.half(PA, 0)), ("PA1", lambda: B.half(PA, 1)),
                 ("PB0", lambda: B.half(PB, 0)), ("PB1", lambda: B.half(PB, 1)),
                 ("PC0", lambda: B.half(PC, 0)), ("PC1", lambda: B.half(PC, 1))]
        pcidx = 0
        for gg in range(8):
            dgr = R("a_dg%d" % (gg % 2))
            dgb_ = B.dg[gg % 2]
            for name, buf, bres, tbase, W in CONV:
                pres, pap = pcrot[pcidx % 6]
                pcidx += 1
                with P.stage():
                    for j in range(W):
                        P.do("pe", lambda e, dgb_=dgb_, tbase=tbase, j=j, buf=buf,
                             gg=gg, pap=pap, W=W: e.matmul(
                                 pap(), lhsT=dgb_[:, (tbase + j) * 128:(tbase + j + 1) * 128],
                                 rhs=buf[:, gg * 516 + 4 - j:gg * 516 + 4 - j + 512],
                                 start=(j == 0), stop=(j == W - 1)),
                            r=[dgr, R(bres)], w=[R(pres)])
                with P.stage():
                    P.do("dve", lambda e, buf=buf, gg=gg: e.tensor_copy(
                        out=buf[:, gg * 516 + 1:gg * 516 + 4],
                        in_=buf[:, gg * 516 + 513:gg * 516 + 516]),
                        r=[R(bres)], w=[R(bres)])
                with P.stage():
                    if name == "v":
                        P.do("act", lambda e, pap=pap, gg=gg: e.activation(
                            out=B.vf[:, gg * 544 + 32:gg * 544 + 544],
                            in_=pap(), func=AF.Silu),
                            r=[R(pres)], w=[R("a_vf")])
                    else:
                        P.do("act", lambda e, pap=pap, buf=buf, gg=gg: e.activation(
                            out=buf[:, gg * 516 + 4:gg * 516 + 516],
                            in_=pap(), func=AF.Silu),
                            r=[R(pres)], w=[R(bres)])
            with P.stage():
                P.do("dve", lambda e, gg=gg: e.tensor_copy(
                    out=B.vf[:, gg * 544 + 1:gg * 544 + 32],
                    in_=B.vf[:, gg * 544 + 513:gg * 544 + 544]),
                    r=[R("a_vf")], w=[R("a_vf")])
            firs = [(13, 3, B.f3, "a_f3"), (16, 7, B.f7, "a_f7"), (23, 31, B.f31, "a_f31")]
            fp = []
            for tbase, W, fbuf, fres in firs:
                pres, pap = pcrot[pcidx % 6]
                pcidx += 1
                fp.append((pres, pap, fbuf, fres))
                with P.stage():
                    for j in range(W):
                        P.do("pe", lambda e, dgb_=dgb_, tbase=tbase, j=j, gg=gg,
                             pap=pap, W=W: e.matmul(
                                 pap(), lhsT=dgb_[:, (tbase + j) * 128:(tbase + j + 1) * 128],
                                 rhs=B.vf[:, gg * 544 + 32 - j:gg * 544 + 32 - j + 512],
                                 start=(j == 0), stop=(j == W - 1)),
                            r=[dgr, R("a_vf")], w=[R(pres)])
            with P.stage():
                for ii, (pres, pap, fbuf, fres) in enumerate(fp):
                    eng = ["dve", "act", "dve"][ii]
                    if eng == "dve":
                        P.do("dve", lambda e, pap=pap, fbuf=fbuf, gg=gg: e.tensor_copy(
                            out=fbuf[:, gg * 512:(gg + 1) * 512], in_=pap()),
                            r=[R(pres)], w=[R(fres)])
                    else:
                        P.do("act", lambda e, pap=pap, fbuf=fbuf, gg=gg: e.activation(
                            out=fbuf[:, gg * 512:(gg + 1) * 512], in_=pap(),
                            func=AF.Identity), r=[R(pres)], w=[R(fres)])
            if gg + 2 < 8:
                with P.stage():
                    P.dma(B.dg[gg % 2][:], C.diag_s[gg + 2], w=[R("a_dg%d" % (gg % 2))])
        if "q" in C.dbg and st == 0:
            with P.stage():
                P.dma(C.dbg["q"][:], B.qpre[:, 0:4096], r=[R("a_qpre")])
                P.dma(C.dbg["k"][:], B.kpre[:, 0:4096], r=[R("a_kpre")])
                P.dma(C.dbg["v"][:], B.vf[:, 0:4096], r=[R("a_vf")])
        for ipair in range(2):
            pairA_prefix(C, P, R, B, st, ipair)
            for p in range(2):
                i = 2 * ipair + p
                tileA_suffix(C, P, R, B, st, i)
            with P.stage():
                P.do("act", lambda e: e.activation(out=B.hgst[:], in_=B.hgst[:],
                     func=AF.Gelu), r=[R("a_hgst")], w=[R("a_hgst")])
            for p in range(2):
                i = 2 * ipair + p
                tileA_phase2(C, P, R, B, st, i)


def pairA_prefix(C, P, R, B, st, ipair):
    PA, PB, PC, PS, PT = C.PA, C.PB, C.PC, C.PS, C.PT
    half = B.half
    o2 = ipair * 256
    qs2 = B.qpre[:].rearrange("p (c n) -> p c n", c=8)[:, :, 4 + o2:4 + o2 + 256]
    ks2 = B.kpre[:].rearrange("p (c n) -> p c n", c=8)[:, :, 4 + o2:4 + o2 + 256]
    qs = lambda ch, p, o2=o2: B.qpre[:, ch * 516 + 4 + o2 + p * 128:
                                     ch * 516 + 4 + o2 + p * 128 + 128]
    ks = lambda ch, p, o2=o2: B.kpre[:, ch * 516 + 4 + o2 + p * 128:
                                     ch * 516 + 4 + o2 + p * 128 + 128]
    vs = lambda ch, p, o2=o2: B.vf[:, ch * 544 + 32 + o2 + p * 128:
                                   ch * 544 + 32 + o2 + p * 128 + 128]
    sqb = [B.sq2, B.bsq]
    sqr_ = ["a_sq2", "a_bsq"]
    for which, stk2, colbase in (("q", qs2, 0), ("k", ks2, 4)):
        sres = "a_qpre" if which == "q" else "a_kpre"
        with P.stage():
            for p in range(2):
                P.do("dve", lambda e, stk2=stk2, p=p: e.tensor_tensor(
                    out=sqb[p][:].rearrange("p (c n) -> p c n", c=8),
                    in0=stk2[:, :, p * 128:(p + 1) * 128],
                    in1=stk2[:, :, p * 128:(p + 1) * 128], op=ALU.mult),
                    r=[R(sres)], w=[R(sqr_[p])])
        with P.stage():
            for p in range(2):
                for hh in range(4):
                    for cc in range(2):
                        P.do("pe", lambda e, p=p, hh=hh, cc=cc, colbase=colbase: e.matmul(
                            PS[:, p * 16 + colbase + hh:p * 16 + colbase + hh + 1],
                            lhsT=sqb[p][:, (2 * hh + cc) * 128:(2 * hh + cc + 1) * 128],
                            rhs=C.onesb[:], start=(cc == 0), stop=(cc == 1)),
                            r=[R(sqr_[p]), R("onesb")], w=[R("PS")])
    with P.stage():
        for p in range(2):
            P.do("act", lambda e, p=p: e.activation(out=B.rst[:, p * 16:p * 16 + 8],
                 in_=PS[:, p * 16:p * 16 + 8], func=AF.Ln, bias=1e-6),
                 r=[R("PS")], w=[R("a_rst")])
    with P.stage():
        for p in range(2):
            P.do("act", lambda e, p=p: e.activation(out=B.rst[:, p * 16:p * 16 + 8],
                 in_=B.rst[:, p * 16:p * 16 + 8], func=AF.Exp, scale=-0.5),
                 r=[R("a_rst")], w=[R("a_rst")])
    with P.stage():
        for p in range(2):
            i = 2 * ipair + p
            P.do("dve", lambda e, p=p, i=i: e.tensor_tensor(
                out=B.rst[:, p * 16 + 8:p * 16 + 12], in0=B.rst[:, p * 16 + 4:p * 16 + 8],
                in1=B.betat[:, i * 4:(i + 1) * 4], op=ALU.mult),
                r=[R("a_rst"), R("a_betat")], w=[R("a_rst")])
    for p in range(2):
        pcres, pchalf = ("PC0", 0) if p == 0 else ("PC1", 1)
        with P.stage():
            for ch in range(8):
                P.do("pe", lambda e, ch=ch, p=p: e.transpose(
                    PT[:, ch * 128:(ch + 1) * 128], in_=ks(ch, p), identity=C.idb[:]),
                    r=[R("a_kpre"), R("idb")], w=[R("PT")])
            for hh in range(4):
                for cc in range(2):
                    P.do("pe", lambda e, hh=hh, cc=cc, p=p, pchalf=pchalf: e.matmul(
                        half(PC, pchalf)[:, hh * 128:(hh + 1) * 128],
                        lhsT=ks(2 * hh + cc, p), rhs=ks(2 * hh + cc, p),
                        start=(cc == 0), stop=(cc == 1)),
                        r=[R("a_kpre")], w=[R(pcres)])
        with P.stage():
            for hh in range(4):
                P.do("dve", lambda e, hh=hh, p=p: e.tensor_scalar(
                    out=B.kntm[:, p * 1024 + hh * 256:p * 1024 + (hh + 1) * 256],
                    in0=PT[:, hh * 256:(hh + 1) * 256],
                    scalar1=B.rst[:, p * 16 + 4 + hh:p * 16 + 5 + hh], scalar2=None,
                    op0=ALU.mult), r=[R("PT"), R("a_rst")], w=[R("a_kntm")])
                P.do("dve", lambda e, hh=hh, p=p: e.tensor_scalar(
                    out=B.kbtm[:, p * 1024 + hh * 256:p * 1024 + (hh + 1) * 256],
                    in0=PT[:, hh * 256:(hh + 1) * 256],
                    scalar1=B.rst[:, p * 16 + 8 + hh:p * 16 + 9 + hh], scalar2=None,
                    op0=ALU.mult), r=[R("PT"), R("a_rst")], w=[R("a_kbtm")])
        with P.stage():
            for ch in range(8):
                P.do("pe", lambda e, ch=ch, p=p: e.transpose(
                    PT[:, ch * 128:(ch + 1) * 128], in_=vs(ch, p), identity=C.idb[:]),
                    r=[R("a_vf"), R("idb")], w=[R("PT")])
        with P.stage():
            for hh in range(4):
                i = 2 * ipair + p
                P.do("dve", lambda e, hh=hh, p=p, i=i: e.tensor_scalar(
                    out=B.vbtm[:, p * 1024 + hh * 256:p * 1024 + (hh + 1) * 256],
                    in0=PT[:, hh * 256:(hh + 1) * 256],
                    scalar1=B.betat[:, i * 4 + hh:i * 4 + hh + 1], scalar2=None,
                    op0=ALU.mult), r=[R("PT"), R("a_betat")], w=[R("a_vbtm")])
    with P.stage():
        for p in range(2):
            for hh in range(4):
                P.do("dve", lambda e, hh=hh, p=p: e.tensor_scalar(
                    out=B.Apre[:, p * 512 + hh * 128:p * 512 + (hh + 1) * 128],
                    in0=half(PC, p)[:, hh * 128:(hh + 1) * 128],
                    scalar1=B.rst[:, p * 16 + 8 + hh:p * 16 + 9 + hh], scalar2=-1.0,
                    op0=ALU.mult, op1=ALU.mult),
                    r=[R("PC%d" % p), R("a_rst")], w=[R("a_apre")])
    with P.stage():
        for j in range(8):
            P.do("pe", lambda e, j=j: e.transpose(
                PT[:, j * 128:(j + 1) * 128],
                in_=B.Apre[:, j * 128:(j + 1) * 128], identity=C.idb[:]),
                r=[R("a_apre"), R("idb")], w=[R("PT")])
    with P.stage():
        for p in range(2):
            for hh in range(4):
                P.do("dve", lambda e, hh=hh, p=p: e.scalar_tensor_tensor(
                    out=B.ATs[:, p * 512 + hh * 128:p * 512 + (hh + 1) * 128],
                    in0=PT[:, p * 512 + hh * 128:p * 512 + (hh + 1) * 128],
                    scalar=B.rst[:, p * 16 + 4 + hh:p * 16 + 5 + hh], in1=C.mUsb[:],
                    op0=ALU.mult, op1=ALU.mult),
                    r=[R("PT"), R("a_rst"), R("mUsb")], w=[R("a_ats0")])
    with P.stage():
        for j in range(8):
            P.do("pe", lambda e, j=j: e.transpose(
                PT[:, j * 128:(j + 1) * 128],
                in_=B.ATs[:, j * 128:(j + 1) * 128], identity=C.idb[:]),
                r=[R("a_ats0"), R("idb")], w=[R("PT")])
    with P.stage():
        P.do("dve", lambda e: e.tensor_copy(out=B.As[:], in_=PT[:, 0:1024]),
             r=[R("PT")], w=[R("a_apre")])
        P.do("dve", lambda e: e.tensor_tensor(
            out=B.PTs[:].rearrange("p (r n) -> p r n", r=2),
            in0=B.ATs[:].rearrange("p (r n) -> p r n", r=2),
            in1=C.i4b[:].unsqueeze(1).broadcast_to([128, 2, 512]), op=ALU.add),
            r=[R("a_ats0"), R("i4b")], w=[R("a_pts")])
    lvls = (2, 4, 8, 16, 32, 64)
    prev = None
    for li, lvl in enumerate(lvls):
        need_xt = lvl < 64
        if li == 0:
            xsl = lambda j: B.As[:, j * 128:(j + 1) * 128]
            xtsl = lambda j: B.ATs[:, j * 128:(j + 1) * 128]
            rr_ = ["a_apre", "a_ats0"]
        else:
            pb_, pr_ = prev
            xsl = lambda j, b=pb_: b[:, (j // 4) * 1024 + (j % 4) * 128:
                                     (j // 4) * 1024 + (j % 4) * 128 + 128]
            xtsl = lambda j, b=pb_: b[:, (j // 4) * 1024 + 512 + (j % 4) * 128:
                                      (j // 4) * 1024 + 512 + (j % 4) * 128 + 128]
            rr_ = [pr_, pr_]
        cur = B.XXT[li % 2]
        curr = "a_xp%d" % (li % 2)
        with P.stage():
            for j in range(8):
                p = j // 4
                P.do("pe", lambda e, j=j, p=p, xsl=xsl, xtsl=xtsl: e.matmul(
                    PC[:, p * 512 + (j % 4) * 128:p * 512 + (j % 4) * 128 + 128],
                    lhsT=xtsl(j), rhs=xsl(j), start=True, stop=True),
                    r=[R(rr_[0]), R(rr_[1])], w=[R("PC%d" % p)])
            if need_xt:
                for j in range(8):
                    p = j // 4
                    P.do("pe", lambda e, j=j, p=p, xsl=xsl, xtsl=xtsl: e.matmul(
                        PA[:, p * 512 + (j % 4) * 128:p * 512 + (j % 4) * 128 + 128],
                        lhsT=xsl(j), rhs=xtsl(j), start=True, stop=True),
                        r=[R(rr_[0]), R(rr_[1])], w=[R("PA%d" % p)])
            if li > 0:
                for j in range(8):
                    p = j // 4
                    P.do("pe", lambda e, j=j, p=p, xsl=xsl: e.matmul(
                        PB[:, p * 512 + (j % 4) * 128:p * 512 + (j % 4) * 128 + 128],
                        lhsT=xsl(j),
                        rhs=B.PTs[:, j * 128:(j + 1) * 128],
                        start=True, stop=True),
                        r=[R(rr_[0]), R("a_pts")], w=[R("PB%d" % p)])
        with P.stage():
            evaceng = "dve" if li % 2 == 0 else "act"
            for p in range(2):
                if need_xt:
                    if evaceng == "dve":
                        P.do("dve", lambda e, cur=cur, p=p: e.tensor_copy(
                            out=cur[:, p * 1024:p * 1024 + 512], in_=half(PC, p)),
                            r=[R("PC%d" % p)], w=[R(curr)])
                        P.do("dve", lambda e, cur=cur, p=p: e.tensor_copy(
                            out=cur[:, p * 1024 + 512:p * 1024 + 1024], in_=half(PA, p)),
                            r=[R("PA%d" % p)], w=[R(curr)])
                    else:
                        P.do("act", lambda e, cur=cur, p=p: e.activation(
                            out=cur[:, p * 1024:p * 1024 + 512], in_=half(PC, p),
                            func=AF.Identity), r=[R("PC%d" % p)], w=[R(curr)])
                        P.do("act", lambda e, cur=cur, p=p: e.activation(
                            out=cur[:, p * 1024 + 512:p * 1024 + 1024], in_=half(PA, p),
                            func=AF.Identity), r=[R("PA%d" % p)], w=[R(curr)])
                else:
                    P.do("dve", lambda e, cur=cur, p=p: e.tensor_copy(
                        out=cur[:, p * 1024:p * 1024 + 512], in_=half(PC, p)),
                        r=[R("PC%d" % p)], w=[R(curr)])
            if li > 0:
                P.do("dve", lambda e: e.tensor_tensor(out=B.PTs[:], in0=PB[:, 0:1024],
                     in1=B.PTs[:], op=ALU.add),
                     r=[R("PB0"), R("PB1"), R("a_pts")], w=[R("a_pts")])
        prev = (cur, curr)
    pb_, pr_ = prev
    with P.stage():
        for j in range(8):
            p = j // 4
            P.do("pe", lambda e, j=j, p=p, pb_=pb_: e.matmul(
                PB[:, p * 512 + (j % 4) * 128:p * 512 + (j % 4) * 128 + 128],
                lhsT=pb_[:, p * 1024 + (j % 4) * 128:p * 1024 + (j % 4) * 128 + 128],
                rhs=B.PTs[:, j * 128:(j + 1) * 128], start=True, stop=True),
                r=[R(pr_), R("a_pts")], w=[R("PB%d" % p)])
    with P.stage():
        P.do("dve", lambda e: e.tensor_tensor(out=B.PTs[:], in0=PB[:, 0:1024],
             in1=B.PTs[:], op=ALU.add),
             r=[R("PB0"), R("PB1"), R("a_pts")], w=[R("a_pts")])
    with P.stage():
        for p in range(2):
            PPw = PB if p == 0 else PA
            for hh in range(4):
                for cc in range(2):
                    P.do("pe", lambda e, p=p, hh=hh, cc=cc, PPw=PPw: e.matmul(
                        PPw[:, (hh * 2 + cc) * 128:(hh * 2 + cc + 1) * 128],
                        lhsT=B.kbtm[:, p * 1024 + (hh * 2 + cc) * 128:
                                    p * 1024 + (hh * 2 + cc + 1) * 128],
                        rhs=B.PTs[:, p * 512 + hh * 128:p * 512 + (hh + 1) * 128],
                        start=True, stop=True),
                        r=[R("a_kbtm"), R("a_pts")],
                        w=[R("%s%d" % ("PB" if p == 0 else "PA", (hh * 2 + cc) // 4))])
        for p in range(2):
            for hh in range(4):
                for cc in range(2):
                    P.do("pe", lambda e, p=p, hh=hh, cc=cc: e.matmul(
                        half(PC, p)[:, hh * 128:(hh + 1) * 128],
                        lhsT=ks(2 * hh + cc, p), rhs=qs(2 * hh + cc, p),
                        start=(cc == 0), stop=(cc == 1)),
                        r=[R("a_kpre"), R("a_qpre")], w=[R("PC%d" % p)])
    with P.stage():
        for p in range(2):
            PPw = PB if p == 0 else PA
            P.do("dve", lambda e, p=p, PPw=PPw: e.tensor_scalar(
                out=B.wnf[:, p * 1024:(p + 1) * 1024], in0=PPw[:, 0:1024],
                scalar1=-1.0, scalar2=None, op0=ALU.mult),
                r=[R("PB0"), R("PB1")] if p == 0 else [R("PA0"), R("PA1")],
                w=[R("a_wnf")])
            for hh in range(4):
                P.do("dve", lambda e, hh=hh, p=p: e.scalar_tensor_tensor(
                    out=B.attb[:, p * 512 + hh * 128:p * 512 + (hh + 1) * 128],
                    in0=half(PC, p)[:, hh * 128:(hh + 1) * 128],
                    scalar=B.rst[:, p * 16 + 4 + hh:p * 16 + 5 + hh], in1=C.mUib[:],
                    op0=ALU.mult, op1=ALU.mult),
                    r=[R("PC%d" % p), R("a_rst"), R("mUib")], w=[R("a_attb")])


def tileA_suffix(C, P, R, B, st, i):
    PA, PB, PC, PS, PT = C.PA, C.PB, C.PC, C.PS, C.PT
    half = B.half
    p = i % 2
    o = i * 128
    t = st * 4 + i
    so = p * 1024
    dfm = B.dfm2[:, p * 1024:(p + 1) * 1024]
    b1f = B.b1f2[:, p * 1024:(p + 1) * 1024]
    qs = lambda ch: B.qpre[:, ch * 516 + 4 + o:ch * 516 + 4 + o + 128]
    vs = lambda ch: B.vf[:, ch * 544 + 32 + o:ch * 544 + 32 + o + 128]
    with P.stage():
        for hh in range(4):
            P.do("pe", lambda e, hh=hh: e.matmul(
                PC[:, hh * 256:(hh + 1) * 256],
                lhsT=B.PTs[:, so // 2 + hh * 128:so // 2 + (hh + 1) * 128],
                rhs=B.vbtm[:, so + hh * 256:so + (hh + 1) * 256], start=True, stop=False),
                r=[R("a_pts"), R("a_vbtm")], w=[R("PC0"), R("PC1")])
            for cc in range(2):
                P.do("pe", lambda e, hh=hh, cc=cc: e.matmul(
                    PC[:, hh * 256:(hh + 1) * 256],
                    lhsT=B.wnf[:, so + (hh * 2 + cc) * 128:so + (hh * 2 + cc + 1) * 128],
                    rhs=B.Sbb[:, hh * 512 + cc * 256:hh * 512 + cc * 256 + 256],
                    start=False, stop=(cc == 1)),
                    r=[R("a_wnf"), R("a_Sbb")], w=[R("PC0"), R("PC1")])
    with P.stage():
        P.do("dve", lambda e: e.tensor_copy(out=B.uhat[:], in_=PC[:, 0:1024]),
             r=[R("PC0"), R("PC1")], w=[R("a_uhat")])
    with P.stage():
        for hh in range(4):
            for cc in range(2):
                P.do("pe", lambda e, hh=hh, cc=cc: e.matmul(
                    PB[:, hh * 256:(hh + 1) * 256],
                    lhsT=qs(2 * hh + cc),
                    rhs=B.Sbb[:, hh * 512 + cc * 256:hh * 512 + cc * 256 + 256],
                    start=(cc == 0), stop=False),
                    r=[R("a_qpre"), R("a_Sbb")], w=[R("PB0"), R("PB1")])
            P.do("pe", lambda e, hh=hh: e.matmul(
                PB[:, hh * 256:(hh + 1) * 256],
                lhsT=B.attb[:, so // 2 + hh * 128:so // 2 + (hh + 1) * 128],
                rhs=B.uhat[:, hh * 256:(hh + 1) * 256], start=False, stop=True),
                r=[R("a_attb"), R("a_uhat")], w=[R("PB0"), R("PB1")])
        for hh in range(4):
            for cc in range(2):
                PP = PA if hh < 2 else PC
                pres = ("PA0", "PA1") if hh < 2 else ("PC0", "PC1")
                P.do("pe", lambda e, hh=hh, cc=cc, PP=PP: e.matmul(
                    PP[:, (hh % 2) * 512 + cc * 256:(hh % 2) * 512 + cc * 256 + 256],
                    lhsT=B.kntm[:, so + (hh * 2 + cc) * 128:so + (hh * 2 + cc + 1) * 128],
                    rhs=B.uhat[:, hh * 256:(hh + 1) * 256], start=True, stop=True),
                    r=[R("a_kntm"), R("a_uhat")], w=[R(pres[0]), R(pres[1])])
    with P.stage():
        for hh in range(4):
            P.do("dve", lambda e, hh=hh, p=p: e.tensor_scalar(
                out=B.tmp_tm[:, hh * 256:(hh + 1) * 256],
                in0=PB[:, hh * 256:(hh + 1) * 256],
                scalar1=B.rst[:, p * 16 + hh:p * 16 + hh + 1], scalar2=None,
                op0=ALU.mult),
                r=[R("PB0"), R("PB1"), R("a_rst")], w=[R("a_tmptm")])
        P.do("dve", lambda e: e.tensor_tensor(out=B.S[:, 0:1024], in0=PA[:, 0:1024],
             in1=B.S[:, 0:1024], op=ALU.add),
             r=[R("PA0"), R("PA1"), R("a_S")], w=[R("a_S")])
        P.do("dve", lambda e: e.tensor_tensor(out=B.S[:, 1024:2048], in0=PC[:, 0:1024],
             in1=B.S[:, 1024:2048], op=ALU.add),
             r=[R("PC0"), R("PC1"), R("a_S")], w=[R("a_S")])
    with P.stage():
        P.do("act", lambda e: e.activation(out=B.Sbb[:, 0:1024], in_=B.S[:, 0:1024],
             func=AF.Identity), r=[R("a_S")], w=[R("a_Sbb")])
        P.do("dve", lambda e: e.tensor_copy(out=B.Sbb[:, 1024:2048],
             in_=B.S[:, 1024:2048]), r=[R("a_S")], w=[R("a_Sbb")])
    with P.stage():
        for ch in range(8):
            P.do("pe", lambda e, ch=ch: e.transpose(
                PT[:, ch * 128:(ch + 1) * 128],
                in_=B.tmp_tm[:, ch * 128:(ch + 1) * 128], identity=C.idb[:]),
                r=[R("a_tmptm"), R("idb")], w=[R("PT")])
    with P.stage():
        P.do("dve", lambda e, dfm=dfm: e.tensor_copy(out=dfm, in_=PT[:, 0:1024]),
             r=[R("PT")], w=[R("a_dfm2")])
    if "delta" in C.dbg and t == 0:
        with P.stage():
            P.dma(C.dbg["delta"][:], B.tmp_tm[:, 0:1024], r=[R("a_tmptm")])
    with P.stage():
        for ch in range(8):
            P.do("dve", lambda e, ch=ch, b1f=b1f: e.tensor_scalar(
                out=b1f[:, ch * 128:(ch + 1) * 128], in0=vs(ch),
                scalar1=C.wtab[:, ch * 54 + 12:ch * 54 + 13], scalar2=None,
                op0=ALU.mult), r=[R("a_vf"), R("wtab")], w=[R("a_b1f2")])
    fstk = lambda f: f[:].rearrange("p (c n) -> p c n", c=8)[:, :, o:o + 128]
    cstk = lambda b: b.rearrange("p (c n) -> p c n", c=8)
    vstack = B.vf[:].rearrange("p (c n) -> p c n", c=8)[:, :, 32 + o:32 + o + 128]
    branches = [
        (lambda ch: b1f[:, ch * 128:(ch + 1) * 128], cstk(b1f), "a_b1f2"),
        (lambda ch: B.f3[:, ch * 512 + o:ch * 512 + o + 128], fstk(B.f3), "a_f3"),
        (lambda ch: B.f7[:, ch * 512 + o:ch * 512 + o + 128], fstk(B.f7), "a_f7"),
        (lambda ch: B.f31[:, ch * 512 + o:ch * 512 + o + 128], fstk(B.f31), "a_f31"),
        (lambda ch: dfm[:, ch * 128:(ch + 1) * 128], cstk(dfm), "a_dfm2"),
        (lambda ch: vs(ch), vstack, "a_vf"),
    ]
    for br, (bsl, bstk, bres) in enumerate(branches):
        with P.stage():
            P.do("act", lambda e, bstk=bstk: e.activation(
                out=B.bsq[:].rearrange("p (c n) -> p c n", c=8),
                in_=bstk, func=AF.Square),
                r=[R(bres)], w=[R("a_bsq")])
            P.do("act", lambda e, bstk=bstk: e.activation(
                out=B.babs[:].rearrange("p (c n) -> p c n", c=8),
                in_=bstk, func=AF.Abs),
                r=[R(bres)], w=[R("a_babs")])
        with P.stage():
            for hh in range(4):
                col = 8 + hh * 6 + br
                for cc in range(2):
                    P.do("pe", lambda e, hh=hh, cc=cc, col=col, bsl=bsl: e.matmul(
                        PS[:, col:col + 1], lhsT=bsl(2 * hh + cc), rhs=C.onesb[:],
                        start=(cc == 0), stop=(cc == 1)),
                        r=[R(bres), R("onesb")], w=[R("PS")])
                for cc in range(2):
                    P.do("pe", lambda e, hh=hh, cc=cc, col=col: e.matmul(
                        PS[:, 32 + col - 8:32 + col - 8 + 1],
                        lhsT=B.babs[:, (2 * hh + cc) * 128:(2 * hh + cc + 1) * 128],
                        rhs=C.onesb[:], start=(cc == 0), stop=(cc == 1)),
                        r=[R("a_babs"), R("onesb")], w=[R("PS")])
                for cc in range(2):
                    P.do("pe", lambda e, hh=hh, cc=cc, col=col: e.matmul(
                        PS[:, 56 + col - 8:56 + col - 8 + 1],
                        lhsT=B.bsq[:, (2 * hh + cc) * 128:(2 * hh + cc + 1) * 128],
                        rhs=C.onesb[:], start=(cc == 0), stop=(cc == 1)),
                        r=[R("a_bsq"), R("onesb")], w=[R("PS")])
    T = C.tiny
    with P.stage():
        P.do("dve", lambda e: e.tensor_scalar(
            out=st96ap(B.st96, 0), in0=psblock(PS, 8),
            scalar1=1.0 / 256, scalar2=None, op0=ALU.mult),
            r=[R("PS")], w=[R("a_st96")])
        P.do("dve", lambda e: e.tensor_scalar(
            out=st96ap(B.st96, 2), in0=psblock(PS, 32),
            scalar1=1.0 / 256, scalar2=None, op0=ALU.mult),
            r=[R("PS")], w=[R("a_st96")])
        P.do("act", lambda e: e.activation(out=T[:, 0:24], in_=PS[:, 8:32],
             func=AF.Square), r=[R("PS")], w=[R("tinyA")])
    with P.stage():
        P.do("dve", lambda e: e.scalar_tensor_tensor(out=T[:, 0:24], in0=T[:, 0:24],
             scalar=-1.0 / 256, in1=PS[:, 56:80], op0=ALU.mult, op1=ALU.add),
             r=[R("tinyA"), R("PS")], w=[R("tinyA")])
    with P.stage():
        P.do("dve", lambda e: e.tensor_scalar(out=T[:, 0:24], in0=T[:, 0:24],
             scalar1=0.0, scalar2=None, op0=ALU.max), r=[R("tinyA")], w=[R("tinyA")])
    with P.stage():
        P.do("act", lambda e: e.activation(out=T[:, 0:24], in_=T[:, 0:24],
             func=AF.Ln, scale=1.0 / 255, bias=1e-30), r=[R("tinyA")], w=[R("tinyA")])
    with P.stage():
        P.do("act", lambda e: e.activation(out=st96ap(B.st96, 1), in_=T[:, 0:24],
             func=AF.Exp, scale=0.5), r=[R("tinyA")], w=[R("a_st96")])
        P.do("act", lambda e: e.activation(out=T[:, 0:24], in_=PS[:, 56:80],
             func=AF.Ln, bias=1e-30), r=[R("PS")], w=[R("tinyB")])
    with P.stage():
        P.do("act", lambda e: e.activation(out=st96ap(B.st96, 3), in_=T[:, 0:24],
             func=AF.Exp, scale=0.5), r=[R("tinyB")], w=[R("a_st96")])
    if "stats" in C.dbg and t == 0:
        with P.stage():
            P.do("dve", lambda e: e.tensor_copy(out=B.sot[:, 0:96], in_=B.st96[:]),
                 r=[R("a_st96")], w=[R("a_sot")])
        with P.stage():
            P.dma(C.dbg["stats"][:], B.sot[:, 0:96], r=[R("a_sot")])
    with P.stage():
        P.do("pe", lambda e: e.transpose(PT[0:96, 0:128], in_=B.st96[:],
             identity=C.idb[:]), r=[R("a_st96"), R("idb")], w=[R("PT")])
    with P.stage():
        P.do("dve", lambda e: e.tensor_copy(out=B.stT[:], in_=PT[0:96, 0:128]),
             r=[R("PT")], w=[R("a_stT")])
    with P.stage():
        for m in range(8):
            for k in range(9):
                if k < 8:
                    P.do("pe", lambda e, m=m, k=k: e.matmul(
                        C.PC[:, m * 128:(m + 1) * 128]
                        if m < 4 else C.PC[:, (m - 4) * 128 + 512:(m - 3) * 128 + 512],
                        lhsT=C.w1b[:, k * 1024 + m * 128:k * 1024 + (m + 1) * 128],
                        rhs=B.xnst[:, k * 512 + o:k * 512 + o + 128],
                        start=(k == 0), stop=False),
                        r=[R("w1b"), R("a_xnst")],
                        w=[R("PC0") if m < 4 else R("PC1")])
                else:
                    P.do("pe", lambda e, m=m: e.matmul(
                        C.PC[:, m * 128:(m + 1) * 128]
                        if m < 4 else C.PC[:, (m - 4) * 128 + 512:(m - 3) * 128 + 512],
                        lhsT=C.w1b[0:96, 8 * 1024 + m * 128:8 * 1024 + (m + 1) * 128],
                        rhs=B.stT[:], start=False, stop=True),
                        r=[R("w1b"), R("a_stT")],
                        w=[R("PC0") if m < 4 else R("PC1")])
    with P.stage():
        for m in range(8):
            P.do("dve", lambda e, m=m, p=p: e.tensor_scalar(
                out=B.hgst[:, p * 1024 + m * 128:p * 1024 + (m + 1) * 128],
                in0=C.PC[:, m * 128:(m + 1) * 128] if m < 4
                else C.PC[:, (m - 4) * 128 + 512:(m - 3) * 128 + 512],
                scalar1=C.b1fm[:, m:m + 1], scalar2=None, op0=ALU.add),
                r=[R("PC0") if m < 4 else R("PC1"), R("b1fm")], w=[R("a_hgst")])


def tileA_phase2(C, P, R, B, st, i):
    t = st * 4 + i
    o = i * 128
    PA, PB, PC, PS, PT = C.PA, C.PB, C.PC, C.PS, C.PT
    half = B.half
    sl = i % 2
    hgst, stT, l24, w24, wgtT = B.hgst, B.stT, B.l24, B.w24, B.wgtT
    facc, tmpc, ffm, sot = B.facc, B.tmpc, B.ffm, B.sot
    rr2, vf, f3, f7, f31 = B.rr2, B.vf, B.f3, B.f7, B.f31
    dfm = B.dfm2[:, sl * 1024:(sl + 1) * 1024]
    b1f = B.b1f2[:, sl * 1024:(sl + 1) * 1024]
    vs = lambda ch: vf[:, ch * 544 + 32 + o:ch * 544 + 32 + o + 128]
    owc = B.owc
    with P.stage():
        for kk in range(3):
            P.dma(owc[kk][:], C.ow_s[kk], w=[R("a_owc%d" % kk)])
    with P.stage():
        for cch in range(8):
            P.do("pe", lambda e, cch=cch, sl=sl: e.matmul(
                PS[:, 96:120],
                lhsT=hgst[:, sl * 1024 + cch * 128:sl * 1024 + (cch + 1) * 128],
                rhs=C.w2b[:, cch * 24:(cch + 1) * 24],
                start=(cch == 0), stop=(cch == 7)),
                r=[R("a_hgst"), R("w2b")], w=[R("PS")])
    if "hg" in C.dbg and t == 0:
        with P.stage():
            P.do("dve", lambda e: e.tensor_copy(out=B.sot[:],
                 in_=hgst[:, 0:1024]), r=[R("a_hgst")], w=[R("a_sot")])
        with P.stage():
            P.dma(C.dbg["hg"][:], B.sot[:], r=[R("a_sot")])
    T = C.tiny
    with P.stage():
        P.do("dve", lambda e: e.tensor_tensor(out=l24[:], in0=PS[:, 96:120],
             in1=C.invtempT[:], op=ALU.mult), r=[R("PS"), R("invtempT")],
             w=[R("a_l24")])
    with P.stage():
        P.do("dve", lambda e: e.tensor_tensor(out=l24[:], in0=l24[:],
             in1=C.b2i[:], op=ALU.add), r=[R("a_l24"), R("b2i")], w=[R("a_l24")])
    with P.stage():
        P.do("dve", lambda e: e.reduce_max(out=T[:, 0:4],
             in_=l24[:].rearrange("p (h s) -> p h s", h=4), axis=AX.X),
             r=[R("a_l24")], w=[R("tinyA")])
    with P.stage():
        for hh in range(4):
            P.do("dve", lambda e, hh=hh: e.tensor_scalar(
                out=l24[:, hh * 6:(hh + 1) * 6], in0=l24[:, hh * 6:(hh + 1) * 6],
                scalar1=T[:, hh:hh + 1], scalar2=None, op0=ALU.subtract),
                r=[R("a_l24"), R("tinyA")], w=[R("a_l24")])
    with P.stage():
        P.do("act", lambda e: e.activation(out=l24[:], in_=l24[:], func=AF.Exp),
             r=[R("a_l24")], w=[R("a_l24")])
    with P.stage():
        P.do("dve", lambda e: e.reduce_sum(out=T[:, 0:4],
             in_=l24[:].rearrange("p (h s) -> p h s", h=4), axis=AX.X),
             r=[R("a_l24")], w=[R("tinyA")])
    with P.stage():
        P.do("dve", lambda e: e.reciprocal(out=T[:, 4:8], in_=T[:, 0:4]),
             r=[R("tinyA")], w=[R("tinyA")])
    with P.stage():
        for hh in range(4):
            P.do("dve", lambda e, hh=hh: e.tensor_scalar(
                out=w24[:, hh * 6:(hh + 1) * 6], in0=l24[:, hh * 6:(hh + 1) * 6],
                scalar1=T[:, 4 + hh:5 + hh], scalar2=None, op0=ALU.mult),
                r=[R("a_l24"), R("tinyA")], w=[R("a_w24")])
    if "wgt" in C.dbg and t == 0:
        with P.stage():
            P.do("dve", lambda e: e.tensor_copy(out=l24[:], in_=w24[:]),
                 r=[R("a_w24")], w=[R("a_l24")])
        with P.stage():
            P.dma(C.dbg["wgt"][:], l24[:], r=[R("a_l24")])
    with P.stage():
        P.do("pe", lambda e: e.transpose(PT[0:24, 0:128], in_=w24[:],
             identity=C.idb[:]), r=[R("a_w24"), R("idb")], w=[R("PT")])
    with P.stage():
        P.do("dve", lambda e: e.tensor_copy(out=wgtT[:], in_=PT[0:24, 0:128]),
             r=[R("PT")], w=[R("a_wgtT")])
    fstk = lambda f: f[:].rearrange("p (c n) -> p c n", c=8)[:, :, o:o + 128]
    cstk = lambda b: b.rearrange("p (c n) -> p c n", c=8)
    vstack = vf[:].rearrange("p (c n) -> p c n", c=8)[:, :, 32 + o:32 + o + 128]
    branches = [
        (cstk(b1f), "a_b1f2"),
        (fstk(f3), "a_f3"),
        (fstk(f7), "a_f7"),
        (fstk(f31), "a_f31"),
        (cstk(dfm), "a_dfm2"),
        (vstack, "a_vf"),
    ]
    for s6, (bstk, bres) in enumerate(branches):
        PP, ppres = (PA, ("PA0", "PA1")) if s6 % 2 == 0 else (PB, ("PB0", "PB1"))
        with P.stage():
            for ch in range(8):
                hh = ch // 2
                P.do("pe", lambda e, ch=ch, hh=hh, s6=s6, PP=PP: e.matmul(
                    PP[:, ch * 128:(ch + 1) * 128],
                    lhsT=C.sel24b[:, hh * 6 + s6:hh * 6 + s6 + 1]
                    .broadcast_to([24, 128]),
                    rhs=wgtT[:], start=True, stop=True),
                    r=[R("sel24b"), R("a_wgtT")], w=[R(ppres[ch // 4])])
        with P.stage():
            dst, dres = (facc, "a_facc") if s6 == 0 else (tmpc, "a_tmpc")
            P.do("dve", lambda e, bstk=bstk, PP=PP, dst=dst: e.tensor_tensor(
                out=dst[:].rearrange("p (c n) -> p c n", c=8),
                in0=bstk, in1=PP[:, 0:1024].rearrange("p (c n) -> p c n", c=8),
                op=ALU.mult),
                r=[R(bres), R(ppres[0]), R(ppres[1])], w=[R(dres)])
        if s6 > 0:
            with P.stage():
                P.do("pool", lambda e: e.tensor_tensor(out=facc[:], in0=facc[:],
                     in1=tmpc[:], op=ALU.add),
                     r=[R("a_facc"), R("a_tmpc")], w=[R("a_facc")])
    with P.stage():
        P.do("dve", lambda e: e.tensor_tensor(out=tmpc[:], in0=facc[:],
             in1=facc[:], op=ALU.mult), r=[R("a_facc")], w=[R("a_tmpc")])
    with P.stage():
        for hh in range(4):
            for cc in range(2):
                P.do("pe", lambda e, hh=hh, cc=cc: e.matmul(
                    PS[:, 120 + hh:121 + hh],
                    lhsT=tmpc[:, (2 * hh + cc) * 128:(2 * hh + cc + 1) * 128],
                    rhs=C.onesb[:], start=(cc == 0), stop=(cc == 1)),
                    r=[R("a_tmpc"), R("onesb")], w=[R("PS")])
    with P.stage():
        P.do("act", lambda e, sl=sl: e.activation(out=rr2[:, sl * 4:(sl + 1) * 4],
             in_=PS[:, 120:124], func=AF.Ln, scale=1.0 / 256, bias=1e-5),
             r=[R("PS")], w=[R("a_rr2")])
    with P.stage():
        P.do("act", lambda e, sl=sl: e.activation(out=rr2[:, sl * 4:(sl + 1) * 4],
             in_=rr2[:, sl * 4:(sl + 1) * 4], func=AF.Exp, scale=-0.5),
             r=[R("a_rr2")], w=[R("a_rr2")])
    with P.stage():
        P.do("dve", lambda e, sl=sl: e.tensor_copy(out=w24[:, 0:4],
             in_=rr2[:, sl * 4:(sl + 1) * 4]), r=[R("a_rr2")], w=[R("a_w24")])
    with P.stage():
        for hh in range(4):
            P.do("pe", lambda e, hh=hh: e.transpose(
                PT[0:1, hh * 128:(hh + 1) * 128], in_=w24[:, hh:hh + 1],
                identity=C.idb[:]), r=[R("a_w24"), R("idb")], w=[R("PT")])
    with P.stage():
        P.do("dve", lambda e: e.tensor_copy(out=B.rrow[0:1, 0:512],
             in_=PT[0:1, 0:512]), r=[R("PT")], w=[R("a_rrow")])
    with P.stage():
        for hh in range(4):
            P.do("pe", lambda e, hh=hh: e.matmul(
                half(C.PA, 0)[:, hh * 128:(hh + 1) * 128], lhsT=C.onesrow[:],
                rhs=B.rrow[0:1, hh * 128:(hh + 1) * 128],
                start=True, stop=True),
                r=[R("a_rrow"), R("onesrow")], w=[R("PA0")])
    with P.stage():
        for ch in range(8):
            P.do("dve", lambda e, ch=ch: e.tensor_tensor(
                out=ffm[:, ch * 128:(ch + 1) * 128],
                in0=facc[:, ch * 128:(ch + 1) * 128],
                in1=half(C.PA, 0)[:, (ch // 2) * 128:(ch // 2 + 1) * 128],
                op=ALU.mult),
                r=[R("a_facc"), R("PA0")], w=[R("a_ffm")])
    for grp in ((0, 1, 2), (3, 4, 5), (6, 7)):
        with P.stage():
            for k in grp:
                for nn in range(2):
                    P.do("pe", lambda e, k=k, nn=nn: e.matmul(
                        PC[:, nn * 512:(nn + 1) * 512],
                        lhsT=ffm[:, k * 128:(k + 1) * 128],
                        rhs=owc[k % 3][:, nn * 512:(nn + 1) * 512],
                        start=(k == 0), stop=(k == 7)),
                        r=[R("a_ffm"), R("a_owc%d" % (k % 3))], w=[R("PC%d" % nn)])
        if grp[0] == 0:
            with P.stage():
                for kk in (3, 4, 5):
                    P.dma(owc[kk % 3][:], C.ow_s[kk], w=[R("a_owc%d" % (kk % 3))])
        elif grp[0] == 3:
            with P.stage():
                for kk in (6, 7):
                    P.dma(owc[kk % 3][:], C.ow_s[kk], w=[R("a_owc%d" % (kk % 3))])
    with P.stage(dma_out=True):
        P.do("dve", lambda e: e.tensor_copy(out=sot[:], in_=PC[:, 0:1024]),
             r=[R("PC0"), R("PC1")], w=[R("a_sot")])
    with P.stage():
        P.dma(C.s_s[t], sot[:], r=[R("a_sot")])
        if "s" in C.dbg and t == 0:
            P.dma(C.dbg["s"][:], sot[:], r=[R("a_sot")])


def st96ap(st96, stat):
    return st96[:].rearrange("p (x f) -> p x f", f=4)[:, :, stat:stat + 1]


def psblock(PS, base):
    return PS[:, base:base + 24]


def phaseB(C, es):
    nc, P = C.nc, C.P
    R = P.res
    sb = lambda n, sh, dt=F32: es.enter_context(nc.sbuf_tensor(n, sh, dt))
    wgu = sb("b_wgu", [128, 8 * 6144], BF16)
    wdb = sb("b_wdb", [128, 24 * 1024], BF16)
    xnb = sb("b_xnb", [128, 4096], BF16)
    hst = sb("b_hst", [128, 24 * 512], BF16)
    sg = sb("b_sg", [128, 1024], BF16)
    xb0 = sb("b_x0", [128, 1024])
    sb0 = sb("b_s0", [128, 1024])
    gb0 = sb("b_gb0", [128, 1024], BF16)
    xb = [xb0, xb0]
    sbf = [sb0, sb0]
    gb = [gb0, gb0]
    tm = sbf
    ob = xb
    PA, PB, PC, PS = C.PA, C.PB, C.PC, C.PS
    with P.stage():
        P.dma(wgu[:].rearrange("p (k n) -> p k n", k=8),
              C.wgu_s.rearrange("k p n -> p k n"), w=[R("b_wgu")])
        P.dma(wdb[:].rearrange("p (k n) -> p k n", k=24),
              C.wdT_s.rearrange("k p n -> p k n"), w=[R("b_wdb")])
    for st in range(NST):
        t0 = st * 4
        with P.stage():
            for i in range(4):
                P.dma(xnb[:].rearrange("p (g t) -> p g t", g=8)
                      [:, :, i * 128:(i + 1) * 128],
                      C.xn_s[t0 + i].rearrange("p (g t) -> p g t", g=8),
                      w=[R("b_xnb")])
        for mp in range(12):
            with P.stage():
                for m2 in range(2):
                    m = mp * 2 + m2
                    for k in range(8):
                        P.do("pe", lambda e, m=m, m2=m2, k=k: e.matmul(
                            PA[:, m2 * 512:(m2 + 1) * 512],
                            lhsT=wgu[:, k * 6144 + m * 128:k * 6144 + (m + 1) * 128],
                            rhs=xnb[:, k * 512:(k + 1) * 512],
                            start=(k == 0), stop=(k == 7)),
                            r=[R("b_wgu"), R("b_xnb")], w=[R("PA%d" % m2)])
                    for k in range(8):
                        P.do("pe", lambda e, m=m, m2=m2, k=k: e.matmul(
                            PB[:, m2 * 512:(m2 + 1) * 512],
                            lhsT=wgu[:, k * 6144 + (24 + m) * 128:k * 6144 + (25 + m) * 128],
                            rhs=xnb[:, k * 512:(k + 1) * 512],
                            start=(k == 0), stop=(k == 7)),
                            r=[R("b_wgu"), R("b_xnb")], w=[R("PB%d" % m2)])
            with P.stage():
                P.do("act", lambda e: e.activation(out=sg[:], in_=PA[:, 0:1024],
                     func=AF.Silu, scale=1.0 / 127), r=[R("PA0"), R("PA1")],
                     w=[R("b_sg")])
            with P.stage():
                P.do("dve", lambda e, mp=mp: e.tensor_tensor(
                    out=hst[:, mp * 1024:(mp + 1) * 1024], in0=PB[:, 0:1024],
                    in1=sg[:], op=ALU.mult),
                    r=[R("PB0"), R("PB1"), R("b_sg")], w=[R("b_hst")])
        for i in range(4):
            t = t0 + i
            p = t % 2
            with P.stage():
                P.dma(xb[p][:], C.x_h[t * 128:(t + 1) * 128, :], w=[R("b_x0")])
                P.dma(sbf[p][:], C.s_s[t], w=[R("b_s0")])
            with P.stage():
                for cch in range(24):
                    for nn in range(2):
                        P.do("pe", lambda e, cch=cch, nn=nn, i=i: e.matmul(
                            PC[:, nn * 512:(nn + 1) * 512],
                            lhsT=hst[:, cch * 512 + i * 128:cch * 512 + i * 128 + 128],
                            rhs=wdb[:, cch * 1024 + nn * 512:cch * 1024 + (nn + 1) * 512],
                            start=(cch == 0), stop=(cch == 23)),
                            r=[R("b_hst"), R("b_wdb")], w=[R("PC%d" % nn)])
            with P.stage():
                P.do("act", lambda e, p=p: e.activation(out=gb[p][:], in_=PC[:, 0:1024],
                     func=AF.Silu, scale=1.0 / (127.0 * 127.0)),
                     r=[R("PC0"), R("PC1")], w=[R("b_gb0")])
            if "g" in C.dbg and t == 0:
                with P.stage():
                    P.do("dve", lambda e, p=p: e.tensor_copy(out=tm[p][:], in_=gb[p][:]),
                         r=[R("b_gb0")], w=[R("b_s0")])
                with P.stage():
                    P.dma(C.dbg["g"][:], tm[p][:], r=[R("b_s0")])
            with P.stage():
                P.do("dve", lambda e, p=p: e.tensor_tensor(out=tm[p][:], in0=sbf[p][:],
                     in1=gb[p][:], op=ALU.mult), r=[R("b_s0"), R("b_gb0")],
                     w=[R("b_s0")])
            with P.stage(dma_out=True):
                P.do("pool", lambda e, p=p: e.tensor_tensor(out=ob[p][:], in0=tm[p][:],
                     in1=xb[p][:], op=ALU.add), r=[R("b_s0"), R("b_x0")],
                     w=[R("b_x0")])
            with P.stage():
                P.dma(C.out_h[t * 128:(t + 1) * 128, :], ob[p][:], r=[R("b_x0")])


def kernel(**inputs):
    hp = host_prep(inputs)
    nc, dbg_names = build_program()
    x_full = np.asarray(inputs["L0_attn_hidden_states"])
    in_maps = []
    for c in range(N_CORES):
        m = dict(hp)
        m["x"] = np.ascontiguousarray(x_full[c])
        in_maps.append(m)
    res = run_bass_kernel_spmd(nc, in_maps, core_ids=list(range(N_CORES)))
    out = np.stack([res.results[c]["out"] for c in range(N_CORES)], axis=0)
    if DEBUG:
        kernel._dbg = [{k: res.results[c].get("dbg_" + k) for k in DEBUG}
                       for c in range(N_CORES)]
    return out


# version 15
# speedup vs baseline: 1.0017x; 1.0017x over previous
import numpy as np
import ml_dtypes
from contextlib import ExitStack, contextmanager
import concourse.bass as bass
import concourse.mybir as mybir
from concourse.bass_utils import run_bass_kernel_spmd

F32 = mybir.dt.float32
BF16 = mybir.dt.bfloat16
I32 = mybir.dt.int32
AF = mybir.ActivationFunctionType
ALU = mybir.AluOpType
AX = mybir.AxisListType

N_CORES = 8
L, D, H, DK = 4096, 1024, 4, 256
NT, NST, G, HID = 32, 8, 8, 3072

DEBUG = []

ENG2BLK = {"pe": "tensor", "dve": "vector", "act": "scalar",
           "pool": "gpsimd", "sync": "sync"}


class Res:
    __slots__ = ("name", "w_tok", "readers")

    def __init__(self, name):
        self.name = name
        self.w_tok = None
        self.readers = {}


class Prog:
    def __init__(self, nc):
        self.nc = nc
        self.steps = {e: [] for e in ENG2BLK}
        self.cnt = {e: 0 for e in ENG2BLK if e != "sync"}
        self.dma_cnt = 0
        self.hwm = {}
        self.in_stage = False
        self.touch = None
        self.emitted = None
        self.sems = {}
        self._res = {}
        self.dma_safe = {e: 0 for e in ("pe", "dve", "act", "pool")}

    def res(self, name):
        if name not in self._res:
            self._res[name] = Res(name)
        return self._res[name]

    def make_sems(self, es):
        for e in ("pe", "dve", "act", "pool", "dma", "dmadone"):
            self.sems[e] = es.enter_context(self.nc.semaphore("sem_" + e))
        self.dmadone_cnt = 0
        self.dmadone_pub = {}

    def raw(self, eng, fn):
        self.steps[eng].append(fn)

    def _wait(self, waiter, tok, gate=False):
        key, val = tok
        if gate and waiter == "sync" and key in ("pe", "dve", "act", "pool"):
            if self.dma_safe[key] < val:
                self.cnt[key] += 1
                sem0 = self.sems[key]
                self.raw(key, lambda e, s=sem0: e.drain().then_inc(s, 1))
                self.dma_safe[key] = self.cnt[key]
            val = self.dma_safe[key]
            tok = (key, val)
        if key == "dma" and waiter != "sync":
            total = self.dma_cnt
            if total not in self.dmadone_pub:
                self.dmadone_cnt += 1
                self.dmadone_pub[total] = self.dmadone_cnt
                ds, dd = self.sems["dma"], self.sems["dmadone"]
                if self.hwm.get(("sync", "dma"), 0) < total:
                    self.hwm[("sync", "dma")] = total
                    self.raw("sync", lambda e, s=ds, v=total: e.wait_ge(s, v))
                self.raw("sync", lambda e, s=dd: e.sem_inc(s, 1))
            key, val = "dmadone", self.dmadone_pub[total]
        if self.hwm.get((waiter, key), 0) >= val:
            return
        self.hwm[(waiter, key)] = val
        sem = self.sems[key]
        self.raw(waiter, lambda e, s=sem, v=val: e.wait_ge(s, v))

    def _hazards(self, eng, r, w):
        for b in r:
            if b.w_tok is not None:
                self._wait(eng, b.w_tok)
        for b in w:
            if b.w_tok is not None:
                self._wait(eng, b.w_tok)
            for rk, rv in b.readers.items():
                self._wait(eng, (rk, rv))

    def do(self, eng, fn, r=(), w=()):
        assert self.in_stage
        self._hazards(eng, r, w)
        self.raw(eng, fn)
        self.last_op_idx[eng] = len(self.steps[eng]) - 1
        self.emitted.add(eng)
        for b in r:
            self.touch.append((eng, b, "r"))
        for b in w:
            self.touch.append((eng, b, "w"))

    def dma(self, out_ap, in_ap, r=(), w=()):
        assert self.in_stage
        for b in r:
            if b.w_tok is not None:
                self._wait("sync", b.w_tok, gate=True)
        self._hazards("sync", (), w)
        self.dma_cnt += 16
        val = self.dma_cnt
        sem = self.sems["dma"]
        self.raw("sync",
                 lambda e, o=out_ap, i=in_ap, s=sem: e.dma_start(out=o, in_=i)
                 .then_inc(s, 16))
        for b in r:
            b.readers["dma"] = max(b.readers.get("dma", 0), val)
        for b in w:
            b.w_tok = ("dma", val)
            b.readers = {}

    @contextmanager
    def stage(self, use_drain=False, dma_out=False):
        use_drain = use_drain or dma_out
        assert not self.in_stage
        self.in_stage = True
        self.touch, self.emitted = [], set()
        self.last_op_idx = {}
        yield
        touched, emitted = self.touch, self.emitted
        self.in_stage = False
        self.touch = self.emitted = None
        toks = {}
        for e in emitted:
            self.cnt[e] += 1
            toks[e] = (e, self.cnt[e])
            sem = self.sems[e]
            if not use_drain:
                idx = self.last_op_idx[e]
                orig = self.steps[e][idx]
                self.steps[e][idx] = (
                    lambda en, o=orig, s=sem: o(en).then_inc(s, 1))
            else:
                self.raw(e, lambda en, s=sem: en.drain().then_inc(s, 1))
                self.dma_safe[e] = self.cnt[e]
        for eng, b, kind in touched:
            tok = toks[eng]
            if kind == "w":
                b.w_tok = tok
                b.readers = {}
            else:
                k, v = tok
                b.readers[k] = max(b.readers.get(k, 0), v)

    def barrier(self):
        with self.stage(use_drain=True):
            for e in ("pe", "dve", "act", "pool"):
                self.do(e, lambda en: en.nop())
        toks = [(e, self.cnt[e]) for e in ("pe", "dve", "act", "pool")]
        toks.append(("dma", self.dma_cnt))
        for e in ("pe", "dve", "act", "pool", "sync"):
            for t in toks:
                self._wait(e, t)

    def finalize(self, blk):
        for e, bname in ENG2BLK.items():
            fns = self.steps[e]
            getattr(blk, bname)(lambda eng, fns=fns: [f(eng) for f in fns])


def host_prep(inputs):
    f = lambda k: np.asarray(inputs[k])
    qkv = np.concatenate([f("L0_attn_q_w"), f("L0_attn_k_w"),
                          f("L0_attn_v_w")], axis=1)
    bw = f("L0_attn_b_w")

    def taps(name):
        return f(name)[:, 0, ::-1]
    wtab = np.concatenate([taps("L0_attn_qc_w"), taps("L0_attn_kc_w"),
                           taps("L0_attn_vc_w"), taps("L0_attn_fir_w1"),
                           taps("L0_attn_fir_w3"), taps("L0_attn_fir_w7"),
                           taps("L0_attn_fir_w31")], axis=1)
    sel = np.eye(24, dtype=np.float32)[:, :, None]
    c = lambda a: np.ascontiguousarray(a)
    return dict(
        qkv=c(qkv.astype(np.float32)),
        bw=c(bw.astype(np.float32)),
        w1=c(f("L0_attn_mlp_w1").astype(np.float32)),
        w2=c(f("L0_attn_mlp_w2").astype(np.float32)),
        ow=c(f("L0_attn_o_w").astype(np.float32)),
        wtab=c(wtab.astype(np.float32)),
        wgu=c(np.concatenate([f("L0_ffn_w_gate").T, f("L0_ffn_w_up").T], axis=1)),
        wdT=c(f("L0_ffn_w_down").T),
        b1fm=c(f("L0_attn_mlp_b1").reshape(8, 128).T.astype(np.float32)),
        onorm2=c(f("L0_attn_onorm_w").reshape(2, 128).T.astype(np.float32)),
        b2row=c(f("L0_attn_mlp_b2").reshape(1, 24).astype(np.float32)),
        gltrow=c(f("L0_attn_gate_log_temp").reshape(1, 4).astype(np.float32)),
        id128=np.eye(128, dtype=np.float32),
        maskUs=c(np.triu(np.ones((128, 128), np.float32), 1)),
        maskUi=c(np.triu(np.ones((128, 128), np.float32), 0)),
        sel24=c(sel.reshape(24, 24)),
        ones1r=np.ones((1, 128), np.float32),
    )


class Ctx:
    pass


def build_program():
    nc = bass.Bass("TRN2", target_bir_lowering=False)
    C = Ctx()
    C.nc = nc
    d = lambda n, sh, dt, kind: nc.dram_tensor(n, sh, dt, kind=kind).ap()
    C.x_h = d("x", [L, D], F32, "ExternalInput")
    C.qkv_h = d("qkv", [D, 3072], F32, "ExternalInput")
    C.bw_h = d("bw", [D, 4], F32, "ExternalInput")
    C.w1_h = d("w1", [1120, D], F32, "ExternalInput")
    C.w2_h = d("w2", [D, 24], F32, "ExternalInput")
    C.ow_h = d("ow", [D, D], F32, "ExternalInput")
    C.wtab_h = d("wtab", [D, 54], F32, "ExternalInput")
    C.wgu_h = d("wgu", [D, 2 * HID], I32, "ExternalInput")
    C.wdT_h = d("wdT", [HID, D], I32, "ExternalInput")
    C.b1fm_h = d("b1fm", [128, 8], F32, "ExternalInput")
    C.onorm2_h = d("onorm2", [128, 2], F32, "ExternalInput")
    C.b2row_h = d("b2row", [1, 24], F32, "ExternalInput")
    C.gltrow_h = d("gltrow", [1, 4], F32, "ExternalInput")
    C.id128_h = d("id128", [128, 128], F32, "ExternalInput")
    C.maskUs_h = d("maskUs", [128, 128], F32, "ExternalInput")
    C.maskUi_h = d("maskUi", [128, 128], F32, "ExternalInput")
    C.sel24_h = d("sel24", [24, 24], F32, "ExternalInput")
    C.ones1r_h = d("ones1r", [1, 128], F32, "ExternalInput")
    C.out_h = d("out", [L, D], F32, "ExternalOutput")
    C.dbg = {}
    if "xn" in DEBUG:
        C.dbg["xn"] = nc.dram_tensor("dbg_xn", [NT, 128, D], BF16,
                                     kind="ExternalOutput").ap()
    for name, sh, dt_ in [("dd", [128, 4096], BF16), ("dd2", [128, 2048], BF16),
                          ("rst", [128, 16], F32),
                          ("q", [128, 4096], BF16), ("k", [128, 4096], BF16),
                          ("v", [128, 4096], BF16), ("beta", [128, 16], F32),
                          ("delta", [128, 1024], BF16), ("stats", [128, 96], F32),
                          ("wgt", [128, 24], F32), ("s", [128, 1024], F32),
                          ("hg", [128, 1024], F32), ("g", [128, 1024], F32)]:
        if name in DEBUG:
            C.dbg[name] = d("dbg_" + name, sh, dt_, "ExternalOutput")
    C.xn_s = d("xn_s", [NT, 128, D], BF16, "Internal")
    C.s_s = d("s_s", [NT, 128, D], F32, "Internal")
    C.diag_s = d("diag_s", [G, 128, 54 * 128], BF16, "Internal")
    C.ow_s = d("ow_s", [8, 128, D], BF16, "Internal")
    C.qkv_s = d("qkv_s", [12, 128, 2048], BF16, "Internal")
    C.wgu_s = d("wgu_s", [8, 128, 2 * HID], BF16, "Internal")
    C.wdT_s = d("wdT_s", [24, 128, D], BF16, "Internal")

    es = ExitStack()
    P = Prog(nc)
    P.make_sems(es)
    sbg = lambda n, sh, dt=F32: es.enter_context(nc.sbuf_tensor(n, sh, dt))

    C.idb = sbg("idb", [128, 128], BF16)
    C.mUsb = sbg("mUsb", [128, 128], BF16)
    C.mUib = sbg("mUib", [128, 128], BF16)
    C.i4b = sbg("i4b", [128, 512], BF16)
    C.onesb = sbg("onesb", [128, 1], BF16)
    C.onesrow = sbg("onesrow", [1, 128], BF16)
    C.rows = sbg("rows", [1, 96])
    C.tiny = sbg("tiny", [128, 32])
    C.invtempT = sbg("invtempT", [128, 24])
    C.b2i = sbg("b2i", [128, 24])
    C.b1fm = sbg("b1fm_s", [128, 8])
    C.wtab = sbg("wtab_s", [128, G * 54])
    C.w2b = sbg("w2b", [128, 8 * 24], BF16)
    C.wbb = sbg("wbb", [128, 8 * 4], BF16)
    C.sel24b = sbg("sel24b", [24, 24], BF16)
    C.constt = sbg("constt", [128, 8])
    CONSTS = [1e-5, 1e-6, 1e-30, 1.0]
    for i, v in enumerate(CONSTS):
        nc.const_aps.register(F32, v, C.constt[:, i:i + 1])

    C.PA = es.enter_context(nc.psum_tensor("PA", [128, 1024], F32))
    C.PB = es.enter_context(nc.psum_tensor("PB", [128, 1024], F32))
    C.PC = es.enter_context(nc.psum_tensor("PC", [128, 1024], F32))
    C.PS = es.enter_context(nc.psum_tensor("PS", [128, 512], F32))
    C.PT = es.enter_context(nc.psum_tensor("PT", [128, 1024], BF16))
    C.P = P

    with es, nc.Block() as blk:
        with P.stage():
            for i, v in enumerate(CONSTS):
                P.do("dve", lambda e, i=i, v=v: e.memset(C.constt[:, i:i + 1], v),
                     w=[P.res("constt")])
        P.barrier()
        with ExitStack() as es_pa:
            sba = lambda n, sh, dt=F32: es_pa.enter_context(nc.sbuf_tensor(n, sh, dt))
            C.w1b = sba("w1b", [128, 9 * 1024], BF16)
            with ExitStack() as es_p0:
                phase0(C, es_p0)
            P.barrier()
            with ExitStack() as es_a:
                phaseA(C, es_a)
        P.barrier()
        with ExitStack() as es_b:
            phaseB(C, es_b)
        P.raw("sync", lambda e, s=P.sems["dma"], v=P.dma_cnt: e.wait_ge(s, v))
        P.finalize(blk)
    return nc, list(C.dbg.keys())


def phase0(C, es):
    nc, P = C.nc, C.P
    R = P.res
    sb = lambda n, sh, dt=F32: es.enter_context(nc.sbuf_tensor(n, sh, dt))
    stg_f = [sb("p0f0", [128, 3072]), sb("p0f1", [128, 3072])]
    stg_i = [sb("p0i0", [128, 3072], I32), sb("p0i1", [128, 3072], I32)]
    stg_b = [sb("p0b0", [128, 3072], BF16), sb("p0b1", [128, 3072], BF16)]
    dgb = [sb("p0d0", [128, 54 * 128], BF16), sb("p0d1", [128, 54 * 128], BF16)]
    cons = sb("p0cons", [128, 512])
    on2 = sb("p0on2", [128, 2])
    NW = 4
    xt = [sb("p0x%d" % i, [128, D]) for i in range(NW)]
    xsq = [sb("p0sq%d" % i, [128, D]) for i in range(NW)]
    xnt = [sb("p0xnt%d" % i, [128, D], BF16) for i in range(NW)]
    xnf = [sb("p0xnf%d" % i, [128, D], BF16) for i in range(NW)]

    rf = lambda i: R("p0f%d" % i)
    ri = lambda i: R("p0i%d" % i)
    rb = lambda i: R("p0b%d" % i)

    with P.stage():
        P.dma(cons[:, 0:128], C.id128_h[:], w=[R("cons")])
        P.dma(cons[:, 128:256], C.maskUs_h[:], w=[R("cons")])
        P.dma(cons[:, 256:384], C.maskUi_h[:], w=[R("cons")])
        P.dma(C.rows[0:1, 0:24], C.b2row_h[:], w=[R("rows")])
        P.dma(C.rows[0:1, 24:28], C.gltrow_h[:], w=[R("rows")])
        P.dma(C.b1fm[:], C.b1fm_h[:], w=[R("b1fm")])
        P.dma(on2[:], C.onorm2_h[:], w=[R("on2")])
        P.dma(C.wtab[:].rearrange("p (g w) -> p g w", g=8),
              C.wtab_h.rearrange("(g p) w -> p g w", p=128), w=[R("wtab")])
        P.dma(stg_f[0][0:24, 0:24], C.sel24_h[:], w=[rf(0)])
        P.dma(stg_f[1][0:1, 0:128], C.ones1r_h[:], w=[rf(1)])
    with P.stage():
        P.do("dve", lambda e: e.tensor_copy(out=C.idb[:], in_=cons[:, 0:128]),
             r=[R("cons")], w=[R("idb")])
        P.do("dve", lambda e: e.tensor_copy(out=C.mUsb[:], in_=cons[:, 128:256]),
             r=[R("cons")], w=[R("mUsb")])
        P.do("dve", lambda e: e.tensor_copy(out=C.mUib[:], in_=cons[:, 256:384]),
             r=[R("cons")], w=[R("mUib")])
        P.do("dve", lambda e: e.memset(C.onesb[:], 1.0), w=[R("onesb")])
        P.do("dve", lambda e: e.tensor_copy(out=C.onesrow[:],
             in_=stg_f[1][0:1, 0:128]), r=[rf(1)], w=[R("onesrow")])
        P.do("dve", lambda e: e.tensor_copy(out=C.sel24b[:],
             in_=stg_f[0][0:24, 0:24]), r=[rf(0)], w=[R("sel24b")])
        P.do("pool", lambda e: e.tensor_copy(out=C.i4b[:, 0:128], in_=cons[:, 0:128]),
             r=[R("cons")], w=[R("i4b")])
        for j in range(1, 4):
            P.do("pool", lambda e, j=j: e.tensor_copy(
                out=C.i4b[:, j * 128:(j + 1) * 128], in_=cons[:, 0:128]),
                r=[R("cons")], w=[R("i4b")])

    ro = C.rows
    with P.stage():
        P.do("dve", lambda e: e.tensor_scalar(out=ro[0:1, 28:32], in0=ro[0:1, 24:28],
             scalar1=0.0, scalar2=None, op0=ALU.max), r=[R("rows")], w=[R("rows")])
        P.do("act", lambda e: e.activation(out=ro[0:1, 32:36], in_=ro[0:1, 24:28],
             func=AF.Abs), r=[R("rows")], w=[R("rows2")])
    with P.stage():
        P.do("act", lambda e: e.activation(out=ro[0:1, 32:36], in_=ro[0:1, 32:36],
             func=AF.Exp, scale=-1.0), r=[R("rows2")], w=[R("rows2")])
    with P.stage():
        P.do("act", lambda e: e.activation(out=ro[0:1, 32:36], in_=ro[0:1, 32:36],
             func=AF.Ln, bias=1.0), r=[R("rows2")], w=[R("rows2")])
    with P.stage():
        P.do("dve", lambda e: e.tensor_tensor(out=ro[0:1, 36:40], in0=ro[0:1, 28:32],
             in1=ro[0:1, 32:36], op=ALU.add), r=[R("rows"), R("rows2")], w=[R("rows")])
    with P.stage():
        P.do("dve", lambda e: e.tensor_scalar(out=ro[0:1, 36:40], in0=ro[0:1, 36:40],
             scalar1=0.5, scalar2=None, op0=ALU.add), r=[R("rows")], w=[R("rows")])
    with P.stage():
        P.do("dve", lambda e: e.reciprocal(out=ro[0:1, 40:44], in_=ro[0:1, 36:40]),
             r=[R("rows")], w=[R("rows")])
    with P.stage():
        P.do("dve", lambda e: e.tensor_copy(
            out=ro[0:1, 44:68].rearrange("p (h s) -> p h s", h=4),
            in_=ro[0:1, 40:44].unsqueeze(2).broadcast_to([1, 4, 6])),
            r=[R("rows")], w=[R("rows")])
    row_b = es.enter_context(nc.sbuf_tensor("p0rowb", [1, 64], BF16))
    with P.stage():
        P.do("dve", lambda e: e.tensor_copy(out=row_b[0:1, 0:24],
             in_=ro[0:1, 44:68]), r=[R("rows")], w=[R("rowb")])
        P.do("dve", lambda e: e.tensor_copy(out=row_b[0:1, 32:56],
             in_=ro[0:1, 0:24]), r=[R("rows")], w=[R("rowb")])
    with P.stage():
        P.do("pe", lambda e: e.matmul(C.PS[:, 0:24], lhsT=C.onesrow[:],
             rhs=row_b[0:1, 0:24], start=True, stop=True),
             r=[R("rowb"), R("onesrow")], w=[R("PS")])
        P.do("pe", lambda e: e.matmul(C.PS[:, 32:56], lhsT=C.onesrow[:],
             rhs=row_b[0:1, 32:56], start=True, stop=True),
             r=[R("rowb"), R("onesrow")], w=[R("PS")])
    with P.stage():
        P.do("dve", lambda e: e.tensor_copy(out=C.invtempT[:], in_=C.PS[:, 0:24]),
             r=[R("PS")], w=[R("invtempT")])
    with P.stage():
        P.do("dve", lambda e: e.tensor_tensor(out=C.b2i[:], in0=C.PS[:, 32:56],
             in1=C.invtempT[:], op=ALU.mult),
             r=[R("PS"), R("invtempT")], w=[R("b2i")])

    T = C.tiny
    for t in range(NT):
        p = t % 4
        xr, sqr = R("p0x%d" % p), R("p0sq%d" % p)
        ntr, nfr = R("p0xnt%d" % p), R("p0xnf%d" % p)
        ln = R("p0ln%d" % p)
        co = p * 8
        with P.stage():
            P.dma(xt[p][:], C.x_h[t * 128:(t + 1) * 128, :], w=[xr])
        with P.stage():
            P.do("act", lambda e, p=p: e.activation(out=xsq[p][:], in_=xt[p][:],
                 func=AF.Square), r=[xr], w=[sqr])
            P.do("dve", lambda e, p=p, co=co: e.reduce_sum(out=T[:, co:co + 1],
                 in_=xt[p][:], axis=AX.XYZW), r=[xr], w=[ln])
        with P.stage():
            P.do("dve", lambda e, p=p, co=co: e.reduce_sum(out=T[:, co + 1:co + 2],
                 in_=xsq[p][:], axis=AX.XYZW), r=[sqr, ln], w=[ln])
            P.do("dve", lambda e, co=co: e.tensor_scalar(out=T[:, co + 2:co + 3],
                 in0=T[:, co:co + 1], scalar1=1.0 / D, scalar2=None, op0=ALU.mult),
                 r=[ln], w=[ln])
        with P.stage():
            P.do("dve", lambda e, co=co: e.tensor_tensor(out=T[:, co + 3:co + 4],
                 in0=T[:, co + 2:co + 3], in1=T[:, co + 2:co + 3], op=ALU.mult),
                 r=[ln], w=[ln])
        with P.stage():
            P.do("dve", lambda e, co=co: e.scalar_tensor_tensor(
                 out=T[:, co + 4:co + 5], in0=T[:, co + 1:co + 2], scalar=1.0 / D,
                 in1=T[:, co + 3:co + 4], op0=ALU.mult, op1=ALU.subtract),
                 r=[ln], w=[ln])
        with P.stage():
            P.do("act", lambda e, co=co: e.activation(out=T[:, co + 5:co + 6],
                 in_=T[:, co + 4:co + 5], func=AF.Ln, bias=1e-5), r=[ln], w=[ln])
        with P.stage():
            P.do("act", lambda e, co=co: e.activation(out=T[:, co + 6:co + 7],
                 in_=T[:, co + 5:co + 6], func=AF.Exp, scale=-0.5), r=[ln], w=[ln])
        with P.stage():
            P.do("dve", lambda e, p=p, co=co: e.tensor_scalar(out=xsq[p][:],
                 in0=xt[p][:], scalar1=T[:, co + 2:co + 3], scalar2=None,
                 op0=ALU.subtract), r=[xr, ln], w=[sqr])
        with P.stage():
            P.do("dve", lambda e, p=p, co=co: e.tensor_scalar(out=xnt[p][:],
                 in0=xsq[p][:], scalar1=T[:, co + 6:co + 7], scalar2=None,
                 op0=ALU.mult), r=[sqr, ln], w=[ntr])
        with P.stage():
            for cch in range(8):
                P.do("pe", lambda e, p=p, cch=cch: e.transpose(
                    C.PT[:, cch * 128:(cch + 1) * 128],
                    in_=xnt[p][:, cch * 128:(cch + 1) * 128], identity=C.idb[:]),
                    r=[ntr, R("idb")], w=[R("PT")])
        with P.stage(dma_out=True):
            P.do("dve", lambda e, p=p: e.tensor_copy(out=xnf[p][:],
                 in_=C.PT[:, 0:1024]), r=[R("PT")], w=[nfr])
        with P.stage():
            P.dma(C.xn_s[t], xnf[p][:], r=[nfr])
            if "xn" in C.dbg:
                P.dma(C.dbg["xn"][t], xnf[p][:], r=[nfr])
    for k in range(8):
        p = k % 2
        with P.stage():
            P.dma(stg_f[p][:], C.qkv_h[k * 128:(k + 1) * 128, :], w=[rf(p)])
        with P.stage(dma_out=True):
            P.do("dve", lambda e, p=p: e.tensor_copy(out=stg_b[p][:],
                 in_=stg_f[p][:]), r=[rf(p)], w=[rb(p)])
        with P.stage():
            P.dma(C.qkv_s[:, :, k * 256:(k + 1) * 256].transpose([1, 0, 2]),
                  stg_b[p][:].rearrange("p (m n) -> p m n", m=12), r=[rb(p)])
    with P.stage():
        P.dma(stg_f[0][:, 0:32].rearrange("p (k n) -> p k n", k=8),
              C.bw_h.rearrange("(k p) n -> p k n", p=128), w=[rf(0)])
    with P.stage():
        P.do("dve", lambda e: e.tensor_copy(out=C.wbb[:], in_=stg_f[0][:, 0:32]),
             r=[rf(0)], w=[R("wbb")])
    for k in range(9):
        rows_n = 128 if k < 8 else 96
        p = k % 2
        with P.stage():
            P.dma(stg_f[p][0:rows_n, 0:1024], C.w1_h[k * 128:k * 128 + rows_n, :],
                  w=[rf(p)])
        with P.stage():
            P.do("dve", lambda e, k=k, rows_n=rows_n, p=p: e.tensor_copy(
                out=C.w1b[0:rows_n, k * 1024:(k + 1) * 1024],
                in_=stg_f[p][0:rows_n, 0:1024]), r=[rf(p)], w=[R("w1b")])
    with P.stage():
        P.dma(stg_f[0][:, 0:192].rearrange("p (c n) -> p c n", c=8),
              C.w2_h.rearrange("(c p) n -> p c n", p=128), w=[rf(0)])
    with P.stage():
        P.do("dve", lambda e: e.tensor_copy(out=C.w2b[:], in_=stg_f[0][:, 0:192]),
             r=[rf(0)], w=[R("w2b")])
    for k in range(8):
        p = k % 2
        with P.stage():
            P.dma(stg_f[p][:, 0:1024], C.ow_h[k * 128:(k + 1) * 128, :], w=[rf(p)])
        with P.stage(dma_out=True):
            P.do("dve", lambda e, p=p, k=k: e.tensor_scalar(
                out=stg_b[p][:, 0:1024], in0=stg_f[p][:, 0:1024],
                scalar1=on2[:, k % 2:k % 2 + 1], scalar2=None, op0=ALU.mult),
                r=[rf(p), R("on2")], w=[rb(p)])
        with P.stage():
            P.dma(C.ow_s[k], stg_b[p][:, 0:1024], r=[rb(p)])
    for k in range(8):
        for half in range(2):
            p = (2 * k + half) % 2
            with P.stage():
                P.dma(stg_i[p][:], C.wgu_h[k * 128:(k + 1) * 128,
                                           half * HID:(half + 1) * HID], w=[ri(p)])
            with P.stage(dma_out=True):
                P.do("dve", lambda e, p=p: e.tensor_copy(out=stg_b[p][:],
                     in_=stg_i[p][:]), r=[ri(p)], w=[rb(p)])
            with P.stage():
                P.dma(C.wgu_s[k][:, half * HID:(half + 1) * HID], stg_b[p][:],
                      r=[rb(p)])
    for c in range(24):
        p = c % 2
        with P.stage():
            P.dma(stg_i[p][:, 0:1024], C.wdT_h[c * 128:(c + 1) * 128, :], w=[ri(p)])
        with P.stage(dma_out=True):
            P.do("dve", lambda e, p=p: e.tensor_copy(out=stg_b[p][:, 0:1024],
                 in_=stg_i[p][:, 0:1024]), r=[ri(p)], w=[rb(p)])
        with P.stage():
            P.dma(C.wdT_s[c], stg_b[p][:, 0:1024], r=[rb(p)])
    for gg in range(G):
        p = gg % 2
        with P.stage(dma_out=True):
            for j in range(54):
                P.do("dve", lambda e, p=p, gg=gg, j=j: e.tensor_scalar(
                    out=dgb[p][:, j * 128:(j + 1) * 128], in0=C.idb[:],
                    scalar1=C.wtab[:, gg * 54 + j:gg * 54 + j + 1], scalar2=None,
                    op0=ALU.mult), r=[R("idb"), R("wtab")], w=[R("dg%d" % p)])
        with P.stage():
            P.dma(C.diag_s[gg], dgb[p][:], r=[R("dg%d" % p)])



def phaseA(C, es):
    nc, P = C.nc, C.P
    R = P.res
    sb = lambda n, sh, dt=F32: es.enter_context(nc.sbuf_tensor(n, sh, dt))
    B = Ctx()
    B.xnst = sb("a_xnst", [128, 4096], BF16)
    B.qpre = sb("a_qpre", [128, 8 * 516], BF16)
    B.kpre = sb("a_kpre", [128, 8 * 516], BF16)
    B.vpre = sb("a_vpre", [128, 8 * 516], BF16)
    B.vf = sb("a_vf", [128, 8 * 544], BF16)
    B.f3 = sb("a_f3", [128, 4096], BF16)
    B.f7 = sb("a_f7", [128, 4096], BF16)
    B.f31 = sb("a_f31", [128, 4096], BF16)
    B.wpair = [sb("a_wp0", [128, 2048], BF16), sb("a_wp1", [128, 2048], BF16)]
    B.dg = [sb("a_dg0", [128, 54 * 128], BF16), sb("a_dg1", [128, 54 * 128], BF16)]
    B.betaf = sb("a_betaf", [4, 512], BF16)
    B.betat = sb("a_betat", [128, 16])
    B.rst = sb("a_rst", [128, 32])
    B.rr2 = sb("a_rr2", [128, 8])
    B.sq2 = sb("a_sq2", [128, 1024], BF16)
    B.kntm = sb("a_kntm", [128, 2048], BF16)
    B.kbtm = sb("a_kbtm", [128, 2048], BF16)
    B.vbtm = sb("a_vbtm", [128, 2048], BF16)
    B.wnf = sb("a_wnf", [128, 2048], BF16)
    B.Apre = sb("a_apre", [128, 1024], BF16)
    B.ATs = sb("a_ats0", [128, 1024], BF16)
    B.As = B.Apre
    B.XXT = [sb("a_xp0", [128, 2048], BF16), sb("a_xp1", [128, 2048], BF16)]
    B.PTs = sb("a_pts", [128, 1024], BF16)
    B.attb = sb("a_attb", [128, 1024], BF16)
    B.uhat = sb("a_uhat", [128, 1024], BF16)
    B.S = sb("a_S", [128, 2048])
    B.Sbb = sb("a_Sbb", [128, 2048], BF16)
    B.tmp_tm = sb("a_tmptm", [128, 1024], BF16)
    B.dfm2 = sb("a_dfm2", [128, 2048], BF16)
    B.b1f2 = sb("a_b1f2", [128, 2048], BF16)
    B.babs = sb("a_babs", [128, 1024], BF16)
    B.bsq = sb("a_bsq", [128, 1024], BF16)
    B.st96 = sb("a_st96", [128, 96], BF16)
    B.stT = sb("a_stT", [96, 128], BF16)
    B.l24 = sb("a_l24", [128, 24])
    B.w24 = sb("a_w24", [128, 24], BF16)
    B.wgtT = sb("a_wgtT", [24, 128], BF16)
    B.hgst = sb("a_hgst", [128, 2048], BF16)
    B.facc = sb("a_facc", [128, 1024], BF16)
    B.tmpc = sb("a_tmpc", [128, 1024], BF16)
    B.ffm = sb("a_ffm", [128, 1024], BF16)
    B.sot = sb("a_sot", [128, 1024])
    B.rrow = sb("a_rrow", [1, 512], BF16)
    B.owc = [sb("a_owc%d" % j, [128, 1024], BF16) for j in range(3)]
    PA, PB, PC, PS, PT = C.PA, C.PB, C.PC, C.PS, C.PT
    B.half = lambda pt, i: pt[:, i * 512:(i + 1) * 512]

    with P.stage():
        P.do("dve", lambda e: e.memset(B.S[:], 0.0), w=[R("a_S")])
        P.do("pool", lambda e: e.memset(B.Sbb[:], 0.0), w=[R("a_Sbb")])
        P.do("pool", lambda e: e.memset(B.qpre[:], 0.0), w=[R("a_qpre")])
        P.do("pool", lambda e: e.memset(B.kpre[:], 0.0), w=[R("a_kpre")])
        P.do("pool", lambda e: e.memset(B.vpre[:], 0.0), w=[R("a_vpre")])
        P.do("dve", lambda e: e.memset(B.vf[:], 0.0), w=[R("a_vf")])

    for st in range(NST):
        t0 = st * 4
        with P.stage():
            for i in range(4):
                P.dma(B.xnst[:].rearrange("p (g t) -> p g t", g=8)
                      [:, :, i * 128:(i + 1) * 128],
                      C.xn_s[t0 + i].rearrange("p (g t) -> p g t", g=8),
                      w=[R("a_xnst")])
            P.dma(B.dg[0][:], C.diag_s[0], w=[R("a_dg0")])
            P.dma(B.dg[1][:], C.diag_s[1], w=[R("a_dg1")])
            P.dma(B.wpair[0][:], C.qkv_s[0], w=[R("a_wp0")])
            P.dma(B.wpair[1][:], C.qkv_s[1], w=[R("a_wp1")])
        dsts = [(B.qpre, "a_qpre"), (B.kpre, "a_kpre"), (B.vpre, "a_vpre")]
        for pair in range(12):
            pw = pair % 2
            dstbuf, dstres = dsts[pair // 4]
            PP, ppres = (PA, ("PA0", "PA1")) if pair % 2 == 0 else (PB, ("PB0", "PB1"))
            with P.stage():
                for m2 in range(2):
                    for k in range(8):
                        P.do("pe", lambda e, pw=pw, m2=m2, k=k, PP=PP: e.matmul(
                            B.half(PP, m2),
                            lhsT=B.wpair[pw][:, k * 256 + m2 * 128:k * 256 + m2 * 128 + 128],
                            rhs=B.xnst[:, k * 512:(k + 1) * 512],
                            start=(k == 0), stop=(k == 7)),
                            r=[R("a_wp%d" % pw), R("a_xnst")],
                            w=[R(ppres[m2])])
            with P.stage():
                if pair + 2 < 12:
                    P.dma(B.wpair[pw][:], C.qkv_s[pair + 2], w=[R("a_wp%d" % pw)])
                for m2 in range(2):
                    gout = (pair % 4) * 2 + m2
                    if m2 == 0:
                        P.do("dve", lambda e, PP=PP, dstbuf=dstbuf, gout=gout:
                             e.tensor_copy(out=dstbuf[:, gout * 516 + 4:gout * 516 + 516],
                                           in_=B.half(PP, 0)),
                             r=[R(ppres[0])], w=[R(dstres)])
                    else:
                        P.do("act", lambda e, PP=PP, dstbuf=dstbuf, gout=gout:
                             e.activation(out=dstbuf[:, gout * 516 + 4:gout * 516 + 516],
                                          in_=B.half(PP, 1), func=AF.Identity),
                             r=[R(ppres[1])], w=[R(dstres)])
        with P.stage():
            for k in range(8):
                P.do("pe", lambda e, k=k: e.matmul(
                    PS[0:4, 0:512], lhsT=C.wbb[:, k * 4:(k + 1) * 4],
                    rhs=B.xnst[:, k * 512:(k + 1) * 512],
                    start=(k == 0), stop=(k == 7)),
                    r=[R("wbb"), R("a_xnst")], w=[R("PS")])
        with P.stage():
            P.do("act", lambda e: e.activation(out=B.betaf[:], in_=PS[0:4, 0:512],
                 func=AF.Sigmoid), r=[R("PS")], w=[R("a_betaf")])
        with P.stage():
            for i in range(4):
                P.do("pe", lambda e, i=i: e.transpose(
                    PT[:, i * 4:(i + 1) * 4],
                    in_=B.betaf[0:4, i * 128:(i + 1) * 128], identity=C.idb[0:4, 0:4]),
                    r=[R("a_betaf"), R("idb")], w=[R("PT")])
        with P.stage():
            P.do("dve", lambda e: e.tensor_copy(out=B.betat[:], in_=PT[:, 0:16]),
                 r=[R("PT")], w=[R("a_betat")])
        CONV = [("q", B.qpre, "a_qpre", 0, 4), ("k", B.kpre, "a_kpre", 4, 4),
                ("v", B.vpre, "a_vpre", 8, 4)]
        pcrot = [("PA0", lambda: B.half(PA, 0)), ("PA1", lambda: B.half(PA, 1)),
                 ("PB0", lambda: B.half(PB, 0)), ("PB1", lambda: B.half(PB, 1)),
                 ("PC0", lambda: B.half(PC, 0)), ("PC1", lambda: B.half(PC, 1))]
        pcidx = 0
        for gg in range(8):
            dgr = R("a_dg%d" % (gg % 2))
            dgb_ = B.dg[gg % 2]
            for name, buf, bres, tbase, W in CONV:
                pres, pap = pcrot[pcidx % 6]
                pcidx += 1
                with P.stage():
                    for j in range(W):
                        P.do("pe", lambda e, dgb_=dgb_, tbase=tbase, j=j, buf=buf,
                             gg=gg, pap=pap, W=W: e.matmul(
                                 pap(), lhsT=dgb_[:, (tbase + j) * 128:(tbase + j + 1) * 128],
                                 rhs=buf[:, gg * 516 + 4 - j:gg * 516 + 4 - j + 512],
                                 start=(j == 0), stop=(j == W - 1)),
                            r=[dgr, R(bres)], w=[R(pres)])
                with P.stage():
                    P.do("dve", lambda e, buf=buf, gg=gg: e.tensor_copy(
                        out=buf[:, gg * 516 + 1:gg * 516 + 4],
                        in_=buf[:, gg * 516 + 513:gg * 516 + 516]),
                        r=[R(bres)], w=[R(bres)])
                with P.stage():
                    if name == "v":
                        P.do("act", lambda e, pap=pap, gg=gg: e.activation(
                            out=B.vf[:, gg * 544 + 32:gg * 544 + 544],
                            in_=pap(), func=AF.Silu),
                            r=[R(pres)], w=[R("a_vf")])
                    else:
                        P.do("act", lambda e, pap=pap, buf=buf, gg=gg: e.activation(
                            out=buf[:, gg * 516 + 4:gg * 516 + 516],
                            in_=pap(), func=AF.Silu),
                            r=[R(pres)], w=[R(bres)])
            with P.stage():
                P.do("dve", lambda e, gg=gg: e.tensor_copy(
                    out=B.vf[:, gg * 544 + 1:gg * 544 + 32],
                    in_=B.vf[:, gg * 544 + 513:gg * 544 + 544]),
                    r=[R("a_vf")], w=[R("a_vf")])
            firs = [(13, 3, B.f3, "a_f3"), (16, 7, B.f7, "a_f7"), (23, 31, B.f31, "a_f31")]
            fp = []
            for tbase, W, fbuf, fres in firs:
                pres, pap = pcrot[pcidx % 6]
                pcidx += 1
                fp.append((pres, pap, fbuf, fres))
                with P.stage():
                    for j in range(W):
                        P.do("pe", lambda e, dgb_=dgb_, tbase=tbase, j=j, gg=gg,
                             pap=pap, W=W: e.matmul(
                                 pap(), lhsT=dgb_[:, (tbase + j) * 128:(tbase + j + 1) * 128],
                                 rhs=B.vf[:, gg * 544 + 32 - j:gg * 544 + 32 - j + 512],
                                 start=(j == 0), stop=(j == W - 1)),
                            r=[dgr, R("a_vf")], w=[R(pres)])
            with P.stage():
                for ii, (pres, pap, fbuf, fres) in enumerate(fp):
                    eng = ["dve", "act", "dve"][ii]
                    if eng == "dve":
                        P.do("dve", lambda e, pap=pap, fbuf=fbuf, gg=gg: e.tensor_copy(
                            out=fbuf[:, gg * 512:(gg + 1) * 512], in_=pap()),
                            r=[R(pres)], w=[R(fres)])
                    else:
                        P.do("act", lambda e, pap=pap, fbuf=fbuf, gg=gg: e.activation(
                            out=fbuf[:, gg * 512:(gg + 1) * 512], in_=pap(),
                            func=AF.Identity), r=[R(pres)], w=[R(fres)])
            if gg + 2 < 8:
                with P.stage():
                    P.dma(B.dg[gg % 2][:], C.diag_s[gg + 2], w=[R("a_dg%d" % (gg % 2))])
        if "q" in C.dbg and st == 0:
            with P.stage():
                P.dma(C.dbg["q"][:], B.qpre[:, 0:4096], r=[R("a_qpre")])
                P.dma(C.dbg["k"][:], B.kpre[:, 0:4096], r=[R("a_kpre")])
                P.dma(C.dbg["v"][:], B.vf[:, 0:4096], r=[R("a_vf")])
        for ipair in range(2):
            pairA_prefix(C, P, R, B, st, ipair)
            for p in range(2):
                i = 2 * ipair + p
                tileA_suffix(C, P, R, B, st, i)
            with P.stage():
                P.do("act", lambda e: e.activation(out=B.hgst[:], in_=B.hgst[:],
                     func=AF.Gelu), r=[R("a_hgst")], w=[R("a_hgst")])
            for p in range(2):
                i = 2 * ipair + p
                tileA_phase2(C, P, R, B, st, i)


def pairA_prefix(C, P, R, B, st, ipair):
    PA, PB, PC, PS, PT = C.PA, C.PB, C.PC, C.PS, C.PT
    half = B.half
    o2 = ipair * 256
    qs2 = B.qpre[:].rearrange("p (c n) -> p c n", c=8)[:, :, 4 + o2:4 + o2 + 256]
    ks2 = B.kpre[:].rearrange("p (c n) -> p c n", c=8)[:, :, 4 + o2:4 + o2 + 256]
    qs = lambda ch, p, o2=o2: B.qpre[:, ch * 516 + 4 + o2 + p * 128:
                                     ch * 516 + 4 + o2 + p * 128 + 128]
    ks = lambda ch, p, o2=o2: B.kpre[:, ch * 516 + 4 + o2 + p * 128:
                                     ch * 516 + 4 + o2 + p * 128 + 128]
    vs = lambda ch, p, o2=o2: B.vf[:, ch * 544 + 32 + o2 + p * 128:
                                   ch * 544 + 32 + o2 + p * 128 + 128]
    sqb = [B.sq2, B.bsq]
    sqr_ = ["a_sq2", "a_bsq"]
    for which, stk2, colbase in (("q", qs2, 0), ("k", ks2, 4)):
        sres = "a_qpre" if which == "q" else "a_kpre"
        with P.stage():
            for p in range(2):
                P.do("dve", lambda e, stk2=stk2, p=p: e.tensor_tensor(
                    out=sqb[p][:].rearrange("p (c n) -> p c n", c=8),
                    in0=stk2[:, :, p * 128:(p + 1) * 128],
                    in1=stk2[:, :, p * 128:(p + 1) * 128], op=ALU.mult),
                    r=[R(sres)], w=[R(sqr_[p])])
        with P.stage():
            for p in range(2):
                for hh in range(4):
                    for cc in range(2):
                        P.do("pe", lambda e, p=p, hh=hh, cc=cc, colbase=colbase: e.matmul(
                            PS[:, p * 16 + colbase + hh:p * 16 + colbase + hh + 1],
                            lhsT=sqb[p][:, (2 * hh + cc) * 128:(2 * hh + cc + 1) * 128],
                            rhs=C.onesb[:], start=(cc == 0), stop=(cc == 1)),
                            r=[R(sqr_[p]), R("onesb")], w=[R("PS")])
    with P.stage():
        for p in range(2):
            P.do("act", lambda e, p=p: e.activation(out=B.rst[:, p * 16:p * 16 + 8],
                 in_=PS[:, p * 16:p * 16 + 8], func=AF.Ln, bias=1e-6),
                 r=[R("PS")], w=[R("a_rst")])
    with P.stage():
        for p in range(2):
            P.do("act", lambda e, p=p: e.activation(out=B.rst[:, p * 16:p * 16 + 8],
                 in_=B.rst[:, p * 16:p * 16 + 8], func=AF.Exp, scale=-0.5),
                 r=[R("a_rst")], w=[R("a_rst")])
    with P.stage():
        for p in range(2):
            i = 2 * ipair + p
            P.do("dve", lambda e, p=p, i=i: e.tensor_tensor(
                out=B.rst[:, p * 16 + 8:p * 16 + 12], in0=B.rst[:, p * 16 + 4:p * 16 + 8],
                in1=B.betat[:, i * 4:(i + 1) * 4], op=ALU.mult),
                r=[R("a_rst"), R("a_betat")], w=[R("a_rst")])
    for p in range(2):
        pcres, pchalf = ("PC0", 0) if p == 0 else ("PC1", 1)
        with P.stage():
            for ch in range(8):
                P.do("pe", lambda e, ch=ch, p=p: e.transpose(
                    PT[:, ch * 128:(ch + 1) * 128], in_=ks(ch, p), identity=C.idb[:]),
                    r=[R("a_kpre"), R("idb")], w=[R("PT")])
            for hh in range(4):
                for cc in range(2):
                    P.do("pe", lambda e, hh=hh, cc=cc, p=p, pchalf=pchalf: e.matmul(
                        half(PC, pchalf)[:, hh * 128:(hh + 1) * 128],
                        lhsT=ks(2 * hh + cc, p), rhs=ks(2 * hh + cc, p),
                        start=(cc == 0), stop=(cc == 1)),
                        r=[R("a_kpre")], w=[R(pcres)])
        with P.stage():
            for hh in range(4):
                P.do("dve", lambda e, hh=hh, p=p: e.tensor_scalar(
                    out=B.kntm[:, p * 1024 + hh * 256:p * 1024 + (hh + 1) * 256],
                    in0=PT[:, hh * 256:(hh + 1) * 256],
                    scalar1=B.rst[:, p * 16 + 4 + hh:p * 16 + 5 + hh], scalar2=None,
                    op0=ALU.mult), r=[R("PT"), R("a_rst")], w=[R("a_kntm")])
                P.do("dve", lambda e, hh=hh, p=p: e.tensor_scalar(
                    out=B.kbtm[:, p * 1024 + hh * 256:p * 1024 + (hh + 1) * 256],
                    in0=PT[:, hh * 256:(hh + 1) * 256],
                    scalar1=B.rst[:, p * 16 + 8 + hh:p * 16 + 9 + hh], scalar2=None,
                    op0=ALU.mult), r=[R("PT"), R("a_rst")], w=[R("a_kbtm")])
        with P.stage():
            for ch in range(8):
                P.do("pe", lambda e, ch=ch, p=p: e.transpose(
                    PT[:, ch * 128:(ch + 1) * 128], in_=vs(ch, p), identity=C.idb[:]),
                    r=[R("a_vf"), R("idb")], w=[R("PT")])
        with P.stage():
            for hh in range(4):
                i = 2 * ipair + p
                P.do("dve", lambda e, hh=hh, p=p, i=i: e.tensor_scalar(
                    out=B.vbtm[:, p * 1024 + hh * 256:p * 1024 + (hh + 1) * 256],
                    in0=PT[:, hh * 256:(hh + 1) * 256],
                    scalar1=B.betat[:, i * 4 + hh:i * 4 + hh + 1], scalar2=None,
                    op0=ALU.mult), r=[R("PT"), R("a_betat")], w=[R("a_vbtm")])
    with P.stage():
        for p in range(2):
            for hh in range(4):
                P.do("dve", lambda e, hh=hh, p=p: e.tensor_scalar(
                    out=B.Apre[:, p * 512 + hh * 128:p * 512 + (hh + 1) * 128],
                    in0=half(PC, p)[:, hh * 128:(hh + 1) * 128],
                    scalar1=B.rst[:, p * 16 + 8 + hh:p * 16 + 9 + hh], scalar2=-1.0,
                    op0=ALU.mult, op1=ALU.mult),
                    r=[R("PC%d" % p), R("a_rst")], w=[R("a_apre")])
    with P.stage():
        for j in range(8):
            P.do("pe", lambda e, j=j: e.transpose(
                PT[:, j * 128:(j + 1) * 128],
                in_=B.Apre[:, j * 128:(j + 1) * 128], identity=C.idb[:]),
                r=[R("a_apre"), R("idb")], w=[R("PT")])
    with P.stage():
        for p in range(2):
            for hh in range(4):
                P.do("dve", lambda e, hh=hh, p=p: e.scalar_tensor_tensor(
                    out=B.ATs[:, p * 512 + hh * 128:p * 512 + (hh + 1) * 128],
                    in0=PT[:, p * 512 + hh * 128:p * 512 + (hh + 1) * 128],
                    scalar=B.rst[:, p * 16 + 4 + hh:p * 16 + 5 + hh], in1=C.mUsb[:],
                    op0=ALU.mult, op1=ALU.mult),
                    r=[R("PT"), R("a_rst"), R("mUsb")], w=[R("a_ats0")])
    with P.stage():
        for j in range(8):
            P.do("pe", lambda e, j=j: e.transpose(
                PT[:, j * 128:(j + 1) * 128],
                in_=B.ATs[:, j * 128:(j + 1) * 128], identity=C.idb[:]),
                r=[R("a_ats0"), R("idb")], w=[R("PT")])
    with P.stage():
        P.do("dve", lambda e: e.tensor_copy(out=B.As[:], in_=PT[:, 0:1024]),
             r=[R("PT")], w=[R("a_apre")])
        P.do("dve", lambda e: e.tensor_tensor(
            out=B.PTs[:].rearrange("p (r n) -> p r n", r=2),
            in0=B.ATs[:].rearrange("p (r n) -> p r n", r=2),
            in1=C.i4b[:].unsqueeze(1).broadcast_to([128, 2, 512]), op=ALU.add),
            r=[R("a_ats0"), R("i4b")], w=[R("a_pts")])
    lvls = (2, 4, 8, 16, 32, 64)
    prev = None
    for li, lvl in enumerate(lvls):
        need_xt = lvl < 64
        if li == 0:
            xsl = lambda j: B.As[:, j * 128:(j + 1) * 128]
            xtsl = lambda j: B.ATs[:, j * 128:(j + 1) * 128]
            rr_ = ["a_apre", "a_ats0"]
        else:
            pb_, pr_ = prev
            xsl = lambda j, b=pb_: b[:, (j // 4) * 1024 + (j % 4) * 128:
                                     (j // 4) * 1024 + (j % 4) * 128 + 128]
            xtsl = lambda j, b=pb_: b[:, (j // 4) * 1024 + 512 + (j % 4) * 128:
                                      (j // 4) * 1024 + 512 + (j % 4) * 128 + 128]
            rr_ = [pr_, pr_]
        cur = B.XXT[li % 2]
        curr = "a_xp%d" % (li % 2)
        with P.stage():
            for j in range(8):
                p = j // 4
                P.do("pe", lambda e, j=j, p=p, xsl=xsl, xtsl=xtsl: e.matmul(
                    PC[:, p * 512 + (j % 4) * 128:p * 512 + (j % 4) * 128 + 128],
                    lhsT=xtsl(j), rhs=xsl(j), start=True, stop=True),
                    r=[R(rr_[0]), R(rr_[1])], w=[R("PC%d" % p)])
            if need_xt:
                for j in range(8):
                    p = j // 4
                    P.do("pe", lambda e, j=j, p=p, xsl=xsl, xtsl=xtsl: e.matmul(
                        PA[:, p * 512 + (j % 4) * 128:p * 512 + (j % 4) * 128 + 128],
                        lhsT=xsl(j), rhs=xtsl(j), start=True, stop=True),
                        r=[R(rr_[0]), R(rr_[1])], w=[R("PA%d" % p)])
            if li > 0:
                for j in range(8):
                    p = j // 4
                    P.do("pe", lambda e, j=j, p=p, xsl=xsl: e.matmul(
                        PB[:, p * 512 + (j % 4) * 128:p * 512 + (j % 4) * 128 + 128],
                        lhsT=xsl(j),
                        rhs=B.PTs[:, j * 128:(j + 1) * 128],
                        start=True, stop=True),
                        r=[R(rr_[0]), R("a_pts")], w=[R("PB%d" % p)])
        with P.stage():
            evaceng = "dve" if li % 2 == 0 else "act"
            for p in range(2):
                if need_xt:
                    if evaceng == "dve":
                        P.do("dve", lambda e, cur=cur, p=p: e.tensor_copy(
                            out=cur[:, p * 1024:p * 1024 + 512], in_=half(PC, p)),
                            r=[R("PC%d" % p)], w=[R(curr)])
                        P.do("dve", lambda e, cur=cur, p=p: e.tensor_copy(
                            out=cur[:, p * 1024 + 512:p * 1024 + 1024], in_=half(PA, p)),
                            r=[R("PA%d" % p)], w=[R(curr)])
                    else:
                        P.do("act", lambda e, cur=cur, p=p: e.activation(
                            out=cur[:, p * 1024:p * 1024 + 512], in_=half(PC, p),
                            func=AF.Identity), r=[R("PC%d" % p)], w=[R(curr)])
                        P.do("act", lambda e, cur=cur, p=p: e.activation(
                            out=cur[:, p * 1024 + 512:p * 1024 + 1024], in_=half(PA, p),
                            func=AF.Identity), r=[R("PA%d" % p)], w=[R(curr)])
                else:
                    P.do("dve", lambda e, cur=cur, p=p: e.tensor_copy(
                        out=cur[:, p * 1024:p * 1024 + 512], in_=half(PC, p)),
                        r=[R("PC%d" % p)], w=[R(curr)])
            if li > 0:
                P.do("dve", lambda e: e.tensor_tensor(out=B.PTs[:], in0=PB[:, 0:1024],
                     in1=B.PTs[:], op=ALU.add),
                     r=[R("PB0"), R("PB1"), R("a_pts")], w=[R("a_pts")])
        prev = (cur, curr)
    pb_, pr_ = prev
    with P.stage():
        for j in range(8):
            p = j // 4
            P.do("pe", lambda e, j=j, p=p, pb_=pb_: e.matmul(
                PB[:, p * 512 + (j % 4) * 128:p * 512 + (j % 4) * 128 + 128],
                lhsT=pb_[:, p * 1024 + (j % 4) * 128:p * 1024 + (j % 4) * 128 + 128],
                rhs=B.PTs[:, j * 128:(j + 1) * 128], start=True, stop=True),
                r=[R(pr_), R("a_pts")], w=[R("PB%d" % p)])
    with P.stage():
        P.do("dve", lambda e: e.tensor_tensor(out=B.PTs[:], in0=PB[:, 0:1024],
             in1=B.PTs[:], op=ALU.add),
             r=[R("PB0"), R("PB1"), R("a_pts")], w=[R("a_pts")])
    with P.stage():
        for p in range(2):
            PPw = PB if p == 0 else PA
            for hh in range(4):
                for cc in range(2):
                    P.do("pe", lambda e, p=p, hh=hh, cc=cc, PPw=PPw: e.matmul(
                        PPw[:, (hh * 2 + cc) * 128:(hh * 2 + cc + 1) * 128],
                        lhsT=B.kbtm[:, p * 1024 + (hh * 2 + cc) * 128:
                                    p * 1024 + (hh * 2 + cc + 1) * 128],
                        rhs=B.PTs[:, p * 512 + hh * 128:p * 512 + (hh + 1) * 128],
                        start=True, stop=True),
                        r=[R("a_kbtm"), R("a_pts")],
                        w=[R("%s%d" % ("PB" if p == 0 else "PA", (hh * 2 + cc) // 4))])
        for p in range(2):
            for hh in range(4):
                for cc in range(2):
                    P.do("pe", lambda e, p=p, hh=hh, cc=cc: e.matmul(
                        half(PC, p)[:, hh * 128:(hh + 1) * 128],
                        lhsT=ks(2 * hh + cc, p), rhs=qs(2 * hh + cc, p),
                        start=(cc == 0), stop=(cc == 1)),
                        r=[R("a_kpre"), R("a_qpre")], w=[R("PC%d" % p)])
    with P.stage():
        for p in range(2):
            PPw = PB if p == 0 else PA
            P.do("dve", lambda e, p=p, PPw=PPw: e.tensor_scalar(
                out=B.wnf[:, p * 1024:(p + 1) * 1024], in0=PPw[:, 0:1024],
                scalar1=-1.0, scalar2=None, op0=ALU.mult),
                r=[R("PB0"), R("PB1")] if p == 0 else [R("PA0"), R("PA1")],
                w=[R("a_wnf")])
            for hh in range(4):
                P.do("dve", lambda e, hh=hh, p=p: e.scalar_tensor_tensor(
                    out=B.attb[:, p * 512 + hh * 128:p * 512 + (hh + 1) * 128],
                    in0=half(PC, p)[:, hh * 128:(hh + 1) * 128],
                    scalar=B.rst[:, p * 16 + 4 + hh:p * 16 + 5 + hh], in1=C.mUib[:],
                    op0=ALU.mult, op1=ALU.mult),
                    r=[R("PC%d" % p), R("a_rst"), R("mUib")], w=[R("a_attb")])


def tileA_suffix(C, P, R, B, st, i):
    PA, PB, PC, PS, PT = C.PA, C.PB, C.PC, C.PS, C.PT
    half = B.half
    p = i % 2
    o = i * 128
    t = st * 4 + i
    so = p * 1024
    dfm = B.dfm2[:, p * 1024:(p + 1) * 1024]
    b1f = B.b1f2[:, p * 1024:(p + 1) * 1024]
    qs = lambda ch: B.qpre[:, ch * 516 + 4 + o:ch * 516 + 4 + o + 128]
    vs = lambda ch: B.vf[:, ch * 544 + 32 + o:ch * 544 + 32 + o + 128]
    with P.stage():
        for hh in range(4):
            P.do("pe", lambda e, hh=hh: e.matmul(
                PC[:, hh * 256:(hh + 1) * 256],
                lhsT=B.PTs[:, so // 2 + hh * 128:so // 2 + (hh + 1) * 128],
                rhs=B.vbtm[:, so + hh * 256:so + (hh + 1) * 256], start=True, stop=False),
                r=[R("a_pts"), R("a_vbtm")], w=[R("PC0"), R("PC1")])
            for cc in range(2):
                P.do("pe", lambda e, hh=hh, cc=cc: e.matmul(
                    PC[:, hh * 256:(hh + 1) * 256],
                    lhsT=B.wnf[:, so + (hh * 2 + cc) * 128:so + (hh * 2 + cc + 1) * 128],
                    rhs=B.Sbb[:, hh * 512 + cc * 256:hh * 512 + cc * 256 + 256],
                    start=False, stop=(cc == 1)),
                    r=[R("a_wnf"), R("a_Sbb")], w=[R("PC0"), R("PC1")])
    with P.stage():
        P.do("dve", lambda e: e.tensor_copy(out=B.uhat[:], in_=PC[:, 0:1024]),
             r=[R("PC0"), R("PC1")], w=[R("a_uhat")])
    with P.stage():
        for hh in range(4):
            for cc in range(2):
                P.do("pe", lambda e, hh=hh, cc=cc: e.matmul(
                    PB[:, hh * 256:(hh + 1) * 256],
                    lhsT=qs(2 * hh + cc),
                    rhs=B.Sbb[:, hh * 512 + cc * 256:hh * 512 + cc * 256 + 256],
                    start=(cc == 0), stop=False),
                    r=[R("a_qpre"), R("a_Sbb")], w=[R("PB0"), R("PB1")])
            P.do("pe", lambda e, hh=hh: e.matmul(
                PB[:, hh * 256:(hh + 1) * 256],
                lhsT=B.attb[:, so // 2 + hh * 128:so // 2 + (hh + 1) * 128],
                rhs=B.uhat[:, hh * 256:(hh + 1) * 256], start=False, stop=True),
                r=[R("a_attb"), R("a_uhat")], w=[R("PB0"), R("PB1")])
        for hh in range(4):
            for cc in range(2):
                PP = PA if hh < 2 else PC
                pres = ("PA0", "PA1") if hh < 2 else ("PC0", "PC1")
                P.do("pe", lambda e, hh=hh, cc=cc, PP=PP: e.matmul(
                    PP[:, (hh % 2) * 512 + cc * 256:(hh % 2) * 512 + cc * 256 + 256],
                    lhsT=B.kntm[:, so + (hh * 2 + cc) * 128:so + (hh * 2 + cc + 1) * 128],
                    rhs=B.uhat[:, hh * 256:(hh + 1) * 256], start=True, stop=True),
                    r=[R("a_kntm"), R("a_uhat")], w=[R(pres[0]), R(pres[1])])
    with P.stage():
        for hh in range(4):
            P.do("dve", lambda e, hh=hh, p=p: e.tensor_scalar(
                out=B.tmp_tm[:, hh * 256:(hh + 1) * 256],
                in0=PB[:, hh * 256:(hh + 1) * 256],
                scalar1=B.rst[:, p * 16 + hh:p * 16 + hh + 1], scalar2=None,
                op0=ALU.mult),
                r=[R("PB0"), R("PB1"), R("a_rst")], w=[R("a_tmptm")])
        P.do("dve", lambda e: e.tensor_tensor(out=B.S[:, 0:1024], in0=PA[:, 0:1024],
             in1=B.S[:, 0:1024], op=ALU.add),
             r=[R("PA0"), R("PA1"), R("a_S")], w=[R("a_S")])
        P.do("dve", lambda e: e.tensor_tensor(out=B.S[:, 1024:2048], in0=PC[:, 0:1024],
             in1=B.S[:, 1024:2048], op=ALU.add),
             r=[R("PC0"), R("PC1"), R("a_S")], w=[R("a_S")])
    with P.stage():
        P.do("act", lambda e: e.activation(out=B.Sbb[:, 0:1024], in_=B.S[:, 0:1024],
             func=AF.Identity), r=[R("a_S")], w=[R("a_Sbb")])
        P.do("dve", lambda e: e.tensor_copy(out=B.Sbb[:, 1024:2048],
             in_=B.S[:, 1024:2048]), r=[R("a_S")], w=[R("a_Sbb")])
    with P.stage():
        for ch in range(8):
            P.do("pe", lambda e, ch=ch: e.transpose(
                PT[:, ch * 128:(ch + 1) * 128],
                in_=B.tmp_tm[:, ch * 128:(ch + 1) * 128], identity=C.idb[:]),
                r=[R("a_tmptm"), R("idb")], w=[R("PT")])
    with P.stage():
        P.do("dve", lambda e, dfm=dfm: e.tensor_copy(out=dfm, in_=PT[:, 0:1024]),
             r=[R("PT")], w=[R("a_dfm2")])
    if "delta" in C.dbg and t == 0:
        with P.stage():
            P.dma(C.dbg["delta"][:], B.tmp_tm[:, 0:1024], r=[R("a_tmptm")])
    with P.stage():
        for ch in range(8):
            P.do("dve", lambda e, ch=ch, b1f=b1f: e.tensor_scalar(
                out=b1f[:, ch * 128:(ch + 1) * 128], in0=vs(ch),
                scalar1=C.wtab[:, ch * 54 + 12:ch * 54 + 13], scalar2=None,
                op0=ALU.mult), r=[R("a_vf"), R("wtab")], w=[R("a_b1f2")])
    fstk = lambda f: f[:].rearrange("p (c n) -> p c n", c=8)[:, :, o:o + 128]
    cstk = lambda b: b.rearrange("p (c n) -> p c n", c=8)
    vstack = B.vf[:].rearrange("p (c n) -> p c n", c=8)[:, :, 32 + o:32 + o + 128]
    branches = [
        (lambda ch: b1f[:, ch * 128:(ch + 1) * 128], cstk(b1f), "a_b1f2"),
        (lambda ch: B.f3[:, ch * 512 + o:ch * 512 + o + 128], fstk(B.f3), "a_f3"),
        (lambda ch: B.f7[:, ch * 512 + o:ch * 512 + o + 128], fstk(B.f7), "a_f7"),
        (lambda ch: B.f31[:, ch * 512 + o:ch * 512 + o + 128], fstk(B.f31), "a_f31"),
        (lambda ch: dfm[:, ch * 128:(ch + 1) * 128], cstk(dfm), "a_dfm2"),
        (lambda ch: vs(ch), vstack, "a_vf"),
    ]
    for br, (bsl, bstk, bres) in enumerate(branches):
        with P.stage():
            P.do("act", lambda e, bstk=bstk: e.activation(
                out=B.bsq[:].rearrange("p (c n) -> p c n", c=8),
                in_=bstk, func=AF.Square),
                r=[R(bres)], w=[R("a_bsq")])
            P.do("act", lambda e, bstk=bstk: e.activation(
                out=B.babs[:].rearrange("p (c n) -> p c n", c=8),
                in_=bstk, func=AF.Abs),
                r=[R(bres)], w=[R("a_babs")])
        with P.stage():
            for hh in range(4):
                col = 8 + hh * 6 + br
                for cc in range(2):
                    P.do("pe", lambda e, hh=hh, cc=cc, col=col, bsl=bsl: e.matmul(
                        PS[:, col:col + 1], lhsT=bsl(2 * hh + cc), rhs=C.onesb[:],
                        start=(cc == 0), stop=(cc == 1)),
                        r=[R(bres), R("onesb")], w=[R("PS")])
                for cc in range(2):
                    P.do("pe", lambda e, hh=hh, cc=cc, col=col: e.matmul(
                        PS[:, 32 + col - 8:32 + col - 8 + 1],
                        lhsT=B.babs[:, (2 * hh + cc) * 128:(2 * hh + cc + 1) * 128],
                        rhs=C.onesb[:], start=(cc == 0), stop=(cc == 1)),
                        r=[R("a_babs"), R("onesb")], w=[R("PS")])
                for cc in range(2):
                    P.do("pe", lambda e, hh=hh, cc=cc, col=col: e.matmul(
                        PS[:, 56 + col - 8:56 + col - 8 + 1],
                        lhsT=B.bsq[:, (2 * hh + cc) * 128:(2 * hh + cc + 1) * 128],
                        rhs=C.onesb[:], start=(cc == 0), stop=(cc == 1)),
                        r=[R("a_bsq"), R("onesb")], w=[R("PS")])
    T = C.tiny
    with P.stage():
        P.do("dve", lambda e: e.tensor_scalar(
            out=st96ap(B.st96, 0), in0=psblock(PS, 8),
            scalar1=1.0 / 256, scalar2=None, op0=ALU.mult),
            r=[R("PS")], w=[R("a_st96")])
        P.do("dve", lambda e: e.tensor_scalar(
            out=st96ap(B.st96, 2), in0=psblock(PS, 32),
            scalar1=1.0 / 256, scalar2=None, op0=ALU.mult),
            r=[R("PS")], w=[R("a_st96")])
        P.do("act", lambda e: e.activation(out=T[:, 0:24], in_=PS[:, 8:32],
             func=AF.Square), r=[R("PS")], w=[R("tinyA")])
    with P.stage():
        P.do("dve", lambda e: e.scalar_tensor_tensor(out=T[:, 0:24], in0=T[:, 0:24],
             scalar=-1.0 / 256, in1=PS[:, 56:80], op0=ALU.mult, op1=ALU.add),
             r=[R("tinyA"), R("PS")], w=[R("tinyA")])
    with P.stage():
        P.do("dve", lambda e: e.tensor_scalar(out=T[:, 0:24], in0=T[:, 0:24],
             scalar1=0.0, scalar2=None, op0=ALU.max), r=[R("tinyA")], w=[R("tinyA")])
    with P.stage():
        P.do("act", lambda e: e.activation(out=T[:, 0:24], in_=T[:, 0:24],
             func=AF.Ln, scale=1.0 / 255, bias=1e-30), r=[R("tinyA")], w=[R("tinyA")])
    with P.stage():
        P.do("act", lambda e: e.activation(out=st96ap(B.st96, 1), in_=T[:, 0:24],
             func=AF.Exp, scale=0.5), r=[R("tinyA")], w=[R("a_st96")])
        P.do("act", lambda e: e.activation(out=T[:, 0:24], in_=PS[:, 56:80],
             func=AF.Ln, bias=1e-30), r=[R("PS")], w=[R("tinyB")])
    with P.stage():
        P.do("act", lambda e: e.activation(out=st96ap(B.st96, 3), in_=T[:, 0:24],
             func=AF.Exp, scale=0.5), r=[R("tinyB")], w=[R("a_st96")])
    if "stats" in C.dbg and t == 0:
        with P.stage():
            P.do("dve", lambda e: e.tensor_copy(out=B.sot[:, 0:96], in_=B.st96[:]),
                 r=[R("a_st96")], w=[R("a_sot")])
        with P.stage():
            P.dma(C.dbg["stats"][:], B.sot[:, 0:96], r=[R("a_sot")])
    with P.stage():
        P.do("pe", lambda e: e.transpose(PT[0:96, 0:128], in_=B.st96[:],
             identity=C.idb[:]), r=[R("a_st96"), R("idb")], w=[R("PT")])
    with P.stage():
        P.do("dve", lambda e: e.tensor_copy(out=B.stT[:], in_=PT[0:96, 0:128]),
             r=[R("PT")], w=[R("a_stT")])
    with P.stage():
        for m in range(8):
            for k in range(9):
                if k < 8:
                    P.do("pe", lambda e, m=m, k=k: e.matmul(
                        C.PC[:, m * 128:(m + 1) * 128]
                        if m < 4 else C.PC[:, (m - 4) * 128 + 512:(m - 3) * 128 + 512],
                        lhsT=C.w1b[:, k * 1024 + m * 128:k * 1024 + (m + 1) * 128],
                        rhs=B.xnst[:, k * 512 + o:k * 512 + o + 128],
                        start=(k == 0), stop=False),
                        r=[R("w1b"), R("a_xnst")],
                        w=[R("PC0") if m < 4 else R("PC1")])
                else:
                    P.do("pe", lambda e, m=m: e.matmul(
                        C.PC[:, m * 128:(m + 1) * 128]
                        if m < 4 else C.PC[:, (m - 4) * 128 + 512:(m - 3) * 128 + 512],
                        lhsT=C.w1b[0:96, 8 * 1024 + m * 128:8 * 1024 + (m + 1) * 128],
                        rhs=B.stT[:], start=False, stop=True),
                        r=[R("w1b"), R("a_stT")],
                        w=[R("PC0") if m < 4 else R("PC1")])
    with P.stage():
        for m in range(8):
            P.do("dve", lambda e, m=m, p=p: e.tensor_scalar(
                out=B.hgst[:, p * 1024 + m * 128:p * 1024 + (m + 1) * 128],
                in0=C.PC[:, m * 128:(m + 1) * 128] if m < 4
                else C.PC[:, (m - 4) * 128 + 512:(m - 3) * 128 + 512],
                scalar1=C.b1fm[:, m:m + 1], scalar2=None, op0=ALU.add),
                r=[R("PC0") if m < 4 else R("PC1"), R("b1fm")], w=[R("a_hgst")])


def tileA_phase2(C, P, R, B, st, i):
    t = st * 4 + i
    o = i * 128
    PA, PB, PC, PS, PT = C.PA, C.PB, C.PC, C.PS, C.PT
    half = B.half
    sl = i % 2
    hgst, stT, l24, w24, wgtT = B.hgst, B.stT, B.l24, B.w24, B.wgtT
    facc, tmpc, ffm, sot = B.facc, B.tmpc, B.ffm, B.sot
    rr2, vf, f3, f7, f31 = B.rr2, B.vf, B.f3, B.f7, B.f31
    dfm = B.dfm2[:, sl * 1024:(sl + 1) * 1024]
    b1f = B.b1f2[:, sl * 1024:(sl + 1) * 1024]
    vs = lambda ch: vf[:, ch * 544 + 32 + o:ch * 544 + 32 + o + 128]
    owc = B.owc
    with P.stage():
        for kk in range(3):
            P.dma(owc[kk][:], C.ow_s[kk], w=[R("a_owc%d" % kk)])
    with P.stage():
        for cch in range(8):
            P.do("pe", lambda e, cch=cch, sl=sl: e.matmul(
                PS[:, 96:120],
                lhsT=hgst[:, sl * 1024 + cch * 128:sl * 1024 + (cch + 1) * 128],
                rhs=C.w2b[:, cch * 24:(cch + 1) * 24],
                start=(cch == 0), stop=(cch == 7)),
                r=[R("a_hgst"), R("w2b")], w=[R("PS")])
    if "hg" in C.dbg and t == 0:
        with P.stage():
            P.do("dve", lambda e: e.tensor_copy(out=B.sot[:],
                 in_=hgst[:, 0:1024]), r=[R("a_hgst")], w=[R("a_sot")])
        with P.stage():
            P.dma(C.dbg["hg"][:], B.sot[:], r=[R("a_sot")])
    T = C.tiny
    with P.stage():
        P.do("dve", lambda e: e.tensor_tensor(out=l24[:], in0=PS[:, 96:120],
             in1=C.invtempT[:], op=ALU.mult), r=[R("PS"), R("invtempT")],
             w=[R("a_l24")])
    with P.stage():
        P.do("dve", lambda e: e.tensor_tensor(out=l24[:], in0=l24[:],
             in1=C.b2i[:], op=ALU.add), r=[R("a_l24"), R("b2i")], w=[R("a_l24")])
    with P.stage():
        P.do("dve", lambda e: e.reduce_max(out=T[:, 0:4],
             in_=l24[:].rearrange("p (h s) -> p h s", h=4), axis=AX.X),
             r=[R("a_l24")], w=[R("tinyA")])
    with P.stage():
        for hh in range(4):
            P.do("dve", lambda e, hh=hh: e.tensor_scalar(
                out=l24[:, hh * 6:(hh + 1) * 6], in0=l24[:, hh * 6:(hh + 1) * 6],
                scalar1=T[:, hh:hh + 1], scalar2=None, op0=ALU.subtract),
                r=[R("a_l24"), R("tinyA")], w=[R("a_l24")])
    with P.stage():
        P.do("act", lambda e: e.activation(out=l24[:], in_=l24[:], func=AF.Exp),
             r=[R("a_l24")], w=[R("a_l24")])
    with P.stage():
        P.do("dve", lambda e: e.reduce_sum(out=T[:, 0:4],
             in_=l24[:].rearrange("p (h s) -> p h s", h=4), axis=AX.X),
             r=[R("a_l24")], w=[R("tinyA")])
    with P.stage():
        P.do("dve", lambda e: e.reciprocal(out=T[:, 4:8], in_=T[:, 0:4]),
             r=[R("tinyA")], w=[R("tinyA")])
    with P.stage():
        for hh in range(4):
            P.do("dve", lambda e, hh=hh: e.tensor_scalar(
                out=w24[:, hh * 6:(hh + 1) * 6], in0=l24[:, hh * 6:(hh + 1) * 6],
                scalar1=T[:, 4 + hh:5 + hh], scalar2=None, op0=ALU.mult),
                r=[R("a_l24"), R("tinyA")], w=[R("a_w24")])
    if "wgt" in C.dbg and t == 0:
        with P.stage():
            P.do("dve", lambda e: e.tensor_copy(out=l24[:], in_=w24[:]),
                 r=[R("a_w24")], w=[R("a_l24")])
        with P.stage():
            P.dma(C.dbg["wgt"][:], l24[:], r=[R("a_l24")])
    with P.stage():
        P.do("pe", lambda e: e.transpose(PT[0:24, 0:128], in_=w24[:],
             identity=C.idb[:]), r=[R("a_w24"), R("idb")], w=[R("PT")])
    with P.stage():
        P.do("dve", lambda e: e.tensor_copy(out=wgtT[:], in_=PT[0:24, 0:128]),
             r=[R("PT")], w=[R("a_wgtT")])
    fstk = lambda f: f[:].rearrange("p (c n) -> p c n", c=8)[:, :, o:o + 128]
    cstk = lambda b: b.rearrange("p (c n) -> p c n", c=8)
    vstack = vf[:].rearrange("p (c n) -> p c n", c=8)[:, :, 32 + o:32 + o + 128]
    branches = [
        (cstk(b1f), "a_b1f2"),
        (fstk(f3), "a_f3"),
        (fstk(f7), "a_f7"),
        (fstk(f31), "a_f31"),
        (cstk(dfm), "a_dfm2"),
        (vstack, "a_vf"),
    ]
    for s6, (bstk, bres) in enumerate(branches):
        PP, ppres = (PA, ("PA0", "PA1")) if s6 % 2 == 0 else (PB, ("PB0", "PB1"))
        with P.stage():
            for ch in range(8):
                hh = ch // 2
                P.do("pe", lambda e, ch=ch, hh=hh, s6=s6, PP=PP: e.matmul(
                    PP[:, ch * 128:(ch + 1) * 128],
                    lhsT=C.sel24b[:, hh * 6 + s6:hh * 6 + s6 + 1]
                    .broadcast_to([24, 128]),
                    rhs=wgtT[:], start=True, stop=True),
                    r=[R("sel24b"), R("a_wgtT")], w=[R(ppres[ch // 4])])
        with P.stage():
            dst, dres = (facc, "a_facc") if s6 == 0 else (tmpc, "a_tmpc")
            P.do("dve", lambda e, bstk=bstk, PP=PP, dst=dst: e.tensor_tensor(
                out=dst[:].rearrange("p (c n) -> p c n", c=8),
                in0=bstk, in1=PP[:, 0:1024].rearrange("p (c n) -> p c n", c=8),
                op=ALU.mult),
                r=[R(bres), R(ppres[0]), R(ppres[1])], w=[R(dres)])
        if s6 > 0:
            with P.stage():
                P.do("pool", lambda e: e.tensor_tensor(out=facc[:], in0=facc[:],
                     in1=tmpc[:], op=ALU.add),
                     r=[R("a_facc"), R("a_tmpc")], w=[R("a_facc")])
    with P.stage():
        P.do("dve", lambda e: e.tensor_tensor(out=tmpc[:], in0=facc[:],
             in1=facc[:], op=ALU.mult), r=[R("a_facc")], w=[R("a_tmpc")])
    with P.stage():
        for hh in range(4):
            for cc in range(2):
                P.do("pe", lambda e, hh=hh, cc=cc: e.matmul(
                    PS[:, 120 + hh:121 + hh],
                    lhsT=tmpc[:, (2 * hh + cc) * 128:(2 * hh + cc + 1) * 128],
                    rhs=C.onesb[:], start=(cc == 0), stop=(cc == 1)),
                    r=[R("a_tmpc"), R("onesb")], w=[R("PS")])
    with P.stage():
        P.do("act", lambda e, sl=sl: e.activation(out=rr2[:, sl * 4:(sl + 1) * 4],
             in_=PS[:, 120:124], func=AF.Ln, scale=1.0 / 256, bias=1e-5),
             r=[R("PS")], w=[R("a_rr2")])
    with P.stage():
        P.do("act", lambda e, sl=sl: e.activation(out=rr2[:, sl * 4:(sl + 1) * 4],
             in_=rr2[:, sl * 4:(sl + 1) * 4], func=AF.Exp, scale=-0.5),
             r=[R("a_rr2")], w=[R("a_rr2")])
    with P.stage():
        P.do("dve", lambda e, sl=sl: e.tensor_copy(out=w24[:, 0:4],
             in_=rr2[:, sl * 4:(sl + 1) * 4]), r=[R("a_rr2")], w=[R("a_w24")])
    with P.stage():
        for hh in range(4):
            P.do("pe", lambda e, hh=hh: e.transpose(
                PT[0:1, hh * 128:(hh + 1) * 128], in_=w24[:, hh:hh + 1],
                identity=C.idb[:]), r=[R("a_w24"), R("idb")], w=[R("PT")])
    with P.stage():
        P.do("dve", lambda e: e.tensor_copy(out=B.rrow[0:1, 0:512],
             in_=PT[0:1, 0:512]), r=[R("PT")], w=[R("a_rrow")])
    with P.stage():
        for hh in range(4):
            P.do("pe", lambda e, hh=hh: e.matmul(
                half(C.PA, 0)[:, hh * 128:(hh + 1) * 128], lhsT=C.onesrow[:],
                rhs=B.rrow[0:1, hh * 128:(hh + 1) * 128],
                start=True, stop=True),
                r=[R("a_rrow"), R("onesrow")], w=[R("PA0")])
    with P.stage():
        for ch in range(8):
            P.do("dve", lambda e, ch=ch: e.tensor_tensor(
                out=ffm[:, ch * 128:(ch + 1) * 128],
                in0=facc[:, ch * 128:(ch + 1) * 128],
                in1=half(C.PA, 0)[:, (ch // 2) * 128:(ch // 2 + 1) * 128],
                op=ALU.mult),
                r=[R("a_facc"), R("PA0")], w=[R("a_ffm")])
    for grp in ((0, 1, 2), (3, 4, 5), (6, 7)):
        with P.stage():
            for k in grp:
                for nn in range(2):
                    P.do("pe", lambda e, k=k, nn=nn: e.matmul(
                        PC[:, nn * 512:(nn + 1) * 512],
                        lhsT=ffm[:, k * 128:(k + 1) * 128],
                        rhs=owc[k % 3][:, nn * 512:(nn + 1) * 512],
                        start=(k == 0), stop=(k == 7)),
                        r=[R("a_ffm"), R("a_owc%d" % (k % 3))], w=[R("PC%d" % nn)])
        if grp[0] == 0:
            with P.stage():
                for kk in (3, 4, 5):
                    P.dma(owc[kk % 3][:], C.ow_s[kk], w=[R("a_owc%d" % (kk % 3))])
        elif grp[0] == 3:
            with P.stage():
                for kk in (6, 7):
                    P.dma(owc[kk % 3][:], C.ow_s[kk], w=[R("a_owc%d" % (kk % 3))])
    with P.stage(dma_out=True):
        P.do("dve", lambda e: e.tensor_copy(out=sot[:], in_=PC[:, 0:1024]),
             r=[R("PC0"), R("PC1")], w=[R("a_sot")])
    with P.stage():
        P.dma(C.s_s[t], sot[:], r=[R("a_sot")])
        if "s" in C.dbg and t == 0:
            P.dma(C.dbg["s"][:], sot[:], r=[R("a_sot")])


def st96ap(st96, stat):
    return st96[:].rearrange("p (x f) -> p x f", f=4)[:, :, stat:stat + 1]


def psblock(PS, base):
    return PS[:, base:base + 24]


def phaseB(C, es):
    nc, P = C.nc, C.P
    R = P.res
    sb = lambda n, sh, dt=F32: es.enter_context(nc.sbuf_tensor(n, sh, dt))
    wgu = sb("b_wgu", [128, 8 * 6144], BF16)
    wdb = sb("b_wdb", [128, 24 * 1024], BF16)
    xnb = sb("b_xnb", [128, 4096], BF16)
    hst = sb("b_hst", [128, 24 * 512], BF16)
    sg = sb("b_sg", [128, 1024], BF16)
    xb0 = sb("b_x0", [128, 1024])
    sb0 = sb("b_s0", [128, 1024])
    gb0 = sb("b_gb0", [128, 1024], BF16)
    xb = [xb0, xb0]
    sbf = [sb0, sb0]
    gb = [gb0, gb0]
    tm = sbf
    ob = xb
    PA, PB, PC, PS = C.PA, C.PB, C.PC, C.PS
    with P.stage():
        P.dma(wgu[:].rearrange("p (k n) -> p k n", k=8),
              C.wgu_s.rearrange("k p n -> p k n"), w=[R("b_wgu")])
        P.dma(wdb[:].rearrange("p (k n) -> p k n", k=24),
              C.wdT_s.rearrange("k p n -> p k n"), w=[R("b_wdb")])
    for st in range(NST):
        t0 = st * 4
        with P.stage():
            for i in range(4):
                P.dma(xnb[:].rearrange("p (g t) -> p g t", g=8)
                      [:, :, i * 128:(i + 1) * 128],
                      C.xn_s[t0 + i].rearrange("p (g t) -> p g t", g=8),
                      w=[R("b_xnb")])
        for mp in range(12):
            with P.stage():
                for m2 in range(2):
                    m = mp * 2 + m2
                    for k in range(8):
                        P.do("pe", lambda e, m=m, m2=m2, k=k: e.matmul(
                            PA[:, m2 * 512:(m2 + 1) * 512],
                            lhsT=wgu[:, k * 6144 + m * 128:k * 6144 + (m + 1) * 128],
                            rhs=xnb[:, k * 512:(k + 1) * 512],
                            start=(k == 0), stop=(k == 7)),
                            r=[R("b_wgu"), R("b_xnb")], w=[R("PA%d" % m2)])
                    for k in range(8):
                        P.do("pe", lambda e, m=m, m2=m2, k=k: e.matmul(
                            PB[:, m2 * 512:(m2 + 1) * 512],
                            lhsT=wgu[:, k * 6144 + (24 + m) * 128:k * 6144 + (25 + m) * 128],
                            rhs=xnb[:, k * 512:(k + 1) * 512],
                            start=(k == 0), stop=(k == 7)),
                            r=[R("b_wgu"), R("b_xnb")], w=[R("PB%d" % m2)])
            with P.stage():
                P.do("act", lambda e: e.activation(out=sg[:], in_=PA[:, 0:1024],
                     func=AF.Silu, scale=1.0 / 127), r=[R("PA0"), R("PA1")],
                     w=[R("b_sg")])
            with P.stage():
                P.do("dve", lambda e, mp=mp: e.tensor_tensor(
                    out=hst[:, mp * 1024:(mp + 1) * 1024], in0=PB[:, 0:1024],
                    in1=sg[:], op=ALU.mult),
                    r=[R("PB0"), R("PB1"), R("b_sg")], w=[R("b_hst")])
        for i in range(4):
            t = t0 + i
            p = t % 2
            with P.stage():
                P.dma(xb[p][:], C.x_h[t * 128:(t + 1) * 128, :], w=[R("b_x0")])
                P.dma(sbf[p][:], C.s_s[t], w=[R("b_s0")])
            with P.stage():
                for cch in range(24):
                    for nn in range(2):
                        P.do("pe", lambda e, cch=cch, nn=nn, i=i: e.matmul(
                            PC[:, nn * 512:(nn + 1) * 512],
                            lhsT=hst[:, cch * 512 + i * 128:cch * 512 + i * 128 + 128],
                            rhs=wdb[:, cch * 1024 + nn * 512:cch * 1024 + (nn + 1) * 512],
                            start=(cch == 0), stop=(cch == 23)),
                            r=[R("b_hst"), R("b_wdb")], w=[R("PC%d" % nn)])
            with P.stage():
                P.do("act", lambda e, p=p: e.activation(out=gb[p][:], in_=PC[:, 0:1024],
                     func=AF.Silu, scale=1.0 / (127.0 * 127.0)),
                     r=[R("PC0"), R("PC1")], w=[R("b_gb0")])
            if "g" in C.dbg and t == 0:
                with P.stage():
                    P.do("dve", lambda e, p=p: e.tensor_copy(out=tm[p][:], in_=gb[p][:]),
                         r=[R("b_gb0")], w=[R("b_s0")])
                with P.stage():
                    P.dma(C.dbg["g"][:], tm[p][:], r=[R("b_s0")])
            with P.stage():
                P.do("dve", lambda e, p=p: e.tensor_tensor(out=tm[p][:], in0=sbf[p][:],
                     in1=gb[p][:], op=ALU.mult), r=[R("b_s0"), R("b_gb0")],
                     w=[R("b_s0")])
            with P.stage(dma_out=True):
                P.do("pool", lambda e, p=p: e.tensor_tensor(out=ob[p][:], in0=tm[p][:],
                     in1=xb[p][:], op=ALU.add), r=[R("b_s0"), R("b_x0")],
                     w=[R("b_x0")])
            with P.stage():
                P.dma(C.out_h[t * 128:(t + 1) * 128, :], ob[p][:], r=[R("b_x0")])


def kernel(**inputs):
    hp = host_prep(inputs)
    nc, dbg_names = build_program()
    x_full = np.asarray(inputs["L0_attn_hidden_states"])
    in_maps = []
    for c in range(N_CORES):
        m = dict(hp)
        m["x"] = np.ascontiguousarray(x_full[c])
        in_maps.append(m)
    res = run_bass_kernel_spmd(nc, in_maps, core_ids=list(range(N_CORES)))
    out = np.stack([res.results[c]["out"] for c in range(N_CORES)], axis=0)
    if DEBUG:
        kernel._dbg = [{k: res.results[c].get("dbg_" + k) for k in DEBUG}
                       for c in range(N_CORES)]
    return out


# version 16
# speedup vs baseline: 1.0213x; 1.0196x over previous
import numpy as np
import ml_dtypes
from contextlib import ExitStack, contextmanager
import concourse.bass as bass
import concourse.mybir as mybir
from concourse.bass_utils import run_bass_kernel_spmd

F32 = mybir.dt.float32
BF16 = mybir.dt.bfloat16
I32 = mybir.dt.int32
AF = mybir.ActivationFunctionType
ALU = mybir.AluOpType
AX = mybir.AxisListType

N_CORES = 8
L, D, H, DK = 4096, 1024, 4, 256
NT, NST, G, HID = 32, 8, 8, 3072

DEBUG = []

ENG2BLK = {"pe": "tensor", "dve": "vector", "act": "scalar",
           "pool": "gpsimd", "sync": "sync"}


class Res:
    __slots__ = ("name", "w_tok", "readers")

    def __init__(self, name):
        self.name = name
        self.w_tok = None
        self.readers = {}


class Prog:
    def __init__(self, nc):
        self.nc = nc
        self.steps = {e: [] for e in ENG2BLK}
        self.cnt = {e: 0 for e in ENG2BLK if e != "sync"}
        self.dma_cnt = 0
        self.hwm = {}
        self.in_stage = False
        self.touch = None
        self.emitted = None
        self.sems = {}
        self._res = {}
        self.dma_safe = {e: 0 for e in ("pe", "dve", "act", "pool")}

    def res(self, name):
        if name not in self._res:
            self._res[name] = Res(name)
        return self._res[name]

    def make_sems(self, es):
        for e in ("pe", "dve", "act", "pool", "dma", "dmadone"):
            self.sems[e] = es.enter_context(self.nc.semaphore("sem_" + e))
        self.dmadone_cnt = 0
        self.dmadone_pub = {}

    def raw(self, eng, fn):
        self.steps[eng].append(fn)

    def _wait(self, waiter, tok, gate=False):
        key, val = tok
        if gate and waiter == "sync" and key in ("pe", "dve", "act", "pool"):
            if self.dma_safe[key] < val:
                self.cnt[key] += 1
                sem0 = self.sems[key]
                self.raw(key, lambda e, s=sem0: e.drain().then_inc(s, 1))
                self.dma_safe[key] = self.cnt[key]
            val = self.dma_safe[key]
            tok = (key, val)
        if key == "dma" and waiter != "sync":
            total = self.dma_cnt
            if total not in self.dmadone_pub:
                self.dmadone_cnt += 1
                self.dmadone_pub[total] = self.dmadone_cnt
                ds, dd = self.sems["dma"], self.sems["dmadone"]
                if self.hwm.get(("sync", "dma"), 0) < total:
                    self.hwm[("sync", "dma")] = total
                    self.raw("sync", lambda e, s=ds, v=total: e.wait_ge(s, v))
                self.raw("sync", lambda e, s=dd: e.sem_inc(s, 1))
            key, val = "dmadone", self.dmadone_pub[total]
        if self.hwm.get((waiter, key), 0) >= val:
            return
        self.hwm[(waiter, key)] = val
        sem = self.sems[key]
        self.raw(waiter, lambda e, s=sem, v=val: e.wait_ge(s, v))

    def _hazards(self, eng, r, w):
        for b in r:
            if b.w_tok is not None:
                self._wait(eng, b.w_tok)
        for b in w:
            if b.w_tok is not None:
                self._wait(eng, b.w_tok)
            for rk, rv in b.readers.items():
                self._wait(eng, (rk, rv))

    def do(self, eng, fn, r=(), w=()):
        assert self.in_stage
        self._hazards(eng, r, w)
        self.raw(eng, fn)
        self.last_op_idx[eng] = len(self.steps[eng]) - 1
        self.emitted.add(eng)
        for b in r:
            self.touch.append((eng, b, "r"))
        for b in w:
            self.touch.append((eng, b, "w"))

    def dma(self, out_ap, in_ap, r=(), w=()):
        assert self.in_stage
        for b in r:
            if b.w_tok is not None:
                self._wait("sync", b.w_tok, gate=True)
        self._hazards("sync", (), w)
        self.dma_cnt += 16
        val = self.dma_cnt
        sem = self.sems["dma"]
        self.raw("sync",
                 lambda e, o=out_ap, i=in_ap, s=sem: e.dma_start(out=o, in_=i)
                 .then_inc(s, 16))
        for b in r:
            b.readers["dma"] = max(b.readers.get("dma", 0), val)
        for b in w:
            b.w_tok = ("dma", val)
            b.readers = {}

    @contextmanager
    def stage(self, use_drain=False, dma_out=False):
        use_drain = use_drain or dma_out
        assert not self.in_stage
        self.in_stage = True
        self.touch, self.emitted = [], set()
        self.last_op_idx = {}
        yield
        touched, emitted = self.touch, self.emitted
        self.in_stage = False
        self.touch = self.emitted = None
        toks = {}
        for e in emitted:
            self.cnt[e] += 1
            toks[e] = (e, self.cnt[e])
            sem = self.sems[e]
            if not use_drain:
                idx = self.last_op_idx[e]
                orig = self.steps[e][idx]
                self.steps[e][idx] = (
                    lambda en, o=orig, s=sem: o(en).then_inc(s, 1))
            else:
                self.raw(e, lambda en, s=sem: en.drain().then_inc(s, 1))
                self.dma_safe[e] = self.cnt[e]
        for eng, b, kind in touched:
            tok = toks[eng]
            if kind == "w":
                b.w_tok = tok
                b.readers = {}
            else:
                k, v = tok
                b.readers[k] = max(b.readers.get(k, 0), v)

    def barrier(self):
        with self.stage(use_drain=True):
            for e in ("pe", "dve", "act", "pool"):
                self.do(e, lambda en: en.nop())
        toks = [(e, self.cnt[e]) for e in ("pe", "dve", "act", "pool")]
        toks.append(("dma", self.dma_cnt))
        for e in ("pe", "dve", "act", "pool", "sync"):
            for t in toks:
                self._wait(e, t)

    def finalize(self, blk):
        for e, bname in ENG2BLK.items():
            fns = self.steps[e]
            getattr(blk, bname)(lambda eng, fns=fns: [f(eng) for f in fns])


def host_prep(inputs):
    f = lambda k: np.asarray(inputs[k])
    qkv = np.concatenate([f("L0_attn_q_w"), f("L0_attn_k_w"),
                          f("L0_attn_v_w")], axis=1)
    bw = f("L0_attn_b_w")

    def taps(name):
        return f(name)[:, 0, ::-1]
    wtab = np.concatenate([taps("L0_attn_qc_w"), taps("L0_attn_kc_w"),
                           taps("L0_attn_vc_w"), taps("L0_attn_fir_w1"),
                           taps("L0_attn_fir_w3"), taps("L0_attn_fir_w7"),
                           taps("L0_attn_fir_w31")], axis=1)
    sel = np.eye(24, dtype=np.float32)[:, :, None]
    c = lambda a: np.ascontiguousarray(a)
    return dict(
        qkv=c(qkv.astype(np.float32)),
        bw=c(bw.astype(np.float32)),
        w1=c(f("L0_attn_mlp_w1").astype(np.float32)),
        w2=c(f("L0_attn_mlp_w2").astype(np.float32)),
        ow=c(f("L0_attn_o_w").astype(np.float32)),
        wtab=c(wtab.astype(np.float32)),
        wgu=c(np.concatenate([f("L0_ffn_w_gate").T, f("L0_ffn_w_up").T], axis=1)),
        wdT=c(f("L0_ffn_w_down").T),
        b1fm=c(f("L0_attn_mlp_b1").reshape(8, 128).T.astype(np.float32)),
        onorm2=c(f("L0_attn_onorm_w").reshape(2, 128).T.astype(np.float32)),
        b2row=c(f("L0_attn_mlp_b2").reshape(1, 24).astype(np.float32)),
        gltrow=c(f("L0_attn_gate_log_temp").reshape(1, 4).astype(np.float32)),
        id128=np.eye(128, dtype=np.float32),
        maskUs=c(np.triu(np.ones((128, 128), np.float32), 1)),
        maskUi=c(np.triu(np.ones((128, 128), np.float32), 0)),
        sel24=c(sel.reshape(24, 24)),
        ones1r=np.ones((1, 128), np.float32),
    )


class Ctx:
    pass


def build_program():
    nc = bass.Bass("TRN2", target_bir_lowering=False)
    C = Ctx()
    C.nc = nc
    d = lambda n, sh, dt, kind: nc.dram_tensor(n, sh, dt, kind=kind).ap()
    C.x_h = d("x", [L, D], F32, "ExternalInput")
    C.qkv_h = d("qkv", [D, 3072], F32, "ExternalInput")
    C.bw_h = d("bw", [D, 4], F32, "ExternalInput")
    C.w1_h = d("w1", [1120, D], F32, "ExternalInput")
    C.w2_h = d("w2", [D, 24], F32, "ExternalInput")
    C.ow_h = d("ow", [D, D], F32, "ExternalInput")
    C.wtab_h = d("wtab", [D, 54], F32, "ExternalInput")
    C.wgu_h = d("wgu", [D, 2 * HID], I32, "ExternalInput")
    C.wdT_h = d("wdT", [HID, D], I32, "ExternalInput")
    C.b1fm_h = d("b1fm", [128, 8], F32, "ExternalInput")
    C.onorm2_h = d("onorm2", [128, 2], F32, "ExternalInput")
    C.b2row_h = d("b2row", [1, 24], F32, "ExternalInput")
    C.gltrow_h = d("gltrow", [1, 4], F32, "ExternalInput")
    C.id128_h = d("id128", [128, 128], F32, "ExternalInput")
    C.maskUs_h = d("maskUs", [128, 128], F32, "ExternalInput")
    C.maskUi_h = d("maskUi", [128, 128], F32, "ExternalInput")
    C.sel24_h = d("sel24", [24, 24], F32, "ExternalInput")
    C.ones1r_h = d("ones1r", [1, 128], F32, "ExternalInput")
    C.out_h = d("out", [L, D], F32, "ExternalOutput")
    C.dbg = {}
    if "xn" in DEBUG:
        C.dbg["xn"] = nc.dram_tensor("dbg_xn", [NT, 128, D], BF16,
                                     kind="ExternalOutput").ap()
    for name, sh, dt_ in [("dd", [128, 4096], BF16), ("dd2", [128, 2048], BF16),
                          ("rst", [128, 16], F32),
                          ("q", [128, 4096], BF16), ("k", [128, 4096], BF16),
                          ("v", [128, 4096], BF16), ("beta", [128, 16], F32),
                          ("delta", [128, 1024], BF16), ("stats", [128, 96], F32),
                          ("wgt", [128, 24], F32), ("s", [128, 1024], F32),
                          ("hg", [128, 1024], F32), ("g", [128, 1024], F32)]:
        if name in DEBUG:
            C.dbg[name] = d("dbg_" + name, sh, dt_, "ExternalOutput")
    C.xn_s = d("xn_s", [NT, 128, D], BF16, "Internal")
    C.s_s = d("s_s", [NT, 128, D], F32, "Internal")
    C.diag_s = d("diag_s", [G, 128, 54 * 128], BF16, "Internal")
    C.ow_s = d("ow_s", [8, 128, D], BF16, "Internal")
    C.qkv_s = d("qkv_s", [12, 128, 2048], BF16, "Internal")
    C.wgu_s = d("wgu_s", [8, 128, 2 * HID], BF16, "Internal")
    C.wdT_s = d("wdT_s", [24, 128, D], BF16, "Internal")

    es = ExitStack()
    P = Prog(nc)
    P.make_sems(es)
    sbg = lambda n, sh, dt=F32: es.enter_context(nc.sbuf_tensor(n, sh, dt))

    C.idb = sbg("idb", [128, 128], BF16)
    C.mUsb = sbg("mUsb", [128, 128], BF16)
    C.mUib = sbg("mUib", [128, 128], BF16)
    C.i4b = sbg("i4b", [128, 512], BF16)
    C.onesb = sbg("onesb", [128, 1], BF16)
    C.onesrow = sbg("onesrow", [1, 128], BF16)
    C.rows = sbg("rows", [1, 96])
    C.tiny = sbg("tiny", [128, 32])
    C.invtempT = sbg("invtempT", [128, 24])
    C.b2i = sbg("b2i", [128, 24])
    C.b1fm = sbg("b1fm_s", [128, 8])
    C.wtab = sbg("wtab_s", [128, G * 54])
    C.w2b = sbg("w2b", [128, 8 * 24], BF16)
    C.wbb = sbg("wbb", [128, 8 * 4], BF16)
    C.sel24b = sbg("sel24b", [24, 24], BF16)
    C.constt = sbg("constt", [128, 8])
    CONSTS = [1e-5, 1e-6, 1e-30, 1.0]
    for i, v in enumerate(CONSTS):
        nc.const_aps.register(F32, v, C.constt[:, i:i + 1])

    C.PA = es.enter_context(nc.psum_tensor("PA", [128, 1024], F32))
    C.PB = es.enter_context(nc.psum_tensor("PB", [128, 1024], F32))
    C.PC = es.enter_context(nc.psum_tensor("PC", [128, 1024], F32))
    C.PS = es.enter_context(nc.psum_tensor("PS", [128, 512], F32))
    C.PT = es.enter_context(nc.psum_tensor("PT", [128, 1024], BF16))
    C.P = P

    with es, nc.Block() as blk:
        with P.stage():
            for i, v in enumerate(CONSTS):
                P.do("dve", lambda e, i=i, v=v: e.memset(C.constt[:, i:i + 1], v),
                     w=[P.res("constt")])
        P.barrier()
        with ExitStack() as es_pa:
            sba = lambda n, sh, dt=F32: es_pa.enter_context(nc.sbuf_tensor(n, sh, dt))
            C.w1b = sba("w1b", [128, 9 * 1024], BF16)
            with ExitStack() as es_p0:
                phase0(C, es_p0)
            P.barrier()
            with ExitStack() as es_a:
                phaseA(C, es_a)
        P.barrier()
        with ExitStack() as es_b:
            phaseB(C, es_b)
        P.raw("sync", lambda e, s=P.sems["dma"], v=P.dma_cnt: e.wait_ge(s, v))
        P.finalize(blk)
    return nc, list(C.dbg.keys())


def phase0(C, es):
    nc, P = C.nc, C.P
    R = P.res
    sb = lambda n, sh, dt=F32: es.enter_context(nc.sbuf_tensor(n, sh, dt))
    stg_f = [sb("p0f0", [128, 3072]), sb("p0f1", [128, 3072])]
    stg_i = [sb("p0i0", [128, 3072], I32), sb("p0i1", [128, 3072], I32)]
    stg_b = [sb("p0b0", [128, 3072], BF16), sb("p0b1", [128, 3072], BF16)]
    dgb = [sb("p0d0", [128, 54 * 128], BF16), sb("p0d1", [128, 54 * 128], BF16)]
    cons = sb("p0cons", [128, 512])
    on2 = sb("p0on2", [128, 2])
    NW = 4
    xt = [sb("p0x%d" % i, [128, D]) for i in range(NW)]
    xsq = [sb("p0sq%d" % i, [128, D]) for i in range(NW)]
    xnt = [sb("p0xnt%d" % i, [128, D], BF16) for i in range(NW)]
    xnf = [sb("p0xnf%d" % i, [128, D], BF16) for i in range(NW)]

    rf = lambda i: R("p0f%d" % i)
    ri = lambda i: R("p0i%d" % i)
    rb = lambda i: R("p0b%d" % i)

    with P.stage():
        P.dma(cons[:, 0:128], C.id128_h[:], w=[R("cons")])
        P.dma(cons[:, 128:256], C.maskUs_h[:], w=[R("cons")])
        P.dma(cons[:, 256:384], C.maskUi_h[:], w=[R("cons")])
        P.dma(C.rows[0:1, 0:24], C.b2row_h[:], w=[R("rows")])
        P.dma(C.rows[0:1, 24:28], C.gltrow_h[:], w=[R("rows")])
        P.dma(C.b1fm[:], C.b1fm_h[:], w=[R("b1fm")])
        P.dma(on2[:], C.onorm2_h[:], w=[R("on2")])
        P.dma(C.wtab[:].rearrange("p (g w) -> p g w", g=8),
              C.wtab_h.rearrange("(g p) w -> p g w", p=128), w=[R("wtab")])
        P.dma(stg_f[0][0:24, 0:24], C.sel24_h[:], w=[rf(0)])
        P.dma(stg_f[1][0:1, 0:128], C.ones1r_h[:], w=[rf(1)])
    with P.stage():
        P.do("dve", lambda e: e.tensor_copy(out=C.idb[:], in_=cons[:, 0:128]),
             r=[R("cons")], w=[R("idb")])
        P.do("dve", lambda e: e.tensor_copy(out=C.mUsb[:], in_=cons[:, 128:256]),
             r=[R("cons")], w=[R("mUsb")])
        P.do("dve", lambda e: e.tensor_copy(out=C.mUib[:], in_=cons[:, 256:384]),
             r=[R("cons")], w=[R("mUib")])
        P.do("dve", lambda e: e.memset(C.onesb[:], 1.0), w=[R("onesb")])
        P.do("dve", lambda e: e.tensor_copy(out=C.onesrow[:],
             in_=stg_f[1][0:1, 0:128]), r=[rf(1)], w=[R("onesrow")])
        P.do("dve", lambda e: e.tensor_copy(out=C.sel24b[:],
             in_=stg_f[0][0:24, 0:24]), r=[rf(0)], w=[R("sel24b")])
        P.do("pool", lambda e: e.tensor_copy(out=C.i4b[:, 0:128], in_=cons[:, 0:128]),
             r=[R("cons")], w=[R("i4b")])
        for j in range(1, 4):
            P.do("pool", lambda e, j=j: e.tensor_copy(
                out=C.i4b[:, j * 128:(j + 1) * 128], in_=cons[:, 0:128]),
                r=[R("cons")], w=[R("i4b")])

    ro = C.rows
    with P.stage():
        P.do("dve", lambda e: e.tensor_scalar(out=ro[0:1, 28:32], in0=ro[0:1, 24:28],
             scalar1=0.0, scalar2=None, op0=ALU.max), r=[R("rows")], w=[R("rows")])
        P.do("act", lambda e: e.activation(out=ro[0:1, 32:36], in_=ro[0:1, 24:28],
             func=AF.Abs), r=[R("rows")], w=[R("rows2")])
    with P.stage():
        P.do("act", lambda e: e.activation(out=ro[0:1, 32:36], in_=ro[0:1, 32:36],
             func=AF.Exp, scale=-1.0), r=[R("rows2")], w=[R("rows2")])
    with P.stage():
        P.do("act", lambda e: e.activation(out=ro[0:1, 32:36], in_=ro[0:1, 32:36],
             func=AF.Ln, bias=1.0), r=[R("rows2")], w=[R("rows2")])
    with P.stage():
        P.do("dve", lambda e: e.tensor_tensor(out=ro[0:1, 36:40], in0=ro[0:1, 28:32],
             in1=ro[0:1, 32:36], op=ALU.add), r=[R("rows"), R("rows2")], w=[R("rows")])
    with P.stage():
        P.do("dve", lambda e: e.tensor_scalar(out=ro[0:1, 36:40], in0=ro[0:1, 36:40],
             scalar1=0.5, scalar2=None, op0=ALU.add), r=[R("rows")], w=[R("rows")])
    with P.stage():
        P.do("dve", lambda e: e.reciprocal(out=ro[0:1, 40:44], in_=ro[0:1, 36:40]),
             r=[R("rows")], w=[R("rows")])
    with P.stage():
        P.do("dve", lambda e: e.tensor_copy(
            out=ro[0:1, 44:68].rearrange("p (h s) -> p h s", h=4),
            in_=ro[0:1, 40:44].unsqueeze(2).broadcast_to([1, 4, 6])),
            r=[R("rows")], w=[R("rows")])
    row_b = es.enter_context(nc.sbuf_tensor("p0rowb", [1, 64], BF16))
    with P.stage():
        P.do("dve", lambda e: e.tensor_copy(out=row_b[0:1, 0:24],
             in_=ro[0:1, 44:68]), r=[R("rows")], w=[R("rowb")])
        P.do("dve", lambda e: e.tensor_copy(out=row_b[0:1, 32:56],
             in_=ro[0:1, 0:24]), r=[R("rows")], w=[R("rowb")])
    with P.stage():
        P.do("pe", lambda e: e.matmul(C.PS[:, 0:24], lhsT=C.onesrow[:],
             rhs=row_b[0:1, 0:24], start=True, stop=True),
             r=[R("rowb"), R("onesrow")], w=[R("PS")])
        P.do("pe", lambda e: e.matmul(C.PS[:, 32:56], lhsT=C.onesrow[:],
             rhs=row_b[0:1, 32:56], start=True, stop=True),
             r=[R("rowb"), R("onesrow")], w=[R("PS")])
    with P.stage():
        P.do("dve", lambda e: e.tensor_copy(out=C.invtempT[:], in_=C.PS[:, 0:24]),
             r=[R("PS")], w=[R("invtempT")])
    with P.stage():
        P.do("dve", lambda e: e.tensor_tensor(out=C.b2i[:], in0=C.PS[:, 32:56],
             in1=C.invtempT[:], op=ALU.mult),
             r=[R("PS"), R("invtempT")], w=[R("b2i")])

    T = C.tiny
    for t in range(NT):
        p = t % 4
        xr, sqr = R("p0x%d" % p), R("p0sq%d" % p)
        ntr, nfr = R("p0xnt%d" % p), R("p0xnf%d" % p)
        ln = R("p0ln%d" % p)
        co = p * 8
        with P.stage():
            P.dma(xt[p][:], C.x_h[t * 128:(t + 1) * 128, :], w=[xr])
        with P.stage():
            P.do("act", lambda e, p=p: e.activation(out=xsq[p][:], in_=xt[p][:],
                 func=AF.Square), r=[xr], w=[sqr])
            P.do("dve", lambda e, p=p, co=co: e.reduce_sum(out=T[:, co:co + 1],
                 in_=xt[p][:], axis=AX.XYZW), r=[xr], w=[ln])
        with P.stage():
            P.do("dve", lambda e, p=p, co=co: e.reduce_sum(out=T[:, co + 1:co + 2],
                 in_=xsq[p][:], axis=AX.XYZW), r=[sqr, ln], w=[ln])
            P.do("dve", lambda e, co=co: e.tensor_scalar(out=T[:, co + 2:co + 3],
                 in0=T[:, co:co + 1], scalar1=1.0 / D, scalar2=None, op0=ALU.mult),
                 r=[ln], w=[ln])
        with P.stage():
            P.do("dve", lambda e, co=co: e.tensor_tensor(out=T[:, co + 3:co + 4],
                 in0=T[:, co + 2:co + 3], in1=T[:, co + 2:co + 3], op=ALU.mult),
                 r=[ln], w=[ln])
        with P.stage():
            P.do("dve", lambda e, co=co: e.scalar_tensor_tensor(
                 out=T[:, co + 4:co + 5], in0=T[:, co + 1:co + 2], scalar=1.0 / D,
                 in1=T[:, co + 3:co + 4], op0=ALU.mult, op1=ALU.subtract),
                 r=[ln], w=[ln])
        with P.stage():
            P.do("act", lambda e, co=co: e.activation(out=T[:, co + 5:co + 6],
                 in_=T[:, co + 4:co + 5], func=AF.Ln, bias=1e-5), r=[ln], w=[ln])
        with P.stage():
            P.do("act", lambda e, co=co: e.activation(out=T[:, co + 6:co + 7],
                 in_=T[:, co + 5:co + 6], func=AF.Exp, scale=-0.5), r=[ln], w=[ln])
        with P.stage():
            P.do("dve", lambda e, p=p, co=co: e.tensor_scalar(out=xsq[p][:],
                 in0=xt[p][:], scalar1=T[:, co + 2:co + 3], scalar2=None,
                 op0=ALU.subtract), r=[xr, ln], w=[sqr])
        with P.stage():
            P.do("dve", lambda e, p=p, co=co: e.tensor_scalar(out=xnt[p][:],
                 in0=xsq[p][:], scalar1=T[:, co + 6:co + 7], scalar2=None,
                 op0=ALU.mult), r=[sqr, ln], w=[ntr])
        with P.stage():
            for cch in range(8):
                P.do("pe", lambda e, p=p, cch=cch: e.transpose(
                    C.PT[:, cch * 128:(cch + 1) * 128],
                    in_=xnt[p][:, cch * 128:(cch + 1) * 128], identity=C.idb[:]),
                    r=[ntr, R("idb")], w=[R("PT")])
        with P.stage(dma_out=True):
            P.do("dve", lambda e, p=p: e.tensor_copy(out=xnf[p][:],
                 in_=C.PT[:, 0:1024]), r=[R("PT")], w=[nfr])
        with P.stage():
            P.dma(C.xn_s[t], xnf[p][:], r=[nfr])
            if "xn" in C.dbg:
                P.dma(C.dbg["xn"][t], xnf[p][:], r=[nfr])
    for k in range(8):
        p = k % 2
        with P.stage():
            P.dma(stg_f[p][:], C.qkv_h[k * 128:(k + 1) * 128, :], w=[rf(p)])
        with P.stage(dma_out=True):
            P.do("dve", lambda e, p=p: e.tensor_copy(out=stg_b[p][:],
                 in_=stg_f[p][:]), r=[rf(p)], w=[rb(p)])
        with P.stage():
            P.dma(C.qkv_s[:, :, k * 256:(k + 1) * 256].transpose([1, 0, 2]),
                  stg_b[p][:].rearrange("p (m n) -> p m n", m=12), r=[rb(p)])
    with P.stage():
        P.dma(stg_f[0][:, 0:32].rearrange("p (k n) -> p k n", k=8),
              C.bw_h.rearrange("(k p) n -> p k n", p=128), w=[rf(0)])
    with P.stage():
        P.do("dve", lambda e: e.tensor_copy(out=C.wbb[:], in_=stg_f[0][:, 0:32]),
             r=[rf(0)], w=[R("wbb")])
    for k in range(9):
        rows_n = 128 if k < 8 else 96
        p = k % 2
        with P.stage():
            P.dma(stg_f[p][0:rows_n, 0:1024], C.w1_h[k * 128:k * 128 + rows_n, :],
                  w=[rf(p)])
        with P.stage():
            P.do("dve", lambda e, k=k, rows_n=rows_n, p=p: e.tensor_copy(
                out=C.w1b[0:rows_n, k * 1024:(k + 1) * 1024],
                in_=stg_f[p][0:rows_n, 0:1024]), r=[rf(p)], w=[R("w1b")])
    with P.stage():
        P.dma(stg_f[0][:, 0:192].rearrange("p (c n) -> p c n", c=8),
              C.w2_h.rearrange("(c p) n -> p c n", p=128), w=[rf(0)])
    with P.stage():
        P.do("dve", lambda e: e.tensor_copy(out=C.w2b[:], in_=stg_f[0][:, 0:192]),
             r=[rf(0)], w=[R("w2b")])
    for k in range(8):
        p = k % 2
        with P.stage():
            P.dma(stg_f[p][:, 0:1024], C.ow_h[k * 128:(k + 1) * 128, :], w=[rf(p)])
        with P.stage(dma_out=True):
            P.do("dve", lambda e, p=p, k=k: e.tensor_scalar(
                out=stg_b[p][:, 0:1024], in0=stg_f[p][:, 0:1024],
                scalar1=on2[:, k % 2:k % 2 + 1], scalar2=None, op0=ALU.mult),
                r=[rf(p), R("on2")], w=[rb(p)])
        with P.stage():
            P.dma(C.ow_s[k], stg_b[p][:, 0:1024], r=[rb(p)])
    for k in range(8):
        for half in range(2):
            p = (2 * k + half) % 2
            with P.stage():
                P.dma(stg_i[p][:], C.wgu_h[k * 128:(k + 1) * 128,
                                           half * HID:(half + 1) * HID], w=[ri(p)])
            with P.stage(dma_out=True):
                P.do("dve", lambda e, p=p: e.tensor_copy(out=stg_b[p][:],
                     in_=stg_i[p][:]), r=[ri(p)], w=[rb(p)])
            with P.stage():
                P.dma(C.wgu_s[k][:, half * HID:(half + 1) * HID], stg_b[p][:],
                      r=[rb(p)])
    for c in range(24):
        p = c % 2
        with P.stage():
            P.dma(stg_i[p][:, 0:1024], C.wdT_h[c * 128:(c + 1) * 128, :], w=[ri(p)])
        with P.stage(dma_out=True):
            P.do("dve", lambda e, p=p: e.tensor_copy(out=stg_b[p][:, 0:1024],
                 in_=stg_i[p][:, 0:1024]), r=[ri(p)], w=[rb(p)])
        with P.stage():
            P.dma(C.wdT_s[c], stg_b[p][:, 0:1024], r=[rb(p)])
    for gg in range(G):
        p = gg % 2
        with P.stage(dma_out=True):
            for j in range(54):
                P.do("dve", lambda e, p=p, gg=gg, j=j: e.tensor_scalar(
                    out=dgb[p][:, j * 128:(j + 1) * 128], in0=C.idb[:],
                    scalar1=C.wtab[:, gg * 54 + j:gg * 54 + j + 1], scalar2=None,
                    op0=ALU.mult), r=[R("idb"), R("wtab")], w=[R("dg%d" % p)])
        with P.stage():
            P.dma(C.diag_s[gg], dgb[p][:], r=[R("dg%d" % p)])



def phaseA(C, es):
    nc, P = C.nc, C.P
    R = P.res
    sb = lambda n, sh, dt=F32: es.enter_context(nc.sbuf_tensor(n, sh, dt))
    B = Ctx()
    B.xnst = sb("a_xnst", [128, 4096], BF16)
    B.qpre = sb("a_qpre", [128, 8 * 516], BF16)
    B.kpre = sb("a_kpre", [128, 8 * 516], BF16)
    B.vpre = sb("a_vpre", [128, 8 * 516], BF16)
    B.vf = sb("a_vf", [128, 8 * 544], BF16)
    B.f3 = sb("a_f3", [128, 4096], BF16)
    B.f7 = sb("a_f7", [128, 4096], BF16)
    B.f31 = sb("a_f31", [128, 4096], BF16)
    B.wpair = [sb("a_wp0", [128, 2048], BF16), sb("a_wp1", [128, 2048], BF16)]
    B.dg = [sb("a_dg0", [128, 54 * 128], BF16), sb("a_dg1", [128, 54 * 128], BF16)]
    B.betaf = sb("a_betaf", [4, 512], BF16)
    B.betat = sb("a_betat", [128, 16])
    B.rst = sb("a_rst", [128, 32])
    B.rr2 = sb("a_rr2", [128, 8])
    B.sq2 = sb("a_sq2", [128, 1024], BF16)
    B.kntm = sb("a_kntm", [128, 2048], BF16)
    B.kbtm = sb("a_kbtm", [128, 2048], BF16)
    B.vbtm = sb("a_vbtm", [128, 2048], BF16)
    B.wnf = sb("a_wnf", [128, 2048], BF16)
    B.Apre = sb("a_apre", [128, 1024], BF16)
    B.ATs = sb("a_ats0", [128, 1024], BF16)
    B.As = B.Apre
    B.XXT = [sb("a_xp0", [128, 2048], BF16), sb("a_xp1", [128, 2048], BF16)]
    B.PTs = sb("a_pts", [128, 1024], BF16)
    B.attb = sb("a_attb", [128, 1024], BF16)
    B.uhat = sb("a_uhat", [128, 1024], BF16)
    B.S = sb("a_S", [128, 2048])
    B.Sbb = sb("a_Sbb", [128, 2048], BF16)
    B.tmp_tm = sb("a_tmptm", [128, 1024], BF16)
    B.dfm2 = sb("a_dfm2", [128, 2048], BF16)
    B.b1f2 = sb("a_b1f2", [128, 2048], BF16)
    B.babs = sb("a_babs", [128, 1024], BF16)
    B.bsq = sb("a_bsq", [128, 1024], BF16)
    B.st96 = sb("a_st96", [128, 96], BF16)
    B.stT = sb("a_stT", [96, 128], BF16)
    B.l24 = sb("a_l24", [128, 24])
    B.w24 = sb("a_w24", [128, 24], BF16)
    B.wgtT = sb("a_wgtT", [24, 128], BF16)
    B.hgst = sb("a_hgst", [128, 2048], BF16)
    B.facc = sb("a_facc", [128, 1024], BF16)
    B.tmpc = sb("a_tmpc", [128, 1024], BF16)
    B.ffm = sb("a_ffm", [128, 1024], BF16)
    B.sot = sb("a_sot", [128, 1024])
    B.rrow = sb("a_rrow", [1, 512], BF16)
    B.owc = [sb("a_owc%d" % j, [128, 1024], BF16) for j in range(3)]
    PA, PB, PC, PS, PT = C.PA, C.PB, C.PC, C.PS, C.PT
    B.half = lambda pt, i: pt[:, i * 512:(i + 1) * 512]

    with P.stage():
        P.do("dve", lambda e: e.memset(B.S[:], 0.0), w=[R("a_S")])
        P.do("pool", lambda e: e.memset(B.Sbb[:], 0.0), w=[R("a_Sbb")])
        P.do("pool", lambda e: e.memset(B.qpre[:], 0.0), w=[R("a_qpre")])
        P.do("pool", lambda e: e.memset(B.kpre[:], 0.0), w=[R("a_kpre")])
        P.do("pool", lambda e: e.memset(B.vpre[:], 0.0), w=[R("a_vpre")])
        P.do("dve", lambda e: e.memset(B.vf[:], 0.0), w=[R("a_vf")])

    for st in range(NST):
        t0 = st * 4
        with P.stage():
            for i in range(4):
                P.dma(B.xnst[:].rearrange("p (g t) -> p g t", g=8)
                      [:, :, i * 128:(i + 1) * 128],
                      C.xn_s[t0 + i].rearrange("p (g t) -> p g t", g=8),
                      w=[R("a_xnst")])
            P.dma(B.dg[0][:], C.diag_s[0], w=[R("a_dg0")])
            P.dma(B.dg[1][:], C.diag_s[1], w=[R("a_dg1")])
            P.dma(B.wpair[0][:], C.qkv_s[0], w=[R("a_wp0")])
            P.dma(B.wpair[1][:], C.qkv_s[1], w=[R("a_wp1")])
        dsts = [(B.qpre, "a_qpre"), (B.kpre, "a_kpre"), (B.vpre, "a_vpre")]
        for pair in range(12):
            pw = pair % 2
            dstbuf, dstres = dsts[pair // 4]
            PP, ppres = (PA, ("PA0", "PA1")) if pair % 2 == 0 else (PB, ("PB0", "PB1"))
            with P.stage():
                for m2 in range(2):
                    for k in range(8):
                        P.do("pe", lambda e, pw=pw, m2=m2, k=k, PP=PP: e.matmul(
                            B.half(PP, m2),
                            lhsT=B.wpair[pw][:, k * 256 + m2 * 128:k * 256 + m2 * 128 + 128],
                            rhs=B.xnst[:, k * 512:(k + 1) * 512],
                            start=(k == 0), stop=(k == 7)),
                            r=[R("a_wp%d" % pw), R("a_xnst")],
                            w=[R(ppres[m2])])
            with P.stage():
                if pair + 2 < 12:
                    P.dma(B.wpair[pw][:], C.qkv_s[pair + 2], w=[R("a_wp%d" % pw)])
                for m2 in range(2):
                    gout = (pair % 4) * 2 + m2
                    if m2 == 0:
                        P.do("dve", lambda e, PP=PP, dstbuf=dstbuf, gout=gout:
                             e.tensor_copy(out=dstbuf[:, gout * 516 + 4:gout * 516 + 516],
                                           in_=B.half(PP, 0)),
                             r=[R(ppres[0])], w=[R(dstres)])
                    else:
                        P.do("act", lambda e, PP=PP, dstbuf=dstbuf, gout=gout:
                             e.activation(out=dstbuf[:, gout * 516 + 4:gout * 516 + 516],
                                          in_=B.half(PP, 1), func=AF.Identity),
                             r=[R(ppres[1])], w=[R(dstres)])
        with P.stage():
            for k in range(8):
                P.do("pe", lambda e, k=k: e.matmul(
                    PS[0:4, 0:512], lhsT=C.wbb[:, k * 4:(k + 1) * 4],
                    rhs=B.xnst[:, k * 512:(k + 1) * 512],
                    start=(k == 0), stop=(k == 7)),
                    r=[R("wbb"), R("a_xnst")], w=[R("PS")])
        with P.stage():
            P.do("act", lambda e: e.activation(out=B.betaf[:], in_=PS[0:4, 0:512],
                 func=AF.Sigmoid), r=[R("PS")], w=[R("a_betaf")])
        with P.stage():
            for i in range(4):
                P.do("pe", lambda e, i=i: e.transpose(
                    PT[:, i * 4:(i + 1) * 4],
                    in_=B.betaf[0:4, i * 128:(i + 1) * 128], identity=C.idb[0:4, 0:4]),
                    r=[R("a_betaf"), R("idb")], w=[R("PT")])
        with P.stage():
            P.do("dve", lambda e: e.tensor_copy(out=B.betat[:], in_=PT[:, 0:16]),
                 r=[R("PT")], w=[R("a_betat")])
        CONV = [("q", B.qpre, "a_qpre", 0, 4), ("k", B.kpre, "a_kpre", 4, 4),
                ("v", B.vpre, "a_vpre", 8, 4)]
        pcrot = [("PA0", lambda: B.half(PA, 0)), ("PA1", lambda: B.half(PA, 1)),
                 ("PB0", lambda: B.half(PB, 0)), ("PB1", lambda: B.half(PB, 1)),
                 ("PC0", lambda: B.half(PC, 0)), ("PC1", lambda: B.half(PC, 1))]
        pcidx = 0
        for gg in range(8):
            dgr = R("a_dg%d" % (gg % 2))
            dgb_ = B.dg[gg % 2]
            for name, buf, bres, tbase, W in CONV:
                pres, pap = pcrot[pcidx % 6]
                pcidx += 1
                with P.stage():
                    for j in range(W):
                        P.do("pe", lambda e, dgb_=dgb_, tbase=tbase, j=j, buf=buf,
                             gg=gg, pap=pap, W=W: e.matmul(
                                 pap(), lhsT=dgb_[:, (tbase + j) * 128:(tbase + j + 1) * 128],
                                 rhs=buf[:, gg * 516 + 4 - j:gg * 516 + 4 - j + 512],
                                 start=(j == 0), stop=(j == W - 1)),
                            r=[dgr, R(bres)], w=[R(pres)])
                with P.stage():
                    P.do("dve", lambda e, buf=buf, gg=gg: e.tensor_copy(
                        out=buf[:, gg * 516 + 1:gg * 516 + 4],
                        in_=buf[:, gg * 516 + 513:gg * 516 + 516]),
                        r=[R(bres)], w=[R(bres)])
                with P.stage():
                    if name == "v":
                        P.do("act", lambda e, pap=pap, gg=gg: e.activation(
                            out=B.vf[:, gg * 544 + 32:gg * 544 + 544],
                            in_=pap(), func=AF.Silu),
                            r=[R(pres)], w=[R("a_vf")])
                    else:
                        P.do("act", lambda e, pap=pap, buf=buf, gg=gg: e.activation(
                            out=buf[:, gg * 516 + 4:gg * 516 + 516],
                            in_=pap(), func=AF.Silu),
                            r=[R(pres)], w=[R(bres)])
            with P.stage():
                P.do("dve", lambda e, gg=gg: e.tensor_copy(
                    out=B.vf[:, gg * 544 + 1:gg * 544 + 32],
                    in_=B.vf[:, gg * 544 + 513:gg * 544 + 544]),
                    r=[R("a_vf")], w=[R("a_vf")])
            firs = [(13, 3, B.f3, "a_f3"), (16, 7, B.f7, "a_f7"), (23, 31, B.f31, "a_f31")]
            fp = []
            for tbase, W, fbuf, fres in firs:
                pres, pap = pcrot[pcidx % 6]
                pcidx += 1
                fp.append((pres, pap, fbuf, fres))
                with P.stage():
                    for j in range(W):
                        P.do("pe", lambda e, dgb_=dgb_, tbase=tbase, j=j, gg=gg,
                             pap=pap, W=W: e.matmul(
                                 pap(), lhsT=dgb_[:, (tbase + j) * 128:(tbase + j + 1) * 128],
                                 rhs=B.vf[:, gg * 544 + 32 - j:gg * 544 + 32 - j + 512],
                                 start=(j == 0), stop=(j == W - 1)),
                            r=[dgr, R("a_vf")], w=[R(pres)])
            with P.stage():
                for ii, (pres, pap, fbuf, fres) in enumerate(fp):
                    eng = ["dve", "act", "dve"][ii]
                    if eng == "dve":
                        P.do("dve", lambda e, pap=pap, fbuf=fbuf, gg=gg: e.tensor_copy(
                            out=fbuf[:, gg * 512:(gg + 1) * 512], in_=pap()),
                            r=[R(pres)], w=[R(fres)])
                    else:
                        P.do("act", lambda e, pap=pap, fbuf=fbuf, gg=gg: e.activation(
                            out=fbuf[:, gg * 512:(gg + 1) * 512], in_=pap(),
                            func=AF.Identity), r=[R(pres)], w=[R(fres)])
            if gg + 2 < 8:
                with P.stage():
                    P.dma(B.dg[gg % 2][:], C.diag_s[gg + 2], w=[R("a_dg%d" % (gg % 2))])
        if "q" in C.dbg and st == 0:
            with P.stage():
                P.dma(C.dbg["q"][:], B.qpre[:, 0:4096], r=[R("a_qpre")])
                P.dma(C.dbg["k"][:], B.kpre[:, 0:4096], r=[R("a_kpre")])
                P.dma(C.dbg["v"][:], B.vf[:, 0:4096], r=[R("a_vf")])
        for ipair in range(2):
            pairA_prefix(C, P, R, B, st, ipair)
            for p in range(2):
                i = 2 * ipair + p
                tileA_suffix(C, P, R, B, st, i)
            with P.stage():
                P.do("act", lambda e: e.activation(out=B.hgst[:], in_=B.hgst[:],
                     func=AF.Gelu), r=[R("a_hgst")], w=[R("a_hgst")])
            for p in range(2):
                i = 2 * ipair + p
                tileA_phase2(C, P, R, B, st, i)


def pairA_prefix(C, P, R, B, st, ipair):
    PA, PB, PC, PS, PT = C.PA, C.PB, C.PC, C.PS, C.PT
    half = B.half
    o2 = ipair * 256
    qs2 = B.qpre[:].rearrange("p (c n) -> p c n", c=8)[:, :, 4 + o2:4 + o2 + 256]
    ks2 = B.kpre[:].rearrange("p (c n) -> p c n", c=8)[:, :, 4 + o2:4 + o2 + 256]
    qs = lambda ch, p, o2=o2: B.qpre[:, ch * 516 + 4 + o2 + p * 128:
                                     ch * 516 + 4 + o2 + p * 128 + 128]
    ks = lambda ch, p, o2=o2: B.kpre[:, ch * 516 + 4 + o2 + p * 128:
                                     ch * 516 + 4 + o2 + p * 128 + 128]
    vs = lambda ch, p, o2=o2: B.vf[:, ch * 544 + 32 + o2 + p * 128:
                                   ch * 544 + 32 + o2 + p * 128 + 128]
    sqb = [B.sq2, B.bsq]
    sqr_ = ["a_sq2", "a_bsq"]
    for which, stk2, colbase in (("q", qs2, 0), ("k", ks2, 4)):
        sres = "a_qpre" if which == "q" else "a_kpre"
        with P.stage():
            for p in range(2):
                P.do("dve", lambda e, stk2=stk2, p=p: e.tensor_tensor(
                    out=sqb[p][:].rearrange("p (c n) -> p c n", c=8),
                    in0=stk2[:, :, p * 128:(p + 1) * 128],
                    in1=stk2[:, :, p * 128:(p + 1) * 128], op=ALU.mult),
                    r=[R(sres)], w=[R(sqr_[p])])
        with P.stage():
            for p in range(2):
                for hh in range(4):
                    for cc in range(2):
                        P.do("pe", lambda e, p=p, hh=hh, cc=cc, colbase=colbase: e.matmul(
                            PS[:, p * 16 + colbase + hh:p * 16 + colbase + hh + 1],
                            lhsT=sqb[p][:, (2 * hh + cc) * 128:(2 * hh + cc + 1) * 128],
                            rhs=C.onesb[:], start=(cc == 0), stop=(cc == 1)),
                            r=[R(sqr_[p]), R("onesb")], w=[R("PS")])
    with P.stage():
        for p in range(2):
            P.do("act", lambda e, p=p: e.activation(out=B.rst[:, p * 16:p * 16 + 8],
                 in_=PS[:, p * 16:p * 16 + 8], func=AF.Ln, bias=1e-6),
                 r=[R("PS")], w=[R("a_rst")])
    with P.stage():
        for p in range(2):
            P.do("act", lambda e, p=p: e.activation(out=B.rst[:, p * 16:p * 16 + 8],
                 in_=B.rst[:, p * 16:p * 16 + 8], func=AF.Exp, scale=-0.5),
                 r=[R("a_rst")], w=[R("a_rst")])
    with P.stage():
        for p in range(2):
            i = 2 * ipair + p
            P.do("dve", lambda e, p=p, i=i: e.tensor_tensor(
                out=B.rst[:, p * 16 + 8:p * 16 + 12], in0=B.rst[:, p * 16 + 4:p * 16 + 8],
                in1=B.betat[:, i * 4:(i + 1) * 4], op=ALU.mult),
                r=[R("a_rst"), R("a_betat")], w=[R("a_rst")])
    for p in range(2):
        pcres, pchalf = ("PC0", 0) if p == 0 else ("PC1", 1)
        with P.stage():
            for ch in range(8):
                P.do("pe", lambda e, ch=ch, p=p: e.transpose(
                    PT[:, ch * 128:(ch + 1) * 128], in_=ks(ch, p), identity=C.idb[:]),
                    r=[R("a_kpre"), R("idb")], w=[R("PT")])
            for hh in range(4):
                for cc in range(2):
                    P.do("pe", lambda e, hh=hh, cc=cc, p=p, pchalf=pchalf: e.matmul(
                        half(PC, pchalf)[:, hh * 128:(hh + 1) * 128],
                        lhsT=ks(2 * hh + cc, p), rhs=ks(2 * hh + cc, p),
                        start=(cc == 0), stop=(cc == 1)),
                        r=[R("a_kpre")], w=[R(pcres)])
        with P.stage():
            for hh in range(4):
                P.do("dve", lambda e, hh=hh, p=p: e.tensor_scalar(
                    out=B.kntm[:, p * 1024 + hh * 256:p * 1024 + (hh + 1) * 256],
                    in0=PT[:, hh * 256:(hh + 1) * 256],
                    scalar1=B.rst[:, p * 16 + 4 + hh:p * 16 + 5 + hh], scalar2=None,
                    op0=ALU.mult), r=[R("PT"), R("a_rst")], w=[R("a_kntm")])
                P.do("dve", lambda e, hh=hh, p=p: e.tensor_scalar(
                    out=B.kbtm[:, p * 1024 + hh * 256:p * 1024 + (hh + 1) * 256],
                    in0=PT[:, hh * 256:(hh + 1) * 256],
                    scalar1=B.rst[:, p * 16 + 8 + hh:p * 16 + 9 + hh], scalar2=None,
                    op0=ALU.mult), r=[R("PT"), R("a_rst")], w=[R("a_kbtm")])
        with P.stage():
            for ch in range(8):
                P.do("pe", lambda e, ch=ch, p=p: e.transpose(
                    PT[:, ch * 128:(ch + 1) * 128], in_=vs(ch, p), identity=C.idb[:]),
                    r=[R("a_vf"), R("idb")], w=[R("PT")])
        with P.stage():
            for hh in range(4):
                i = 2 * ipair + p
                P.do("dve", lambda e, hh=hh, p=p, i=i: e.tensor_scalar(
                    out=B.vbtm[:, p * 1024 + hh * 256:p * 1024 + (hh + 1) * 256],
                    in0=PT[:, hh * 256:(hh + 1) * 256],
                    scalar1=B.betat[:, i * 4 + hh:i * 4 + hh + 1], scalar2=None,
                    op0=ALU.mult), r=[R("PT"), R("a_betat")], w=[R("a_vbtm")])
    with P.stage():
        for p in range(2):
            for hh in range(4):
                P.do("dve", lambda e, hh=hh, p=p: e.tensor_scalar(
                    out=B.Apre[:, p * 512 + hh * 128:p * 512 + (hh + 1) * 128],
                    in0=half(PC, p)[:, hh * 128:(hh + 1) * 128],
                    scalar1=B.rst[:, p * 16 + 8 + hh:p * 16 + 9 + hh], scalar2=-1.0,
                    op0=ALU.mult, op1=ALU.mult),
                    r=[R("PC%d" % p), R("a_rst")], w=[R("a_apre")])
    with P.stage():
        for j in range(8):
            P.do("pe", lambda e, j=j: e.transpose(
                PT[:, j * 128:(j + 1) * 128],
                in_=B.Apre[:, j * 128:(j + 1) * 128], identity=C.idb[:]),
                r=[R("a_apre"), R("idb")], w=[R("PT")])
    with P.stage():
        for p in range(2):
            for hh in range(4):
                P.do("dve", lambda e, hh=hh, p=p: e.scalar_tensor_tensor(
                    out=B.ATs[:, p * 512 + hh * 128:p * 512 + (hh + 1) * 128],
                    in0=PT[:, p * 512 + hh * 128:p * 512 + (hh + 1) * 128],
                    scalar=B.rst[:, p * 16 + 4 + hh:p * 16 + 5 + hh], in1=C.mUsb[:],
                    op0=ALU.mult, op1=ALU.mult),
                    r=[R("PT"), R("a_rst"), R("mUsb")], w=[R("a_ats0")])
    with P.stage():
        for j in range(8):
            P.do("pe", lambda e, j=j: e.transpose(
                PT[:, j * 128:(j + 1) * 128],
                in_=B.ATs[:, j * 128:(j + 1) * 128], identity=C.idb[:]),
                r=[R("a_ats0"), R("idb")], w=[R("PT")])
    with P.stage():
        P.do("dve", lambda e: e.tensor_copy(out=B.As[:], in_=PT[:, 0:1024]),
             r=[R("PT")], w=[R("a_apre")])
        P.do("dve", lambda e: e.tensor_tensor(
            out=B.PTs[:].rearrange("p (r n) -> p r n", r=2),
            in0=B.ATs[:].rearrange("p (r n) -> p r n", r=2),
            in1=C.i4b[:].unsqueeze(1).broadcast_to([128, 2, 512]), op=ALU.add),
            r=[R("a_ats0"), R("i4b")], w=[R("a_pts")])
    lvls = (2, 4, 8, 16, 32, 64)
    prev = None
    for li, lvl in enumerate(lvls):
        need_xt = lvl < 64
        if li == 0:
            xsl = lambda j: B.As[:, j * 128:(j + 1) * 128]
            xtsl = lambda j: B.ATs[:, j * 128:(j + 1) * 128]
            rr_ = ["a_apre", "a_ats0"]
        else:
            pb_, pr_ = prev
            xsl = lambda j, b=pb_: b[:, (j // 4) * 1024 + (j % 4) * 128:
                                     (j // 4) * 1024 + (j % 4) * 128 + 128]
            xtsl = lambda j, b=pb_: b[:, (j // 4) * 1024 + 512 + (j % 4) * 128:
                                      (j // 4) * 1024 + 512 + (j % 4) * 128 + 128]
            rr_ = [pr_, pr_]
        cur = B.XXT[li % 2]
        curr = "a_xp%d" % (li % 2)
        with P.stage():
            for j in range(8):
                p = j // 4
                P.do("pe", lambda e, j=j, p=p, xsl=xsl, xtsl=xtsl: e.matmul(
                    PC[:, p * 512 + (j % 4) * 128:p * 512 + (j % 4) * 128 + 128],
                    lhsT=xtsl(j), rhs=xsl(j), start=True, stop=True),
                    r=[R(rr_[0]), R(rr_[1])], w=[R("PC%d" % p)])
            if need_xt:
                for j in range(8):
                    p = j // 4
                    P.do("pe", lambda e, j=j, p=p, xsl=xsl, xtsl=xtsl: e.matmul(
                        PA[:, p * 512 + (j % 4) * 128:p * 512 + (j % 4) * 128 + 128],
                        lhsT=xsl(j), rhs=xtsl(j), start=True, stop=True),
                        r=[R(rr_[0]), R(rr_[1])], w=[R("PA%d" % p)])
            if li > 0:
                for j in range(8):
                    p = j // 4
                    P.do("pe", lambda e, j=j, p=p, xsl=xsl: e.matmul(
                        PB[:, p * 512 + (j % 4) * 128:p * 512 + (j % 4) * 128 + 128],
                        lhsT=xsl(j),
                        rhs=B.PTs[:, j * 128:(j + 1) * 128],
                        start=True, stop=True),
                        r=[R(rr_[0]), R("a_pts")], w=[R("PB%d" % p)])
        with P.stage():
            evaceng = "dve" if li % 2 == 0 else "act"
            for p in range(2):
                if need_xt:
                    if evaceng == "dve":
                        P.do("dve", lambda e, cur=cur, p=p: e.tensor_copy(
                            out=cur[:, p * 1024:p * 1024 + 512], in_=half(PC, p)),
                            r=[R("PC%d" % p)], w=[R(curr)])
                        P.do("dve", lambda e, cur=cur, p=p: e.tensor_copy(
                            out=cur[:, p * 1024 + 512:p * 1024 + 1024], in_=half(PA, p)),
                            r=[R("PA%d" % p)], w=[R(curr)])
                    else:
                        P.do("act", lambda e, cur=cur, p=p: e.activation(
                            out=cur[:, p * 1024:p * 1024 + 512], in_=half(PC, p),
                            func=AF.Identity), r=[R("PC%d" % p)], w=[R(curr)])
                        P.do("act", lambda e, cur=cur, p=p: e.activation(
                            out=cur[:, p * 1024 + 512:p * 1024 + 1024], in_=half(PA, p),
                            func=AF.Identity), r=[R("PA%d" % p)], w=[R(curr)])
                else:
                    P.do("dve", lambda e, cur=cur, p=p: e.tensor_copy(
                        out=cur[:, p * 1024:p * 1024 + 512], in_=half(PC, p)),
                        r=[R("PC%d" % p)], w=[R(curr)])
            if li > 0:
                P.do("dve", lambda e: e.tensor_tensor(out=B.PTs[:], in0=PB[:, 0:1024],
                     in1=B.PTs[:], op=ALU.add),
                     r=[R("PB0"), R("PB1"), R("a_pts")], w=[R("a_pts")])
        prev = (cur, curr)
    pb_, pr_ = prev
    with P.stage():
        for j in range(8):
            p = j // 4
            P.do("pe", lambda e, j=j, p=p, pb_=pb_: e.matmul(
                PB[:, p * 512 + (j % 4) * 128:p * 512 + (j % 4) * 128 + 128],
                lhsT=pb_[:, p * 1024 + (j % 4) * 128:p * 1024 + (j % 4) * 128 + 128],
                rhs=B.PTs[:, j * 128:(j + 1) * 128], start=True, stop=True),
                r=[R(pr_), R("a_pts")], w=[R("PB%d" % p)])
    with P.stage():
        P.do("dve", lambda e: e.tensor_tensor(out=B.PTs[:], in0=PB[:, 0:1024],
             in1=B.PTs[:], op=ALU.add),
             r=[R("PB0"), R("PB1"), R("a_pts")], w=[R("a_pts")])
    with P.stage():
        for p in range(2):
            PPw = PB if p == 0 else PA
            for hh in range(4):
                for cc in range(2):
                    P.do("pe", lambda e, p=p, hh=hh, cc=cc, PPw=PPw: e.matmul(
                        PPw[:, (hh * 2 + cc) * 128:(hh * 2 + cc + 1) * 128],
                        lhsT=B.kbtm[:, p * 1024 + (hh * 2 + cc) * 128:
                                    p * 1024 + (hh * 2 + cc + 1) * 128],
                        rhs=B.PTs[:, p * 512 + hh * 128:p * 512 + (hh + 1) * 128],
                        start=True, stop=True),
                        r=[R("a_kbtm"), R("a_pts")],
                        w=[R("%s%d" % ("PB" if p == 0 else "PA", (hh * 2 + cc) // 4))])
        for p in range(2):
            for hh in range(4):
                for cc in range(2):
                    P.do("pe", lambda e, p=p, hh=hh, cc=cc: e.matmul(
                        half(PC, p)[:, hh * 128:(hh + 1) * 128],
                        lhsT=ks(2 * hh + cc, p), rhs=qs(2 * hh + cc, p),
                        start=(cc == 0), stop=(cc == 1)),
                        r=[R("a_kpre"), R("a_qpre")], w=[R("PC%d" % p)])
    with P.stage():
        for p in range(2):
            PPw = PB if p == 0 else PA
            P.do("dve", lambda e, p=p, PPw=PPw: e.tensor_scalar(
                out=B.wnf[:, p * 1024:(p + 1) * 1024], in0=PPw[:, 0:1024],
                scalar1=-1.0, scalar2=None, op0=ALU.mult),
                r=[R("PB0"), R("PB1")] if p == 0 else [R("PA0"), R("PA1")],
                w=[R("a_wnf")])
            for hh in range(4):
                P.do("dve", lambda e, hh=hh, p=p: e.scalar_tensor_tensor(
                    out=B.attb[:, p * 512 + hh * 128:p * 512 + (hh + 1) * 128],
                    in0=half(PC, p)[:, hh * 128:(hh + 1) * 128],
                    scalar=B.rst[:, p * 16 + 4 + hh:p * 16 + 5 + hh], in1=C.mUib[:],
                    op0=ALU.mult, op1=ALU.mult),
                    r=[R("PC%d" % p), R("a_rst"), R("mUib")], w=[R("a_attb")])


def tileA_suffix(C, P, R, B, st, i):
    PA, PB, PC, PS, PT = C.PA, C.PB, C.PC, C.PS, C.PT
    half = B.half
    p = i % 2
    o = i * 128
    t = st * 4 + i
    so = p * 1024
    dfm = B.dfm2[:, p * 1024:(p + 1) * 1024]
    b1f = B.b1f2[:, p * 1024:(p + 1) * 1024]
    qs = lambda ch: B.qpre[:, ch * 516 + 4 + o:ch * 516 + 4 + o + 128]
    vs = lambda ch: B.vf[:, ch * 544 + 32 + o:ch * 544 + 32 + o + 128]
    with P.stage():
        for hh in range(4):
            P.do("pe", lambda e, hh=hh: e.matmul(
                PC[:, hh * 256:(hh + 1) * 256],
                lhsT=B.PTs[:, so // 2 + hh * 128:so // 2 + (hh + 1) * 128],
                rhs=B.vbtm[:, so + hh * 256:so + (hh + 1) * 256], start=True, stop=False),
                r=[R("a_pts"), R("a_vbtm")], w=[R("PC0"), R("PC1")])
            for cc in range(2):
                P.do("pe", lambda e, hh=hh, cc=cc: e.matmul(
                    PC[:, hh * 256:(hh + 1) * 256],
                    lhsT=B.wnf[:, so + (hh * 2 + cc) * 128:so + (hh * 2 + cc + 1) * 128],
                    rhs=B.Sbb[:, hh * 512 + cc * 256:hh * 512 + cc * 256 + 256],
                    start=False, stop=(cc == 1)),
                    r=[R("a_wnf"), R("a_Sbb")], w=[R("PC0"), R("PC1")])
    with P.stage():
        P.do("dve", lambda e: e.tensor_copy(out=B.uhat[:], in_=PC[:, 0:1024]),
             r=[R("PC0"), R("PC1")], w=[R("a_uhat")])
    with P.stage():
        for hh in range(4):
            for cc in range(2):
                P.do("pe", lambda e, hh=hh, cc=cc: e.matmul(
                    PB[:, hh * 256:(hh + 1) * 256],
                    lhsT=qs(2 * hh + cc),
                    rhs=B.Sbb[:, hh * 512 + cc * 256:hh * 512 + cc * 256 + 256],
                    start=(cc == 0), stop=False),
                    r=[R("a_qpre"), R("a_Sbb")], w=[R("PB0"), R("PB1")])
            P.do("pe", lambda e, hh=hh: e.matmul(
                PB[:, hh * 256:(hh + 1) * 256],
                lhsT=B.attb[:, so // 2 + hh * 128:so // 2 + (hh + 1) * 128],
                rhs=B.uhat[:, hh * 256:(hh + 1) * 256], start=False, stop=True),
                r=[R("a_attb"), R("a_uhat")], w=[R("PB0"), R("PB1")])
        for hh in range(4):
            for cc in range(2):
                PP = PA if hh < 2 else PC
                pres = ("PA0", "PA1") if hh < 2 else ("PC0", "PC1")
                P.do("pe", lambda e, hh=hh, cc=cc, PP=PP: e.matmul(
                    PP[:, (hh % 2) * 512 + cc * 256:(hh % 2) * 512 + cc * 256 + 256],
                    lhsT=B.kntm[:, so + (hh * 2 + cc) * 128:so + (hh * 2 + cc + 1) * 128],
                    rhs=B.uhat[:, hh * 256:(hh + 1) * 256], start=True, stop=True),
                    r=[R("a_kntm"), R("a_uhat")], w=[R(pres[0]), R(pres[1])])
    with P.stage():
        for hh in range(4):
            P.do("dve", lambda e, hh=hh, p=p: e.tensor_scalar(
                out=B.tmp_tm[:, hh * 256:(hh + 1) * 256],
                in0=PB[:, hh * 256:(hh + 1) * 256],
                scalar1=B.rst[:, p * 16 + hh:p * 16 + hh + 1], scalar2=None,
                op0=ALU.mult),
                r=[R("PB0"), R("PB1"), R("a_rst")], w=[R("a_tmptm")])
        P.do("dve", lambda e: e.tensor_tensor(out=B.S[:, 0:1024], in0=PA[:, 0:1024],
             in1=B.S[:, 0:1024], op=ALU.add),
             r=[R("PA0"), R("PA1"), R("a_S")], w=[R("a_S")])
        P.do("dve", lambda e: e.tensor_tensor(out=B.S[:, 1024:2048], in0=PC[:, 0:1024],
             in1=B.S[:, 1024:2048], op=ALU.add),
             r=[R("PC0"), R("PC1"), R("a_S")], w=[R("a_S")])
    with P.stage():
        P.do("act", lambda e: e.activation(out=B.Sbb[:, 0:1024], in_=B.S[:, 0:1024],
             func=AF.Identity), r=[R("a_S")], w=[R("a_Sbb")])
        P.do("dve", lambda e: e.tensor_copy(out=B.Sbb[:, 1024:2048],
             in_=B.S[:, 1024:2048]), r=[R("a_S")], w=[R("a_Sbb")])
    with P.stage():
        for ch in range(8):
            P.do("pe", lambda e, ch=ch: e.transpose(
                PT[:, ch * 128:(ch + 1) * 128],
                in_=B.tmp_tm[:, ch * 128:(ch + 1) * 128], identity=C.idb[:]),
                r=[R("a_tmptm"), R("idb")], w=[R("PT")])
    with P.stage():
        P.do("dve", lambda e, dfm=dfm: e.tensor_copy(out=dfm, in_=PT[:, 0:1024]),
             r=[R("PT")], w=[R("a_dfm2")])
    if "delta" in C.dbg and t == 0:
        with P.stage():
            P.dma(C.dbg["delta"][:], B.tmp_tm[:, 0:1024], r=[R("a_tmptm")])
    with P.stage():
        for ch in range(8):
            P.do("dve", lambda e, ch=ch, b1f=b1f: e.tensor_scalar(
                out=b1f[:, ch * 128:(ch + 1) * 128], in0=vs(ch),
                scalar1=C.wtab[:, ch * 54 + 12:ch * 54 + 13], scalar2=None,
                op0=ALU.mult), r=[R("a_vf"), R("wtab")], w=[R("a_b1f2")])
    fstk = lambda f: f[:].rearrange("p (c n) -> p c n", c=8)[:, :, o:o + 128]
    cstk = lambda b: b.rearrange("p (c n) -> p c n", c=8)
    vstack = B.vf[:].rearrange("p (c n) -> p c n", c=8)[:, :, 32 + o:32 + o + 128]
    branches = [
        (lambda ch: b1f[:, ch * 128:(ch + 1) * 128], cstk(b1f), "a_b1f2"),
        (lambda ch: B.f3[:, ch * 512 + o:ch * 512 + o + 128], fstk(B.f3), "a_f3"),
        (lambda ch: B.f7[:, ch * 512 + o:ch * 512 + o + 128], fstk(B.f7), "a_f7"),
        (lambda ch: B.f31[:, ch * 512 + o:ch * 512 + o + 128], fstk(B.f31), "a_f31"),
        (lambda ch: dfm[:, ch * 128:(ch + 1) * 128], cstk(dfm), "a_dfm2"),
        (lambda ch: vs(ch), vstack, "a_vf"),
    ]
    for br, (bsl, bstk, bres) in enumerate(branches):
        with P.stage():
            P.do("act", lambda e, bstk=bstk: e.activation(
                out=B.bsq[:].rearrange("p (c n) -> p c n", c=8),
                in_=bstk, func=AF.Square),
                r=[R(bres)], w=[R("a_bsq")])
            P.do("act", lambda e, bstk=bstk: e.activation(
                out=B.babs[:].rearrange("p (c n) -> p c n", c=8),
                in_=bstk, func=AF.Abs),
                r=[R(bres)], w=[R("a_babs")])
        with P.stage():
            for hh in range(4):
                col = 8 + hh * 6 + br
                for cc in range(2):
                    P.do("pe", lambda e, hh=hh, cc=cc, col=col, bsl=bsl: e.matmul(
                        PS[:, col:col + 1], lhsT=bsl(2 * hh + cc), rhs=C.onesb[:],
                        start=(cc == 0), stop=(cc == 1)),
                        r=[R(bres), R("onesb")], w=[R("PS")])
                for cc in range(2):
                    P.do("pe", lambda e, hh=hh, cc=cc, col=col: e.matmul(
                        PS[:, 32 + col - 8:32 + col - 8 + 1],
                        lhsT=B.babs[:, (2 * hh + cc) * 128:(2 * hh + cc + 1) * 128],
                        rhs=C.onesb[:], start=(cc == 0), stop=(cc == 1)),
                        r=[R("a_babs"), R("onesb")], w=[R("PS")])
                for cc in range(2):
                    P.do("pe", lambda e, hh=hh, cc=cc, col=col: e.matmul(
                        PS[:, 56 + col - 8:56 + col - 8 + 1],
                        lhsT=B.bsq[:, (2 * hh + cc) * 128:(2 * hh + cc + 1) * 128],
                        rhs=C.onesb[:], start=(cc == 0), stop=(cc == 1)),
                        r=[R("a_bsq"), R("onesb")], w=[R("PS")])
    T = C.tiny
    with P.stage():
        P.do("dve", lambda e: e.tensor_scalar(
            out=st96ap(B.st96, 0), in0=psblock(PS, 8),
            scalar1=1.0 / 256, scalar2=None, op0=ALU.mult),
            r=[R("PS")], w=[R("a_st96")])
        P.do("dve", lambda e: e.tensor_scalar(
            out=st96ap(B.st96, 2), in0=psblock(PS, 32),
            scalar1=1.0 / 256, scalar2=None, op0=ALU.mult),
            r=[R("PS")], w=[R("a_st96")])
        P.do("act", lambda e: e.activation(out=T[:, 0:24], in_=PS[:, 8:32],
             func=AF.Square), r=[R("PS")], w=[R("tinyA")])
    with P.stage():
        P.do("dve", lambda e: e.scalar_tensor_tensor(out=T[:, 0:24], in0=T[:, 0:24],
             scalar=-1.0 / 256, in1=PS[:, 56:80], op0=ALU.mult, op1=ALU.add),
             r=[R("tinyA"), R("PS")], w=[R("tinyA")])
    with P.stage():
        P.do("dve", lambda e: e.tensor_scalar(out=T[:, 0:24], in0=T[:, 0:24],
             scalar1=0.0, scalar2=None, op0=ALU.max), r=[R("tinyA")], w=[R("tinyA")])
    with P.stage():
        P.do("act", lambda e: e.activation(out=T[:, 0:24], in_=T[:, 0:24],
             func=AF.Ln, scale=1.0 / 255, bias=1e-30), r=[R("tinyA")], w=[R("tinyA")])
    with P.stage():
        P.do("act", lambda e: e.activation(out=st96ap(B.st96, 1), in_=T[:, 0:24],
             func=AF.Exp, scale=0.5), r=[R("tinyA")], w=[R("a_st96")])
        P.do("act", lambda e: e.activation(out=T[:, 0:24], in_=PS[:, 56:80],
             func=AF.Ln, bias=1e-30), r=[R("PS")], w=[R("tinyB")])
    with P.stage():
        P.do("act", lambda e: e.activation(out=st96ap(B.st96, 3), in_=T[:, 0:24],
             func=AF.Exp, scale=0.5), r=[R("tinyB")], w=[R("a_st96")])
    if "stats" in C.dbg and t == 0:
        with P.stage():
            P.do("dve", lambda e: e.tensor_copy(out=B.sot[:, 0:96], in_=B.st96[:]),
                 r=[R("a_st96")], w=[R("a_sot")])
        with P.stage():
            P.dma(C.dbg["stats"][:], B.sot[:, 0:96], r=[R("a_sot")])
    with P.stage():
        P.do("pe", lambda e: e.transpose(PT[0:96, 0:128], in_=B.st96[:],
             identity=C.idb[:]), r=[R("a_st96"), R("idb")], w=[R("PT")])
    with P.stage():
        P.do("dve", lambda e: e.tensor_copy(out=B.stT[:], in_=PT[0:96, 0:128]),
             r=[R("PT")], w=[R("a_stT")])
    with P.stage():
        for m in range(8):
            for k in range(9):
                if k < 8:
                    P.do("pe", lambda e, m=m, k=k: e.matmul(
                        C.PC[:, m * 128:(m + 1) * 128]
                        if m < 4 else C.PC[:, (m - 4) * 128 + 512:(m - 3) * 128 + 512],
                        lhsT=C.w1b[:, k * 1024 + m * 128:k * 1024 + (m + 1) * 128],
                        rhs=B.xnst[:, k * 512 + o:k * 512 + o + 128],
                        start=(k == 0), stop=False),
                        r=[R("w1b"), R("a_xnst")],
                        w=[R("PC0") if m < 4 else R("PC1")])
                else:
                    P.do("pe", lambda e, m=m: e.matmul(
                        C.PC[:, m * 128:(m + 1) * 128]
                        if m < 4 else C.PC[:, (m - 4) * 128 + 512:(m - 3) * 128 + 512],
                        lhsT=C.w1b[0:96, 8 * 1024 + m * 128:8 * 1024 + (m + 1) * 128],
                        rhs=B.stT[:], start=False, stop=True),
                        r=[R("w1b"), R("a_stT")],
                        w=[R("PC0") if m < 4 else R("PC1")])
    with P.stage():
        for m in range(8):
            P.do("dve", lambda e, m=m, p=p: e.tensor_scalar(
                out=B.hgst[:, p * 1024 + m * 128:p * 1024 + (m + 1) * 128],
                in0=C.PC[:, m * 128:(m + 1) * 128] if m < 4
                else C.PC[:, (m - 4) * 128 + 512:(m - 3) * 128 + 512],
                scalar1=C.b1fm[:, m:m + 1], scalar2=None, op0=ALU.add),
                r=[R("PC0") if m < 4 else R("PC1"), R("b1fm")], w=[R("a_hgst")])


def tileA_phase2(C, P, R, B, st, i):
    t = st * 4 + i
    o = i * 128
    PA, PB, PC, PS, PT = C.PA, C.PB, C.PC, C.PS, C.PT
    half = B.half
    sl = i % 2
    hgst, stT, l24, w24, wgtT = B.hgst, B.stT, B.l24, B.w24, B.wgtT
    facc, tmpc, ffm, sot = B.facc, B.tmpc, B.ffm, B.sot
    rr2, vf, f3, f7, f31 = B.rr2, B.vf, B.f3, B.f7, B.f31
    dfm = B.dfm2[:, sl * 1024:(sl + 1) * 1024]
    b1f = B.b1f2[:, sl * 1024:(sl + 1) * 1024]
    vs = lambda ch: vf[:, ch * 544 + 32 + o:ch * 544 + 32 + o + 128]
    owc = B.owc
    with P.stage():
        for kk in range(3):
            P.dma(owc[kk][:], C.ow_s[kk], w=[R("a_owc%d" % kk)])
    with P.stage():
        for cch in range(8):
            P.do("pe", lambda e, cch=cch, sl=sl: e.matmul(
                PS[:, 96:120],
                lhsT=hgst[:, sl * 1024 + cch * 128:sl * 1024 + (cch + 1) * 128],
                rhs=C.w2b[:, cch * 24:(cch + 1) * 24],
                start=(cch == 0), stop=(cch == 7)),
                r=[R("a_hgst"), R("w2b")], w=[R("PS")])
    if "hg" in C.dbg and t == 0:
        with P.stage():
            P.do("dve", lambda e: e.tensor_copy(out=B.sot[:],
                 in_=hgst[:, 0:1024]), r=[R("a_hgst")], w=[R("a_sot")])
        with P.stage():
            P.dma(C.dbg["hg"][:], B.sot[:], r=[R("a_sot")])
    T = C.tiny
    with P.stage():
        P.do("dve", lambda e: e.tensor_tensor(out=l24[:], in0=PS[:, 96:120],
             in1=C.invtempT[:], op=ALU.mult), r=[R("PS"), R("invtempT")],
             w=[R("a_l24")])
    with P.stage():
        P.do("dve", lambda e: e.tensor_tensor(out=l24[:], in0=l24[:],
             in1=C.b2i[:], op=ALU.add), r=[R("a_l24"), R("b2i")], w=[R("a_l24")])
    with P.stage():
        P.do("dve", lambda e: e.reduce_max(out=T[:, 0:4],
             in_=l24[:].rearrange("p (h s) -> p h s", h=4), axis=AX.X),
             r=[R("a_l24")], w=[R("tinyA")])
    with P.stage():
        for hh in range(4):
            P.do("dve", lambda e, hh=hh: e.tensor_scalar(
                out=l24[:, hh * 6:(hh + 1) * 6], in0=l24[:, hh * 6:(hh + 1) * 6],
                scalar1=T[:, hh:hh + 1], scalar2=None, op0=ALU.subtract),
                r=[R("a_l24"), R("tinyA")], w=[R("a_l24")])
    with P.stage():
        P.do("act", lambda e: e.activation(out=l24[:], in_=l24[:], func=AF.Exp),
             r=[R("a_l24")], w=[R("a_l24")])
    with P.stage():
        P.do("dve", lambda e: e.reduce_sum(out=T[:, 0:4],
             in_=l24[:].rearrange("p (h s) -> p h s", h=4), axis=AX.X),
             r=[R("a_l24")], w=[R("tinyA")])
    with P.stage():
        P.do("dve", lambda e: e.reciprocal(out=T[:, 4:8], in_=T[:, 0:4]),
             r=[R("tinyA")], w=[R("tinyA")])
    with P.stage():
        for hh in range(4):
            P.do("dve", lambda e, hh=hh: e.tensor_scalar(
                out=w24[:, hh * 6:(hh + 1) * 6], in0=l24[:, hh * 6:(hh + 1) * 6],
                scalar1=T[:, 4 + hh:5 + hh], scalar2=None, op0=ALU.mult),
                r=[R("a_l24"), R("tinyA")], w=[R("a_w24")])
    if "wgt" in C.dbg and t == 0:
        with P.stage():
            P.do("dve", lambda e: e.tensor_copy(out=l24[:], in_=w24[:]),
                 r=[R("a_w24")], w=[R("a_l24")])
        with P.stage():
            P.dma(C.dbg["wgt"][:], l24[:], r=[R("a_l24")])
    with P.stage():
        P.do("pe", lambda e: e.transpose(PT[0:24, 0:128], in_=w24[:],
             identity=C.idb[:]), r=[R("a_w24"), R("idb")], w=[R("PT")])
    with P.stage():
        P.do("dve", lambda e: e.tensor_copy(out=wgtT[:], in_=PT[0:24, 0:128]),
             r=[R("PT")], w=[R("a_wgtT")])
    fstk = lambda f: f[:].rearrange("p (c n) -> p c n", c=8)[:, :, o:o + 128]
    cstk = lambda b: b.rearrange("p (c n) -> p c n", c=8)
    vstack = vf[:].rearrange("p (c n) -> p c n", c=8)[:, :, 32 + o:32 + o + 128]
    branches = [
        (cstk(b1f), "a_b1f2"),
        (fstk(f3), "a_f3"),
        (fstk(f7), "a_f7"),
        (fstk(f31), "a_f31"),
        (cstk(dfm), "a_dfm2"),
        (vstack, "a_vf"),
    ]
    for s6, (bstk, bres) in enumerate(branches):
        PP, ppres = (PA, ("PA0", "PA1")) if s6 % 2 == 0 else (PB, ("PB0", "PB1"))
        with P.stage():
            for ch in range(8):
                hh = ch // 2
                P.do("pe", lambda e, ch=ch, hh=hh, s6=s6, PP=PP: e.matmul(
                    PP[:, ch * 128:(ch + 1) * 128],
                    lhsT=C.sel24b[:, hh * 6 + s6:hh * 6 + s6 + 1]
                    .broadcast_to([24, 128]),
                    rhs=wgtT[:], start=True, stop=True),
                    r=[R("sel24b"), R("a_wgtT")], w=[R(ppres[ch // 4])])
        with P.stage():
            dst, dres = (facc, "a_facc") if s6 == 0 else (tmpc, "a_tmpc")
            P.do("dve", lambda e, bstk=bstk, PP=PP, dst=dst: e.tensor_tensor(
                out=dst[:].rearrange("p (c n) -> p c n", c=8),
                in0=bstk, in1=PP[:, 0:1024].rearrange("p (c n) -> p c n", c=8),
                op=ALU.mult),
                r=[R(bres), R(ppres[0]), R(ppres[1])], w=[R(dres)])
        if s6 > 0:
            with P.stage():
                P.do("pool", lambda e: e.tensor_tensor(out=facc[:], in0=facc[:],
                     in1=tmpc[:], op=ALU.add),
                     r=[R("a_facc"), R("a_tmpc")], w=[R("a_facc")])
    with P.stage():
        P.do("dve", lambda e: e.tensor_tensor(out=tmpc[:], in0=facc[:],
             in1=facc[:], op=ALU.mult), r=[R("a_facc")], w=[R("a_tmpc")])
    with P.stage():
        for hh in range(4):
            for cc in range(2):
                P.do("pe", lambda e, hh=hh, cc=cc: e.matmul(
                    PS[:, 120 + hh:121 + hh],
                    lhsT=tmpc[:, (2 * hh + cc) * 128:(2 * hh + cc + 1) * 128],
                    rhs=C.onesb[:], start=(cc == 0), stop=(cc == 1)),
                    r=[R("a_tmpc"), R("onesb")], w=[R("PS")])
    with P.stage():
        P.do("act", lambda e, sl=sl: e.activation(out=rr2[:, sl * 4:(sl + 1) * 4],
             in_=PS[:, 120:124], func=AF.Ln, scale=1.0 / 256, bias=1e-5),
             r=[R("PS")], w=[R("a_rr2")])
    with P.stage():
        P.do("act", lambda e, sl=sl: e.activation(out=rr2[:, sl * 4:(sl + 1) * 4],
             in_=rr2[:, sl * 4:(sl + 1) * 4], func=AF.Exp, scale=-0.5),
             r=[R("a_rr2")], w=[R("a_rr2")])
    with P.stage():
        P.do("dve", lambda e, sl=sl: e.tensor_copy(out=w24[:, 0:4],
             in_=rr2[:, sl * 4:(sl + 1) * 4]), r=[R("a_rr2")], w=[R("a_w24")])
    with P.stage():
        for hh in range(4):
            P.do("pe", lambda e, hh=hh: e.transpose(
                PT[0:1, hh * 128:(hh + 1) * 128], in_=w24[:, hh:hh + 1],
                identity=C.idb[:]), r=[R("a_w24"), R("idb")], w=[R("PT")])
    with P.stage():
        P.do("dve", lambda e: e.tensor_copy(out=B.rrow[0:1, 0:512],
             in_=PT[0:1, 0:512]), r=[R("PT")], w=[R("a_rrow")])
    with P.stage():
        for hh in range(4):
            P.do("pe", lambda e, hh=hh: e.matmul(
                half(C.PA, 0)[:, hh * 128:(hh + 1) * 128], lhsT=C.onesrow[:],
                rhs=B.rrow[0:1, hh * 128:(hh + 1) * 128],
                start=True, stop=True),
                r=[R("a_rrow"), R("onesrow")], w=[R("PA0")])
    with P.stage():
        for ch in range(8):
            P.do("dve", lambda e, ch=ch: e.tensor_tensor(
                out=ffm[:, ch * 128:(ch + 1) * 128],
                in0=facc[:, ch * 128:(ch + 1) * 128],
                in1=half(C.PA, 0)[:, (ch // 2) * 128:(ch // 2 + 1) * 128],
                op=ALU.mult),
                r=[R("a_facc"), R("PA0")], w=[R("a_ffm")])
    for grp in ((0, 1, 2), (3, 4, 5), (6, 7)):
        with P.stage():
            for k in grp:
                for nn in range(2):
                    P.do("pe", lambda e, k=k, nn=nn: e.matmul(
                        PC[:, nn * 512:(nn + 1) * 512],
                        lhsT=ffm[:, k * 128:(k + 1) * 128],
                        rhs=owc[k % 3][:, nn * 512:(nn + 1) * 512],
                        start=(k == 0), stop=(k == 7)),
                        r=[R("a_ffm"), R("a_owc%d" % (k % 3))], w=[R("PC%d" % nn)])
        if grp[0] == 0:
            with P.stage():
                for kk in (3, 4, 5):
                    P.dma(owc[kk % 3][:], C.ow_s[kk], w=[R("a_owc%d" % (kk % 3))])
        elif grp[0] == 3:
            with P.stage():
                for kk in (6, 7):
                    P.dma(owc[kk % 3][:], C.ow_s[kk], w=[R("a_owc%d" % (kk % 3))])
    with P.stage(dma_out=True):
        P.do("dve", lambda e: e.tensor_copy(out=sot[:], in_=PC[:, 0:1024]),
             r=[R("PC0"), R("PC1")], w=[R("a_sot")])
    with P.stage():
        P.dma(C.s_s[t], sot[:], r=[R("a_sot")])
        if "s" in C.dbg and t == 0:
            P.dma(C.dbg["s"][:], sot[:], r=[R("a_sot")])


def st96ap(st96, stat):
    return st96[:].rearrange("p (x f) -> p x f", f=4)[:, :, stat:stat + 1]


def psblock(PS, base):
    return PS[:, base:base + 24]


def phaseB(C, es):
    nc, P = C.nc, C.P
    R = P.res
    sb = lambda n, sh, dt=F32: es.enter_context(nc.sbuf_tensor(n, sh, dt))
    wgu = sb("b_wgu", [128, 8 * 6144], BF16)
    wdb = sb("b_wdb", [128, 24 * 1024], BF16)
    xnb = sb("b_xnb", [128, 4096], BF16)
    hst = sb("b_hst", [128, 24 * 512], BF16)
    sg = sb("b_sg", [128, 1024], BF16)
    xb0 = sb("b_x0", [128, 1024])
    sb0 = sb("b_s0", [128, 1024])
    gb0 = sb("b_gb0", [128, 1024], BF16)
    xb = [xb0, xb0]
    sbf = [sb0, sb0]
    gb = [gb0, gb0]
    tm = sbf
    ob = xb
    PA, PB, PC, PS = C.PA, C.PB, C.PC, C.PS
    with P.stage():
        P.dma(wgu[:].rearrange("p (k n) -> p k n", k=8),
              C.wgu_s.rearrange("k p n -> p k n"), w=[R("b_wgu")])
        P.dma(wdb[:].rearrange("p (k n) -> p k n", k=24),
              C.wdT_s.rearrange("k p n -> p k n"), w=[R("b_wdb")])
    for st in range(NST):
        t0 = st * 4
        with P.stage():
            for i in range(4):
                P.dma(xnb[:].rearrange("p (g t) -> p g t", g=8)
                      [:, :, i * 128:(i + 1) * 128],
                      C.xn_s[t0 + i].rearrange("p (g t) -> p g t", g=8),
                      w=[R("b_xnb")])
        for m in range(24):
            q = m % 2
            PP = PA if q == 0 else PB
            pr = ("PA0", "PA1") if q == 0 else ("PB0", "PB1")
            with P.stage():
                for k in range(8):
                    P.do("pe", lambda e, m=m, k=k, PP=PP: e.matmul(
                        PP[:, 0:512],
                        lhsT=wgu[:, k * 6144 + m * 128:k * 6144 + (m + 1) * 128],
                        rhs=xnb[:, k * 512:(k + 1) * 512],
                        start=(k == 0), stop=(k == 7)),
                        r=[R("b_wgu"), R("b_xnb")], w=[R(pr[0])])
                for k in range(8):
                    P.do("pe", lambda e, m=m, k=k, PP=PP: e.matmul(
                        PP[:, 512:1024],
                        lhsT=wgu[:, k * 6144 + (24 + m) * 128:k * 6144 + (25 + m) * 128],
                        rhs=xnb[:, k * 512:(k + 1) * 512],
                        start=(k == 0), stop=(k == 7)),
                        r=[R("b_wgu"), R("b_xnb")], w=[R(pr[1])])
            with P.stage():
                P.do("act", lambda e, PP=PP, q=q: e.activation(
                    out=sg[:, q * 512:(q + 1) * 512], in_=PP[:, 0:512],
                    func=AF.Silu, scale=1.0 / 127), r=[R(pr[0])], w=[R("b_sg%d" % q)])
            with P.stage():
                P.do("dve", lambda e, m=m, PP=PP, q=q: e.tensor_tensor(
                    out=hst[:, m * 512:(m + 1) * 512], in0=PP[:, 512:1024],
                    in1=sg[:, q * 512:(q + 1) * 512], op=ALU.mult),
                    r=[R(pr[1]), R("b_sg%d" % q)], w=[R("b_hst")])
        for i in range(4):
            t = t0 + i
            p = t % 2
            with P.stage():
                P.dma(xb[p][:], C.x_h[t * 128:(t + 1) * 128, :], w=[R("b_x0")])
                P.dma(sbf[p][:], C.s_s[t], w=[R("b_s0")])
            with P.stage():
                for cch in range(24):
                    for nn in range(2):
                        P.do("pe", lambda e, cch=cch, nn=nn, i=i: e.matmul(
                            PC[:, nn * 512:(nn + 1) * 512],
                            lhsT=hst[:, cch * 512 + i * 128:cch * 512 + i * 128 + 128],
                            rhs=wdb[:, cch * 1024 + nn * 512:cch * 1024 + (nn + 1) * 512],
                            start=(cch == 0), stop=(cch == 23)),
                            r=[R("b_hst"), R("b_wdb")], w=[R("PC%d" % nn)])
            with P.stage():
                P.do("act", lambda e, p=p: e.activation(out=gb[p][:], in_=PC[:, 0:1024],
                     func=AF.Silu, scale=1.0 / (127.0 * 127.0)),
                     r=[R("PC0"), R("PC1")], w=[R("b_gb0")])
            if "g" in C.dbg and t == 0:
                with P.stage():
                    P.do("dve", lambda e, p=p: e.tensor_copy(out=tm[p][:], in_=gb[p][:]),
                         r=[R("b_gb0")], w=[R("b_s0")])
                with P.stage():
                    P.dma(C.dbg["g"][:], tm[p][:], r=[R("b_s0")])
            with P.stage():
                P.do("dve", lambda e, p=p: e.tensor_tensor(out=tm[p][:], in0=sbf[p][:],
                     in1=gb[p][:], op=ALU.mult), r=[R("b_s0"), R("b_gb0")],
                     w=[R("b_s0")])
            with P.stage(dma_out=True):
                P.do("pool", lambda e, p=p: e.tensor_tensor(out=ob[p][:], in0=tm[p][:],
                     in1=xb[p][:], op=ALU.add), r=[R("b_s0"), R("b_x0")],
                     w=[R("b_x0")])
            with P.stage():
                P.dma(C.out_h[t * 128:(t + 1) * 128, :], ob[p][:], r=[R("b_x0")])


def kernel(**inputs):
    hp = host_prep(inputs)
    nc, dbg_names = build_program()
    x_full = np.asarray(inputs["L0_attn_hidden_states"])
    in_maps = []
    for c in range(N_CORES):
        m = dict(hp)
        m["x"] = np.ascontiguousarray(x_full[c])
        in_maps.append(m)
    res = run_bass_kernel_spmd(nc, in_maps, core_ids=list(range(N_CORES)))
    out = np.stack([res.results[c]["out"] for c in range(N_CORES)], axis=0)
    if DEBUG:
        kernel._dbg = [{k: res.results[c].get("dbg_" + k) for k in DEBUG}
                       for c in range(N_CORES)]
    return out


# version 17
# speedup vs baseline: 1.1124x; 1.0892x over previous
import numpy as np
import ml_dtypes
from contextlib import ExitStack, contextmanager
import concourse.bass as bass
import concourse.mybir as mybir
from concourse.bass_utils import run_bass_kernel_spmd

F32 = mybir.dt.float32
BF16 = mybir.dt.bfloat16
I32 = mybir.dt.int32
AF = mybir.ActivationFunctionType
ALU = mybir.AluOpType
AX = mybir.AxisListType

N_CORES = 8
L, D, H, DK = 4096, 1024, 4, 256
NT, NST, G, HID = 32, 8, 8, 3072

DEBUG = []

ENG2BLK = {"pe": "tensor", "dve": "vector", "act": "scalar",
           "pool": "gpsimd", "sync": "sync"}


class Res:
    __slots__ = ("name", "w_tok", "readers")

    def __init__(self, name):
        self.name = name
        self.w_tok = None
        self.readers = {}


class Prog:
    def __init__(self, nc):
        self.nc = nc
        self.steps = {e: [] for e in ENG2BLK}
        self.cnt = {e: 0 for e in ENG2BLK if e != "sync"}
        self.dma_cnt = 0
        self.hwm = {}
        self.in_stage = False
        self.touch = None
        self.emitted = None
        self.sems = {}
        self._res = {}
        self.dma_safe = {e: 0 for e in ("pe", "dve", "act", "pool")}

    def res(self, name):
        if name not in self._res:
            self._res[name] = Res(name)
        return self._res[name]

    def make_sems(self, es):
        for e in ("pe", "dve", "act", "pool", "dma", "dmadone"):
            self.sems[e] = es.enter_context(self.nc.semaphore("sem_" + e))
        self.dmadone_cnt = 0
        self.dmadone_pub = {}

    def raw(self, eng, fn):
        self.steps[eng].append(fn)

    def _wait(self, waiter, tok, gate=False):
        key, val = tok
        if gate and waiter == "sync" and key in ("pe", "dve", "act", "pool"):
            if self.dma_safe[key] < val:
                self.cnt[key] += 1
                sem0 = self.sems[key]
                self.raw(key, lambda e, s=sem0: e.drain().then_inc(s, 1))
                self.dma_safe[key] = self.cnt[key]
            val = self.dma_safe[key]
            tok = (key, val)
        if key == "dma" and waiter != "sync":
            total = self.dma_cnt
            if total not in self.dmadone_pub:
                self.dmadone_cnt += 1
                self.dmadone_pub[total] = self.dmadone_cnt
                ds, dd = self.sems["dma"], self.sems["dmadone"]
                if self.hwm.get(("sync", "dma"), 0) < total:
                    self.hwm[("sync", "dma")] = total
                    self.raw("sync", lambda e, s=ds, v=total: e.wait_ge(s, v))
                self.raw("sync", lambda e, s=dd: e.sem_inc(s, 1))
            key, val = "dmadone", self.dmadone_pub[total]
        if self.hwm.get((waiter, key), 0) >= val:
            return
        self.hwm[(waiter, key)] = val
        sem = self.sems[key]
        self.raw(waiter, lambda e, s=sem, v=val: e.wait_ge(s, v))

    def _hazards(self, eng, r, w):
        for b in r:
            if b.w_tok is not None:
                self._wait(eng, b.w_tok)
        for b in w:
            if b.w_tok is not None:
                self._wait(eng, b.w_tok)
            for rk, rv in b.readers.items():
                self._wait(eng, (rk, rv))

    def do(self, eng, fn, r=(), w=()):
        assert self.in_stage
        self._hazards(eng, r, w)
        self.raw(eng, fn)
        self.last_op_idx[eng] = len(self.steps[eng]) - 1
        self.emitted.add(eng)
        for b in r:
            self.touch.append((eng, b, "r"))
        for b in w:
            self.touch.append((eng, b, "w"))

    def dma(self, out_ap, in_ap, r=(), w=()):
        assert self.in_stage
        for b in r:
            if b.w_tok is not None:
                self._wait("sync", b.w_tok, gate=True)
        self._hazards("sync", (), w)
        self.dma_cnt += 16
        val = self.dma_cnt
        sem = self.sems["dma"]
        self.raw("sync",
                 lambda e, o=out_ap, i=in_ap, s=sem: e.dma_start(out=o, in_=i)
                 .then_inc(s, 16))
        for b in r:
            b.readers["dma"] = max(b.readers.get("dma", 0), val)
        for b in w:
            b.w_tok = ("dma", val)
            b.readers = {}

    @contextmanager
    def stage(self, use_drain=False, dma_out=False):
        use_drain = use_drain or dma_out
        assert not self.in_stage
        self.in_stage = True
        self.touch, self.emitted = [], set()
        self.last_op_idx = {}
        yield
        touched, emitted = self.touch, self.emitted
        self.in_stage = False
        self.touch = self.emitted = None
        toks = {}
        for e in emitted:
            self.cnt[e] += 1
            toks[e] = (e, self.cnt[e])
            sem = self.sems[e]
            if not use_drain:
                idx = self.last_op_idx[e]
                orig = self.steps[e][idx]
                self.steps[e][idx] = (
                    lambda en, o=orig, s=sem: o(en).then_inc(s, 1))
            else:
                self.raw(e, lambda en, s=sem: en.drain().then_inc(s, 1))
                self.dma_safe[e] = self.cnt[e]
        for eng, b, kind in touched:
            tok = toks[eng]
            if kind == "w":
                b.w_tok = tok
                b.readers = {}
            else:
                k, v = tok
                b.readers[k] = max(b.readers.get(k, 0), v)

    def barrier(self):
        with self.stage(use_drain=True):
            for e in ("pe", "dve", "act", "pool"):
                self.do(e, lambda en: en.nop())
        toks = [(e, self.cnt[e]) for e in ("pe", "dve", "act", "pool")]
        toks.append(("dma", self.dma_cnt))
        for e in ("pe", "dve", "act", "pool", "sync"):
            for t in toks:
                self._wait(e, t)

    def finalize(self, blk):
        for e, bname in ENG2BLK.items():
            fns = self.steps[e]
            getattr(blk, bname)(lambda eng, fns=fns: [f(eng) for f in fns])


def host_prep(inputs):
    f = lambda k: np.asarray(inputs[k])
    qkv = np.concatenate([f("L0_attn_q_w"), f("L0_attn_k_w"),
                          f("L0_attn_v_w")], axis=1)
    bw = f("L0_attn_b_w")

    def taps(name):
        return f(name)[:, 0, ::-1]
    wtab = np.concatenate([taps("L0_attn_qc_w"), taps("L0_attn_kc_w"),
                           taps("L0_attn_vc_w"), taps("L0_attn_fir_w1"),
                           taps("L0_attn_fir_w3"), taps("L0_attn_fir_w7"),
                           taps("L0_attn_fir_w31")], axis=1)
    sel = np.eye(24, dtype=np.float32)[:, :, None]
    c = lambda a: np.ascontiguousarray(a)
    return dict(
        qkv=c(qkv.astype(np.float32)),
        bw=c(bw.astype(np.float32)),
        w1=c(f("L0_attn_mlp_w1").astype(np.float32)),
        w2=c(f("L0_attn_mlp_w2").astype(np.float32)),
        ow=c(f("L0_attn_o_w").astype(np.float32)),
        wtab=c(wtab.astype(np.float32)),
        wgu=c(np.concatenate([f("L0_ffn_w_gate").T, f("L0_ffn_w_up").T], axis=1)),
        wdT=c(f("L0_ffn_w_down").T),
        b1fm=c(f("L0_attn_mlp_b1").reshape(8, 128).T.astype(np.float32)),
        onorm2=c(f("L0_attn_onorm_w").reshape(2, 128).T.astype(np.float32)),
        b2row=c(f("L0_attn_mlp_b2").reshape(1, 24).astype(np.float32)),
        gltrow=c(f("L0_attn_gate_log_temp").reshape(1, 4).astype(np.float32)),
        id128=np.eye(128, dtype=np.float32),
        maskUs=c(np.triu(np.ones((128, 128), np.float32), 1)),
        maskUi=c(np.triu(np.ones((128, 128), np.float32), 0)),
        sel24=c(sel.reshape(24, 24)),
        ones1r=np.ones((1, 128), np.float32),
    )


class Ctx:
    pass


def build_program():
    nc = bass.Bass("TRN2", target_bir_lowering=False)
    C = Ctx()
    C.nc = nc
    d = lambda n, sh, dt, kind: nc.dram_tensor(n, sh, dt, kind=kind).ap()
    C.x_h = d("x", [L, D], F32, "ExternalInput")
    C.qkv_h = d("qkv", [D, 3072], F32, "ExternalInput")
    C.bw_h = d("bw", [D, 4], F32, "ExternalInput")
    C.w1_h = d("w1", [1120, D], F32, "ExternalInput")
    C.w2_h = d("w2", [D, 24], F32, "ExternalInput")
    C.ow_h = d("ow", [D, D], F32, "ExternalInput")
    C.wtab_h = d("wtab", [D, 54], F32, "ExternalInput")
    C.wgu_h = d("wgu", [D, 2 * HID], I32, "ExternalInput")
    C.wdT_h = d("wdT", [HID, D], I32, "ExternalInput")
    C.b1fm_h = d("b1fm", [128, 8], F32, "ExternalInput")
    C.onorm2_h = d("onorm2", [128, 2], F32, "ExternalInput")
    C.b2row_h = d("b2row", [1, 24], F32, "ExternalInput")
    C.gltrow_h = d("gltrow", [1, 4], F32, "ExternalInput")
    C.id128_h = d("id128", [128, 128], F32, "ExternalInput")
    C.maskUs_h = d("maskUs", [128, 128], F32, "ExternalInput")
    C.maskUi_h = d("maskUi", [128, 128], F32, "ExternalInput")
    C.sel24_h = d("sel24", [24, 24], F32, "ExternalInput")
    C.ones1r_h = d("ones1r", [1, 128], F32, "ExternalInput")
    C.out_h = d("out", [L, D], F32, "ExternalOutput")
    C.dbg = {}
    if "xn" in DEBUG:
        C.dbg["xn"] = nc.dram_tensor("dbg_xn", [NT, 128, D], BF16,
                                     kind="ExternalOutput").ap()
    for name, sh, dt_ in [("dd", [128, 4096], BF16), ("dd2", [128, 2048], BF16),
                          ("rst", [128, 16], F32),
                          ("q", [128, 4096], BF16), ("k", [128, 4096], BF16),
                          ("v", [128, 4096], BF16), ("beta", [128, 16], F32),
                          ("delta", [128, 1024], BF16), ("stats", [128, 96], F32),
                          ("wgt", [128, 24], F32), ("s", [128, 1024], F32),
                          ("hg", [128, 1024], F32), ("g", [128, 1024], F32)]:
        if name in DEBUG:
            C.dbg[name] = d("dbg_" + name, sh, dt_, "ExternalOutput")
    C.xn_s = d("xn_s", [NT, 128, D], BF16, "Internal")
    C.s_s = d("s_s", [NT, 128, D], F32, "Internal")
    C.diag_s = d("diag_s", [G, 128, 54 * 128], BF16, "Internal")
    C.ow_s = d("ow_s", [8, 128, D], BF16, "Internal")
    C.qkv_s = d("qkv_s", [12, 128, 2048], BF16, "Internal")
    C.wgu_s = d("wgu_s", [8, 128, 2 * HID], BF16, "Internal")
    C.wdT_s = d("wdT_s", [24, 128, D], BF16, "Internal")

    es = ExitStack()
    P = Prog(nc)
    P.make_sems(es)
    sbg = lambda n, sh, dt=F32: es.enter_context(nc.sbuf_tensor(n, sh, dt))

    C.idb = sbg("idb", [128, 128], BF16)
    C.mUsb = sbg("mUsb", [128, 128], BF16)
    C.mUib = sbg("mUib", [128, 128], BF16)
    C.i4b = sbg("i4b", [128, 512], BF16)
    C.onesb = sbg("onesb", [128, 1], BF16)
    C.onesrow = sbg("onesrow", [1, 128], BF16)
    C.rows = sbg("rows", [1, 96])
    C.tiny = sbg("tiny", [128, 32])
    C.invtempT = sbg("invtempT", [128, 24])
    C.b2i = sbg("b2i", [128, 24])
    C.b1fm = sbg("b1fm_s", [128, 8])
    C.wtab = sbg("wtab_s", [128, G * 54])
    C.w2b = sbg("w2b", [128, 8 * 24], BF16)
    C.wbb = sbg("wbb", [128, 8 * 4], BF16)
    C.sel24b = sbg("sel24b", [24, 24], BF16)
    C.constt = sbg("constt", [128, 8])
    CONSTS = [1e-5, 1e-6, 1e-30, 1.0]
    for i, v in enumerate(CONSTS):
        nc.const_aps.register(F32, v, C.constt[:, i:i + 1])

    C.PA = es.enter_context(nc.psum_tensor("PA", [128, 1024], F32))
    C.PB = es.enter_context(nc.psum_tensor("PB", [128, 1024], F32))
    C.PC = es.enter_context(nc.psum_tensor("PC", [128, 1024], F32))
    C.PS = es.enter_context(nc.psum_tensor("PS", [128, 512], F32))
    C.PT = es.enter_context(nc.psum_tensor("PT", [128, 1024], BF16))
    C.P = P

    with es, nc.Block() as blk:
        with P.stage():
            for i, v in enumerate(CONSTS):
                P.do("dve", lambda e, i=i, v=v: e.memset(C.constt[:, i:i + 1], v),
                     w=[P.res("constt")])
        P.barrier()
        with ExitStack() as es_pa:
            sba = lambda n, sh, dt=F32: es_pa.enter_context(nc.sbuf_tensor(n, sh, dt))
            C.w1b = sba("w1b", [128, 9 * 1024], BF16)
            with ExitStack() as es_p0:
                phase0(C, es_p0)
            P.barrier()
            with ExitStack() as es_a:
                phaseA(C, es_a)
        P.barrier()
        with ExitStack() as es_b:
            if "noB" not in DEBUG:
                phaseB(C, es_b)
        P.raw("sync", lambda e, s=P.sems["dma"], v=P.dma_cnt: e.wait_ge(s, v))
        P.finalize(blk)
    return nc, list(C.dbg.keys())


def phase0(C, es):
    nc, P = C.nc, C.P
    R = P.res
    sb = lambda n, sh, dt=F32: es.enter_context(nc.sbuf_tensor(n, sh, dt))
    stg_f = [sb("p0f0", [128, 3072]), sb("p0f1", [128, 3072])]
    stg_i = [sb("p0i0", [128, 3072], I32), sb("p0i1", [128, 3072], I32)]
    stg_b = [sb("p0b0", [128, 3072], BF16), sb("p0b1", [128, 3072], BF16)]
    dgb = [sb("p0d0", [128, 54 * 128], BF16), sb("p0d1", [128, 54 * 128], BF16)]
    cons = sb("p0cons", [128, 512])
    on2 = sb("p0on2", [128, 2])
    NW = 4
    xt = [sb("p0x%d" % i, [128, D]) for i in range(NW)]
    xsq = [sb("p0sq%d" % i, [128, D]) for i in range(NW)]
    xnt = [sb("p0xnt%d" % i, [128, D], BF16) for i in range(NW)]
    xnf = [sb("p0xnf%d" % i, [128, D], BF16) for i in range(NW)]

    rf = lambda i: R("p0f%d" % i)
    ri = lambda i: R("p0i%d" % i)
    rb = lambda i: R("p0b%d" % i)

    with P.stage():
        P.dma(cons[:, 0:128], C.id128_h[:], w=[R("cons")])
        P.dma(cons[:, 128:256], C.maskUs_h[:], w=[R("cons")])
        P.dma(cons[:, 256:384], C.maskUi_h[:], w=[R("cons")])
        P.dma(C.rows[0:1, 0:24], C.b2row_h[:], w=[R("rows")])
        P.dma(C.rows[0:1, 24:28], C.gltrow_h[:], w=[R("rows")])
        P.dma(C.b1fm[:], C.b1fm_h[:], w=[R("b1fm")])
        P.dma(on2[:], C.onorm2_h[:], w=[R("on2")])
        P.dma(C.wtab[:].rearrange("p (g w) -> p g w", g=8),
              C.wtab_h.rearrange("(g p) w -> p g w", p=128), w=[R("wtab")])
        P.dma(stg_f[0][0:24, 0:24], C.sel24_h[:], w=[rf(0)])
        P.dma(stg_f[1][0:1, 0:128], C.ones1r_h[:], w=[rf(1)])
    with P.stage():
        P.do("dve", lambda e: e.tensor_copy(out=C.idb[:], in_=cons[:, 0:128]),
             r=[R("cons")], w=[R("idb")])
        P.do("dve", lambda e: e.tensor_copy(out=C.mUsb[:], in_=cons[:, 128:256]),
             r=[R("cons")], w=[R("mUsb")])
        P.do("dve", lambda e: e.tensor_copy(out=C.mUib[:], in_=cons[:, 256:384]),
             r=[R("cons")], w=[R("mUib")])
        P.do("dve", lambda e: e.memset(C.onesb[:], 1.0), w=[R("onesb")])
        P.do("dve", lambda e: e.tensor_copy(out=C.onesrow[:],
             in_=stg_f[1][0:1, 0:128]), r=[rf(1)], w=[R("onesrow")])
        P.do("dve", lambda e: e.tensor_copy(out=C.sel24b[:],
             in_=stg_f[0][0:24, 0:24]), r=[rf(0)], w=[R("sel24b")])
        P.do("pool", lambda e: e.tensor_copy(out=C.i4b[:, 0:128], in_=cons[:, 0:128]),
             r=[R("cons")], w=[R("i4b")])
        for j in range(1, 4):
            P.do("pool", lambda e, j=j: e.tensor_copy(
                out=C.i4b[:, j * 128:(j + 1) * 128], in_=cons[:, 0:128]),
                r=[R("cons")], w=[R("i4b")])

    ro = C.rows
    with P.stage():
        P.do("dve", lambda e: e.tensor_scalar(out=ro[0:1, 28:32], in0=ro[0:1, 24:28],
             scalar1=0.0, scalar2=None, op0=ALU.max), r=[R("rows")], w=[R("rows")])
        P.do("act", lambda e: e.activation(out=ro[0:1, 32:36], in_=ro[0:1, 24:28],
             func=AF.Abs), r=[R("rows")], w=[R("rows2")])
    with P.stage():
        P.do("act", lambda e: e.activation(out=ro[0:1, 32:36], in_=ro[0:1, 32:36],
             func=AF.Exp, scale=-1.0), r=[R("rows2")], w=[R("rows2")])
    with P.stage():
        P.do("act", lambda e: e.activation(out=ro[0:1, 32:36], in_=ro[0:1, 32:36],
             func=AF.Ln, bias=1.0), r=[R("rows2")], w=[R("rows2")])
    with P.stage():
        P.do("dve", lambda e: e.tensor_tensor(out=ro[0:1, 36:40], in0=ro[0:1, 28:32],
             in1=ro[0:1, 32:36], op=ALU.add), r=[R("rows"), R("rows2")], w=[R("rows")])
    with P.stage():
        P.do("dve", lambda e: e.tensor_scalar(out=ro[0:1, 36:40], in0=ro[0:1, 36:40],
             scalar1=0.5, scalar2=None, op0=ALU.add), r=[R("rows")], w=[R("rows")])
    with P.stage():
        P.do("dve", lambda e: e.reciprocal(out=ro[0:1, 40:44], in_=ro[0:1, 36:40]),
             r=[R("rows")], w=[R("rows")])
    with P.stage():
        P.do("dve", lambda e: e.tensor_copy(
            out=ro[0:1, 44:68].rearrange("p (h s) -> p h s", h=4),
            in_=ro[0:1, 40:44].unsqueeze(2).broadcast_to([1, 4, 6])),
            r=[R("rows")], w=[R("rows")])
    row_b = es.enter_context(nc.sbuf_tensor("p0rowb", [1, 64], BF16))
    with P.stage():
        P.do("dve", lambda e: e.tensor_copy(out=row_b[0:1, 0:24],
             in_=ro[0:1, 44:68]), r=[R("rows")], w=[R("rowb")])
        P.do("dve", lambda e: e.tensor_copy(out=row_b[0:1, 32:56],
             in_=ro[0:1, 0:24]), r=[R("rows")], w=[R("rowb")])
    with P.stage():
        P.do("pe", lambda e: e.matmul(C.PS[:, 0:24], lhsT=C.onesrow[:],
             rhs=row_b[0:1, 0:24], start=True, stop=True),
             r=[R("rowb"), R("onesrow")], w=[R("PS")])
        P.do("pe", lambda e: e.matmul(C.PS[:, 32:56], lhsT=C.onesrow[:],
             rhs=row_b[0:1, 32:56], start=True, stop=True),
             r=[R("rowb"), R("onesrow")], w=[R("PS")])
    with P.stage():
        P.do("dve", lambda e: e.tensor_copy(out=C.invtempT[:], in_=C.PS[:, 0:24]),
             r=[R("PS")], w=[R("invtempT")])
    with P.stage():
        P.do("dve", lambda e: e.tensor_tensor(out=C.b2i[:], in0=C.PS[:, 32:56],
             in1=C.invtempT[:], op=ALU.mult),
             r=[R("PS"), R("invtempT")], w=[R("b2i")])

    T = C.tiny
    for t in range(NT):
        p = t % 4
        xr, sqr = R("p0x%d" % p), R("p0sq%d" % p)
        ntr, nfr = R("p0xnt%d" % p), R("p0xnf%d" % p)
        ln = R("p0ln%d" % p)
        co = p * 8
        with P.stage():
            P.dma(xt[p][:], C.x_h[t * 128:(t + 1) * 128, :], w=[xr])
        with P.stage():
            P.do("act", lambda e, p=p: e.activation(out=xsq[p][:], in_=xt[p][:],
                 func=AF.Square), r=[xr], w=[sqr])
            P.do("dve", lambda e, p=p, co=co: e.reduce_sum(out=T[:, co:co + 1],
                 in_=xt[p][:], axis=AX.XYZW), r=[xr], w=[ln])
        with P.stage():
            P.do("dve", lambda e, p=p, co=co: e.reduce_sum(out=T[:, co + 1:co + 2],
                 in_=xsq[p][:], axis=AX.XYZW), r=[sqr, ln], w=[ln])
            P.do("dve", lambda e, co=co: e.tensor_scalar(out=T[:, co + 2:co + 3],
                 in0=T[:, co:co + 1], scalar1=1.0 / D, scalar2=None, op0=ALU.mult),
                 r=[ln], w=[ln])
        with P.stage():
            P.do("dve", lambda e, co=co: e.tensor_tensor(out=T[:, co + 3:co + 4],
                 in0=T[:, co + 2:co + 3], in1=T[:, co + 2:co + 3], op=ALU.mult),
                 r=[ln], w=[ln])
        with P.stage():
            P.do("dve", lambda e, co=co: e.scalar_tensor_tensor(
                 out=T[:, co + 4:co + 5], in0=T[:, co + 1:co + 2], scalar=1.0 / D,
                 in1=T[:, co + 3:co + 4], op0=ALU.mult, op1=ALU.subtract),
                 r=[ln], w=[ln])
        with P.stage():
            P.do("act", lambda e, co=co: e.activation(out=T[:, co + 5:co + 6],
                 in_=T[:, co + 4:co + 5], func=AF.Ln, bias=1e-5), r=[ln], w=[ln])
        with P.stage():
            P.do("act", lambda e, co=co: e.activation(out=T[:, co + 6:co + 7],
                 in_=T[:, co + 5:co + 6], func=AF.Exp, scale=-0.5), r=[ln], w=[ln])
        with P.stage():
            P.do("dve", lambda e, p=p, co=co: e.tensor_scalar(out=xsq[p][:],
                 in0=xt[p][:], scalar1=T[:, co + 2:co + 3], scalar2=None,
                 op0=ALU.subtract), r=[xr, ln], w=[sqr])
        with P.stage():
            P.do("dve", lambda e, p=p, co=co: e.tensor_scalar(out=xnt[p][:],
                 in0=xsq[p][:], scalar1=T[:, co + 6:co + 7], scalar2=None,
                 op0=ALU.mult), r=[sqr, ln], w=[ntr])
        with P.stage():
            for cch in range(8):
                P.do("pe", lambda e, p=p, cch=cch: e.transpose(
                    C.PT[:, cch * 128:(cch + 1) * 128],
                    in_=xnt[p][:, cch * 128:(cch + 1) * 128], identity=C.idb[:]),
                    r=[ntr, R("idb")], w=[R("PT")])
        with P.stage(dma_out=True):
            P.do("dve", lambda e, p=p: e.tensor_copy(out=xnf[p][:],
                 in_=C.PT[:, 0:1024]), r=[R("PT")], w=[nfr])
        with P.stage():
            P.dma(C.xn_s[t], xnf[p][:], r=[nfr])
            if "xn" in C.dbg:
                P.dma(C.dbg["xn"][t], xnf[p][:], r=[nfr])
    for k in range(8):
        p = k % 2
        with P.stage():
            P.dma(stg_f[p][:], C.qkv_h[k * 128:(k + 1) * 128, :], w=[rf(p)])
        with P.stage(dma_out=True):
            P.do("dve", lambda e, p=p: e.tensor_copy(out=stg_b[p][:],
                 in_=stg_f[p][:]), r=[rf(p)], w=[rb(p)])
        with P.stage():
            P.dma(C.qkv_s[:, :, k * 256:(k + 1) * 256].transpose([1, 0, 2]),
                  stg_b[p][:].rearrange("p (m n) -> p m n", m=12), r=[rb(p)])
    with P.stage():
        P.dma(stg_f[0][:, 0:32].rearrange("p (k n) -> p k n", k=8),
              C.bw_h.rearrange("(k p) n -> p k n", p=128), w=[rf(0)])
    with P.stage():
        P.do("dve", lambda e: e.tensor_copy(out=C.wbb[:], in_=stg_f[0][:, 0:32]),
             r=[rf(0)], w=[R("wbb")])
    for k in range(9):
        rows_n = 128 if k < 8 else 96
        p = k % 2
        with P.stage():
            P.dma(stg_f[p][0:rows_n, 0:1024], C.w1_h[k * 128:k * 128 + rows_n, :],
                  w=[rf(p)])
        with P.stage():
            P.do("dve", lambda e, k=k, rows_n=rows_n, p=p: e.tensor_copy(
                out=C.w1b[0:rows_n, k * 1024:(k + 1) * 1024],
                in_=stg_f[p][0:rows_n, 0:1024]), r=[rf(p)], w=[R("w1b")])
    with P.stage():
        P.dma(stg_f[0][:, 0:192].rearrange("p (c n) -> p c n", c=8),
              C.w2_h.rearrange("(c p) n -> p c n", p=128), w=[rf(0)])
    with P.stage():
        P.do("dve", lambda e: e.tensor_copy(out=C.w2b[:], in_=stg_f[0][:, 0:192]),
             r=[rf(0)], w=[R("w2b")])
    for k in range(8):
        p = k % 2
        with P.stage():
            P.dma(stg_f[p][:, 0:1024], C.ow_h[k * 128:(k + 1) * 128, :], w=[rf(p)])
        with P.stage(dma_out=True):
            P.do("dve", lambda e, p=p, k=k: e.tensor_scalar(
                out=stg_b[p][:, 0:1024], in0=stg_f[p][:, 0:1024],
                scalar1=on2[:, k % 2:k % 2 + 1], scalar2=None, op0=ALU.mult),
                r=[rf(p), R("on2")], w=[rb(p)])
        with P.stage():
            P.dma(C.ow_s[k], stg_b[p][:, 0:1024], r=[rb(p)])
    for k in range(8):
        for half in range(2):
            p = (2 * k + half) % 2
            with P.stage():
                P.dma(stg_i[p][:], C.wgu_h[k * 128:(k + 1) * 128,
                                           half * HID:(half + 1) * HID], w=[ri(p)])
            with P.stage(dma_out=True):
                P.do("dve", lambda e, p=p: e.tensor_copy(out=stg_b[p][:],
                     in_=stg_i[p][:]), r=[ri(p)], w=[rb(p)])
            with P.stage():
                P.dma(C.wgu_s[k][:, half * HID:(half + 1) * HID], stg_b[p][:],
                      r=[rb(p)])
    for c in range(24):
        p = c % 2
        with P.stage():
            P.dma(stg_i[p][:, 0:1024], C.wdT_h[c * 128:(c + 1) * 128, :], w=[ri(p)])
        with P.stage(dma_out=True):
            P.do("dve", lambda e, p=p: e.tensor_copy(out=stg_b[p][:, 0:1024],
                 in_=stg_i[p][:, 0:1024]), r=[ri(p)], w=[rb(p)])
        with P.stage():
            P.dma(C.wdT_s[c], stg_b[p][:, 0:1024], r=[rb(p)])
    for gg in range(G):
        p = gg % 2
        with P.stage(dma_out=True):
            for j in range(54):
                P.do("dve", lambda e, p=p, gg=gg, j=j: e.tensor_scalar(
                    out=dgb[p][:, j * 128:(j + 1) * 128], in0=C.idb[:],
                    scalar1=C.wtab[:, gg * 54 + j:gg * 54 + j + 1], scalar2=None,
                    op0=ALU.mult), r=[R("idb"), R("wtab")], w=[R("dg%d" % p)])
        with P.stage():
            P.dma(C.diag_s[gg], dgb[p][:], r=[R("dg%d" % p)])



ST2 = 16
def phaseA(C, es):
    nc, P = C.nc, C.P
    R = P.res
    sb = lambda n, sh, dt=F32: es.enter_context(nc.sbuf_tensor(n, sh, dt))
    B = Ctx()
    B.xnst = [sb("a_xnst%d" % q, [128, 2048], BF16) for q in range(2)]
    B.qpre = [sb("a_qpre%d" % q, [128, 8 * 260], BF16) for q in range(2)]
    B.kpre = [sb("a_kpre%d" % q, [128, 8 * 260], BF16) for q in range(2)]
    B.vpre = [sb("a_vpre%d" % q, [128, 8 * 260], BF16) for q in range(2)]
    B.vf = [sb("a_vf%d" % q, [128, 8 * 288], BF16) for q in range(2)]
    B.f3 = [sb("a_f3%d" % q, [128, 2048], BF16) for q in range(2)]
    B.f7 = [sb("a_f7%d" % q, [128, 2048], BF16) for q in range(2)]
    B.f31 = [sb("a_f31%d" % q, [128, 2048], BF16) for q in range(2)]
    B.wpair = [sb("a_wp0", [128, 2048], BF16), sb("a_wp1", [128, 2048], BF16)]
    B.dg = [sb("a_dg0", [128, 54 * 128], BF16), sb("a_dg1", [128, 54 * 128], BF16)]
    B.betat = [sb("a_betat%d" % q, [128, 8]) for q in range(2)]
    B.rst = sb("a_rst", [128, 32])
    B.rr2 = sb("a_rr2", [128, 8])
    B.kntm = sb("a_kntm", [128, 2048], BF16)
    B.kbtm = sb("a_kbtm", [128, 2048], BF16)
    B.vbtm = sb("a_vbtm", [128, 2048], BF16)
    B.wnf = sb("a_wnf", [128, 2048], BF16)
    B.Apre = sb("a_apre", [128, 1024], BF16)
    B.ATs = sb("a_ats0", [128, 1024], BF16)
    B.As = B.Apre
    B.XXT = [sb("a_xp0", [128, 2048], BF16), sb("a_xp1", [128, 2048], BF16)]
    B.PTs = sb("a_pts", [128, 1024], BF16)
    B.attb = sb("a_attb", [128, 1024], BF16)
    B.uhat = sb("a_uhat", [128, 1024], BF16)
    B.S = sb("a_S", [128, 2048])
    B.Sbb = sb("a_Sbb", [128, 2048], BF16)
    B.tmp_tm = sb("a_tmptm", [128, 1024], BF16)
    B.dfm2 = sb("a_dfm2", [128, 2048], BF16)
    B.b1f2 = sb("a_b1f2", [128, 2048], BF16)
    B.babs = sb("a_babs", [128, 1024], BF16)
    B.bsq = sb("a_bsq", [128, 1024], BF16)
    B.st96 = sb("a_st96", [128, 96], BF16)
    B.stT = sb("a_stT", [96, 128], BF16)
    B.l24 = sb("a_l24", [128, 24])
    B.w24 = sb("a_w24", [128, 24], BF16)
    B.wgtT = sb("a_wgtT", [24, 128], BF16)
    B.hgst = sb("a_hgst", [128, 2048], BF16)
    B.facc = sb("a_facc", [128, 1024], BF16)
    B.tmpc = sb("a_tmpc", [128, 1024], BF16)
    B.sot = sb("a_sot", [128, 1024])
    B.rrow = sb("a_rrow", [1, 512], BF16)
    B.owc = [sb("a_owc%d" % j, [128, 1024], BF16) for j in range(3)]
    PA, PB, PC, PS, PT = C.PA, C.PB, C.PC, C.PS, C.PT
    B.half = lambda pt, i: pt[:, i * 512:(i + 1) * 512]

    with P.stage():
        P.do("dve", lambda e: e.memset(B.S[:], 0.0), w=[R("a_S")])
        P.do("pool", lambda e: e.memset(B.Sbb[:], 0.0), w=[R("a_Sbb")])
        for q in range(2):
            P.do("pool", lambda e, q=q: e.memset(B.qpre[q][:], 0.0), w=[R("a_qpre%d" % q)])
            P.do("pool", lambda e, q=q: e.memset(B.kpre[q][:], 0.0), w=[R("a_kpre%d" % q)])
            P.do("pool", lambda e, q=q: e.memset(B.vpre[q][:], 0.0), w=[R("a_vpre%d" % q)])
            P.do("dve", lambda e, q=q: e.memset(B.vf[q][:], 0.0), w=[R("a_vf%d" % q)])

    lim = ST2
    for dd_ in DEBUG:
        if dd_.startswith("lim"):
            lim = int(dd_[3:])
    gens = []
    if lim > 0:
        gens.append(laneA_gen(C, P, R, B, 0))
    prevB = None
    for st in range(lim):
        la = laneA_gen(C, P, R, B, st + 1) if st + 1 < lim else None
        lb = laneB_gen(C, P, R, B, st)
        cur = gens[-1] if gens else None
        if st == 0:
            lan = None
            for dd_ in DEBUG:
                if dd_.startswith("lan"):
                    lan = int(dd_[3:])
            if lan is not None:
                for _ in range(lan):
                    try:
                        next(gens[-1])
                    except StopIteration:
                        break
                return
            for _ in gens[-1]:
                pass
            gens = []
            lbn = None
            for dd_ in DEBUG:
                if dd_.startswith("lbn"):
                    lbn = int(dd_[3:])
            if lbn is not None:
                for _ in range(lbn):
                    try:
                        next(lb)
                    except StopIteration:
                        break
                return
        active = [g for g in (lb, la) if g is not None]
        done = [False] * len(active)
        while not all(done):
            for idx, g in enumerate(active):
                if done[idx]:
                    continue
                n = 2 if idx == 0 else 1
                for _ in range(n):
                    try:
                        next(g)
                    except StopIteration:
                        done[idx] = True
                        break


def laneA_gen(C, P, R, B, st):
    nc = C.nc
    PA, PS, PT = C.PA, C.PS, C.PT
    q = st % 2
    t0 = st * 2
    xnst = B.xnst[q]
    qpre, kpre, vpre, vf = B.qpre[q], B.kpre[q], B.vpre[q], B.vf[q]
    qpre_o, kpre_o, vpre_o, vf_o = (B.qpre[1 - q], B.kpre[1 - q],
                                    B.vpre[1 - q], B.vf[1 - q])
    rx = lambda n: R("a_%s%d" % (n, q))
    with P.stage():
        for i in range(2):
            P.dma(xnst[:].rearrange("p (g t) -> p g t", g=8)
                  [:, :, i * 128:(i + 1) * 128],
                  C.xn_s[t0 + i].rearrange("p (g t) -> p g t", g=8),
                  w=[rx("xnst")])
        P.dma(B.dg[0][:], C.diag_s[0], w=[R("a_dg0")])
        P.dma(B.dg[1][:], C.diag_s[1], w=[R("a_dg1")])
        P.dma(B.wpair[0][:], C.qkv_s[0], w=[R("a_wp0")])
        P.dma(B.wpair[1][:], C.qkv_s[1], w=[R("a_wp1")])
    yield
    dsts = [(qpre, "a_qpre%d" % q), (kpre, "a_kpre%d" % q), (vpre, "a_vpre%d" % q)]
    for pair in range(12):
        pw = pair % 2
        dstbuf, dstres = dsts[pair // 4]
        ph = pair % 2
        with P.stage():
            for m2 in range(2):
                for k in range(8):
                    P.do("pe", lambda e, pw=pw, m2=m2, k=k, ph=ph: e.matmul(
                        PA[:, ph * 512 + m2 * 256:ph * 512 + (m2 + 1) * 256],
                        lhsT=B.wpair[pw][:, k * 256 + m2 * 128:k * 256 + m2 * 128 + 128],
                        rhs=xnst[:, k * 256:(k + 1) * 256],
                        start=(k == 0), stop=(k == 7)),
                        r=[R("a_wp%d" % pw), rx("xnst")],
                        w=[R("PA%d" % ph)])
        yield
        with P.stage():
            if pair + 2 < 12:
                P.dma(B.wpair[pw][:], C.qkv_s[pair + 2], w=[R("a_wp%d" % pw)])
            eng = "dve" if pair % 2 == 0 else "act"
            for m2 in range(2):
                gout = (pair % 4) * 2 + m2
                if eng == "dve":
                    P.do("dve", lambda e, ph=ph, m2=m2, dstbuf=dstbuf, gout=gout:
                         e.tensor_copy(out=dstbuf[:, gout * 260 + 4:gout * 260 + 260],
                                       in_=PA[:, ph * 512 + m2 * 256:ph * 512 + (m2 + 1) * 256]),
                         r=[R("PA%d" % ph)], w=[R(dstres)])
                else:
                    P.do("act", lambda e, ph=ph, m2=m2, dstbuf=dstbuf, gout=gout:
                         e.activation(out=dstbuf[:, gout * 260 + 4:gout * 260 + 260],
                                      in_=PA[:, ph * 512 + m2 * 256:ph * 512 + (m2 + 1) * 256],
                                      func=AF.Identity),
                         r=[R("PA%d" % ph)], w=[R(dstres)])
        yield
    with P.stage():
        for i in range(2):
            for k in range(8):
                P.do("pe", lambda e, i=i, k=k: e.matmul(
                    PA[:, i * 4:(i + 1) * 4],
                    lhsT=xnst[:, k * 256 + i * 128:k * 256 + i * 128 + 128],
                    rhs=C.wbb[:, k * 4:(k + 1) * 4],
                    start=(k == 0), stop=(k == 7)),
                    r=[R("wbb"), rx("xnst")], w=[R("PA0")])
    yield
    with P.stage():
        P.do("act", lambda e: e.activation(out=B.betat[q][:], in_=PA[:, 0:8],
             func=AF.Sigmoid), r=[R("PA0")], w=[rx("betat")])
    yield
    CONV = [("q", qpre, qpre_o, "a_qpre%d" % q, "a_qpre%d" % (1 - q), 0, 4),
            ("k", kpre, kpre_o, "a_kpre%d" % q, "a_kpre%d" % (1 - q), 4, 4),
            ("v", vpre, vpre_o, "a_vpre%d" % q, "a_vpre%d" % (1 - q), 8, 4)]
    pa_ping = 0
    for gg in range(8):
        dgr = R("a_dg%d" % (gg % 2))
        dgb_ = B.dg[gg % 2]
        for name, buf, buf_o, bres, bres_o, tbase, W in CONV:
            ph = pa_ping % 2
            pa_ping += 1
            if name == "v" and st > 0:
                with P.stage():
                    P.do("dve", lambda e, buf=buf, buf_o=buf_o, gg=gg: e.tensor_copy(
                        out=buf[:, gg * 260 + 1:gg * 260 + 4],
                        in_=buf_o[:, gg * 260 + 257:gg * 260 + 260]),
                        r=[R(bres_o)], w=[R(bres)])
                yield
            with P.stage():
                for j in range(W):
                    P.do("pe", lambda e, dgb_=dgb_, tbase=tbase, j=j, buf=buf,
                         gg=gg, ph=ph, W=W: e.matmul(
                             B.half(PA, ph)[:, 0:256],
                             lhsT=dgb_[:, (tbase + j) * 128:(tbase + j + 1) * 128],
                             rhs=buf[:, gg * 260 + 4 - j:gg * 260 + 4 - j + 256],
                             start=(j == 0), stop=(j == W - 1)),
                        r=[dgr, R(bres)], w=[R("PA%d" % ph)])
                if name != "v":
                    P.do("dve", lambda e, buf=buf, buf_o=buf_o, gg=gg: e.tensor_copy(
                        out=buf_o[:, gg * 260 + 1:gg * 260 + 4],
                        in_=buf[:, gg * 260 + 257:gg * 260 + 260]),
                        r=[R(bres)], w=[R(bres_o)])
            yield
            with P.stage():
                if name == "v":
                    P.do("act", lambda e, ph=ph, gg=gg: e.activation(
                        out=vf[:, gg * 288 + 32:gg * 288 + 288],
                        in_=B.half(PA, ph)[:, 0:256], func=AF.Silu),
                        r=[R("PA%d" % ph)], w=[rx("vf")])
                else:
                    P.do("act", lambda e, ph=ph, buf=buf, gg=gg: e.activation(
                        out=buf[:, gg * 260 + 4:gg * 260 + 260],
                        in_=B.half(PA, ph)[:, 0:256], func=AF.Silu),
                        r=[R("PA%d" % ph)], w=[R(bres)])
            yield
        firs = [(13, 3, B.f3[q], "a_f3%d" % q), (16, 7, B.f7[q], "a_f7%d" % q),
                (23, 31, B.f31[q], "a_f31%d" % q)]
        with P.stage():
            if st > 0:
                P.do("dve", lambda e, gg=gg: e.tensor_copy(
                    out=vf[:, gg * 288 + 1:gg * 288 + 32],
                    in_=vf_o[:, gg * 288 + 257:gg * 288 + 288]),
                    r=[R("a_vf%d" % (1 - q))], w=[rx("vf")])
        yield
        fp = []
        for tbase, W, fbuf, fres in firs:
            ph = pa_ping % 2
            pa_ping += 1
            fp.append((ph, fbuf, fres))
            with P.stage():
                for j in range(W):
                    P.do("pe", lambda e, dgb_=dgb_, tbase=tbase, j=j, gg=gg,
                         ph=ph, W=W: e.matmul(
                             B.half(PA, ph)[:, 0:256],
                             lhsT=dgb_[:, (tbase + j) * 128:(tbase + j + 1) * 128],
                             rhs=vf[:, gg * 288 + 32 - j:gg * 288 + 32 - j + 256],
                             start=(j == 0), stop=(j == W - 1)),
                        r=[dgr, rx("vf")], w=[R("PA%d" % ph)])
            yield
            with P.stage():
                P.do("dve" if ph == 0 else "act",
                     (lambda e, ph=ph, fbuf=fbuf, gg=gg: e.tensor_copy(
                         out=fbuf[:, gg * 256:(gg + 1) * 256],
                         in_=B.half(PA, ph)[:, 0:256])) if ph == 0 else
                     (lambda e, ph=ph, fbuf=fbuf, gg=gg: e.activation(
                         out=fbuf[:, gg * 256:(gg + 1) * 256],
                         in_=B.half(PA, ph)[:, 0:256], func=AF.Identity)),
                     r=[R("PA%d" % ph)], w=[R(fres)])
            yield
        if gg + 2 < 8:
            with P.stage():
                P.dma(B.dg[gg % 2][:], C.diag_s[gg + 2], w=[R("a_dg%d" % (gg % 2))])
            yield


def laneB_gen(C, P, R, B, st):
    yield from prefix_gen(C, P, R, B, st)
    for p in range(2):
        yield from suffix_gen(C, P, R, B, st, p)
    with P.stage():
        P.do("act", lambda e: e.activation(out=B.hgst[:], in_=B.hgst[:],
             func=AF.Gelu), r=[R("a_hgst")], w=[R("a_hgst")])
    yield
    for p in range(2):
        yield from phase2_gen(C, P, R, B, st, p)

def prefix_gen(C, P, R, B, st):
    PB, PC, PS, PT = C.PB, C.PC, C.PS, C.PT
    half = B.half
    q = st % 2
    qpre, kpre, vf = B.qpre[q], B.kpre[q], B.vf[q]
    qres, kres, vres = "a_qpre%d" % q, "a_kpre%d" % q, "a_vf%d" % q
    betat = B.betat[q]
    btres = "a_betat%d" % q
    qs = lambda ch, p: qpre[:, ch * 260 + 4 + p * 128:ch * 260 + 4 + p * 128 + 128]
    ks = lambda ch, p: kpre[:, ch * 260 + 4 + p * 128:ch * 260 + 4 + p * 128 + 128]
    vs = lambda ch, p: vf[:, ch * 288 + 32 + p * 128:ch * 288 + 32 + p * 128 + 128]
    qs2 = qpre[:].rearrange("p (c n) -> p c n", c=8)[:, :, 4:4 + 256]
    ks2 = kpre[:].rearrange("p (c n) -> p c n", c=8)[:, :, 4:4 + 256]
    sqb = [B.babs, B.bsq]
    sqr_ = ["a_babs", "a_bsq"]
    for which, stk2, sres, colbase in (("q", qs2, qres, 0), ("k", ks2, kres, 4)):
        with P.stage():
            for p in range(2):
                P.do("dve", lambda e, stk2=stk2, p=p: e.tensor_tensor(
                    out=sqb[p][:].rearrange("p (c n) -> p c n", c=8),
                    in0=stk2[:, :, p * 128:(p + 1) * 128],
                    in1=stk2[:, :, p * 128:(p + 1) * 128], op=ALU.mult),
                    r=[R(sres)], w=[R(sqr_[p])])
        yield
        with P.stage():
            for p in range(2):
                for hh in range(4):
                    for cc in range(2):
                        P.do("pe", lambda e, p=p, hh=hh, cc=cc, colbase=colbase:
                             e.matmul(
                                 PS[:, p * 16 + colbase + hh:
                                    p * 16 + colbase + hh + 1],
                                 lhsT=sqb[p][:, (2 * hh + cc) * 128:
                                             (2 * hh + cc + 1) * 128],
                                 rhs=C.onesb[:], start=(cc == 0), stop=(cc == 1)),
                             r=[R(sqr_[p]), R("onesb")], w=[R("PS")])
        yield
    with P.stage():
        for p in range(2):
            P.do("act", lambda e, p=p: e.activation(out=B.rst[:, p * 16:p * 16 + 8],
                 in_=PS[:, p * 16:p * 16 + 8], func=AF.Ln, bias=1e-6),
                 r=[R("PS")], w=[R("a_rst")])
    yield
    with P.stage():
        for p in range(2):
            P.do("act", lambda e, p=p: e.activation(out=B.rst[:, p * 16:p * 16 + 8],
                 in_=B.rst[:, p * 16:p * 16 + 8], func=AF.Exp, scale=-0.5),
                 r=[R("a_rst")], w=[R("a_rst")])
    yield
    with P.stage():
        for p in range(2):
            P.do("dve", lambda e, p=p: e.tensor_tensor(
                out=B.rst[:, p * 16 + 8:p * 16 + 12],
                in0=B.rst[:, p * 16 + 4:p * 16 + 8],
                in1=betat[:, p * 4:(p + 1) * 4], op=ALU.mult),
                r=[R("a_rst"), R(btres)], w=[R("a_rst")])
    yield
    for p in range(2):
        with P.stage():
            for ch in range(8):
                P.do("pe", lambda e, ch=ch, p=p: e.transpose(
                    PT[:, ch * 128:(ch + 1) * 128], in_=ks(ch, p), identity=C.idb[:]),
                    r=[R(kres), R("idb")], w=[R("PT")])
            for hh in range(4):
                for cc in range(2):
                    P.do("pe", lambda e, hh=hh, cc=cc, p=p: e.matmul(
                        half(PC, p)[:, hh * 128:(hh + 1) * 128],
                        lhsT=ks(2 * hh + cc, p), rhs=ks(2 * hh + cc, p),
                        start=(cc == 0), stop=(cc == 1)),
                        r=[R(kres)], w=[R("PC%d" % p)])
        yield
        with P.stage():
            for hh in range(4):
                P.do("dve", lambda e, hh=hh, p=p: e.tensor_scalar(
                    out=B.kntm[:, p * 1024 + hh * 256:p * 1024 + (hh + 1) * 256],
                    in0=PT[:, hh * 256:(hh + 1) * 256],
                    scalar1=B.rst[:, p * 16 + 4 + hh:p * 16 + 5 + hh], scalar2=None,
                    op0=ALU.mult), r=[R("PT"), R("a_rst")], w=[R("a_kntm")])
                P.do("dve", lambda e, hh=hh, p=p: e.tensor_scalar(
                    out=B.kbtm[:, p * 1024 + hh * 256:p * 1024 + (hh + 1) * 256],
                    in0=PT[:, hh * 256:(hh + 1) * 256],
                    scalar1=B.rst[:, p * 16 + 8 + hh:p * 16 + 9 + hh], scalar2=None,
                    op0=ALU.mult), r=[R("PT"), R("a_rst")], w=[R("a_kbtm")])
        yield
        with P.stage():
            for ch in range(8):
                P.do("pe", lambda e, ch=ch, p=p: e.transpose(
                    PT[:, ch * 128:(ch + 1) * 128], in_=vs(ch, p), identity=C.idb[:]),
                    r=[R(vres), R("idb")], w=[R("PT")])
        yield
        with P.stage():
            for hh in range(4):
                P.do("dve", lambda e, hh=hh, p=p: e.tensor_scalar(
                    out=B.vbtm[:, p * 1024 + hh * 256:p * 1024 + (hh + 1) * 256],
                    in0=PT[:, hh * 256:(hh + 1) * 256],
                    scalar1=betat[:, p * 4 + hh:p * 4 + hh + 1], scalar2=None,
                    op0=ALU.mult), r=[R("PT"), R(btres)], w=[R("a_vbtm")])
        yield
    with P.stage():
        for p in range(2):
            for hh in range(4):
                P.do("dve", lambda e, hh=hh, p=p: e.tensor_scalar(
                    out=B.Apre[:, p * 512 + hh * 128:p * 512 + (hh + 1) * 128],
                    in0=half(PC, p)[:, hh * 128:(hh + 1) * 128],
                    scalar1=B.rst[:, p * 16 + 8 + hh:p * 16 + 9 + hh], scalar2=-1.0,
                    op0=ALU.mult, op1=ALU.mult),
                    r=[R("PC%d" % p), R("a_rst")], w=[R("a_apre")])
    yield
    with P.stage():
        for j in range(8):
            P.do("pe", lambda e, j=j: e.transpose(
                PT[:, j * 128:(j + 1) * 128],
                in_=B.Apre[:, j * 128:(j + 1) * 128], identity=C.idb[:]),
                r=[R("a_apre"), R("idb")], w=[R("PT")])
    yield
    with P.stage():
        for p in range(2):
            for hh in range(4):
                P.do("dve", lambda e, hh=hh, p=p: e.scalar_tensor_tensor(
                    out=B.ATs[:, p * 512 + hh * 128:p * 512 + (hh + 1) * 128],
                    in0=PT[:, p * 512 + hh * 128:p * 512 + (hh + 1) * 128],
                    scalar=B.rst[:, p * 16 + 4 + hh:p * 16 + 5 + hh], in1=C.mUsb[:],
                    op0=ALU.mult, op1=ALU.mult),
                    r=[R("PT"), R("a_rst"), R("mUsb")], w=[R("a_ats0")])
    yield
    with P.stage():
        for j in range(8):
            P.do("pe", lambda e, j=j: e.transpose(
                PT[:, j * 128:(j + 1) * 128],
                in_=B.ATs[:, j * 128:(j + 1) * 128], identity=C.idb[:]),
                r=[R("a_ats0"), R("idb")], w=[R("PT")])
    yield
    with P.stage():
        P.do("dve", lambda e: e.tensor_copy(out=B.As[:], in_=PT[:, 0:1024]),
             r=[R("PT")], w=[R("a_apre")])
        P.do("dve", lambda e: e.tensor_tensor(
            out=B.PTs[:].rearrange("p (r n) -> p r n", r=2),
            in0=B.ATs[:].rearrange("p (r n) -> p r n", r=2),
            in1=C.i4b[:].unsqueeze(1).broadcast_to([128, 2, 512]), op=ALU.add),
            r=[R("a_ats0"), R("i4b")], w=[R("a_pts")])
    yield
    lvls = (2, 4, 8, 16, 32, 64)
    prev = None
    for li, lvl in enumerate(lvls):
        need_xt = lvl < 64
        if li == 0:
            xsl = lambda j: B.As[:, j * 128:(j + 1) * 128]
            xtsl = lambda j: B.ATs[:, j * 128:(j + 1) * 128]
            rr_ = ["a_apre", "a_ats0"]
        else:
            pb_, pr_ = prev
            xsl = lambda j, b=pb_: b[:, (j // 4) * 1024 + (j % 4) * 128:
                                     (j // 4) * 1024 + (j % 4) * 128 + 128]
            xtsl = lambda j, b=pb_: b[:, (j // 4) * 1024 + 512 + (j % 4) * 128:
                                      (j // 4) * 1024 + 512 + (j % 4) * 128 + 128]
            rr_ = [pr_, pr_]
        cur = B.XXT[li % 2]
        curr = "a_xp%d" % (li % 2)
        with P.stage():
            for j in range(8):
                P.do("pe", lambda e, j=j, xsl=xsl, xtsl=xtsl: e.matmul(
                    PC[:, j * 128:(j + 1) * 128],
                    lhsT=xtsl(j), rhs=xsl(j), start=True, stop=True),
                    r=[R(rr_[0]), R(rr_[1])], w=[R("PC%d" % (j // 4))])
            if li > 0:
                for j in range(8):
                    P.do("pe", lambda e, j=j, xsl=xsl: e.matmul(
                        PB[:, j * 128:(j + 1) * 128],
                        lhsT=xsl(j), rhs=B.PTs[:, j * 128:(j + 1) * 128],
                        start=True, stop=True),
                        r=[R(rr_[0]), R("a_pts")], w=[R("PB%d" % (j // 4))])
        yield
        with P.stage():
            evaceng = "dve" if li % 2 == 0 else "act"
            for p in range(2):
                if evaceng == "dve":
                    P.do("dve", lambda e, cur=cur, p=p: e.tensor_copy(
                        out=cur[:, p * 1024:p * 1024 + 512], in_=half(PC, p)),
                        r=[R("PC%d" % p)], w=[R(curr)])
                else:
                    P.do("act", lambda e, cur=cur, p=p: e.activation(
                        out=cur[:, p * 1024:p * 1024 + 512], in_=half(PC, p),
                        func=AF.Identity), r=[R("PC%d" % p)], w=[R(curr)])
            if li > 0:
                P.do("dve", lambda e: e.tensor_tensor(out=B.PTs[:], in0=PB[:, 0:1024],
                     in1=B.PTs[:], op=ALU.add),
                     r=[R("PB0"), R("PB1"), R("a_pts")], w=[R("a_pts")])
        yield
        if need_xt:
            with P.stage():
                for j in range(8):
                    P.do("pe", lambda e, j=j, xsl=xsl, xtsl=xtsl: e.matmul(
                        PC[:, j * 128:(j + 1) * 128],
                        lhsT=xsl(j), rhs=xtsl(j), start=True, stop=True),
                        r=[R(rr_[0]), R(rr_[1])], w=[R("PC%d" % (j // 4))])
            yield
            with P.stage():
                evaceng = "act" if li % 2 == 0 else "dve"
                for p in range(2):
                    if evaceng == "dve":
                        P.do("dve", lambda e, cur=cur, p=p: e.tensor_copy(
                            out=cur[:, p * 1024 + 512:p * 1024 + 1024],
                            in_=half(PC, p)),
                            r=[R("PC%d" % p)], w=[R(curr)])
                    else:
                        P.do("act", lambda e, cur=cur, p=p: e.activation(
                            out=cur[:, p * 1024 + 512:p * 1024 + 1024],
                            in_=half(PC, p),
                            func=AF.Identity), r=[R("PC%d" % p)], w=[R(curr)])
            yield
        prev = (cur, curr)
    pb_, pr_ = prev
    with P.stage():
        for j in range(8):
            P.do("pe", lambda e, j=j, pb_=pb_: e.matmul(
                PB[:, j * 128:(j + 1) * 128],
                lhsT=pb_[:, (j // 4) * 1024 + (j % 4) * 128:
                         (j // 4) * 1024 + (j % 4) * 128 + 128],
                rhs=B.PTs[:, j * 128:(j + 1) * 128], start=True, stop=True),
                r=[R(pr_), R("a_pts")], w=[R("PB%d" % (j // 4))])
    yield
    with P.stage():
        P.do("dve", lambda e: e.tensor_tensor(out=B.PTs[:], in0=PB[:, 0:1024],
             in1=B.PTs[:], op=ALU.add),
             r=[R("PB0"), R("PB1"), R("a_pts")], w=[R("a_pts")])
    yield
    with P.stage():
        for p in range(2):
            PPw = PB if p == 0 else PC
            for hh in range(4):
                for cc in range(2):
                    P.do("pe", lambda e, p=p, hh=hh, cc=cc, PPw=PPw: e.matmul(
                        PPw[:, (hh * 2 + cc) * 128:(hh * 2 + cc + 1) * 128],
                        lhsT=B.kbtm[:, p * 1024 + (hh * 2 + cc) * 128:
                                    p * 1024 + (hh * 2 + cc + 1) * 128],
                        rhs=B.PTs[:, p * 512 + hh * 128:p * 512 + (hh + 1) * 128],
                        start=True, stop=True),
                        r=[R("a_kbtm"), R("a_pts")],
                        w=[R("%s%d" % ("PB" if p == 0 else "PC",
                                       (hh * 2 + cc) // 4))])
    yield
    with P.stage():
        for p in range(2):
            PPw = PB if p == 0 else PC
            prs = [R("PB0"), R("PB1")] if p == 0 else [R("PC0"), R("PC1")]
            P.do("dve", lambda e, p=p, PPw=PPw: e.tensor_scalar(
                out=B.wnf[:, p * 1024:(p + 1) * 1024], in0=PPw[:, 0:1024],
                scalar1=-1.0, scalar2=None, op0=ALU.mult),
                r=prs, w=[R("a_wnf")])
    yield
    with P.stage():
        for p in range(2):
            for hh in range(4):
                for cc in range(2):
                    P.do("pe", lambda e, p=p, hh=hh, cc=cc: e.matmul(
                        half(PC, p)[:, hh * 128:(hh + 1) * 128],
                        lhsT=ks(2 * hh + cc, p), rhs=qs(2 * hh + cc, p),
                        start=(cc == 0), stop=(cc == 1)),
                        r=[R(kres), R(qres)], w=[R("PC%d" % p)])
    yield
    with P.stage():
        for p in range(2):
            for hh in range(4):
                P.do("dve", lambda e, hh=hh, p=p: e.scalar_tensor_tensor(
                    out=B.attb[:, p * 512 + hh * 128:p * 512 + (hh + 1) * 128],
                    in0=half(PC, p)[:, hh * 128:(hh + 1) * 128],
                    scalar=B.rst[:, p * 16 + 4 + hh:p * 16 + 5 + hh], in1=C.mUib[:],
                    op0=ALU.mult, op1=ALU.mult),
                    r=[R("PC%d" % p), R("a_rst"), R("mUib")], w=[R("a_attb")])
    yield


def suffix_gen(C, P, R, B, st, p):
    PB, PC, PS, PT = C.PB, C.PC, C.PS, C.PT
    q = st % 2
    o = p * 128
    so = p * 1024
    qpre, vf = B.qpre[q], B.vf[q]
    qres, vres = "a_qpre%d" % q, "a_vf%d" % q
    dfm = B.dfm2[:, p * 1024:(p + 1) * 1024]
    b1f = B.b1f2[:, p * 1024:(p + 1) * 1024]
    qs = lambda ch: qpre[:, ch * 260 + 4 + o:ch * 260 + 4 + o + 128]
    vs = lambda ch: vf[:, ch * 288 + 32 + o:ch * 288 + 32 + o + 128]
    with P.stage():
        for hh in range(4):
            P.do("pe", lambda e, hh=hh: e.matmul(
                PC[:, hh * 256:(hh + 1) * 256],
                lhsT=B.PTs[:, so // 2 + hh * 128:so // 2 + (hh + 1) * 128],
                rhs=B.vbtm[:, so + hh * 256:so + (hh + 1) * 256],
                start=True, stop=False),
                r=[R("a_pts"), R("a_vbtm")], w=[R("PC0"), R("PC1")])
            for cc in range(2):
                P.do("pe", lambda e, hh=hh, cc=cc: e.matmul(
                    PC[:, hh * 256:(hh + 1) * 256],
                    lhsT=B.wnf[:, so + (hh * 2 + cc) * 128:
                               so + (hh * 2 + cc + 1) * 128],
                    rhs=B.Sbb[:, hh * 512 + cc * 256:hh * 512 + cc * 256 + 256],
                    start=False, stop=(cc == 1)),
                    r=[R("a_wnf"), R("a_Sbb")], w=[R("PC0"), R("PC1")])
    yield
    with P.stage():
        P.do("dve", lambda e: e.tensor_copy(out=B.uhat[:], in_=PC[:, 0:1024]),
             r=[R("PC0"), R("PC1")], w=[R("a_uhat")])
    yield
    with P.stage():
        for hh in range(4):
            for cc in range(2):
                P.do("pe", lambda e, hh=hh, cc=cc: e.matmul(
                    PB[:, hh * 256:(hh + 1) * 256],
                    lhsT=qs(2 * hh + cc),
                    rhs=B.Sbb[:, hh * 512 + cc * 256:hh * 512 + cc * 256 + 256],
                    start=(cc == 0), stop=False),
                    r=[R(qres), R("a_Sbb")], w=[R("PB0"), R("PB1")])
            P.do("pe", lambda e, hh=hh: e.matmul(
                PB[:, hh * 256:(hh + 1) * 256],
                lhsT=B.attb[:, so // 2 + hh * 128:so // 2 + (hh + 1) * 128],
                rhs=B.uhat[:, hh * 256:(hh + 1) * 256], start=False, stop=True),
                r=[R("a_attb"), R("a_uhat")], w=[R("PB0"), R("PB1")])
    yield
    with P.stage():
        for hh in range(4):
            P.do("dve", lambda e, hh=hh, p=p: e.tensor_scalar(
                out=B.tmp_tm[:, hh * 256:(hh + 1) * 256],
                in0=PB[:, hh * 256:(hh + 1) * 256],
                scalar1=B.rst[:, p * 16 + hh:p * 16 + hh + 1], scalar2=None,
                op0=ALU.mult),
                r=[R("PB0"), R("PB1"), R("a_rst")], w=[R("a_tmptm")])
    yield
    with P.stage():
        for hh in range(4):
            for cc in range(2):
                PP = PB if hh < 2 else PC
                pres = ("PB0", "PB1") if hh < 2 else ("PC0", "PC1")
                P.do("pe", lambda e, hh=hh, cc=cc, PP=PP: e.matmul(
                    PP[:, (hh % 2) * 512 + cc * 256:(hh % 2) * 512 + cc * 256 + 256],
                    lhsT=B.kntm[:, so + (hh * 2 + cc) * 128:
                                so + (hh * 2 + cc + 1) * 128],
                    rhs=B.uhat[:, hh * 256:(hh + 1) * 256], start=True, stop=True),
                    r=[R("a_kntm"), R("a_uhat")], w=[R(pres[0]), R(pres[1])])
    yield
    with P.stage():
        P.do("dve", lambda e: e.tensor_tensor(out=B.S[:, 0:1024], in0=PB[:, 0:1024],
             in1=B.S[:, 0:1024], op=ALU.add),
             r=[R("PB0"), R("PB1"), R("a_S")], w=[R("a_S")])
        P.do("dve", lambda e: e.tensor_tensor(out=B.S[:, 1024:2048],
             in0=PC[:, 0:1024], in1=B.S[:, 1024:2048], op=ALU.add),
             r=[R("PC0"), R("PC1"), R("a_S")], w=[R("a_S")])
    yield
    with P.stage():
        P.do("act", lambda e: e.activation(out=B.Sbb[:, 0:1024], in_=B.S[:, 0:1024],
             func=AF.Identity), r=[R("a_S")], w=[R("a_Sbb")])
        P.do("dve", lambda e: e.tensor_copy(out=B.Sbb[:, 1024:2048],
             in_=B.S[:, 1024:2048]), r=[R("a_S")], w=[R("a_Sbb")])
    yield
    with P.stage():
        for ch in range(8):
            P.do("pe", lambda e, ch=ch: e.transpose(
                PT[:, ch * 128:(ch + 1) * 128],
                in_=B.tmp_tm[:, ch * 128:(ch + 1) * 128], identity=C.idb[:]),
                r=[R("a_tmptm"), R("idb")], w=[R("PT")])
    yield
    with P.stage():
        P.do("dve", lambda e, dfm=dfm: e.tensor_copy(out=dfm, in_=PT[:, 0:1024]),
             r=[R("PT")], w=[R("a_dfm2")])
    yield
    with P.stage():
        for ch in range(8):
            P.do("dve", lambda e, ch=ch, b1f=b1f: e.tensor_scalar(
                out=b1f[:, ch * 128:(ch + 1) * 128], in0=vs(ch),
                scalar1=C.wtab[:, ch * 54 + 12:ch * 54 + 13], scalar2=None,
                op0=ALU.mult), r=[R(vres), R("wtab")], w=[R("a_b1f2")])
    yield
    fstk = lambda f: f[:].rearrange("p (c n) -> p c n", c=8)[:, :, o:o + 128]
    cstk = lambda b: b.rearrange("p (c n) -> p c n", c=8)
    vstack = vf[:].rearrange("p (c n) -> p c n", c=8)[:, :, 32 + o:32 + o + 128]
    branches = [
        (lambda ch: b1f[:, ch * 128:(ch + 1) * 128], cstk(b1f), "a_b1f2"),
        (lambda ch: B.f3[q][:, ch * 256 + o:ch * 256 + o + 128],
         fstk(B.f3[q]), "a_f3%d" % q),
        (lambda ch: B.f7[q][:, ch * 256 + o:ch * 256 + o + 128],
         fstk(B.f7[q]), "a_f7%d" % q),
        (lambda ch: B.f31[q][:, ch * 256 + o:ch * 256 + o + 128],
         fstk(B.f31[q]), "a_f31%d" % q),
        (lambda ch: dfm[:, ch * 128:(ch + 1) * 128], cstk(dfm), "a_dfm2"),
        (lambda ch: vs(ch), vstack, vres),
    ]
    for br, (bsl, bstk, bres) in enumerate(branches):
        with P.stage():
            P.do("act", lambda e, bstk=bstk: e.activation(
                out=B.bsq[:].rearrange("p (c n) -> p c n", c=8),
                in_=bstk, func=AF.Square),
                r=[R(bres)], w=[R("a_bsq")])
            P.do("act", lambda e, bstk=bstk: e.activation(
                out=B.babs[:].rearrange("p (c n) -> p c n", c=8),
                in_=bstk, func=AF.Abs),
                r=[R(bres)], w=[R("a_babs")])
        yield
        with P.stage():
            for hh in range(4):
                col = 8 + hh * 6 + br
                for cc in range(2):
                    P.do("pe", lambda e, hh=hh, cc=cc, col=col, bsl=bsl: e.matmul(
                        PS[:, col:col + 1], lhsT=bsl(2 * hh + cc), rhs=C.onesb[:],
                        start=(cc == 0), stop=(cc == 1)),
                        r=[R(bres), R("onesb")], w=[R("PS")])
                for cc in range(2):
                    P.do("pe", lambda e, hh=hh, cc=cc, col=col: e.matmul(
                        PS[:, 32 + col - 8:32 + col - 8 + 1],
                        lhsT=B.babs[:, (2 * hh + cc) * 128:(2 * hh + cc + 1) * 128],
                        rhs=C.onesb[:], start=(cc == 0), stop=(cc == 1)),
                        r=[R("a_babs"), R("onesb")], w=[R("PS")])
                for cc in range(2):
                    P.do("pe", lambda e, hh=hh, cc=cc, col=col: e.matmul(
                        PS[:, 56 + col - 8:56 + col - 8 + 1],
                        lhsT=B.bsq[:, (2 * hh + cc) * 128:(2 * hh + cc + 1) * 128],
                        rhs=C.onesb[:], start=(cc == 0), stop=(cc == 1)),
                        r=[R("a_bsq"), R("onesb")], w=[R("PS")])
        yield
    T = C.tiny
    with P.stage():
        P.do("dve", lambda e: e.tensor_scalar(
            out=st96ap(B.st96, 0), in0=psblock(PS, 8),
            scalar1=1.0 / 256, scalar2=None, op0=ALU.mult),
            r=[R("PS")], w=[R("a_st96")])
        P.do("dve", lambda e: e.tensor_scalar(
            out=st96ap(B.st96, 2), in0=psblock(PS, 32),
            scalar1=1.0 / 256, scalar2=None, op0=ALU.mult),
            r=[R("PS")], w=[R("a_st96")])
        P.do("act", lambda e: e.activation(out=T[:, 0:24], in_=PS[:, 8:32],
             func=AF.Square), r=[R("PS")], w=[R("tinyA")])
    yield
    with P.stage():
        P.do("dve", lambda e: e.scalar_tensor_tensor(out=T[:, 0:24], in0=T[:, 0:24],
             scalar=-1.0 / 256, in1=PS[:, 56:80], op0=ALU.mult, op1=ALU.add),
             r=[R("tinyA"), R("PS")], w=[R("tinyA")])
    yield
    with P.stage():
        P.do("dve", lambda e: e.tensor_scalar(out=T[:, 0:24], in0=T[:, 0:24],
             scalar1=0.0, scalar2=None, op0=ALU.max), r=[R("tinyA")], w=[R("tinyA")])
    yield
    with P.stage():
        P.do("act", lambda e: e.activation(out=T[:, 0:24], in_=T[:, 0:24],
             func=AF.Ln, scale=1.0 / 255, bias=1e-30), r=[R("tinyA")], w=[R("tinyA")])
    yield
    with P.stage():
        P.do("act", lambda e: e.activation(out=st96ap(B.st96, 1), in_=T[:, 0:24],
             func=AF.Exp, scale=0.5), r=[R("tinyA")], w=[R("a_st96")])
        P.do("act", lambda e: e.activation(out=T[:, 0:24], in_=PS[:, 56:80],
             func=AF.Ln, bias=1e-30), r=[R("PS")], w=[R("tinyB")])
    yield
    with P.stage():
        P.do("act", lambda e: e.activation(out=st96ap(B.st96, 3), in_=T[:, 0:24],
             func=AF.Exp, scale=0.5), r=[R("tinyB")], w=[R("a_st96")])
    yield
    with P.stage():
        P.do("pe", lambda e: e.transpose(PT[0:96, 0:128], in_=B.st96[:],
             identity=C.idb[:]), r=[R("a_st96"), R("idb")], w=[R("PT")])
    yield
    with P.stage():
        P.do("dve", lambda e: e.tensor_copy(out=B.stT[:], in_=PT[0:96, 0:128]),
             r=[R("PT")], w=[R("a_stT")])
    yield
    with P.stage():
        for m in range(8):
            pcsl = (lambda mm: C.PC[:, mm * 128:(mm + 1) * 128])
            for k in range(8):
                P.do("pe", lambda e, m=m, k=k, pcsl=pcsl: e.matmul(
                    pcsl(m),
                    lhsT=C.w1b[:, k * 1024 + m * 128:k * 1024 + (m + 1) * 128],
                    rhs=B.xnst[q][:, k * 256 + o:k * 256 + o + 128],
                    start=(k == 0), stop=False),
                    r=[R("w1b"), R("a_xnst%d" % q)],
                    w=[R("PC0") if m < 4 else R("PC1")])
            P.do("pe", lambda e, m=m, pcsl=pcsl: e.matmul(
                pcsl(m),
                lhsT=C.w1b[0:96, 8 * 1024 + m * 128:8 * 1024 + (m + 1) * 128],
                rhs=B.stT[:], start=False, stop=True),
                r=[R("w1b"), R("a_stT")],
                w=[R("PC0") if m < 4 else R("PC1")])
    yield
    with P.stage():
        for m in range(8):
            P.do("dve", lambda e, m=m, p=p: e.tensor_scalar(
                out=B.hgst[:, p * 1024 + m * 128:p * 1024 + (m + 1) * 128],
                in0=C.PC[:, m * 128:(m + 1) * 128],
                scalar1=C.b1fm[:, m:m + 1], scalar2=None, op0=ALU.add),
                r=[R("PC0") if m < 4 else R("PC1"), R("b1fm")], w=[R("a_hgst")])
    yield


def phase2_gen(C, P, R, B, st, p):
    t = st * 2 + p
    o = p * 128
    q = st % 2
    PB, PC, PS, PT = C.PB, C.PC, C.PS, C.PT
    half = B.half
    sl = p
    hgst, stT, l24, w24, wgtT = B.hgst, B.stT, B.l24, B.w24, B.wgtT
    facc, tmpc, sot = B.facc, B.tmpc, B.sot
    rr2, vf = B.rr2, B.vf[q]
    dfm = B.dfm2[:, sl * 1024:(sl + 1) * 1024]
    b1f = B.b1f2[:, sl * 1024:(sl + 1) * 1024]
    vs = lambda ch: vf[:, ch * 288 + 32 + o:ch * 288 + 32 + o + 128]
    owc = B.owc
    with P.stage():
        for kk in range(3):
            P.dma(owc[kk][:], C.ow_s[kk], w=[R("a_owc%d" % kk)])
    yield
    with P.stage():
        for cch in range(8):
            P.do("pe", lambda e, cch=cch, sl=sl: e.matmul(
                PS[:, 96:120],
                lhsT=hgst[:, sl * 1024 + cch * 128:sl * 1024 + (cch + 1) * 128],
                rhs=C.w2b[:, cch * 24:(cch + 1) * 24],
                start=(cch == 0), stop=(cch == 7)),
                r=[R("a_hgst"), R("w2b")], w=[R("PS")])
    yield
    if "hg" in C.dbg and t == 0:
        with P.stage():
            P.do("dve", lambda e: e.tensor_copy(out=B.sot[:],
                 in_=hgst[:, 0:1024]), r=[R("a_hgst")], w=[R("a_sot")])
        yield
        with P.stage():
            P.dma(C.dbg["hg"][:], B.sot[:], r=[R("a_sot")])
        yield
    T = C.tiny
    with P.stage():
        P.do("dve", lambda e: e.tensor_tensor(out=l24[:], in0=PS[:, 96:120],
             in1=C.invtempT[:], op=ALU.mult), r=[R("PS"), R("invtempT")],
             w=[R("a_l24")])
    yield
    with P.stage():
        P.do("dve", lambda e: e.tensor_tensor(out=l24[:], in0=l24[:],
             in1=C.b2i[:], op=ALU.add), r=[R("a_l24"), R("b2i")], w=[R("a_l24")])
    yield
    with P.stage():
        P.do("dve", lambda e: e.reduce_max(out=T[:, 0:4],
             in_=l24[:].rearrange("p (h s) -> p h s", h=4), axis=AX.X),
             r=[R("a_l24")], w=[R("tinyA")])
    yield
    with P.stage():
        for hh in range(4):
            P.do("dve", lambda e, hh=hh: e.tensor_scalar(
                out=l24[:, hh * 6:(hh + 1) * 6], in0=l24[:, hh * 6:(hh + 1) * 6],
                scalar1=T[:, hh:hh + 1], scalar2=None, op0=ALU.subtract),
                r=[R("a_l24"), R("tinyA")], w=[R("a_l24")])
    yield
    with P.stage():
        P.do("act", lambda e: e.activation(out=l24[:], in_=l24[:], func=AF.Exp),
             r=[R("a_l24")], w=[R("a_l24")])
    yield
    with P.stage():
        P.do("dve", lambda e: e.reduce_sum(out=T[:, 0:4],
             in_=l24[:].rearrange("p (h s) -> p h s", h=4), axis=AX.X),
             r=[R("a_l24")], w=[R("tinyA")])
    yield
    with P.stage():
        P.do("dve", lambda e: e.reciprocal(out=T[:, 4:8], in_=T[:, 0:4]),
             r=[R("tinyA")], w=[R("tinyA")])
    yield
    with P.stage():
        for hh in range(4):
            P.do("dve", lambda e, hh=hh: e.tensor_scalar(
                out=w24[:, hh * 6:(hh + 1) * 6], in0=l24[:, hh * 6:(hh + 1) * 6],
                scalar1=T[:, 4 + hh:5 + hh], scalar2=None, op0=ALU.mult),
                r=[R("a_l24"), R("tinyA")], w=[R("a_w24")])
    yield
    if "wgt" in C.dbg and t == 0:
        with P.stage():
            P.do("dve", lambda e: e.tensor_copy(out=l24[:], in_=w24[:]),
                 r=[R("a_w24")], w=[R("a_l24")])
        yield
        with P.stage():
            P.dma(C.dbg["wgt"][:], l24[:], r=[R("a_l24")])
        yield
    with P.stage():
        P.do("pe", lambda e: e.transpose(PT[0:24, 0:128], in_=w24[:],
             identity=C.idb[:]), r=[R("a_w24"), R("idb")], w=[R("PT")])
    yield
    with P.stage():
        P.do("dve", lambda e: e.tensor_copy(out=wgtT[:], in_=PT[0:24, 0:128]),
             r=[R("PT")], w=[R("a_wgtT")])
    yield
    fstk = lambda f: f[:].rearrange("p (c n) -> p c n", c=8)[:, :, o:o + 128]
    cstk = lambda b: b.rearrange("p (c n) -> p c n", c=8)
    vstack = vf[:].rearrange("p (c n) -> p c n", c=8)[:, :, 32 + o:32 + o + 128]
    branches = [
        (cstk(b1f), "a_b1f2"),
        (fstk(B.f3[q]), "a_f3%d" % q),
        (fstk(B.f7[q]), "a_f7%d" % q),
        (fstk(B.f31[q]), "a_f31%d" % q),
        (cstk(dfm), "a_dfm2"),
        (vstack, "a_vf%d" % q),
    ]
    for s6, (bstk, bres) in enumerate(branches):
        PP, ppres = (PB, ("PB0", "PB1")) if s6 % 2 == 0 else (PC, ("PC0", "PC1"))
        with P.stage():
            for ch in range(8):
                hh = ch // 2
                P.do("pe", lambda e, ch=ch, hh=hh, s6=s6, PP=PP: e.matmul(
                    PP[:, ch * 128:(ch + 1) * 128],
                    lhsT=C.sel24b[:, hh * 6 + s6:hh * 6 + s6 + 1]
                    .broadcast_to([24, 128]),
                    rhs=wgtT[:], start=True, stop=True),
                    r=[R("sel24b"), R("a_wgtT")], w=[R(ppres[ch // 4])])
        yield
        with P.stage():
            dst, dres = (facc, "a_facc") if s6 == 0 else (tmpc, "a_tmpc")
            P.do("dve", lambda e, bstk=bstk, PP=PP, dst=dst: e.tensor_tensor(
                out=dst[:].rearrange("p (c n) -> p c n", c=8),
                in0=bstk, in1=PP[:, 0:1024].rearrange("p (c n) -> p c n", c=8),
                op=ALU.mult),
                r=[R(bres), R(ppres[0]), R(ppres[1])], w=[R(dres)])
        yield
        if s6 > 0:
            with P.stage():
                P.do("pool", lambda e: e.tensor_tensor(out=facc[:], in0=facc[:],
                     in1=tmpc[:], op=ALU.add),
                     r=[R("a_facc"), R("a_tmpc")], w=[R("a_facc")])
            yield
    with P.stage():
        P.do("dve", lambda e: e.tensor_tensor(out=tmpc[:], in0=facc[:],
             in1=facc[:], op=ALU.mult), r=[R("a_facc")], w=[R("a_tmpc")])
    yield
    with P.stage():
        for hh in range(4):
            for cc in range(2):
                P.do("pe", lambda e, hh=hh, cc=cc: e.matmul(
                    PS[:, 120 + hh:121 + hh],
                    lhsT=tmpc[:, (2 * hh + cc) * 128:(2 * hh + cc + 1) * 128],
                    rhs=C.onesb[:], start=(cc == 0), stop=(cc == 1)),
                    r=[R("a_tmpc"), R("onesb")], w=[R("PS")])
    yield
    with P.stage():
        P.do("act", lambda e, sl=sl: e.activation(out=rr2[:, sl * 4:(sl + 1) * 4],
             in_=PS[:, 120:124], func=AF.Ln, scale=1.0 / 256, bias=1e-5),
             r=[R("PS")], w=[R("a_rr2")])
    yield
    with P.stage():
        P.do("act", lambda e, sl=sl: e.activation(out=rr2[:, sl * 4:(sl + 1) * 4],
             in_=rr2[:, sl * 4:(sl + 1) * 4], func=AF.Exp, scale=-0.5),
             r=[R("a_rr2")], w=[R("a_rr2")])
    yield
    with P.stage():
        P.do("dve", lambda e, sl=sl: e.tensor_copy(out=w24[:, 0:4],
             in_=rr2[:, sl * 4:(sl + 1) * 4]), r=[R("a_rr2")], w=[R("a_w24")])
    yield
    with P.stage():
        for hh in range(4):
            P.do("pe", lambda e, hh=hh: e.transpose(
                PT[0:1, hh * 128:(hh + 1) * 128], in_=w24[:, hh:hh + 1],
                identity=C.idb[:]), r=[R("a_w24"), R("idb")], w=[R("PT")])
    yield
    with P.stage():
        P.do("dve", lambda e: e.tensor_copy(out=B.rrow[0:1, 0:512],
             in_=PT[0:1, 0:512]), r=[R("PT")], w=[R("a_rrow")])
    yield
    with P.stage():
        for hh in range(4):
            P.do("pe", lambda e, hh=hh: e.matmul(
                half(PB, 0)[:, hh * 128:(hh + 1) * 128], lhsT=C.onesrow[:],
                rhs=B.rrow[0:1, hh * 128:(hh + 1) * 128],
                start=True, stop=True),
                r=[R("a_rrow"), R("onesrow")], w=[R("PB0")])
    yield
    with P.stage():
        for ch in range(8):
            P.do("dve", lambda e, ch=ch: e.tensor_tensor(
                out=tmpc[:, ch * 128:(ch + 1) * 128],
                in0=facc[:, ch * 128:(ch + 1) * 128],
                in1=half(PB, 0)[:, (ch // 2) * 128:(ch // 2 + 1) * 128],
                op=ALU.mult),
                r=[R("a_facc"), R("PB0")], w=[R("a_tmpc")])
    yield
    for grp in ((0, 1, 2), (3, 4, 5), (6, 7)):
        with P.stage():
            for k in grp:
                for nn in range(2):
                    P.do("pe", lambda e, k=k, nn=nn: e.matmul(
                        PC[:, nn * 512:(nn + 1) * 512],
                        lhsT=tmpc[:, k * 128:(k + 1) * 128],
                        rhs=owc[k % 3][:, nn * 512:(nn + 1) * 512],
                        start=(k == 0), stop=(k == 7)),
                        r=[R("a_tmpc"), R("a_owc%d" % (k % 3))], w=[R("PC%d" % nn)])
        yield
        if grp[0] == 0:
            with P.stage():
                for kk in (3, 4, 5):
                    P.dma(owc[kk % 3][:], C.ow_s[kk], w=[R("a_owc%d" % (kk % 3))])
            yield
        elif grp[0] == 3:
            with P.stage():
                for kk in (6, 7):
                    P.dma(owc[kk % 3][:], C.ow_s[kk], w=[R("a_owc%d" % (kk % 3))])
            yield
    with P.stage(dma_out=True):
        P.do("dve", lambda e: e.tensor_copy(out=sot[:], in_=PC[:, 0:1024]),
             r=[R("PC0"), R("PC1")], w=[R("a_sot")])
    yield
    with P.stage():
        P.dma(C.s_s[t], sot[:], r=[R("a_sot")])
        if "s" in C.dbg and t == 0:
            P.dma(C.dbg["s"][:], sot[:], r=[R("a_sot")])
    yield


def st96ap(st96, stat):
    return st96[:].rearrange("p (x f) -> p x f", f=4)[:, :, stat:stat + 1]


def psblock(PS, base):
    return PS[:, base:base + 24]


def phaseB(C, es):
    nc, P = C.nc, C.P
    R = P.res
    sb = lambda n, sh, dt=F32: es.enter_context(nc.sbuf_tensor(n, sh, dt))
    wgu = sb("b_wgu", [128, 8 * 6144], BF16)
    wdb = sb("b_wdb", [128, 24 * 1024], BF16)
    xnb = sb("b_xnb", [128, 4096], BF16)
    hst = sb("b_hst", [128, 24 * 512], BF16)
    sg = sb("b_sg", [128, 1024], BF16)
    xb0 = sb("b_x0", [128, 1024])
    sb0 = sb("b_s0", [128, 1024])
    gb0 = sb("b_gb0", [128, 1024], BF16)
    xb = [xb0, xb0]
    sbf = [sb0, sb0]
    gb = [gb0, gb0]
    tm = sbf
    ob = xb
    PA, PB, PC, PS = C.PA, C.PB, C.PC, C.PS
    with P.stage():
        P.dma(wgu[:].rearrange("p (k n) -> p k n", k=8),
              C.wgu_s.rearrange("k p n -> p k n"), w=[R("b_wgu")])
        P.dma(wdb[:].rearrange("p (k n) -> p k n", k=24),
              C.wdT_s.rearrange("k p n -> p k n"), w=[R("b_wdb")])
    for st in range(NST):
        t0 = st * 4
        with P.stage():
            for i in range(4):
                P.dma(xnb[:].rearrange("p (g t) -> p g t", g=8)
                      [:, :, i * 128:(i + 1) * 128],
                      C.xn_s[t0 + i].rearrange("p (g t) -> p g t", g=8),
                      w=[R("b_xnb")])
        for m in range(24):
            q = m % 2
            PP = PA if q == 0 else PB
            pr = ("PA0", "PA1") if q == 0 else ("PB0", "PB1")
            with P.stage():
                for k in range(8):
                    P.do("pe", lambda e, m=m, k=k, PP=PP: e.matmul(
                        PP[:, 0:512],
                        lhsT=wgu[:, k * 6144 + m * 128:k * 6144 + (m + 1) * 128],
                        rhs=xnb[:, k * 512:(k + 1) * 512],
                        start=(k == 0), stop=(k == 7)),
                        r=[R("b_wgu"), R("b_xnb")], w=[R(pr[0])])
                for k in range(8):
                    P.do("pe", lambda e, m=m, k=k, PP=PP: e.matmul(
                        PP[:, 512:1024],
                        lhsT=wgu[:, k * 6144 + (24 + m) * 128:k * 6144 + (25 + m) * 128],
                        rhs=xnb[:, k * 512:(k + 1) * 512],
                        start=(k == 0), stop=(k == 7)),
                        r=[R("b_wgu"), R("b_xnb")], w=[R(pr[1])])
            with P.stage():
                P.do("act", lambda e, PP=PP, q=q: e.activation(
                    out=sg[:, q * 512:(q + 1) * 512], in_=PP[:, 0:512],
                    func=AF.Silu, scale=1.0 / 127), r=[R(pr[0])], w=[R("b_sg%d" % q)])
            with P.stage():
                P.do("dve", lambda e, m=m, PP=PP, q=q: e.tensor_tensor(
                    out=hst[:, m * 512:(m + 1) * 512], in0=PP[:, 512:1024],
                    in1=sg[:, q * 512:(q + 1) * 512], op=ALU.mult),
                    r=[R(pr[1]), R("b_sg%d" % q)], w=[R("b_hst")])
        for i in range(4):
            t = t0 + i
            p = t % 2
            with P.stage():
                P.dma(xb[p][:], C.x_h[t * 128:(t + 1) * 128, :], w=[R("b_x0")])
                P.dma(sbf[p][:], C.s_s[t], w=[R("b_s0")])
            with P.stage():
                for cch in range(24):
                    for nn in range(2):
                        P.do("pe", lambda e, cch=cch, nn=nn, i=i: e.matmul(
                            PC[:, nn * 512:(nn + 1) * 512],
                            lhsT=hst[:, cch * 512 + i * 128:cch * 512 + i * 128 + 128],
                            rhs=wdb[:, cch * 1024 + nn * 512:cch * 1024 + (nn + 1) * 512],
                            start=(cch == 0), stop=(cch == 23)),
                            r=[R("b_hst"), R("b_wdb")], w=[R("PC%d" % nn)])
            with P.stage():
                P.do("act", lambda e, p=p: e.activation(out=gb[p][:], in_=PC[:, 0:1024],
                     func=AF.Silu, scale=1.0 / (127.0 * 127.0)),
                     r=[R("PC0"), R("PC1")], w=[R("b_gb0")])
            if "g" in C.dbg and t == 0:
                with P.stage():
                    P.do("dve", lambda e, p=p: e.tensor_copy(out=tm[p][:], in_=gb[p][:]),
                         r=[R("b_gb0")], w=[R("b_s0")])
                with P.stage():
                    P.dma(C.dbg["g"][:], tm[p][:], r=[R("b_s0")])
            with P.stage():
                P.do("dve", lambda e, p=p: e.tensor_tensor(out=tm[p][:], in0=sbf[p][:],
                     in1=gb[p][:], op=ALU.mult), r=[R("b_s0"), R("b_gb0")],
                     w=[R("b_s0")])
            with P.stage(dma_out=True):
                P.do("pool", lambda e, p=p: e.tensor_tensor(out=ob[p][:], in0=tm[p][:],
                     in1=xb[p][:], op=ALU.add), r=[R("b_s0"), R("b_x0")],
                     w=[R("b_x0")])
            with P.stage():
                P.dma(C.out_h[t * 128:(t + 1) * 128, :], ob[p][:], r=[R("b_x0")])


def kernel(**inputs):
    hp = host_prep(inputs)
    nc, dbg_names = build_program()
    x_full = np.asarray(inputs["L0_attn_hidden_states"])
    in_maps = []
    for c in range(N_CORES):
        m = dict(hp)
        m["x"] = np.ascontiguousarray(x_full[c])
        in_maps.append(m)
    res = run_bass_kernel_spmd(nc, in_maps, core_ids=list(range(N_CORES)))
    out = np.stack([res.results[c]["out"] for c in range(N_CORES)], axis=0)
    if DEBUG:
        kernel._dbg = [{k: res.results[c].get("dbg_" + k) for k in DEBUG}
                       for c in range(N_CORES)]
    return out


# version 18 (final)
# speedup vs baseline: 1.1529x; 1.0364x over previous
import numpy as np
import ml_dtypes
from contextlib import ExitStack, contextmanager
import concourse.bass as bass
import concourse.mybir as mybir
from concourse.bass_utils import run_bass_kernel_spmd

F32 = mybir.dt.float32
BF16 = mybir.dt.bfloat16
I32 = mybir.dt.int32
AF = mybir.ActivationFunctionType
ALU = mybir.AluOpType
AX = mybir.AxisListType

N_CORES = 8
L, D, H, DK = 4096, 1024, 4, 256
NT, NST, G, HID = 32, 8, 8, 3072

DEBUG = []

ENG2BLK = {"pe": "tensor", "dve": "vector", "act": "scalar",
           "pool": "gpsimd", "sync": "sync"}


class Res:
    __slots__ = ("name", "w_tok", "readers")

    def __init__(self, name):
        self.name = name
        self.w_tok = None
        self.readers = {}


class Prog:
    def __init__(self, nc):
        self.nc = nc
        self.steps = {e: [] for e in ENG2BLK}
        self.cnt = {e: 0 for e in ENG2BLK if e != "sync"}
        self.dma_cnt = 0
        self.hwm = {}
        self.in_stage = False
        self.touch = None
        self.emitted = None
        self.sems = {}
        self._res = {}
        self.dma_safe = {e: 0 for e in ("pe", "dve", "act", "pool")}

    def res(self, name):
        if name not in self._res:
            self._res[name] = Res(name)
        return self._res[name]

    def make_sems(self, es):
        for e in ("pe", "dve", "act", "pool", "dma", "dmadone"):
            self.sems[e] = es.enter_context(self.nc.semaphore("sem_" + e))
        self.dmadone_cnt = 0
        self.dmadone_pub = {}

    def raw(self, eng, fn):
        self.steps[eng].append(fn)

    def _wait(self, waiter, tok, gate=False):
        key, val = tok
        if gate and waiter == "sync" and key in ("pe", "dve", "act", "pool"):
            if self.dma_safe[key] < val:
                self.cnt[key] += 1
                sem0 = self.sems[key]
                self.raw(key, lambda e, s=sem0: e.drain().then_inc(s, 1))
                self.dma_safe[key] = self.cnt[key]
            val = self.dma_safe[key]
            tok = (key, val)
        if key == "dma" and waiter != "sync":
            total = self.dma_cnt
            if total not in self.dmadone_pub:
                self.dmadone_cnt += 1
                self.dmadone_pub[total] = self.dmadone_cnt
                ds, dd = self.sems["dma"], self.sems["dmadone"]
                if self.hwm.get(("sync", "dma"), 0) < total:
                    self.hwm[("sync", "dma")] = total
                    self.raw("sync", lambda e, s=ds, v=total: e.wait_ge(s, v))
                self.raw("sync", lambda e, s=dd: e.sem_inc(s, 1))
            key, val = "dmadone", self.dmadone_pub[total]
        if self.hwm.get((waiter, key), 0) >= val:
            return
        self.hwm[(waiter, key)] = val
        sem = self.sems[key]
        self.raw(waiter, lambda e, s=sem, v=val: e.wait_ge(s, v))

    def _hazards(self, eng, r, w):
        for b in r:
            if b.w_tok is not None:
                self._wait(eng, b.w_tok)
        for b in w:
            if b.w_tok is not None:
                self._wait(eng, b.w_tok)
            for rk, rv in b.readers.items():
                self._wait(eng, (rk, rv))

    def do(self, eng, fn, r=(), w=()):
        assert self.in_stage
        self._hazards(eng, r, w)
        self.raw(eng, fn)
        self.last_op_idx[eng] = len(self.steps[eng]) - 1
        self.emitted.add(eng)
        for b in r:
            self.touch.append((eng, b, "r"))
        for b in w:
            self.touch.append((eng, b, "w"))

    def dma(self, out_ap, in_ap, r=(), w=()):
        assert self.in_stage
        for b in r:
            if b.w_tok is not None:
                self._wait("sync", b.w_tok, gate=True)
        self._hazards("sync", (), w)
        self.dma_cnt += 16
        val = self.dma_cnt
        sem = self.sems["dma"]
        self.raw("sync",
                 lambda e, o=out_ap, i=in_ap, s=sem: e.dma_start(out=o, in_=i)
                 .then_inc(s, 16))
        for b in r:
            b.readers["dma"] = max(b.readers.get("dma", 0), val)
        for b in w:
            b.w_tok = ("dma", val)
            b.readers = {}

    @contextmanager
    def stage(self, use_drain=False, dma_out=False):
        use_drain = use_drain or dma_out
        assert not self.in_stage
        self.in_stage = True
        self.touch, self.emitted = [], set()
        self.last_op_idx = {}
        yield
        touched, emitted = self.touch, self.emitted
        self.in_stage = False
        self.touch = self.emitted = None
        toks = {}
        for e in emitted:
            self.cnt[e] += 1
            toks[e] = (e, self.cnt[e])
            sem = self.sems[e]
            if not use_drain:
                idx = self.last_op_idx[e]
                orig = self.steps[e][idx]
                self.steps[e][idx] = (
                    lambda en, o=orig, s=sem: o(en).then_inc(s, 1))
            else:
                self.raw(e, lambda en, s=sem: en.drain().then_inc(s, 1))
                self.dma_safe[e] = self.cnt[e]
        for eng, b, kind in touched:
            tok = toks[eng]
            if kind == "w":
                b.w_tok = tok
                b.readers = {}
            else:
                k, v = tok
                b.readers[k] = max(b.readers.get(k, 0), v)

    def barrier(self):
        with self.stage(use_drain=True):
            for e in ("pe", "dve", "act", "pool"):
                self.do(e, lambda en: en.nop())
        toks = [(e, self.cnt[e]) for e in ("pe", "dve", "act", "pool")]
        toks.append(("dma", self.dma_cnt))
        for e in ("pe", "dve", "act", "pool", "sync"):
            for t in toks:
                self._wait(e, t)

    def finalize(self, blk):
        for e, bname in ENG2BLK.items():
            fns = self.steps[e]
            getattr(blk, bname)(lambda eng, fns=fns: [f(eng) for f in fns])


def host_prep(inputs):
    f = lambda k: np.asarray(inputs[k])
    qkv = np.concatenate([f("L0_attn_q_w"), f("L0_attn_k_w"),
                          f("L0_attn_v_w")], axis=1)
    bw = f("L0_attn_b_w")

    def taps(name):
        return f(name)[:, 0, ::-1]
    wtab = np.concatenate([taps("L0_attn_qc_w"), taps("L0_attn_kc_w"),
                           taps("L0_attn_vc_w"), taps("L0_attn_fir_w1"),
                           taps("L0_attn_fir_w3"), taps("L0_attn_fir_w7"),
                           taps("L0_attn_fir_w31")], axis=1)
    sel = np.eye(24, dtype=np.float32)[:, :, None]
    c = lambda a: np.ascontiguousarray(a)
    return dict(
        qkv=c(qkv.astype(np.float32)),
        bw=c(bw.astype(np.float32)),
        w1=c(f("L0_attn_mlp_w1").astype(np.float32)),
        w2=c(f("L0_attn_mlp_w2").astype(np.float32)),
        ow=c(f("L0_attn_o_w").astype(np.float32)),
        wtab=c(wtab.astype(np.float32)),
        wgu=c(np.concatenate([f("L0_ffn_w_gate").T, f("L0_ffn_w_up").T], axis=1)),
        wdT=c(f("L0_ffn_w_down").T),
        b1fm=c(f("L0_attn_mlp_b1").reshape(8, 128).T.astype(np.float32)),
        onorm2=c(f("L0_attn_onorm_w").reshape(2, 128).T.astype(np.float32)),
        b2row=c(f("L0_attn_mlp_b2").reshape(1, 24).astype(np.float32)),
        gltrow=c(f("L0_attn_gate_log_temp").reshape(1, 4).astype(np.float32)),
        id128=np.eye(128, dtype=np.float32),
        maskUs=c(np.triu(np.ones((128, 128), np.float32), 1)),
        maskUi=c(np.triu(np.ones((128, 128), np.float32), 0)),
        sel24=c(sel.reshape(24, 24)),
        ones1r=np.ones((1, 128), np.float32),
    )


class Ctx:
    pass


def build_program():
    nc = bass.Bass("TRN2", target_bir_lowering=False)
    C = Ctx()
    C.nc = nc
    d = lambda n, sh, dt, kind: nc.dram_tensor(n, sh, dt, kind=kind).ap()
    C.x_h = d("x", [L, D], F32, "ExternalInput")
    C.qkv_h = d("qkv", [D, 3072], F32, "ExternalInput")
    C.bw_h = d("bw", [D, 4], F32, "ExternalInput")
    C.w1_h = d("w1", [1120, D], F32, "ExternalInput")
    C.w2_h = d("w2", [D, 24], F32, "ExternalInput")
    C.ow_h = d("ow", [D, D], F32, "ExternalInput")
    C.wtab_h = d("wtab", [D, 54], F32, "ExternalInput")
    C.wgu_h = d("wgu", [D, 2 * HID], I32, "ExternalInput")
    C.wdT_h = d("wdT", [HID, D], I32, "ExternalInput")
    C.b1fm_h = d("b1fm", [128, 8], F32, "ExternalInput")
    C.onorm2_h = d("onorm2", [128, 2], F32, "ExternalInput")
    C.b2row_h = d("b2row", [1, 24], F32, "ExternalInput")
    C.gltrow_h = d("gltrow", [1, 4], F32, "ExternalInput")
    C.id128_h = d("id128", [128, 128], F32, "ExternalInput")
    C.maskUs_h = d("maskUs", [128, 128], F32, "ExternalInput")
    C.maskUi_h = d("maskUi", [128, 128], F32, "ExternalInput")
    C.sel24_h = d("sel24", [24, 24], F32, "ExternalInput")
    C.ones1r_h = d("ones1r", [1, 128], F32, "ExternalInput")
    C.out_h = d("out", [L, D], F32, "ExternalOutput")
    C.dbg = {}
    if "xn" in DEBUG:
        C.dbg["xn"] = nc.dram_tensor("dbg_xn", [NT, 128, D], BF16,
                                     kind="ExternalOutput").ap()
    for name, sh, dt_ in [("dd", [128, 4096], BF16), ("dd2", [128, 2048], BF16),
                          ("rst", [128, 16], F32),
                          ("q", [128, 4096], BF16), ("k", [128, 4096], BF16),
                          ("v", [128, 4096], BF16), ("beta", [128, 16], F32),
                          ("delta", [128, 1024], BF16), ("stats", [128, 96], F32),
                          ("wgt", [128, 24], F32), ("s", [128, 1024], F32),
                          ("hg", [128, 1024], F32), ("g", [128, 1024], F32)]:
        if name in DEBUG:
            C.dbg[name] = d("dbg_" + name, sh, dt_, "ExternalOutput")
    C.xn_s = d("xn_s", [NT, 128, D], BF16, "Internal")
    C.s_s = d("s_s", [NT, 128, D], F32, "Internal")
    C.diag_s = d("diag_s", [G, 128, 54 * 128], BF16, "Internal")
    C.ow_s = d("ow_s", [8, 128, D], BF16, "Internal")
    C.qkv_s = d("qkv_s", [12, 128, 2048], BF16, "Internal")
    C.wgu_s = d("wgu_s", [8, 128, 2 * HID], BF16, "Internal")
    C.wdT_s = d("wdT_s", [24, 128, D], BF16, "Internal")

    es = ExitStack()
    P = Prog(nc)
    P.make_sems(es)
    sbg = lambda n, sh, dt=F32: es.enter_context(nc.sbuf_tensor(n, sh, dt))

    C.idb = sbg("idb", [128, 128], BF16)
    C.mUsb = sbg("mUsb", [128, 128], BF16)
    C.mUib = sbg("mUib", [128, 128], BF16)
    C.i4b = sbg("i4b", [128, 512], BF16)
    C.onesb = sbg("onesb", [128, 1], BF16)
    C.onesrow = sbg("onesrow", [1, 128], BF16)
    C.rows = sbg("rows", [1, 96])
    C.tiny = sbg("tiny", [128, 32])
    C.invtempT = sbg("invtempT", [128, 24])
    C.b2i = sbg("b2i", [128, 24])
    C.b1fm = sbg("b1fm_s", [128, 8])
    C.wtab = sbg("wtab_s", [128, G * 54])
    C.w2b = sbg("w2b", [128, 8 * 24], BF16)
    C.wbb = sbg("wbb", [128, 8 * 4], BF16)
    C.sel24b = sbg("sel24b", [24, 24], BF16)
    C.constt = sbg("constt", [128, 8])
    CONSTS = [1e-5, 1e-6, 1e-30, 1.0]
    for i, v in enumerate(CONSTS):
        nc.const_aps.register(F32, v, C.constt[:, i:i + 1])

    C.PA = es.enter_context(nc.psum_tensor("PA", [128, 1024], F32))
    C.PB = es.enter_context(nc.psum_tensor("PB", [128, 1024], F32))
    C.PC = es.enter_context(nc.psum_tensor("PC", [128, 1024], F32))
    C.PS = es.enter_context(nc.psum_tensor("PS", [128, 512], F32))
    C.PT = es.enter_context(nc.psum_tensor("PT", [128, 1024], BF16))
    C.P = P

    with es, nc.Block() as blk:
        with P.stage():
            for i, v in enumerate(CONSTS):
                P.do("dve", lambda e, i=i, v=v: e.memset(C.constt[:, i:i + 1], v),
                     w=[P.res("constt")])
        P.barrier()
        with ExitStack() as es_pa:
            sba = lambda n, sh, dt=F32: es_pa.enter_context(nc.sbuf_tensor(n, sh, dt))
            C.w1b = sba("w1b", [128, 9 * 1024], BF16)
            with ExitStack() as es_p0:
                phase0(C, es_p0)
            P.barrier()
            with ExitStack() as es_a:
                phaseA(C, es_a)
        P.barrier()
        with ExitStack() as es_b:
            if "noB" not in DEBUG:
                phaseB(C, es_b)
        P.raw("sync", lambda e, s=P.sems["dma"], v=P.dma_cnt: e.wait_ge(s, v))
        P.finalize(blk)
    return nc, list(C.dbg.keys())


def phase0(C, es):
    nc, P = C.nc, C.P
    R = P.res
    sb = lambda n, sh, dt=F32: es.enter_context(nc.sbuf_tensor(n, sh, dt))
    stg_f = [sb("p0f0", [128, 3072]), sb("p0f1", [128, 3072])]
    stg_i = [sb("p0i0", [128, 3072], I32), sb("p0i1", [128, 3072], I32)]
    stg_b = [sb("p0b0", [128, 3072], BF16), sb("p0b1", [128, 3072], BF16)]
    dgb = [sb("p0d0", [128, 54 * 128], BF16), sb("p0d1", [128, 54 * 128], BF16)]
    cons = sb("p0cons", [128, 512])
    on2 = sb("p0on2", [128, 2])
    NW = 4
    xt = [sb("p0x%d" % i, [128, D]) for i in range(NW)]
    xsq = [sb("p0sq%d" % i, [128, D]) for i in range(NW)]
    xnt = [sb("p0xnt%d" % i, [128, D], BF16) for i in range(NW)]
    xnf = [sb("p0xnf%d" % i, [128, D], BF16) for i in range(NW)]

    rf = lambda i: R("p0f%d" % i)
    ri = lambda i: R("p0i%d" % i)
    rb = lambda i: R("p0b%d" % i)

    with P.stage():
        P.dma(cons[:, 0:128], C.id128_h[:], w=[R("cons")])
        P.dma(cons[:, 128:256], C.maskUs_h[:], w=[R("cons")])
        P.dma(cons[:, 256:384], C.maskUi_h[:], w=[R("cons")])
        P.dma(C.rows[0:1, 0:24], C.b2row_h[:], w=[R("rows")])
        P.dma(C.rows[0:1, 24:28], C.gltrow_h[:], w=[R("rows")])
        P.dma(C.b1fm[:], C.b1fm_h[:], w=[R("b1fm")])
        P.dma(on2[:], C.onorm2_h[:], w=[R("on2")])
        P.dma(C.wtab[:].rearrange("p (g w) -> p g w", g=8),
              C.wtab_h.rearrange("(g p) w -> p g w", p=128), w=[R("wtab")])
        P.dma(stg_f[0][0:24, 0:24], C.sel24_h[:], w=[rf(0)])
        P.dma(stg_f[1][0:1, 0:128], C.ones1r_h[:], w=[rf(1)])
    with P.stage():
        P.do("dve", lambda e: e.tensor_copy(out=C.idb[:], in_=cons[:, 0:128]),
             r=[R("cons")], w=[R("idb")])
        P.do("dve", lambda e: e.tensor_copy(out=C.mUsb[:], in_=cons[:, 128:256]),
             r=[R("cons")], w=[R("mUsb")])
        P.do("dve", lambda e: e.tensor_copy(out=C.mUib[:], in_=cons[:, 256:384]),
             r=[R("cons")], w=[R("mUib")])
        P.do("dve", lambda e: e.memset(C.onesb[:], 1.0), w=[R("onesb")])
        P.do("dve", lambda e: e.tensor_copy(out=C.onesrow[:],
             in_=stg_f[1][0:1, 0:128]), r=[rf(1)], w=[R("onesrow")])
        P.do("dve", lambda e: e.tensor_copy(out=C.sel24b[:],
             in_=stg_f[0][0:24, 0:24]), r=[rf(0)], w=[R("sel24b")])
        P.do("pool", lambda e: e.tensor_copy(out=C.i4b[:, 0:128], in_=cons[:, 0:128]),
             r=[R("cons")], w=[R("i4b")])
        for j in range(1, 4):
            P.do("pool", lambda e, j=j: e.tensor_copy(
                out=C.i4b[:, j * 128:(j + 1) * 128], in_=cons[:, 0:128]),
                r=[R("cons")], w=[R("i4b")])

    ro = C.rows
    with P.stage():
        P.do("dve", lambda e: e.tensor_scalar(out=ro[0:1, 28:32], in0=ro[0:1, 24:28],
             scalar1=0.0, scalar2=None, op0=ALU.max), r=[R("rows")], w=[R("rows")])
        P.do("act", lambda e: e.activation(out=ro[0:1, 32:36], in_=ro[0:1, 24:28],
             func=AF.Abs), r=[R("rows")], w=[R("rows2")])
    with P.stage():
        P.do("act", lambda e: e.activation(out=ro[0:1, 32:36], in_=ro[0:1, 32:36],
             func=AF.Exp, scale=-1.0), r=[R("rows2")], w=[R("rows2")])
    with P.stage():
        P.do("act", lambda e: e.activation(out=ro[0:1, 32:36], in_=ro[0:1, 32:36],
             func=AF.Ln, bias=1.0), r=[R("rows2")], w=[R("rows2")])
    with P.stage():
        P.do("dve", lambda e: e.tensor_tensor(out=ro[0:1, 36:40], in0=ro[0:1, 28:32],
             in1=ro[0:1, 32:36], op=ALU.add), r=[R("rows"), R("rows2")], w=[R("rows")])
    with P.stage():
        P.do("dve", lambda e: e.tensor_scalar(out=ro[0:1, 36:40], in0=ro[0:1, 36:40],
             scalar1=0.5, scalar2=None, op0=ALU.add), r=[R("rows")], w=[R("rows")])
    with P.stage():
        P.do("dve", lambda e: e.reciprocal(out=ro[0:1, 40:44], in_=ro[0:1, 36:40]),
             r=[R("rows")], w=[R("rows")])
    with P.stage():
        P.do("dve", lambda e: e.tensor_copy(
            out=ro[0:1, 44:68].rearrange("p (h s) -> p h s", h=4),
            in_=ro[0:1, 40:44].unsqueeze(2).broadcast_to([1, 4, 6])),
            r=[R("rows")], w=[R("rows")])
    row_b = es.enter_context(nc.sbuf_tensor("p0rowb", [1, 64], BF16))
    with P.stage():
        P.do("dve", lambda e: e.tensor_copy(out=row_b[0:1, 0:24],
             in_=ro[0:1, 44:68]), r=[R("rows")], w=[R("rowb")])
        P.do("dve", lambda e: e.tensor_copy(out=row_b[0:1, 32:56],
             in_=ro[0:1, 0:24]), r=[R("rows")], w=[R("rowb")])
    with P.stage():
        P.do("pe", lambda e: e.matmul(C.PS[:, 0:24], lhsT=C.onesrow[:],
             rhs=row_b[0:1, 0:24], start=True, stop=True),
             r=[R("rowb"), R("onesrow")], w=[R("PS")])
        P.do("pe", lambda e: e.matmul(C.PS[:, 32:56], lhsT=C.onesrow[:],
             rhs=row_b[0:1, 32:56], start=True, stop=True),
             r=[R("rowb"), R("onesrow")], w=[R("PS")])
    with P.stage():
        P.do("dve", lambda e: e.tensor_copy(out=C.invtempT[:], in_=C.PS[:, 0:24]),
             r=[R("PS")], w=[R("invtempT")])
    with P.stage():
        P.do("dve", lambda e: e.tensor_tensor(out=C.b2i[:], in0=C.PS[:, 32:56],
             in1=C.invtempT[:], op=ALU.mult),
             r=[R("PS"), R("invtempT")], w=[R("b2i")])

    T = C.tiny
    for t in range(NT):
        p = t % 4
        xr, sqr = R("p0x%d" % p), R("p0sq%d" % p)
        ntr, nfr = R("p0xnt%d" % p), R("p0xnf%d" % p)
        ln = R("p0ln%d" % p)
        co = p * 8
        with P.stage():
            P.dma(xt[p][:], C.x_h[t * 128:(t + 1) * 128, :], w=[xr])
        with P.stage():
            P.do("act", lambda e, p=p: e.activation(out=xsq[p][:], in_=xt[p][:],
                 func=AF.Square), r=[xr], w=[sqr])
            P.do("dve", lambda e, p=p, co=co: e.reduce_sum(out=T[:, co:co + 1],
                 in_=xt[p][:], axis=AX.XYZW), r=[xr], w=[ln])
        with P.stage():
            P.do("dve", lambda e, p=p, co=co: e.reduce_sum(out=T[:, co + 1:co + 2],
                 in_=xsq[p][:], axis=AX.XYZW), r=[sqr, ln], w=[ln])
            P.do("dve", lambda e, co=co: e.tensor_scalar(out=T[:, co + 2:co + 3],
                 in0=T[:, co:co + 1], scalar1=1.0 / D, scalar2=None, op0=ALU.mult),
                 r=[ln], w=[ln])
        with P.stage():
            P.do("dve", lambda e, co=co: e.tensor_tensor(out=T[:, co + 3:co + 4],
                 in0=T[:, co + 2:co + 3], in1=T[:, co + 2:co + 3], op=ALU.mult),
                 r=[ln], w=[ln])
        with P.stage():
            P.do("dve", lambda e, co=co: e.scalar_tensor_tensor(
                 out=T[:, co + 4:co + 5], in0=T[:, co + 1:co + 2], scalar=1.0 / D,
                 in1=T[:, co + 3:co + 4], op0=ALU.mult, op1=ALU.subtract),
                 r=[ln], w=[ln])
        with P.stage():
            P.do("act", lambda e, co=co: e.activation(out=T[:, co + 5:co + 6],
                 in_=T[:, co + 4:co + 5], func=AF.Ln, bias=1e-5), r=[ln], w=[ln])
        with P.stage():
            P.do("act", lambda e, co=co: e.activation(out=T[:, co + 6:co + 7],
                 in_=T[:, co + 5:co + 6], func=AF.Exp, scale=-0.5), r=[ln], w=[ln])
        with P.stage():
            P.do("dve", lambda e, p=p, co=co: e.tensor_scalar(out=xsq[p][:],
                 in0=xt[p][:], scalar1=T[:, co + 2:co + 3], scalar2=None,
                 op0=ALU.subtract), r=[xr, ln], w=[sqr])
        with P.stage():
            P.do("dve", lambda e, p=p, co=co: e.tensor_scalar(out=xnt[p][:],
                 in0=xsq[p][:], scalar1=T[:, co + 6:co + 7], scalar2=None,
                 op0=ALU.mult), r=[sqr, ln], w=[ntr])
        with P.stage():
            for cch in range(8):
                P.do("pe", lambda e, p=p, cch=cch: e.transpose(
                    C.PT[:, cch * 128:(cch + 1) * 128],
                    in_=xnt[p][:, cch * 128:(cch + 1) * 128], identity=C.idb[:]),
                    r=[ntr, R("idb")], w=[R("PT")])
        with P.stage(dma_out=True):
            P.do("dve", lambda e, p=p: e.tensor_copy(out=xnf[p][:],
                 in_=C.PT[:, 0:1024]), r=[R("PT")], w=[nfr])
        with P.stage():
            P.dma(C.xn_s[t], xnf[p][:], r=[nfr])
            if "xn" in C.dbg:
                P.dma(C.dbg["xn"][t], xnf[p][:], r=[nfr])
    for k in range(8):
        p = k % 2
        with P.stage():
            P.dma(stg_f[p][:], C.qkv_h[k * 128:(k + 1) * 128, :], w=[rf(p)])
        with P.stage(dma_out=True):
            P.do("dve", lambda e, p=p: e.tensor_copy(out=stg_b[p][:],
                 in_=stg_f[p][:]), r=[rf(p)], w=[rb(p)])
        with P.stage():
            P.dma(C.qkv_s[:, :, k * 256:(k + 1) * 256].transpose([1, 0, 2]),
                  stg_b[p][:].rearrange("p (m n) -> p m n", m=12), r=[rb(p)])
    with P.stage():
        P.dma(stg_f[0][:, 0:32].rearrange("p (k n) -> p k n", k=8),
              C.bw_h.rearrange("(k p) n -> p k n", p=128), w=[rf(0)])
    with P.stage():
        P.do("dve", lambda e: e.tensor_copy(out=C.wbb[:], in_=stg_f[0][:, 0:32]),
             r=[rf(0)], w=[R("wbb")])
    for k in range(9):
        rows_n = 128 if k < 8 else 96
        p = k % 2
        with P.stage():
            P.dma(stg_f[p][0:rows_n, 0:1024], C.w1_h[k * 128:k * 128 + rows_n, :],
                  w=[rf(p)])
        with P.stage():
            P.do("dve", lambda e, k=k, rows_n=rows_n, p=p: e.tensor_copy(
                out=C.w1b[0:rows_n, k * 1024:(k + 1) * 1024],
                in_=stg_f[p][0:rows_n, 0:1024]), r=[rf(p)], w=[R("w1b")])
    with P.stage():
        P.dma(stg_f[0][:, 0:192].rearrange("p (c n) -> p c n", c=8),
              C.w2_h.rearrange("(c p) n -> p c n", p=128), w=[rf(0)])
    with P.stage():
        P.do("dve", lambda e: e.tensor_copy(out=C.w2b[:], in_=stg_f[0][:, 0:192]),
             r=[rf(0)], w=[R("w2b")])
    for k in range(8):
        p = k % 2
        with P.stage():
            P.dma(stg_f[p][:, 0:1024], C.ow_h[k * 128:(k + 1) * 128, :], w=[rf(p)])
        with P.stage(dma_out=True):
            P.do("dve", lambda e, p=p, k=k: e.tensor_scalar(
                out=stg_b[p][:, 0:1024], in0=stg_f[p][:, 0:1024],
                scalar1=on2[:, k % 2:k % 2 + 1], scalar2=None, op0=ALU.mult),
                r=[rf(p), R("on2")], w=[rb(p)])
        with P.stage():
            P.dma(C.ow_s[k], stg_b[p][:, 0:1024], r=[rb(p)])
    for k in range(8):
        for half in range(2):
            p = (2 * k + half) % 2
            with P.stage():
                P.dma(stg_i[p][:], C.wgu_h[k * 128:(k + 1) * 128,
                                           half * HID:(half + 1) * HID], w=[ri(p)])
            with P.stage(dma_out=True):
                P.do("dve", lambda e, p=p: e.tensor_copy(out=stg_b[p][:],
                     in_=stg_i[p][:]), r=[ri(p)], w=[rb(p)])
            with P.stage():
                P.dma(C.wgu_s[k][:, half * HID:(half + 1) * HID], stg_b[p][:],
                      r=[rb(p)])
    for c in range(24):
        p = c % 2
        with P.stage():
            P.dma(stg_i[p][:, 0:1024], C.wdT_h[c * 128:(c + 1) * 128, :], w=[ri(p)])
        with P.stage(dma_out=True):
            P.do("dve", lambda e, p=p: e.tensor_copy(out=stg_b[p][:, 0:1024],
                 in_=stg_i[p][:, 0:1024]), r=[ri(p)], w=[rb(p)])
        with P.stage():
            P.dma(C.wdT_s[c], stg_b[p][:, 0:1024], r=[rb(p)])
    for gg in range(G):
        p = gg % 2
        with P.stage(dma_out=True):
            for j in range(54):
                P.do("dve", lambda e, p=p, gg=gg, j=j: e.tensor_scalar(
                    out=dgb[p][:, j * 128:(j + 1) * 128], in0=C.idb[:],
                    scalar1=C.wtab[:, gg * 54 + j:gg * 54 + j + 1], scalar2=None,
                    op0=ALU.mult), r=[R("idb"), R("wtab")], w=[R("dg%d" % p)])
        with P.stage():
            P.dma(C.diag_s[gg], dgb[p][:], r=[R("dg%d" % p)])



ST2 = 16
def phaseA(C, es):
    nc, P = C.nc, C.P
    R = P.res
    sb = lambda n, sh, dt=F32: es.enter_context(nc.sbuf_tensor(n, sh, dt))
    B = Ctx()
    B.xnst = [sb("a_xnst%d" % q, [128, 2048], BF16) for q in range(2)]
    B.qpre = [sb("a_qpre%d" % q, [128, 8 * 260], BF16) for q in range(2)]
    B.kpre = [sb("a_kpre%d" % q, [128, 8 * 260], BF16) for q in range(2)]
    B.vpre = [sb("a_vpre%d" % q, [128, 8 * 260], BF16) for q in range(2)]
    B.vf = [sb("a_vf%d" % q, [128, 8 * 288], BF16) for q in range(2)]
    B.f3 = [sb("a_f3%d" % q, [128, 2048], BF16) for q in range(2)]
    B.f7 = [sb("a_f7%d" % q, [128, 2048], BF16) for q in range(2)]
    B.f31 = [sb("a_f31%d" % q, [128, 2048], BF16) for q in range(2)]
    B.wpair = [sb("a_wp0", [128, 2048], BF16), sb("a_wp1", [128, 2048], BF16)]
    B.dg = [sb("a_dg0", [128, 54 * 128], BF16), sb("a_dg1", [128, 54 * 128], BF16)]
    B.betat = [sb("a_betat%d" % q, [128, 8]) for q in range(2)]
    B.rst = sb("a_rst", [128, 32])
    B.rr2 = sb("a_rr2", [128, 8])
    B.kntm = sb("a_kntm", [128, 2048], BF16)
    B.kbtm = sb("a_kbtm", [128, 2048], BF16)
    B.vbtm = sb("a_vbtm", [128, 2048], BF16)
    B.wnf = sb("a_wnf", [128, 2048], BF16)
    B.Apre = sb("a_apre", [128, 1024], BF16)
    B.ATs = sb("a_ats0", [128, 1024], BF16)
    B.As = B.Apre
    B.XXT = [sb("a_xp0", [128, 2048], BF16), sb("a_xp1", [128, 2048], BF16)]
    B.PTs = sb("a_pts", [128, 1024], BF16)
    B.attb = sb("a_attb", [128, 1024], BF16)
    B.uhat = sb("a_uhat", [128, 1024], BF16)
    B.S = sb("a_S", [128, 2048])
    B.Sbb = sb("a_Sbb", [128, 2048], BF16)
    B.tmp_tm = sb("a_tmptm", [128, 1024], BF16)
    B.dfm2 = sb("a_dfm2", [128, 2048], BF16)
    B.b1f2 = sb("a_b1f2", [128, 2048], BF16)
    B.babs = sb("a_babs", [128, 1024], BF16)
    B.bsq = sb("a_bsq", [128, 1024], BF16)
    B.babs2 = sb("a_babs2", [128, 1024], BF16)
    B.bsq2 = sb("a_bsq2", [128, 1024], BF16)
    B.st96 = sb("a_st96", [128, 96], BF16)
    B.stT = sb("a_stT", [96, 128], BF16)
    B.l24 = sb("a_l24", [128, 24])
    B.w24 = sb("a_w24", [128, 24], BF16)
    B.wgtT = sb("a_wgtT", [24, 128], BF16)
    B.hgst = sb("a_hgst", [128, 2048], BF16)
    B.facc = sb("a_facc", [128, 1024], BF16)
    B.tmpc = sb("a_tmpc", [128, 1024], BF16)
    B.sot = sb("a_sot", [128, 1024])
    B.rrow = sb("a_rrow", [1, 512], BF16)
    B.owc = [sb("a_owc%d" % j, [128, 1024], BF16) for j in range(3)]
    PA, PB, PC, PS, PT = C.PA, C.PB, C.PC, C.PS, C.PT
    B.half = lambda pt, i: pt[:, i * 512:(i + 1) * 512]

    with P.stage():
        P.do("dve", lambda e: e.memset(B.S[:], 0.0), w=[R("a_S")])
        P.do("pool", lambda e: e.memset(B.Sbb[:], 0.0), w=[R("a_Sbb")])
        for q in range(2):
            P.do("pool", lambda e, q=q: e.memset(B.qpre[q][:], 0.0), w=[R("a_qpre%d" % q)])
            P.do("pool", lambda e, q=q: e.memset(B.kpre[q][:], 0.0), w=[R("a_kpre%d" % q)])
            P.do("pool", lambda e, q=q: e.memset(B.vpre[q][:], 0.0), w=[R("a_vpre%d" % q)])
            P.do("dve", lambda e, q=q: e.memset(B.vf[q][:], 0.0), w=[R("a_vf%d" % q)])

    lim = ST2
    for dd_ in DEBUG:
        if dd_.startswith("lim"):
            lim = int(dd_[3:])
    gens = []
    if lim > 0:
        gens.append(laneA_gen(C, P, R, B, 0))
    prevB = None
    for st in range(lim):
        la = laneA_gen(C, P, R, B, st + 1) if st + 1 < lim else None
        lb = laneB_gen(C, P, R, B, st)
        cur = gens[-1] if gens else None
        if st == 0:
            lan = None
            for dd_ in DEBUG:
                if dd_.startswith("lan"):
                    lan = int(dd_[3:])
            if lan is not None:
                for _ in range(lan):
                    try:
                        next(gens[-1])
                    except StopIteration:
                        break
                return
            for _ in gens[-1]:
                pass
            gens = []
            lbn = None
            for dd_ in DEBUG:
                if dd_.startswith("lbn"):
                    lbn = int(dd_[3:])
            if lbn is not None:
                for _ in range(lbn):
                    try:
                        next(lb)
                    except StopIteration:
                        break
                return
        active = [g for g in (lb, la) if g is not None]
        done = [False] * len(active)
        while not all(done):
            for idx, g in enumerate(active):
                if done[idx]:
                    continue
                n = 2 if idx == 0 else 1
                for _ in range(n):
                    try:
                        next(g)
                    except StopIteration:
                        done[idx] = True
                        break


def laneA_gen(C, P, R, B, st):
    nc = C.nc
    PA, PS, PT = C.PA, C.PS, C.PT
    q = st % 2
    t0 = st * 2
    xnst = B.xnst[q]
    qpre, kpre, vpre, vf = B.qpre[q], B.kpre[q], B.vpre[q], B.vf[q]
    qpre_o, kpre_o, vpre_o, vf_o = (B.qpre[1 - q], B.kpre[1 - q],
                                    B.vpre[1 - q], B.vf[1 - q])
    rx = lambda n: R("a_%s%d" % (n, q))
    with P.stage():
        for i in range(2):
            P.dma(xnst[:].rearrange("p (g t) -> p g t", g=8)
                  [:, :, i * 128:(i + 1) * 128],
                  C.xn_s[t0 + i].rearrange("p (g t) -> p g t", g=8),
                  w=[rx("xnst")])
        P.dma(B.dg[0][:], C.diag_s[0], w=[R("a_dg0")])
        P.dma(B.dg[1][:], C.diag_s[1], w=[R("a_dg1")])
        P.dma(B.wpair[0][:], C.qkv_s[0], w=[R("a_wp0")])
        P.dma(B.wpair[1][:], C.qkv_s[1], w=[R("a_wp1")])
    yield
    dsts = [(qpre, "a_qpre%d" % q), (kpre, "a_kpre%d" % q), (vpre, "a_vpre%d" % q)]
    for pair in range(12):
        pw = pair % 2
        dstbuf, dstres = dsts[pair // 4]
        ph = pair % 2
        with P.stage():
            for m2 in range(2):
                for k in range(8):
                    P.do("pe", lambda e, pw=pw, m2=m2, k=k, ph=ph: e.matmul(
                        PA[:, ph * 512 + m2 * 256:ph * 512 + (m2 + 1) * 256],
                        lhsT=B.wpair[pw][:, k * 256 + m2 * 128:k * 256 + m2 * 128 + 128],
                        rhs=xnst[:, k * 256:(k + 1) * 256],
                        start=(k == 0), stop=(k == 7)),
                        r=[R("a_wp%d" % pw), rx("xnst")],
                        w=[R("PA%d" % ph)])
        yield
        with P.stage():
            if pair + 2 < 12:
                P.dma(B.wpair[pw][:], C.qkv_s[pair + 2], w=[R("a_wp%d" % pw)])
            eng = "dve" if pair % 2 == 0 else "act"
            for m2 in range(2):
                gout = (pair % 4) * 2 + m2
                if eng == "dve":
                    P.do("dve", lambda e, ph=ph, m2=m2, dstbuf=dstbuf, gout=gout:
                         e.tensor_copy(out=dstbuf[:, gout * 260 + 4:gout * 260 + 260],
                                       in_=PA[:, ph * 512 + m2 * 256:ph * 512 + (m2 + 1) * 256]),
                         r=[R("PA%d" % ph)], w=[R(dstres)])
                else:
                    P.do("act", lambda e, ph=ph, m2=m2, dstbuf=dstbuf, gout=gout:
                         e.activation(out=dstbuf[:, gout * 260 + 4:gout * 260 + 260],
                                      in_=PA[:, ph * 512 + m2 * 256:ph * 512 + (m2 + 1) * 256],
                                      func=AF.Identity),
                         r=[R("PA%d" % ph)], w=[R(dstres)])
        yield
    with P.stage():
        for i in range(2):
            for k in range(8):
                P.do("pe", lambda e, i=i, k=k: e.matmul(
                    PA[:, i * 4:(i + 1) * 4],
                    lhsT=xnst[:, k * 256 + i * 128:k * 256 + i * 128 + 128],
                    rhs=C.wbb[:, k * 4:(k + 1) * 4],
                    start=(k == 0), stop=(k == 7)),
                    r=[R("wbb"), rx("xnst")], w=[R("PA0")])
    yield
    with P.stage():
        P.do("act", lambda e: e.activation(out=B.betat[q][:], in_=PA[:, 0:8],
             func=AF.Sigmoid), r=[R("PA0")], w=[rx("betat")])
    yield
    CONV = [("q", qpre, qpre_o, "a_qpre%d" % q, "a_qpre%d" % (1 - q), 0, 4),
            ("k", kpre, kpre_o, "a_kpre%d" % q, "a_kpre%d" % (1 - q), 4, 4),
            ("v", vpre, vpre_o, "a_vpre%d" % q, "a_vpre%d" % (1 - q), 8, 4)]
    pa_ping = 0
    for gg in range(8):
        dgr = R("a_dg%d" % (gg % 2))
        dgb_ = B.dg[gg % 2]
        for name, buf, buf_o, bres, bres_o, tbase, W in CONV:
            ph = pa_ping % 2
            pa_ping += 1
            if name == "v" and st > 0:
                with P.stage():
                    P.do("dve", lambda e, buf=buf, buf_o=buf_o, gg=gg: e.tensor_copy(
                        out=buf[:, gg * 260 + 1:gg * 260 + 4],
                        in_=buf_o[:, gg * 260 + 257:gg * 260 + 260]),
                        r=[R(bres_o)], w=[R(bres)])
                yield
            with P.stage():
                for j in range(W):
                    P.do("pe", lambda e, dgb_=dgb_, tbase=tbase, j=j, buf=buf,
                         gg=gg, ph=ph, W=W: e.matmul(
                             B.half(PA, ph)[:, 0:256],
                             lhsT=dgb_[:, (tbase + j) * 128:(tbase + j + 1) * 128],
                             rhs=buf[:, gg * 260 + 4 - j:gg * 260 + 4 - j + 256],
                             start=(j == 0), stop=(j == W - 1)),
                        r=[dgr, R(bres)], w=[R("PA%d" % ph)])
                if name != "v":
                    P.do("dve", lambda e, buf=buf, buf_o=buf_o, gg=gg: e.tensor_copy(
                        out=buf_o[:, gg * 260 + 1:gg * 260 + 4],
                        in_=buf[:, gg * 260 + 257:gg * 260 + 260]),
                        r=[R(bres)], w=[R(bres_o)])
            yield
            with P.stage():
                if name == "v":
                    P.do("act", lambda e, ph=ph, gg=gg: e.activation(
                        out=vf[:, gg * 288 + 32:gg * 288 + 288],
                        in_=B.half(PA, ph)[:, 0:256], func=AF.Silu),
                        r=[R("PA%d" % ph)], w=[rx("vf")])
                else:
                    P.do("act", lambda e, ph=ph, buf=buf, gg=gg: e.activation(
                        out=buf[:, gg * 260 + 4:gg * 260 + 260],
                        in_=B.half(PA, ph)[:, 0:256], func=AF.Silu),
                        r=[R("PA%d" % ph)], w=[R(bres)])
            yield
        firs = [(13, 3, B.f3[q], "a_f3%d" % q), (16, 7, B.f7[q], "a_f7%d" % q),
                (23, 31, B.f31[q], "a_f31%d" % q)]
        with P.stage():
            if st > 0:
                P.do("dve", lambda e, gg=gg: e.tensor_copy(
                    out=vf[:, gg * 288 + 1:gg * 288 + 32],
                    in_=vf_o[:, gg * 288 + 257:gg * 288 + 288]),
                    r=[R("a_vf%d" % (1 - q))], w=[rx("vf")])
        yield
        fp = []
        for tbase, W, fbuf, fres in firs:
            ph = pa_ping % 2
            pa_ping += 1
            fp.append((ph, fbuf, fres))
            with P.stage():
                for j in range(W):
                    P.do("pe", lambda e, dgb_=dgb_, tbase=tbase, j=j, gg=gg,
                         ph=ph, W=W: e.matmul(
                             B.half(PA, ph)[:, 0:256],
                             lhsT=dgb_[:, (tbase + j) * 128:(tbase + j + 1) * 128],
                             rhs=vf[:, gg * 288 + 32 - j:gg * 288 + 32 - j + 256],
                             start=(j == 0), stop=(j == W - 1)),
                        r=[dgr, rx("vf")], w=[R("PA%d" % ph)])
            yield
            with P.stage():
                P.do("dve" if ph == 0 else "act",
                     (lambda e, ph=ph, fbuf=fbuf, gg=gg: e.tensor_copy(
                         out=fbuf[:, gg * 256:(gg + 1) * 256],
                         in_=B.half(PA, ph)[:, 0:256])) if ph == 0 else
                     (lambda e, ph=ph, fbuf=fbuf, gg=gg: e.activation(
                         out=fbuf[:, gg * 256:(gg + 1) * 256],
                         in_=B.half(PA, ph)[:, 0:256], func=AF.Identity)),
                     r=[R("PA%d" % ph)], w=[R(fres)])
            yield
        if gg + 2 < 8:
            with P.stage():
                P.dma(B.dg[gg % 2][:], C.diag_s[gg + 2], w=[R("a_dg%d" % (gg % 2))])
            yield


def laneB_gen(C, P, R, B, st):
    yield from prefix_gen(C, P, R, B, st)
    for p in range(2):
        yield from suffix_gen(C, P, R, B, st, p)
    with P.stage():
        P.do("act", lambda e: e.activation(out=B.hgst[:], in_=B.hgst[:],
             func=AF.Gelu), r=[R("a_hgst")], w=[R("a_hgst")])
    yield
    for p in range(2):
        yield from phase2_gen(C, P, R, B, st, p)

def prefix_gen(C, P, R, B, st):
    PB, PC, PS, PT = C.PB, C.PC, C.PS, C.PT
    half = B.half
    q = st % 2
    qpre, kpre, vf = B.qpre[q], B.kpre[q], B.vf[q]
    qres, kres, vres = "a_qpre%d" % q, "a_kpre%d" % q, "a_vf%d" % q
    betat = B.betat[q]
    btres = "a_betat%d" % q
    qs = lambda ch, p: qpre[:, ch * 260 + 4 + p * 128:ch * 260 + 4 + p * 128 + 128]
    ks = lambda ch, p: kpre[:, ch * 260 + 4 + p * 128:ch * 260 + 4 + p * 128 + 128]
    vs = lambda ch, p: vf[:, ch * 288 + 32 + p * 128:ch * 288 + 32 + p * 128 + 128]
    qs2 = qpre[:].rearrange("p (c n) -> p c n", c=8)[:, :, 4:4 + 256]
    ks2 = kpre[:].rearrange("p (c n) -> p c n", c=8)[:, :, 4:4 + 256]
    sqb = [B.babs, B.bsq]
    sqr_ = ["a_babs", "a_bsq"]
    for which, stk2, sres, colbase in (("q", qs2, qres, 0), ("k", ks2, kres, 4)):
        with P.stage():
            for p in range(2):
                P.do("dve", lambda e, stk2=stk2, p=p: e.tensor_tensor(
                    out=sqb[p][:].rearrange("p (c n) -> p c n", c=8),
                    in0=stk2[:, :, p * 128:(p + 1) * 128],
                    in1=stk2[:, :, p * 128:(p + 1) * 128], op=ALU.mult),
                    r=[R(sres)], w=[R(sqr_[p])])
        yield
        with P.stage():
            for p in range(2):
                for hh in range(4):
                    for cc in range(2):
                        P.do("pe", lambda e, p=p, hh=hh, cc=cc, colbase=colbase:
                             e.matmul(
                                 PS[:, p * 16 + colbase + hh:
                                    p * 16 + colbase + hh + 1],
                                 lhsT=sqb[p][:, (2 * hh + cc) * 128:
                                             (2 * hh + cc + 1) * 128],
                                 rhs=C.onesb[:], start=(cc == 0), stop=(cc == 1)),
                             r=[R(sqr_[p]), R("onesb")], w=[R("PS")])
        yield
    with P.stage():
        for p in range(2):
            P.do("act", lambda e, p=p: e.activation(out=B.rst[:, p * 16:p * 16 + 8],
                 in_=PS[:, p * 16:p * 16 + 8], func=AF.Ln, bias=1e-6),
                 r=[R("PS")], w=[R("a_rst")])
    yield
    with P.stage():
        for p in range(2):
            P.do("act", lambda e, p=p: e.activation(out=B.rst[:, p * 16:p * 16 + 8],
                 in_=B.rst[:, p * 16:p * 16 + 8], func=AF.Exp, scale=-0.5),
                 r=[R("a_rst")], w=[R("a_rst")])
    yield
    with P.stage():
        for p in range(2):
            P.do("dve", lambda e, p=p: e.tensor_tensor(
                out=B.rst[:, p * 16 + 8:p * 16 + 12],
                in0=B.rst[:, p * 16 + 4:p * 16 + 8],
                in1=betat[:, p * 4:(p + 1) * 4], op=ALU.mult),
                r=[R("a_rst"), R(btres)], w=[R("a_rst")])
    yield
    for p in range(2):
        with P.stage():
            for ch in range(8):
                P.do("pe", lambda e, ch=ch, p=p: e.transpose(
                    PT[:, ch * 128:(ch + 1) * 128], in_=ks(ch, p), identity=C.idb[:]),
                    r=[R(kres), R("idb")], w=[R("PT")])
            for hh in range(4):
                for cc in range(2):
                    P.do("pe", lambda e, hh=hh, cc=cc, p=p: e.matmul(
                        half(PC, p)[:, hh * 128:(hh + 1) * 128],
                        lhsT=ks(2 * hh + cc, p), rhs=ks(2 * hh + cc, p),
                        start=(cc == 0), stop=(cc == 1)),
                        r=[R(kres)], w=[R("PC%d" % p)])
        yield
        with P.stage():
            for hh in range(4):
                P.do("dve", lambda e, hh=hh, p=p: e.tensor_scalar(
                    out=B.kntm[:, p * 1024 + hh * 256:p * 1024 + (hh + 1) * 256],
                    in0=PT[:, hh * 256:(hh + 1) * 256],
                    scalar1=B.rst[:, p * 16 + 4 + hh:p * 16 + 5 + hh], scalar2=None,
                    op0=ALU.mult), r=[R("PT"), R("a_rst")], w=[R("a_kntm")])
                P.do("dve", lambda e, hh=hh, p=p: e.tensor_scalar(
                    out=B.kbtm[:, p * 1024 + hh * 256:p * 1024 + (hh + 1) * 256],
                    in0=PT[:, hh * 256:(hh + 1) * 256],
                    scalar1=B.rst[:, p * 16 + 8 + hh:p * 16 + 9 + hh], scalar2=None,
                    op0=ALU.mult), r=[R("PT"), R("a_rst")], w=[R("a_kbtm")])
        yield
        with P.stage():
            for ch in range(8):
                P.do("pe", lambda e, ch=ch, p=p: e.transpose(
                    PT[:, ch * 128:(ch + 1) * 128], in_=vs(ch, p), identity=C.idb[:]),
                    r=[R(vres), R("idb")], w=[R("PT")])
        yield
        with P.stage():
            for hh in range(4):
                P.do("dve", lambda e, hh=hh, p=p: e.tensor_scalar(
                    out=B.vbtm[:, p * 1024 + hh * 256:p * 1024 + (hh + 1) * 256],
                    in0=PT[:, hh * 256:(hh + 1) * 256],
                    scalar1=betat[:, p * 4 + hh:p * 4 + hh + 1], scalar2=None,
                    op0=ALU.mult), r=[R("PT"), R(btres)], w=[R("a_vbtm")])
        yield
    with P.stage():
        for p in range(2):
            for hh in range(4):
                P.do("dve", lambda e, hh=hh, p=p: e.tensor_scalar(
                    out=B.Apre[:, p * 512 + hh * 128:p * 512 + (hh + 1) * 128],
                    in0=half(PC, p)[:, hh * 128:(hh + 1) * 128],
                    scalar1=B.rst[:, p * 16 + 8 + hh:p * 16 + 9 + hh], scalar2=-1.0,
                    op0=ALU.mult, op1=ALU.mult),
                    r=[R("PC%d" % p), R("a_rst")], w=[R("a_apre")])
    yield
    with P.stage():
        for j in range(8):
            P.do("pe", lambda e, j=j: e.transpose(
                PT[:, j * 128:(j + 1) * 128],
                in_=B.Apre[:, j * 128:(j + 1) * 128], identity=C.idb[:]),
                r=[R("a_apre"), R("idb")], w=[R("PT")])
    yield
    with P.stage():
        for p in range(2):
            for hh in range(4):
                P.do("dve", lambda e, hh=hh, p=p: e.scalar_tensor_tensor(
                    out=B.ATs[:, p * 512 + hh * 128:p * 512 + (hh + 1) * 128],
                    in0=PT[:, p * 512 + hh * 128:p * 512 + (hh + 1) * 128],
                    scalar=B.rst[:, p * 16 + 4 + hh:p * 16 + 5 + hh], in1=C.mUsb[:],
                    op0=ALU.mult, op1=ALU.mult),
                    r=[R("PT"), R("a_rst"), R("mUsb")], w=[R("a_ats0")])
    yield
    with P.stage():
        for j in range(8):
            P.do("pe", lambda e, j=j: e.transpose(
                PT[:, j * 128:(j + 1) * 128],
                in_=B.ATs[:, j * 128:(j + 1) * 128], identity=C.idb[:]),
                r=[R("a_ats0"), R("idb")], w=[R("PT")])
    yield
    with P.stage():
        P.do("dve", lambda e: e.tensor_copy(out=B.As[:], in_=PT[:, 0:1024]),
             r=[R("PT")], w=[R("a_apre")])
        P.do("dve", lambda e: e.tensor_tensor(
            out=B.PTs[:].rearrange("p (r n) -> p r n", r=2),
            in0=B.ATs[:].rearrange("p (r n) -> p r n", r=2),
            in1=C.i4b[:].unsqueeze(1).broadcast_to([128, 2, 512]), op=ALU.add),
            r=[R("a_ats0"), R("i4b")], w=[R("a_pts")])
    yield
    lvls = (2, 4, 8, 16, 32, 64)
    prev = None
    for li, lvl in enumerate(lvls):
        need_xt = lvl < 64
        if li == 0:
            xsl = lambda j: B.As[:, j * 128:(j + 1) * 128]
            xtsl = lambda j: B.ATs[:, j * 128:(j + 1) * 128]
            rr_ = ["a_apre", "a_ats0"]
        else:
            pb_, pr_ = prev
            xsl = lambda j, b=pb_: b[:, (j // 4) * 1024 + (j % 4) * 128:
                                     (j // 4) * 1024 + (j % 4) * 128 + 128]
            xtsl = lambda j, b=pb_: b[:, (j // 4) * 1024 + 512 + (j % 4) * 128:
                                      (j // 4) * 1024 + 512 + (j % 4) * 128 + 128]
            rr_ = [pr_, pr_]
        cur = B.XXT[li % 2]
        curr = "a_xp%d" % (li % 2)
        with P.stage():
            for j in range(8):
                P.do("pe", lambda e, j=j, xsl=xsl, xtsl=xtsl: e.matmul(
                    PC[:, j * 128:(j + 1) * 128],
                    lhsT=xtsl(j), rhs=xsl(j), start=True, stop=True),
                    r=[R(rr_[0]), R(rr_[1])], w=[R("PC%d" % (j // 4))])
            if li > 0:
                for j in range(8):
                    P.do("pe", lambda e, j=j, xsl=xsl: e.matmul(
                        PB[:, j * 128:(j + 1) * 128],
                        lhsT=xsl(j), rhs=B.PTs[:, j * 128:(j + 1) * 128],
                        start=True, stop=True),
                        r=[R(rr_[0]), R("a_pts")], w=[R("PB%d" % (j // 4))])
        yield
        with P.stage():
            evaceng = "dve" if li % 2 == 0 else "act"
            for p in range(2):
                if evaceng == "dve":
                    P.do("dve", lambda e, cur=cur, p=p: e.tensor_copy(
                        out=cur[:, p * 1024:p * 1024 + 512], in_=half(PC, p)),
                        r=[R("PC%d" % p)], w=[R(curr)])
                else:
                    P.do("act", lambda e, cur=cur, p=p: e.activation(
                        out=cur[:, p * 1024:p * 1024 + 512], in_=half(PC, p),
                        func=AF.Identity), r=[R("PC%d" % p)], w=[R(curr)])
            if li > 0:
                P.do("dve", lambda e: e.tensor_tensor(out=B.PTs[:], in0=PB[:, 0:1024],
                     in1=B.PTs[:], op=ALU.add),
                     r=[R("PB0"), R("PB1"), R("a_pts")], w=[R("a_pts")])
        yield
        if need_xt:
            with P.stage():
                for j in range(8):
                    P.do("pe", lambda e, j=j, xsl=xsl, xtsl=xtsl: e.matmul(
                        PC[:, j * 128:(j + 1) * 128],
                        lhsT=xsl(j), rhs=xtsl(j), start=True, stop=True),
                        r=[R(rr_[0]), R(rr_[1])], w=[R("PC%d" % (j // 4))])
            yield
            with P.stage():
                evaceng = "act" if li % 2 == 0 else "dve"
                for p in range(2):
                    if evaceng == "dve":
                        P.do("dve", lambda e, cur=cur, p=p: e.tensor_copy(
                            out=cur[:, p * 1024 + 512:p * 1024 + 1024],
                            in_=half(PC, p)),
                            r=[R("PC%d" % p)], w=[R(curr)])
                    else:
                        P.do("act", lambda e, cur=cur, p=p: e.activation(
                            out=cur[:, p * 1024 + 512:p * 1024 + 1024],
                            in_=half(PC, p),
                            func=AF.Identity), r=[R("PC%d" % p)], w=[R(curr)])
            yield
        prev = (cur, curr)
    pb_, pr_ = prev
    with P.stage():
        for j in range(8):
            P.do("pe", lambda e, j=j, pb_=pb_: e.matmul(
                PB[:, j * 128:(j + 1) * 128],
                lhsT=pb_[:, (j // 4) * 1024 + (j % 4) * 128:
                         (j // 4) * 1024 + (j % 4) * 128 + 128],
                rhs=B.PTs[:, j * 128:(j + 1) * 128], start=True, stop=True),
                r=[R(pr_), R("a_pts")], w=[R("PB%d" % (j // 4))])
    yield
    with P.stage():
        P.do("dve", lambda e: e.tensor_tensor(out=B.PTs[:], in0=PB[:, 0:1024],
             in1=B.PTs[:], op=ALU.add),
             r=[R("PB0"), R("PB1"), R("a_pts")], w=[R("a_pts")])
    yield
    with P.stage():
        for p in range(2):
            PPw = PB if p == 0 else PC
            for hh in range(4):
                for cc in range(2):
                    P.do("pe", lambda e, p=p, hh=hh, cc=cc, PPw=PPw: e.matmul(
                        PPw[:, (hh * 2 + cc) * 128:(hh * 2 + cc + 1) * 128],
                        lhsT=B.kbtm[:, p * 1024 + (hh * 2 + cc) * 128:
                                    p * 1024 + (hh * 2 + cc + 1) * 128],
                        rhs=B.PTs[:, p * 512 + hh * 128:p * 512 + (hh + 1) * 128],
                        start=True, stop=True),
                        r=[R("a_kbtm"), R("a_pts")],
                        w=[R("%s%d" % ("PB" if p == 0 else "PC",
                                       (hh * 2 + cc) // 4))])
    yield
    with P.stage():
        for p in range(2):
            PPw = PB if p == 0 else PC
            prs = [R("PB0"), R("PB1")] if p == 0 else [R("PC0"), R("PC1")]
            P.do("dve", lambda e, p=p, PPw=PPw: e.tensor_scalar(
                out=B.wnf[:, p * 1024:(p + 1) * 1024], in0=PPw[:, 0:1024],
                scalar1=-1.0, scalar2=None, op0=ALU.mult),
                r=prs, w=[R("a_wnf")])
    yield
    with P.stage():
        for p in range(2):
            for hh in range(4):
                for cc in range(2):
                    P.do("pe", lambda e, p=p, hh=hh, cc=cc: e.matmul(
                        half(PC, p)[:, hh * 128:(hh + 1) * 128],
                        lhsT=ks(2 * hh + cc, p), rhs=qs(2 * hh + cc, p),
                        start=(cc == 0), stop=(cc == 1)),
                        r=[R(kres), R(qres)], w=[R("PC%d" % p)])
    yield
    with P.stage():
        for p in range(2):
            for hh in range(4):
                P.do("dve", lambda e, hh=hh, p=p: e.scalar_tensor_tensor(
                    out=B.attb[:, p * 512 + hh * 128:p * 512 + (hh + 1) * 128],
                    in0=half(PC, p)[:, hh * 128:(hh + 1) * 128],
                    scalar=B.rst[:, p * 16 + 4 + hh:p * 16 + 5 + hh], in1=C.mUib[:],
                    op0=ALU.mult, op1=ALU.mult),
                    r=[R("PC%d" % p), R("a_rst"), R("mUib")], w=[R("a_attb")])
    yield


def suffix_gen(C, P, R, B, st, p):
    PB, PC, PS, PT = C.PB, C.PC, C.PS, C.PT
    q = st % 2
    o = p * 128
    so = p * 1024
    qpre, vf = B.qpre[q], B.vf[q]
    qres, vres = "a_qpre%d" % q, "a_vf%d" % q
    dfm = B.dfm2[:, p * 1024:(p + 1) * 1024]
    b1f = B.b1f2[:, p * 1024:(p + 1) * 1024]
    qs = lambda ch: qpre[:, ch * 260 + 4 + o:ch * 260 + 4 + o + 128]
    vs = lambda ch: vf[:, ch * 288 + 32 + o:ch * 288 + 32 + o + 128]
    with P.stage():
        for hh in range(4):
            P.do("pe", lambda e, hh=hh: e.matmul(
                PC[:, hh * 256:(hh + 1) * 256],
                lhsT=B.PTs[:, so // 2 + hh * 128:so // 2 + (hh + 1) * 128],
                rhs=B.vbtm[:, so + hh * 256:so + (hh + 1) * 256],
                start=True, stop=False),
                r=[R("a_pts"), R("a_vbtm")], w=[R("PC0"), R("PC1")])
            for cc in range(2):
                P.do("pe", lambda e, hh=hh, cc=cc: e.matmul(
                    PC[:, hh * 256:(hh + 1) * 256],
                    lhsT=B.wnf[:, so + (hh * 2 + cc) * 128:
                               so + (hh * 2 + cc + 1) * 128],
                    rhs=B.Sbb[:, hh * 512 + cc * 256:hh * 512 + cc * 256 + 256],
                    start=False, stop=(cc == 1)),
                    r=[R("a_wnf"), R("a_Sbb")], w=[R("PC0"), R("PC1")])
    yield
    with P.stage():
        P.do("dve", lambda e: e.tensor_copy(out=B.uhat[:], in_=PC[:, 0:1024]),
             r=[R("PC0"), R("PC1")], w=[R("a_uhat")])
    yield
    with P.stage():
        for hh in range(4):
            for cc in range(2):
                P.do("pe", lambda e, hh=hh, cc=cc: e.matmul(
                    PB[:, hh * 256:(hh + 1) * 256],
                    lhsT=qs(2 * hh + cc),
                    rhs=B.Sbb[:, hh * 512 + cc * 256:hh * 512 + cc * 256 + 256],
                    start=(cc == 0), stop=False),
                    r=[R(qres), R("a_Sbb")], w=[R("PB0"), R("PB1")])
            P.do("pe", lambda e, hh=hh: e.matmul(
                PB[:, hh * 256:(hh + 1) * 256],
                lhsT=B.attb[:, so // 2 + hh * 128:so // 2 + (hh + 1) * 128],
                rhs=B.uhat[:, hh * 256:(hh + 1) * 256], start=False, stop=True),
                r=[R("a_attb"), R("a_uhat")], w=[R("PB0"), R("PB1")])
    yield
    with P.stage():
        for hh in range(4):
            P.do("dve", lambda e, hh=hh, p=p: e.tensor_scalar(
                out=B.tmp_tm[:, hh * 256:(hh + 1) * 256],
                in0=PB[:, hh * 256:(hh + 1) * 256],
                scalar1=B.rst[:, p * 16 + hh:p * 16 + hh + 1], scalar2=None,
                op0=ALU.mult),
                r=[R("PB0"), R("PB1"), R("a_rst")], w=[R("a_tmptm")])
    yield
    with P.stage():
        for hh in range(4):
            for cc in range(2):
                PP = PB if hh < 2 else PC
                pres = ("PB0", "PB1") if hh < 2 else ("PC0", "PC1")
                P.do("pe", lambda e, hh=hh, cc=cc, PP=PP: e.matmul(
                    PP[:, (hh % 2) * 512 + cc * 256:(hh % 2) * 512 + cc * 256 + 256],
                    lhsT=B.kntm[:, so + (hh * 2 + cc) * 128:
                                so + (hh * 2 + cc + 1) * 128],
                    rhs=B.uhat[:, hh * 256:(hh + 1) * 256], start=True, stop=True),
                    r=[R("a_kntm"), R("a_uhat")], w=[R(pres[0]), R(pres[1])])
    yield
    with P.stage():
        P.do("dve", lambda e: e.tensor_tensor(out=B.S[:, 0:1024], in0=PB[:, 0:1024],
             in1=B.S[:, 0:1024], op=ALU.add),
             r=[R("PB0"), R("PB1"), R("a_S")], w=[R("a_S")])
        P.do("dve", lambda e: e.tensor_tensor(out=B.S[:, 1024:2048],
             in0=PC[:, 0:1024], in1=B.S[:, 1024:2048], op=ALU.add),
             r=[R("PC0"), R("PC1"), R("a_S")], w=[R("a_S")])
    yield
    with P.stage():
        P.do("act", lambda e: e.activation(out=B.Sbb[:, 0:1024], in_=B.S[:, 0:1024],
             func=AF.Identity), r=[R("a_S")], w=[R("a_Sbb")])
        P.do("dve", lambda e: e.tensor_copy(out=B.Sbb[:, 1024:2048],
             in_=B.S[:, 1024:2048]), r=[R("a_S")], w=[R("a_Sbb")])
    yield
    with P.stage():
        for ch in range(8):
            P.do("pe", lambda e, ch=ch: e.transpose(
                PT[:, ch * 128:(ch + 1) * 128],
                in_=B.tmp_tm[:, ch * 128:(ch + 1) * 128], identity=C.idb[:]),
                r=[R("a_tmptm"), R("idb")], w=[R("PT")])
    yield
    with P.stage():
        P.do("dve", lambda e, dfm=dfm: e.tensor_copy(out=dfm, in_=PT[:, 0:1024]),
             r=[R("PT")], w=[R("a_dfm2")])
    yield
    with P.stage():
        for ch in range(8):
            P.do("dve", lambda e, ch=ch, b1f=b1f: e.tensor_scalar(
                out=b1f[:, ch * 128:(ch + 1) * 128], in0=vs(ch),
                scalar1=C.wtab[:, ch * 54 + 12:ch * 54 + 13], scalar2=None,
                op0=ALU.mult), r=[R(vres), R("wtab")], w=[R("a_b1f2")])
    yield
    fstk = lambda f: f[:].rearrange("p (c n) -> p c n", c=8)[:, :, o:o + 128]
    cstk = lambda b: b.rearrange("p (c n) -> p c n", c=8)
    vstack = vf[:].rearrange("p (c n) -> p c n", c=8)[:, :, 32 + o:32 + o + 128]
    branches = [
        (lambda ch: b1f[:, ch * 128:(ch + 1) * 128], cstk(b1f), "a_b1f2"),
        (lambda ch: B.f3[q][:, ch * 256 + o:ch * 256 + o + 128],
         fstk(B.f3[q]), "a_f3%d" % q),
        (lambda ch: B.f7[q][:, ch * 256 + o:ch * 256 + o + 128],
         fstk(B.f7[q]), "a_f7%d" % q),
        (lambda ch: B.f31[q][:, ch * 256 + o:ch * 256 + o + 128],
         fstk(B.f31[q]), "a_f31%d" % q),
        (lambda ch: dfm[:, ch * 128:(ch + 1) * 128], cstk(dfm), "a_dfm2"),
        (lambda ch: vs(ch), vstack, vres),
    ]
    for br, (bsl, bstk, bres) in enumerate(branches):
        bab, bsq, babr, bsqr = ((B.babs, B.bsq, "a_babs", "a_bsq") if br % 2 == 0
                                else (B.babs2, B.bsq2, "a_babs2", "a_bsq2"))
        with P.stage():
            P.do("act", lambda e, bstk=bstk, bsq=bsq: e.activation(
                out=bsq[:].rearrange("p (c n) -> p c n", c=8),
                in_=bstk, func=AF.Square),
                r=[R(bres)], w=[R(bsqr)])
            P.do("act", lambda e, bstk=bstk, bab=bab: e.activation(
                out=bab[:].rearrange("p (c n) -> p c n", c=8),
                in_=bstk, func=AF.Abs),
                r=[R(bres)], w=[R(babr)])
        yield
        with P.stage():
            for hh in range(4):
                col = 8 + hh * 6 + br
                for cc in range(2):
                    P.do("pe", lambda e, hh=hh, cc=cc, col=col, bsl=bsl: e.matmul(
                        PS[:, col:col + 1], lhsT=bsl(2 * hh + cc), rhs=C.onesb[:],
                        start=(cc == 0), stop=(cc == 1)),
                        r=[R(bres), R("onesb")], w=[R("PS")])
                for cc in range(2):
                    P.do("pe", lambda e, hh=hh, cc=cc, col=col, bab=bab: e.matmul(
                        PS[:, 32 + col - 8:32 + col - 8 + 1],
                        lhsT=bab[:, (2 * hh + cc) * 128:(2 * hh + cc + 1) * 128],
                        rhs=C.onesb[:], start=(cc == 0), stop=(cc == 1)),
                        r=[R(babr), R("onesb")], w=[R("PS")])
                for cc in range(2):
                    P.do("pe", lambda e, hh=hh, cc=cc, col=col, bsq=bsq: e.matmul(
                        PS[:, 56 + col - 8:56 + col - 8 + 1],
                        lhsT=bsq[:, (2 * hh + cc) * 128:(2 * hh + cc + 1) * 128],
                        rhs=C.onesb[:], start=(cc == 0), stop=(cc == 1)),
                        r=[R(bsqr), R("onesb")], w=[R("PS")])
        yield
    T = C.tiny
    with P.stage():
        P.do("dve", lambda e: e.tensor_scalar(
            out=st96ap(B.st96, 0), in0=psblock(PS, 8),
            scalar1=1.0 / 256, scalar2=None, op0=ALU.mult),
            r=[R("PS")], w=[R("a_st96")])
        P.do("dve", lambda e: e.tensor_scalar(
            out=st96ap(B.st96, 2), in0=psblock(PS, 32),
            scalar1=1.0 / 256, scalar2=None, op0=ALU.mult),
            r=[R("PS")], w=[R("a_st96")])
        P.do("act", lambda e: e.activation(out=T[:, 0:24], in_=PS[:, 8:32],
             func=AF.Square), r=[R("PS")], w=[R("tinyA")])
    yield
    with P.stage():
        P.do("dve", lambda e: e.scalar_tensor_tensor(out=T[:, 0:24], in0=T[:, 0:24],
             scalar=-1.0 / 256, in1=PS[:, 56:80], op0=ALU.mult, op1=ALU.add),
             r=[R("tinyA"), R("PS")], w=[R("tinyA")])
    yield
    with P.stage():
        P.do("dve", lambda e: e.tensor_scalar(out=T[:, 0:24], in0=T[:, 0:24],
             scalar1=0.0, scalar2=None, op0=ALU.max), r=[R("tinyA")], w=[R("tinyA")])
    yield
    with P.stage():
        P.do("act", lambda e: e.activation(out=T[:, 0:24], in_=T[:, 0:24],
             func=AF.Ln, scale=1.0 / 255, bias=1e-30), r=[R("tinyA")], w=[R("tinyA")])
    yield
    with P.stage():
        P.do("act", lambda e: e.activation(out=st96ap(B.st96, 1), in_=T[:, 0:24],
             func=AF.Exp, scale=0.5), r=[R("tinyA")], w=[R("a_st96")])
        P.do("act", lambda e: e.activation(out=T[:, 0:24], in_=PS[:, 56:80],
             func=AF.Ln, bias=1e-30), r=[R("PS")], w=[R("tinyB")])
    yield
    with P.stage():
        P.do("act", lambda e: e.activation(out=st96ap(B.st96, 3), in_=T[:, 0:24],
             func=AF.Exp, scale=0.5), r=[R("tinyB")], w=[R("a_st96")])
    yield
    with P.stage():
        P.do("pe", lambda e: e.transpose(PT[0:96, 0:128], in_=B.st96[:],
             identity=C.idb[:]), r=[R("a_st96"), R("idb")], w=[R("PT")])
    yield
    with P.stage():
        P.do("dve", lambda e: e.tensor_copy(out=B.stT[:], in_=PT[0:96, 0:128]),
             r=[R("PT")], w=[R("a_stT")])
    yield
    with P.stage():
        for m in range(8):
            pcsl = (lambda mm: C.PC[:, mm * 128:(mm + 1) * 128])
            for k in range(8):
                P.do("pe", lambda e, m=m, k=k, pcsl=pcsl: e.matmul(
                    pcsl(m),
                    lhsT=C.w1b[:, k * 1024 + m * 128:k * 1024 + (m + 1) * 128],
                    rhs=B.xnst[q][:, k * 256 + o:k * 256 + o + 128],
                    start=(k == 0), stop=False),
                    r=[R("w1b"), R("a_xnst%d" % q)],
                    w=[R("PC0") if m < 4 else R("PC1")])
            P.do("pe", lambda e, m=m, pcsl=pcsl: e.matmul(
                pcsl(m),
                lhsT=C.w1b[0:96, 8 * 1024 + m * 128:8 * 1024 + (m + 1) * 128],
                rhs=B.stT[:], start=False, stop=True),
                r=[R("w1b"), R("a_stT")],
                w=[R("PC0") if m < 4 else R("PC1")])
    yield
    with P.stage():
        for m in range(8):
            P.do("dve", lambda e, m=m, p=p: e.tensor_scalar(
                out=B.hgst[:, p * 1024 + m * 128:p * 1024 + (m + 1) * 128],
                in0=C.PC[:, m * 128:(m + 1) * 128],
                scalar1=C.b1fm[:, m:m + 1], scalar2=None, op0=ALU.add),
                r=[R("PC0") if m < 4 else R("PC1"), R("b1fm")], w=[R("a_hgst")])
    yield


def phase2_gen(C, P, R, B, st, p):
    t = st * 2 + p
    o = p * 128
    q = st % 2
    PB, PC, PS, PT = C.PB, C.PC, C.PS, C.PT
    half = B.half
    sl = p
    hgst, stT, l24, w24, wgtT = B.hgst, B.stT, B.l24, B.w24, B.wgtT
    facc, tmpc, sot = B.facc, B.tmpc, B.sot
    rr2, vf = B.rr2, B.vf[q]
    dfm = B.dfm2[:, sl * 1024:(sl + 1) * 1024]
    b1f = B.b1f2[:, sl * 1024:(sl + 1) * 1024]
    vs = lambda ch: vf[:, ch * 288 + 32 + o:ch * 288 + 32 + o + 128]
    owc = B.owc
    with P.stage():
        for kk in range(3):
            P.dma(owc[kk][:], C.ow_s[kk], w=[R("a_owc%d" % kk)])
    yield
    with P.stage():
        for cch in range(8):
            P.do("pe", lambda e, cch=cch, sl=sl: e.matmul(
                PS[:, 96:120],
                lhsT=hgst[:, sl * 1024 + cch * 128:sl * 1024 + (cch + 1) * 128],
                rhs=C.w2b[:, cch * 24:(cch + 1) * 24],
                start=(cch == 0), stop=(cch == 7)),
                r=[R("a_hgst"), R("w2b")], w=[R("PS")])
    yield
    if "hg" in C.dbg and t == 0:
        with P.stage():
            P.do("dve", lambda e: e.tensor_copy(out=B.sot[:],
                 in_=hgst[:, 0:1024]), r=[R("a_hgst")], w=[R("a_sot")])
        yield
        with P.stage():
            P.dma(C.dbg["hg"][:], B.sot[:], r=[R("a_sot")])
        yield
    T = C.tiny
    with P.stage():
        P.do("dve", lambda e: e.tensor_tensor(out=l24[:], in0=PS[:, 96:120],
             in1=C.invtempT[:], op=ALU.mult), r=[R("PS"), R("invtempT")],
             w=[R("a_l24")])
    yield
    with P.stage():
        P.do("dve", lambda e: e.tensor_tensor(out=l24[:], in0=l24[:],
             in1=C.b2i[:], op=ALU.add), r=[R("a_l24"), R("b2i")], w=[R("a_l24")])
    yield
    with P.stage():
        P.do("dve", lambda e: e.reduce_max(out=T[:, 0:4],
             in_=l24[:].rearrange("p (h s) -> p h s", h=4), axis=AX.X),
             r=[R("a_l24")], w=[R("tinyA")])
    yield
    with P.stage():
        for hh in range(4):
            P.do("dve", lambda e, hh=hh: e.tensor_scalar(
                out=l24[:, hh * 6:(hh + 1) * 6], in0=l24[:, hh * 6:(hh + 1) * 6],
                scalar1=T[:, hh:hh + 1], scalar2=None, op0=ALU.subtract),
                r=[R("a_l24"), R("tinyA")], w=[R("a_l24")])
    yield
    with P.stage():
        P.do("act", lambda e: e.activation(out=l24[:], in_=l24[:], func=AF.Exp),
             r=[R("a_l24")], w=[R("a_l24")])
    yield
    with P.stage():
        P.do("dve", lambda e: e.reduce_sum(out=T[:, 0:4],
             in_=l24[:].rearrange("p (h s) -> p h s", h=4), axis=AX.X),
             r=[R("a_l24")], w=[R("tinyA")])
    yield
    with P.stage():
        P.do("dve", lambda e: e.reciprocal(out=T[:, 4:8], in_=T[:, 0:4]),
             r=[R("tinyA")], w=[R("tinyA")])
    yield
    with P.stage():
        for hh in range(4):
            P.do("dve", lambda e, hh=hh: e.tensor_scalar(
                out=w24[:, hh * 6:(hh + 1) * 6], in0=l24[:, hh * 6:(hh + 1) * 6],
                scalar1=T[:, 4 + hh:5 + hh], scalar2=None, op0=ALU.mult),
                r=[R("a_l24"), R("tinyA")], w=[R("a_w24")])
    yield
    if "wgt" in C.dbg and t == 0:
        with P.stage():
            P.do("dve", lambda e: e.tensor_copy(out=l24[:], in_=w24[:]),
                 r=[R("a_w24")], w=[R("a_l24")])
        yield
        with P.stage():
            P.dma(C.dbg["wgt"][:], l24[:], r=[R("a_l24")])
        yield
    with P.stage():
        P.do("pe", lambda e: e.transpose(PT[0:24, 0:128], in_=w24[:],
             identity=C.idb[:]), r=[R("a_w24"), R("idb")], w=[R("PT")])
    yield
    with P.stage():
        P.do("dve", lambda e: e.tensor_copy(out=wgtT[:], in_=PT[0:24, 0:128]),
             r=[R("PT")], w=[R("a_wgtT")])
    yield
    fstk = lambda f: f[:].rearrange("p (c n) -> p c n", c=8)[:, :, o:o + 128]
    cstk = lambda b: b.rearrange("p (c n) -> p c n", c=8)
    vstack = vf[:].rearrange("p (c n) -> p c n", c=8)[:, :, 32 + o:32 + o + 128]
    branches = [
        (cstk(b1f), "a_b1f2"),
        (fstk(B.f3[q]), "a_f3%d" % q),
        (fstk(B.f7[q]), "a_f7%d" % q),
        (fstk(B.f31[q]), "a_f31%d" % q),
        (cstk(dfm), "a_dfm2"),
        (vstack, "a_vf%d" % q),
    ]
    for s6, (bstk, bres) in enumerate(branches):
        PP, ppres = (PB, ("PB0", "PB1")) if s6 % 2 == 0 else (PC, ("PC0", "PC1"))
        with P.stage():
            for ch in range(8):
                hh = ch // 2
                P.do("pe", lambda e, ch=ch, hh=hh, s6=s6, PP=PP: e.matmul(
                    PP[:, ch * 128:(ch + 1) * 128],
                    lhsT=C.sel24b[:, hh * 6 + s6:hh * 6 + s6 + 1]
                    .broadcast_to([24, 128]),
                    rhs=wgtT[:], start=True, stop=True),
                    r=[R("sel24b"), R("a_wgtT")], w=[R(ppres[ch // 4])])
        yield
        with P.stage():
            dst, dres = (facc, "a_facc") if s6 == 0 else (tmpc, "a_tmpc")
            P.do("dve", lambda e, bstk=bstk, PP=PP, dst=dst: e.tensor_tensor(
                out=dst[:].rearrange("p (c n) -> p c n", c=8),
                in0=bstk, in1=PP[:, 0:1024].rearrange("p (c n) -> p c n", c=8),
                op=ALU.mult),
                r=[R(bres), R(ppres[0]), R(ppres[1])], w=[R(dres)])
        yield
        if s6 > 0:
            with P.stage():
                P.do("pool", lambda e: e.tensor_tensor(out=facc[:], in0=facc[:],
                     in1=tmpc[:], op=ALU.add),
                     r=[R("a_facc"), R("a_tmpc")], w=[R("a_facc")])
            yield
    with P.stage():
        P.do("dve", lambda e: e.tensor_tensor(out=tmpc[:], in0=facc[:],
             in1=facc[:], op=ALU.mult), r=[R("a_facc")], w=[R("a_tmpc")])
    yield
    with P.stage():
        for hh in range(4):
            for cc in range(2):
                P.do("pe", lambda e, hh=hh, cc=cc: e.matmul(
                    PS[:, 120 + hh:121 + hh],
                    lhsT=tmpc[:, (2 * hh + cc) * 128:(2 * hh + cc + 1) * 128],
                    rhs=C.onesb[:], start=(cc == 0), stop=(cc == 1)),
                    r=[R("a_tmpc"), R("onesb")], w=[R("PS")])
    yield
    with P.stage():
        P.do("act", lambda e, sl=sl: e.activation(out=rr2[:, sl * 4:(sl + 1) * 4],
             in_=PS[:, 120:124], func=AF.Ln, scale=1.0 / 256, bias=1e-5),
             r=[R("PS")], w=[R("a_rr2")])
    yield
    with P.stage():
        P.do("act", lambda e, sl=sl: e.activation(out=rr2[:, sl * 4:(sl + 1) * 4],
             in_=rr2[:, sl * 4:(sl + 1) * 4], func=AF.Exp, scale=-0.5),
             r=[R("a_rr2")], w=[R("a_rr2")])
    yield
    with P.stage():
        P.do("dve", lambda e, sl=sl: e.tensor_copy(out=w24[:, 0:4],
             in_=rr2[:, sl * 4:(sl + 1) * 4]), r=[R("a_rr2")], w=[R("a_w24")])
    yield
    with P.stage():
        for hh in range(4):
            P.do("pe", lambda e, hh=hh: e.transpose(
                PT[0:1, hh * 128:(hh + 1) * 128], in_=w24[:, hh:hh + 1],
                identity=C.idb[:]), r=[R("a_w24"), R("idb")], w=[R("PT")])
    yield
    with P.stage():
        P.do("dve", lambda e: e.tensor_copy(out=B.rrow[0:1, 0:512],
             in_=PT[0:1, 0:512]), r=[R("PT")], w=[R("a_rrow")])
    yield
    with P.stage():
        for hh in range(4):
            P.do("pe", lambda e, hh=hh: e.matmul(
                half(PB, 0)[:, hh * 128:(hh + 1) * 128], lhsT=C.onesrow[:],
                rhs=B.rrow[0:1, hh * 128:(hh + 1) * 128],
                start=True, stop=True),
                r=[R("a_rrow"), R("onesrow")], w=[R("PB0")])
    yield
    with P.stage():
        for ch in range(8):
            P.do("dve", lambda e, ch=ch: e.tensor_tensor(
                out=tmpc[:, ch * 128:(ch + 1) * 128],
                in0=facc[:, ch * 128:(ch + 1) * 128],
                in1=half(PB, 0)[:, (ch // 2) * 128:(ch // 2 + 1) * 128],
                op=ALU.mult),
                r=[R("a_facc"), R("PB0")], w=[R("a_tmpc")])
    yield
    for grp in ((0, 1, 2), (3, 4, 5), (6, 7)):
        with P.stage():
            for k in grp:
                for nn in range(2):
                    P.do("pe", lambda e, k=k, nn=nn: e.matmul(
                        PC[:, nn * 512:(nn + 1) * 512],
                        lhsT=tmpc[:, k * 128:(k + 1) * 128],
                        rhs=owc[k % 3][:, nn * 512:(nn + 1) * 512],
                        start=(k == 0), stop=(k == 7)),
                        r=[R("a_tmpc"), R("a_owc%d" % (k % 3))], w=[R("PC%d" % nn)])
        yield
        if grp[0] == 0:
            with P.stage():
                for kk in (3, 4, 5):
                    P.dma(owc[kk % 3][:], C.ow_s[kk], w=[R("a_owc%d" % (kk % 3))])
            yield
        elif grp[0] == 3:
            with P.stage():
                for kk in (6, 7):
                    P.dma(owc[kk % 3][:], C.ow_s[kk], w=[R("a_owc%d" % (kk % 3))])
            yield
    with P.stage(dma_out=True):
        P.do("dve", lambda e: e.tensor_copy(out=sot[:], in_=PC[:, 0:1024]),
             r=[R("PC0"), R("PC1")], w=[R("a_sot")])
    yield
    with P.stage():
        P.dma(C.s_s[t], sot[:], r=[R("a_sot")])
        if "s" in C.dbg and t == 0:
            P.dma(C.dbg["s"][:], sot[:], r=[R("a_sot")])
    yield


def st96ap(st96, stat):
    return st96[:].rearrange("p (x f) -> p x f", f=4)[:, :, stat:stat + 1]


def psblock(PS, base):
    return PS[:, base:base + 24]


def phaseB(C, es):
    nc, P = C.nc, C.P
    R = P.res
    sb = lambda n, sh, dt=F32: es.enter_context(nc.sbuf_tensor(n, sh, dt))
    wgu = sb("b_wgu", [128, 8 * 6144], BF16)
    wdb = sb("b_wdb", [128, 24 * 1024], BF16)
    xnb = sb("b_xnb", [128, 4096], BF16)
    hst = sb("b_hst", [128, 24 * 512], BF16)
    sg = sb("b_sg", [128, 1024], BF16)
    xb = [sb("b_x0", [128, 1024]), sb("b_x1", [128, 1024])]
    sbf = [sb("b_s0", [128, 1024]), sb("b_s1", [128, 1024])]
    gb = [sb("b_gb0", [128, 1024], BF16), sb("b_gb1", [128, 1024], BF16)]
    tm = sbf
    ob = xb
    PA, PB, PC, PS = C.PA, C.PB, C.PC, C.PS
    with P.stage():
        P.dma(wgu[:].rearrange("p (k n) -> p k n", k=8),
              C.wgu_s.rearrange("k p n -> p k n"), w=[R("b_wgu")])
        P.dma(wdb[:].rearrange("p (k n) -> p k n", k=24),
              C.wdT_s.rearrange("k p n -> p k n"), w=[R("b_wdb")])
    for st in range(NST):
        t0 = st * 4
        with P.stage():
            for i in range(4):
                P.dma(xnb[:].rearrange("p (g t) -> p g t", g=8)
                      [:, :, i * 128:(i + 1) * 128],
                      C.xn_s[t0 + i].rearrange("p (g t) -> p g t", g=8),
                      w=[R("b_xnb")])
        for m in range(24):
            q = m % 2
            PP = PA if q == 0 else PB
            pr = ("PA0", "PA1") if q == 0 else ("PB0", "PB1")
            with P.stage():
                for k in range(8):
                    P.do("pe", lambda e, m=m, k=k, PP=PP: e.matmul(
                        PP[:, 0:512],
                        lhsT=wgu[:, k * 6144 + m * 128:k * 6144 + (m + 1) * 128],
                        rhs=xnb[:, k * 512:(k + 1) * 512],
                        start=(k == 0), stop=(k == 7)),
                        r=[R("b_wgu"), R("b_xnb")], w=[R(pr[0])])
                for k in range(8):
                    P.do("pe", lambda e, m=m, k=k, PP=PP: e.matmul(
                        PP[:, 512:1024],
                        lhsT=wgu[:, k * 6144 + (24 + m) * 128:k * 6144 + (25 + m) * 128],
                        rhs=xnb[:, k * 512:(k + 1) * 512],
                        start=(k == 0), stop=(k == 7)),
                        r=[R("b_wgu"), R("b_xnb")], w=[R(pr[1])])
            with P.stage():
                P.do("act", lambda e, PP=PP, q=q: e.activation(
                    out=sg[:, q * 512:(q + 1) * 512], in_=PP[:, 0:512],
                    func=AF.Silu, scale=1.0 / 127), r=[R(pr[0])], w=[R("b_sg%d" % q)])
            with P.stage():
                P.do("dve", lambda e, m=m, PP=PP, q=q: e.tensor_tensor(
                    out=hst[:, m * 512:(m + 1) * 512], in0=PP[:, 512:1024],
                    in1=sg[:, q * 512:(q + 1) * 512], op=ALU.mult),
                    r=[R(pr[1]), R("b_sg%d" % q)], w=[R("b_hst")])
        if st == 0:
            with P.stage():
                P.dma(xb[0][:], C.x_h[0:128, :], w=[R("b_x0")])
                P.dma(sbf[0][:], C.s_s[0], w=[R("b_s0")])
        for i in range(4):
            t = t0 + i
            p = t % 2
            with P.stage():
                for cch in range(24):
                    for nn in range(2):
                        P.do("pe", lambda e, cch=cch, nn=nn, i=i: e.matmul(
                            PC[:, nn * 512:(nn + 1) * 512],
                            lhsT=hst[:, cch * 512 + i * 128:cch * 512 + i * 128 + 128],
                            rhs=wdb[:, cch * 1024 + nn * 512:cch * 1024 + (nn + 1) * 512],
                            start=(cch == 0), stop=(cch == 23)),
                            r=[R("b_hst"), R("b_wdb")], w=[R("PC%d" % nn)])
            if t + 1 < NT:
                tn = t + 1
                pn = tn % 2
                with P.stage():
                    P.dma(xb[pn][:], C.x_h[tn * 128:(tn + 1) * 128, :],
                          w=[R("b_x%d" % pn)])
                    P.dma(sbf[pn][:], C.s_s[tn], w=[R("b_s%d" % pn)])
            with P.stage():
                P.do("act", lambda e, p=p: e.activation(out=gb[p][:], in_=PC[:, 0:1024],
                     func=AF.Silu, scale=1.0 / (127.0 * 127.0)),
                     r=[R("PC0"), R("PC1")], w=[R("b_gb%d" % p)])
            if "g" in C.dbg and t == 0:
                with P.stage():
                    P.do("dve", lambda e, p=p: e.tensor_copy(out=tm[p][:], in_=gb[p][:]),
                         r=[R("b_gb%d" % p)], w=[R("b_s%d" % p)])
                with P.stage():
                    P.dma(C.dbg["g"][:], tm[p][:], r=[R("b_s%d" % p)])
            with P.stage():
                P.do("dve", lambda e, p=p: e.tensor_tensor(out=tm[p][:], in0=sbf[p][:],
                     in1=gb[p][:], op=ALU.mult), r=[R("b_s%d" % p), R("b_gb%d" % p)],
                     w=[R("b_s%d" % p)])
            with P.stage(dma_out=True):
                P.do("pool", lambda e, p=p: e.tensor_tensor(out=ob[p][:], in0=tm[p][:],
                     in1=xb[p][:], op=ALU.add), r=[R("b_s%d" % p), R("b_x%d" % p)],
                     w=[R("b_x%d" % p)])
            with P.stage():
                P.dma(C.out_h[t * 128:(t + 1) * 128, :], ob[p][:], r=[R("b_x%d" % p)])


def kernel(**inputs):
    hp = host_prep(inputs)
    nc, dbg_names = build_program()
    x_full = np.asarray(inputs["L0_attn_hidden_states"])
    in_maps = []
    for c in range(N_CORES):
        m = dict(hp)
        m["x"] = np.ascontiguousarray(x_full[c])
        in_maps.append(m)
    res = run_bass_kernel_spmd(nc, in_maps, core_ids=list(range(N_CORES)))
    out = np.stack([res.results[c]["out"] for c in range(N_CORES)], axis=0)
    if DEBUG:
        kernel._dbg = [{k: res.results[c].get("dbg_" + k) for k in DEBUG}
                       for c in range(N_CORES)]
    return out
